# version 03
# baseline (speedup 1.0000x reference)
import traceback
import numpy as np
from contextlib import ExitStack

import concourse.bass as bass
import concourse.mybir as mybir
from concourse.bass_utils import run_bass_kernel_spmd

F32 = mybir.dt.float32
BF16 = mybir.dt.bfloat16
I32 = mybir.dt.int32
ALU = mybir.AluOpType
ACT = mybir.ActivationFunctionType
AX = mybir.AxisListType

NCORES = 8
TC = 2048
D = 1024
NLAYERS = 4
SKIP_G = False
DBG = None

ENGINES = ("te", "ve", "se", "ge", "sy")
RQ = 8
QSEMS = tuple(f"{e}q{r}" for e in ENGINES for r in range(RQ))
ALL_SEMS = ENGINES + QSEMS + ("cc",)


class Prog:
    def __init__(self, nc):
        self.nc = nc
        self.thunks = {e: [] for e in ENGINES}
        self.counts = {s: 0 for s in ALL_SEMS}
        self.clock = {e: {s: 0 for s in ALL_SEMS} for e in ENGINES}
        self.dirty = {e: set() for e in ENGINES}
        self.qseq = {e: 0 for e in ENGINES}
        self.bufs = {}
        self.sems = {}

    def reg(self, name, ready=False):
        self.bufs[name] = {"w": ("sy", 0) if ready else None, "r": []}
        return name

    def emit(self, eng, fn, ins=(), outs=(), dma=False, cc=False):
        ins = [i for i in ins if i]
        outs = [o for o in outs if o]
        waits = {}
        drain = False

        def need(src, val):
            nonlocal drain
            if src == eng:
                return
            if self.clock[eng].get(src, 0) < val:
                waits[src] = max(waits.get(src, 0), val)

        for b in ins:
            st = self.bufs[b]
            if st["w"]:
                src, val = st["w"]
                if src == eng and b in self.dirty[eng]:
                    drain = True
                need(src, val)
        for b in outs:
            st = self.bufs[b]
            for src, val in ([st["w"]] if st["w"] else []) + st["r"]:
                if src == eng and b in self.dirty[eng]:
                    drain = True
                need(src, val)
        if cc:
            sem_key, step = "cc", 1
        elif dma:
            i = self.qseq[eng]
            self.qseq[eng] += 1
            sem_key, step = f"{eng}q{i % RQ}", 16
            if i >= RQ:
                gen_prev = 16 * (i // RQ)
                if self.clock[eng].get(sem_key, 0) < gen_prev:
                    waits[sem_key] = max(waits.get(sem_key, 0), gen_prev)
        else:
            sem_key, step = eng, 1
        self.counts[sem_key] += step
        val_after = self.counts[sem_key]
        waits_l = sorted(waits.items())
        for src, val in waits_l:
            self.clock[eng][src] = val
        sems = self.sems

        tb = traceback.extract_stack(limit=8)

        def thunk(h, waits_l=waits_l, drain=drain, fn=fn, sem_key=sem_key, step=step):
            for src, val in waits_l:
                h.wait_ge(sems[src], val)
            if drain:
                h.drain()
            try:
                fn(h).then_inc(sems[sem_key], step)
            except Exception:
                print("EMIT SITE:")
                for fr in tb:
                    print("  ", fr.filename, fr.lineno, fr.line)
                raise

        self.thunks[eng].append(thunk)
        if drain:
            self.dirty[eng].clear()
        for b in ins:
            self.bufs[b]["r"].append((sem_key, val_after))
            self.dirty[eng].add(b)
        for b in outs:
            self.bufs[b]["w"] = (sem_key, val_after)
            self.bufs[b]["r"] = []
            self.dirty[eng].add(b)
        return val_after

    def barrier(self):
        snapq = dict(self.counts)
        for e in ENGINES:
            self.counts[e] += 1
        snape = dict(self.counts)
        sems = self.sems
        for e in ENGINES:
            def thunk(h, e=e):
                try:
                    h.drain()
                except Exception:
                    pass
                h.sem_inc(sems[e], 1)
                for s in ALL_SEMS:
                    tgt = snape[s] if s in ENGINES else snapq[s]
                    if tgt > 0 and s != e:
                        h.wait_ge(sems[s], tgt)
            self.thunks[e].append(thunk)
        for e in ENGINES:
            for s in ALL_SEMS:
                self.clock[e][s] = max(self.clock[e][s],
                                       snape[s] if s in ENGINES else snapq[s])
            self.dirty[e].clear()
        for b in self.bufs.values():
            if b["w"] is not None:
                b["r"] = []

    def finish(self):
        snap = dict(self.counts)
        sems = self.sems

        def thunk(h):
            for s in ALL_SEMS:
                if snap[s] > 0:
                    h.wait_ge(sems[s], snap[s])
        self.thunks["sy"].append(thunk)


def _col(v, rows=128):
    v = np.ascontiguousarray(np.asarray(v, np.float32))
    return np.ascontiguousarray(v.reshape(v.shape[0] // rows, rows).T)


def _host_prep(inputs):
    x_full = np.asarray(inputs["L0_attn_x"], np.float32)
    per_core = []
    eye = np.eye(128, dtype=np.float32)
    tri = np.triu(np.ones((128, 128), np.float32))
    inv_freq = 1.0 / 10000.0 ** (np.arange(0, 64, 2, dtype=np.float64) / 64.0)
    for c in range(NCORES):
        b, h = c // 2, c % 2
        m = {"x": np.ascontiguousarray(x_full[b, h * TC:(h + 1) * TC, :]),
             "eye": eye, "tri": tri,
             "hsel": np.full((128, 1), float(h), np.float32)}
        pos = np.arange(h * TC, (h + 1) * TC, dtype=np.float64)
        fr = pos[:, None] * inv_freq[None, :]
        m["rope_cos"] = np.cos(fr).astype(np.float32)
        m["rope_sin"] = np.sin(fr).astype(np.float32)
        if NLAYERS >= 1:
            m["w_attn0"] = np.asarray(inputs["L0_attn_w_attn"], np.float32)
            m["w_proj0"] = np.asarray(inputs["L0_attn_w_proj"], np.float32)
            m["w_ffn0"] = np.asarray(inputs["L0_ffn_weight"], np.float32)
            m["ln_g0"] = _col(inputs["L0_ffn_ln_gamma"])
            m["ln_b0"] = _col(inputs["L0_ffn_ln_beta"])
        if NLAYERS >= 2:
            m["w_qkv1"] = np.asarray(inputs["L1_attn_w_qkv"], np.float32)
            m["b_qkv1"] = _col(inputs["L1_attn_b_qkv"])
            m["b_qkv1r"] = np.asarray(inputs["L1_attn_b_qkv"], np.float32).reshape(1, 3072)
            m["w_fc1"] = np.asarray(inputs["L1_attn_w_fc"], np.float32)
            m["b_fc1"] = _col(inputs["L1_attn_b_fc"])
            m["w_r1"] = np.ascontiguousarray(np.asarray(inputs["L1_ffn_W_r"], np.float32).T)
            m["w_k1"] = np.ascontiguousarray(np.asarray(inputs["L1_ffn_W_k"], np.float32).T)
            m["w_v1"] = np.ascontiguousarray(np.asarray(inputs["L1_ffn_W_v"], np.float32).T)
            m["mu_r1"] = _col(inputs["L1_ffn_mu_r"])
            m["mu_k1"] = _col(inputs["L1_ffn_mu_k"])
        if NLAYERS >= 3:
            for nm, key in (("wq2", "L2_attn_Wq"), ("wk2", "L2_attn_Wk"),
                            ("wv2", "L2_attn_Wv"), ("wo2", "L2_attn_Wo")):
                m[nm] = np.asarray(inputs[key], np.float32)
            for nm, key in (("bq2", "L2_attn_bq"), ("bk2", "L2_attn_bk"),
                            ("bv2", "L2_attn_bv"), ("bo2", "L2_attn_bo")):
                m[nm] = _col(inputs[key])
            m["clip2"] = np.array([[inputs["L2_ffn_clip_x"][0],
                                    inputs["L2_ffn_clip_down"][0]]], np.float32)
            wg = np.asarray(inputs["L2_ffn_w_gate"], np.float32)
            wu = np.asarray(inputs["L2_ffn_w_up"], np.float32)
            wd = np.asarray(inputs["L2_ffn_w_down"], np.float32)
            m["wg2"], m["wu2"], m["wd2"] = wg, wu, wd
            m["wg2T"] = np.ascontiguousarray(wg.T)
            m["wu2T"] = np.ascontiguousarray(wu.T)
            m["wd2T"] = np.ascontiguousarray(wd.T)
        if NLAYERS >= 4:
            nw = np.asarray(inputs["L3_attn_norm_w"], np.float32)
            m["nw3"] = np.stack([_col(nw[0]), _col(nw[1])])
            ipw = np.asarray(inputs["L3_attn_in_proj_w"], np.float32)
            m["inw3_l0"] = np.ascontiguousarray(ipw[0].T)
            m["inw3_l1"] = np.ascontiguousarray(ipw[1].T)
            cw = np.asarray(inputs["L3_attn_conv_w"], np.float32)
            m["cw3"] = np.ascontiguousarray(
                cw.reshape(2, 16, 128, 4).transpose(0, 2, 1, 3).reshape(2, 128, 64))
            cb = np.asarray(inputs["L3_attn_conv_b"], np.float32)
            m["cb3"] = np.stack([_col(cb[0]), _col(cb[1])])
            xpw = np.asarray(inputs["L3_attn_x_proj_w"], np.float32)
            m["xpw3_l0"] = np.ascontiguousarray(xpw[0].T)
            m["xpw3_l1"] = np.ascontiguousarray(xpw[1].T)
            dtw = np.asarray(inputs["L3_attn_dt_proj_w"], np.float32)
            m["dtw3_l0"] = np.ascontiguousarray(dtw[0].T)
            m["dtw3_l1"] = np.ascontiguousarray(dtw[1].T)
            dtb = np.asarray(inputs["L3_attn_dt_proj_b"], np.float32)
            m["dtb3"] = np.stack([_col(dtb[0]), _col(dtb[1])])
            al = np.asarray(inputs["L3_attn_A_log"], np.float32)
            m["al3"] = np.ascontiguousarray(
                al.reshape(2, 16, 128, 16).transpose(0, 2, 1, 3).reshape(2, 128, 256))
            dp = np.asarray(inputs["L3_attn_D_param"], np.float32)
            m["dp3"] = np.stack([_col(dp[0]), _col(dp[1])])
            opw = np.asarray(inputs["L3_attn_out_proj_w"], np.float32)
            m["opw3_l0"] = np.ascontiguousarray(opw[0].T)
            m["opw3_l1"] = np.ascontiguousarray(opw[1].T)
            m["w13"] = np.ascontiguousarray(np.asarray(inputs["L3_ffn_w1"], np.float32)[c])
            m["b13"] = _col(np.asarray(inputs["L3_ffn_b1"], np.float32)[c])
            m["w23"] = np.ascontiguousarray(np.asarray(inputs["L3_ffn_w2"], np.float32)[c])
            m["b23"] = _col(np.asarray(inputs["L3_ffn_b2"], np.float32)[c])
        per_core.append(m)
    return per_core


class KB:
    def __init__(self, nc, stk):
        self.nc = nc
        self.stk = stk
        self.p = Prog(nc)
        self.ap = {}
        self.sb = {}
        for s in ALL_SEMS:
            self.p.sems[s] = stk.enter_context(nc.semaphore(s))

    def din(self, name, shape, dt=F32):
        self.ap[name] = self.nc.dram_tensor(name, list(shape), dt, kind="ExternalInput").ap()
        self.p.reg("d:" + name, ready=True)
        return self.ap[name]

    def dout(self, name, shape, dt=F32):
        self.ap[name] = self.nc.dram_tensor(name, list(shape), dt, kind="ExternalOutput").ap()
        self.p.reg("d:" + name)
        return self.ap[name]

    def dram(self, name, shape, dt=F32):
        self.ap[name] = self.nc.dram_tensor(name, list(shape), dt).ap()
        self.p.reg("d:" + name)
        return self.ap[name]

    def sbuf(self, name, shape, dt=F32, scope=None):
        t = (scope or self.stk).enter_context(self.nc.sbuf_tensor(name, list(shape), dt))
        self.sb[name] = t
        self.p.reg(name)
        return t

    def psum(self, name, shape, dt=F32, scope=None):
        t = (scope or self.stk).enter_context(self.nc.psum_tensor(name, list(shape), dt))
        self.sb[name] = t
        self.p.reg(name)
        return t

    def dma(self, eng, dst, src, ins, outs):
        self.p.emit(eng, lambda h: h.dma_start(out=dst, in_=src), ins=ins, outs=outs, dma=True)

    def mm(self, out, lhsT, rhs, ins, outs, start=True, stop=True):
        self.p.emit("te", lambda h: h.matmul(out, lhsT, rhs, start=start, stop=stop),
                    ins=ins, outs=outs)

    def tr(self, out, in_, ins, outs, bf=False):
        eye = self.sb["EYEB" if bf else "EYE"]
        nm = "EYEB" if bf else "EYE"
        self.p.emit("te", lambda h: h.transpose(out, in_, eye[:]),
                    ins=list(ins) + [nm], outs=outs)

    def ve(self, fn, ins=(), outs=()):
        self.p.emit("ve", fn, ins=ins, outs=outs)

    def se(self, fn, ins=(), outs=()):
        self.p.emit("se", fn, ins=ins, outs=outs)

    def ge(self, fn, ins=(), outs=()):
        self.p.emit("ge", fn, ins=ins, outs=outs)

    def cc(self, kind, op, groups, in_ap, out_ap, ins, outs):
        self.p.emit("ge", lambda h: h.collective_compute(
            kind, op, replica_groups=groups, ins=[in_ap], outs=[out_ap]),
            ins=ins, outs=outs, cc=True)

    def bcast_row(self, psum_slice, row_ap, ins, outs):
        self.mm(psum_slice, self.sb["ONESROW"][:], row_ap,
                ins=list(ins) + ["ONESROW"], outs=outs)


PAIRS = [[0, 1], [2, 3], [4, 5], [6, 7]]
ALL8 = [list(range(8))]


def _build(nc, stk):
    k = KB(nc, stk)
    p = k.p
    ap = k.ap

    k.din("x", [TC, D])
    k.din("eye", [128, 128])
    k.din("tri", [128, 128])
    k.din("rope_cos", [TC, 32])
    k.din("rope_sin", [TC, 32])
    k.din("hsel", [128, 1])
    k.dout("y", [TC, D])

    if NLAYERS >= 1:
        k.din("w_attn0", [D, 3 * D]); k.din("w_proj0", [D, D]); k.din("w_ffn0", [D, D])
        k.din("ln_g0", [128, 8]); k.din("ln_b0", [128, 8])
    if NLAYERS >= 2:
        k.din("w_qkv1", [D, 3 * D]); k.din("b_qkv1", [128, 24]); k.din("b_qkv1r", [1, 3 * D])
        k.din("w_fc1", [D, D]); k.din("b_fc1", [128, 8])
        k.din("w_r1", [D, D]); k.din("w_k1", [D, D]); k.din("w_v1", [D, D])
        k.din("mu_r1", [128, 8]); k.din("mu_k1", [128, 8])
    if NLAYERS >= 3:
        for nm in ("wq2", "wk2", "wv2", "wo2"):
            k.din(nm, [D, D])
        for nm in ("bq2", "bk2", "bv2", "bo2"):
            k.din(nm, [128, 8])
        k.din("clip2", [1, 2])
        k.din("wg2", [2816, D]); k.din("wu2", [2816, D]); k.din("wd2", [D, 2816])
        k.din("wg2T", [D, 2816]); k.din("wu2T", [D, 2816]); k.din("wd2T", [2816, D])
    if NLAYERS >= 4:
        k.din("nw3", [2, 128, 8])
        k.din("inw3_l0", [D, 4096]); k.din("inw3_l1", [D, 4096])
        k.din("cw3", [2, 128, 64]); k.din("cb3", [2, 128, 16])
        k.din("xpw3_l0", [2048, 96]); k.din("xpw3_l1", [2048, 96])
        k.din("dtw3_l0", [64, 2048]); k.din("dtw3_l1", [64, 2048])
        k.din("dtb3", [2, 128, 16])
        k.din("al3", [2, 128, 256]); k.din("dp3", [2, 128, 16])
        k.din("opw3_l0", [2048, D]); k.din("opw3_l1", [2048, D])
        k.din("w13", [D, 4096]); k.din("b13", [128, 32])
        k.din("w23", [4096, D]); k.din("b23", [128, 8])

    k.dram("xr", [128, 8 * TC])
    k.dram("s_sp", [128, 8 * TC], BF16)
    k.dram("g_sp", [128, 8 * TC], BF16)

    XNB = k.sbuf("XNB", [128, 8 * TC], BF16)
    ONES = k.sbuf("ONESB", [128, 1], BF16)
    ONESF = k.sbuf("ONESF", [128, 1], F32)
    ONESROW = k.sbuf("ONESROW", [1, 128], F32)
    EYE = k.sbuf("EYE", [128, 128], F32)
    EYEB = k.sbuf("EYEB", [128, 128], BF16)
    TRIF = k.sbuf("TRIF", [128, 128], F32)
    TRIB = k.sbuf("TRIB", [128, 128], BF16)
    HSEL = k.sbuf("HSEL", [128, 1], F32)
    ROWS = k.sbuf("ROWS", [128, 1536], F32)
    SQB = k.sbuf("SQB", [128, 4096], BF16)
    XST = k.sbuf("XST", [128, 512], F32)
    XST2 = k.sbuf("XST2", [128, 512], F32)
    PRODB = k.sbuf("PRODB", [128, 512], F32)
    GST = k.sbuf("GST", [128, 2048], BF16)
    SST = k.sbuf("SST", [128, 2048], BF16)
    TOK = k.sbuf("TOK", [128, 1024], F32)
    TOK2 = k.sbuf("TOK2", [128, 1024], F32)
    WS0 = k.sbuf("WS0", [128, 1024], F32)
    WS1 = k.sbuf("WS1", [128, 1024], F32)
    WB0 = k.sbuf("WB0", [128, 1024], BF16)
    WB1 = k.sbuf("WB1", [128, 1024], BF16)

    PA = k.psum("PA", [128, 2048], F32)
    PB = k.psum("PB", [128, 512], F32)
    PC = k.psum("PC", [128, 512], F32)
    PD = k.psum("PD", [128, 512], F32)
    PE = k.psum("PE", [128, 512], F32)

    k.ve(lambda h: h.memset(ONES[:], 1.0), outs=["ONESB"])
    k.ve(lambda h: h.memset(ONESF[:], 1.0), outs=["ONESF"])
    k.ve(lambda h: h.memset(ONESROW[:], 1.0), outs=["ONESROW"])
    k.dma("sy", EYE[:], ap["eye"], ins=["d:eye"], outs=["EYE"])
    k.dma("sy", HSEL[:], ap["hsel"], ins=["d:hsel"], outs=["HSEL"])
    k.dma("sy", TRIF[:], ap["tri"], ins=["d:tri"], outs=["TRIF"])
    k.ve(lambda h: h.tensor_copy(TRIB[:], TRIF[:]), ins=["TRIF"], outs=["TRIB"])
    k.ve(lambda h: h.tensor_copy(EYEB[:], EYE[:]), ins=["EYE"], outs=["EYEB"])

    for tt in range(16):
        tok, tokn = (TOK, "TOK") if tt % 2 == 0 else (TOK2, "TOK2")
        k.dma("sy", tok[:], ap["x"][tt * 128:(tt + 1) * 128, :], ins=["d:x"], outs=[tokn])
        st, stn = (XST, "XST") if tt % 2 == 0 else (XST2, "XST2")
        for dc in range(8):
            bank, bn = (PB, "PB") if dc % 2 == 0 else (PC, "PC")
            k.tr(bank[:, 0:128], tok[:, dc * 128:(dc + 1) * 128], ins=[tokn], outs=[bn])
            k.ve(lambda h, st=st, bank=bank, dc=dc: h.tensor_copy(
                st[:, 0:128], bank[:, 0:128]), ins=[bn], outs=[stn])
            k.dma("ge", ap["xr"].rearrange("p (a t) -> p a t", t=TC)[:, dc:dc + 1,
                                                                    tt * 128:(tt + 1) * 128],
                  st[:, 0:128][:, None, :], ins=[stn], outs=["d:xr"])

    def ln_stats_chunk(t0, eps, into_rows):
        for dc in range(8):
            k.mm(PB[0:1, 0:512], ONES[:], XNB[:, dc * TC + t0: dc * TC + t0 + 512],
                 ins=["ONESB", "XNB"], outs=["PB"], start=(dc == 0), stop=(dc == 7))
        for dc in range(8):
            k.se(lambda h, dc=dc: h.activation(
                SQB[:, dc * 512:(dc + 1) * 512],
                XNB[:, dc * TC + t0: dc * TC + t0 + 512], ACT.Square),
                ins=["XNB"], outs=["SQB"])
        for dc in range(8):
            k.mm(PC[0:1, 0:512], ONES[:], SQB[:, dc * 512:(dc + 1) * 512],
                 ins=["ONESB", "SQB"], outs=["PC"], start=(dc == 0), stop=(dc == 7))
        A = ROWS[0:1, 0:512]
        Bq = ROWS[0:1, 512:1024]
        Cq = ROWS[0:1, 1024:1536]
        k.ve(lambda h: h.tensor_scalar_mul(A, PB[0:1, 0:512], 1.0 / D),
             ins=["PB"], outs=["ROWS"])
        k.ve(lambda h: h.tensor_scalar_mul(Bq, PC[0:1, 0:512], 1.0 / D),
             ins=["PC"], outs=["ROWS"])
        k.ve(lambda h: h.tensor_tensor(Cq, A, A, ALU.mult), ins=["ROWS"], outs=["ROWS"])
        k.ve(lambda h: h.tensor_tensor(Bq, Bq, Cq, ALU.subtract), ins=["ROWS"], outs=["ROWS"])
        k.ve(lambda h, eps=eps: h.tensor_scalar_add(Bq, Bq, eps), ins=["ROWS"], outs=["ROWS"])
        k.ve(lambda h: h.reciprocal(Bq, Bq), ins=["ROWS"], outs=["ROWS"])
        k.se(lambda h: h.activation(Bq, Bq, ACT.Sqrt), ins=["ROWS"], outs=["ROWS"])
        k.ve(lambda h: h.tensor_tensor(Cq, A, Bq, ALU.mult), ins=["ROWS"], outs=["ROWS"])
        k.ve(lambda h: h.tensor_scalar_mul(Cq, Cq, -1.0), ins=["ROWS"], outs=["ROWS"])

    def emit_layernorm(eps=1e-5):
        for dc in range(8):
            for tc4 in range(4):
                st, stn = (XST, "XST") if (dc * 4 + tc4) % 2 == 0 else (XST2, "XST2")
                k.dma("sy", st[:],
                      ap["xr"].rearrange("p (a t) -> p a t", t=TC)
                      [:, dc, tc4 * 512:(tc4 + 1) * 512], ins=["d:xr"], outs=[stn])
                k.ve(lambda h, st=st, dc=dc, tc4=tc4: h.tensor_copy(
                    XNB[:, dc * TC + tc4 * 512: dc * TC + (tc4 + 1) * 512], st[:]),
                    ins=[stn], outs=["XNB"])
        for tc4 in range(4):
            t0 = tc4 * 512
            ln_stats_chunk(t0, eps, None)
            k.bcast_row(PB[:, 0:512], ROWS[0:1, 512:1024], ins=["ROWS"], outs=["PB"])
            k.bcast_row(PC[:, 0:512], ROWS[0:1, 1024:1536], ins=["ROWS"], outs=["PC"])
            for dc in range(8):
                k.ve(lambda h, dc=dc, t0=t0: h.tensor_tensor(
                    SQB[:, dc * 512:(dc + 1) * 512],
                    XNB[:, dc * TC + t0: dc * TC + t0 + 512], PB[:, 0:512], ALU.mult),
                    ins=["XNB", "PB"], outs=["SQB"])
            for dc in range(8):
                k.ve(lambda h, dc=dc, t0=t0: h.tensor_tensor(
                    XNB[:, dc * TC + t0: dc * TC + t0 + 512],
                    SQB[:, dc * 512:(dc + 1) * 512], PC[:, 0:512], ALU.add),
                    ins=["SQB", "PC"], outs=["XNB"])

    def emit_combine():
        for dc in range(8):
            k.dma("sy", SST[:], ap["s_sp"].rearrange("p (a t) -> p a t", t=TC)[:, dc],
                  ins=["d:s_sp"], outs=["SST"])
            k.dma("sy", GST[:], ap["g_sp"].rearrange("p (a t) -> p a t", t=TC)[:, dc],
                  ins=["d:g_sp"], outs=["GST"])
            for q in range(4):
                st, stn = (XST, "XST") if q % 2 == 0 else (XST2, "XST2")
                k.dma("sy", st[:],
                      ap["xr"].rearrange("p (a t) -> p a t", t=TC)
                      [:, dc, q * 512:(q + 1) * 512], ins=["d:xr"], outs=[stn])
                k.ve(lambda h, st=st, q=q: h.tensor_tensor(
                    PRODB[:, 0:512], SST[:, q * 512:(q + 1) * 512],
                    GST[:, q * 512:(q + 1) * 512], ALU.mult),
                    ins=["SST", "GST"], outs=["PRODB"])
                k.ve(lambda h, st=st: h.tensor_tensor(
                    st[:], st[:], PRODB[:, 0:512], ALU.add), ins=[stn, "PRODB"], outs=[stn])
                k.dma("ge", ap["xr"].rearrange("p (a t) -> p a t", t=TC)
                      [:, dc, q * 512:(q + 1) * 512], st[:], ins=[stn], outs=["d:xr"])

    def load_w_block(src_name, m0, mwid, idx, kdim=1024):
        wsb, wsn = (WS0, "WS0") if idx % 2 == 0 else (WS1, "WS1")
        wbb, wbn = (WB0, "WB0") if idx % 2 == 0 else (WB1, "WB1")
        nk = kdim // 128
        src = ap[src_name].rearrange("(a p) m -> p a m", p=128)
        k.dma("sy", wsb[:, 0:nk * mwid].rearrange("p (a m) -> p a m", m=mwid),
              src[:, :, m0:m0 + mwid], ins=["d:" + src_name], outs=[wsn])
        k.ge(lambda h: h.tensor_copy(wbb[:, 0:nk * mwid], wsb[:, 0:nk * mwid]),
             ins=[wsn], outs=[wbn])
        return wbb, wbn, nk

    def dump_bf16_to_xr(src_ap_name):
        for dc in range(8):
            k.dma("sy", GST[:], ap[src_ap_name].rearrange("p (a t) -> p a t", t=TC)[:, dc],
                  ins=["d:" + src_ap_name], outs=["GST"])
            for q in range(4):
                st, stn = (XST, "XST") if q % 2 == 0 else (XST2, "XST2")
                k.ve(lambda h, st=st, q=q: h.tensor_copy(st[:], GST[:, q * 512:(q + 1) * 512]),
                     ins=["GST"], outs=[stn])
                k.dma("ge", ap["xr"].rearrange("p (a t) -> p a t", t=TC)
                      [:, dc, q * 512:(q + 1) * 512], st[:], ins=[stn], outs=["d:xr"])

    def dump_xnb_to_xr():
        for dc in range(8):
            for q in range(4):
                st, stn = (XST, "XST") if q % 2 == 0 else (XST2, "XST2")
                k.ve(lambda h, st=st, dc=dc, q=q: h.tensor_copy(
                    st[:], XNB[:, dc * TC + q * 512: dc * TC + (q + 1) * 512]),
                    ins=["XNB"], outs=[stn])
                k.dma("ge", ap["xr"].rearrange("p (a t) -> p a t", t=TC)
                      [:, dc, q * 512:(q + 1) * 512], st[:], ins=[stn], outs=["d:xr"])

    if NLAYERS >= 1:
        emit_layernorm()
        if DBG == "xn":
            dump_xnb_to_xr()
        l0 = ExitStack()
        XL = k.sbuf("XL", [128, 8 * TC], BF16, scope=l0)
        WQ0 = k.sbuf("WQ0", [128, 8192], BF16, scope=l0)
        WQI = k.sbuf("WQI", [128, 1024], I32, scope=l0)
        XQ0 = k.sbuf("XQ0", [128, 4096], BF16, scope=l0)
        COLS = k.sbuf("COLS", [128, 16], F32, scope=l0)
        SCL = k.sbuf("SCL", [1, 128], F32, scope=l0)
        LNG = k.sbuf("LNG", [128, 8], F32, scope=l0)
        LNB = k.sbuf("LNB", [128, 8], F32, scope=l0)
        k.dma("sy", LNG[:], ap["ln_g0"], ins=["d:ln_g0"], outs=["LNG"])
        k.dma("sy", LNB[:], ap["ln_b0"], ins=["d:ln_b0"], outs=["LNB"])
        k.dram("cc_gam_in", [1, 128])
        k.dram("cc_gam_out", [1, 128])
        k.dram("cc_s0_in", [128, 520])
        k.dram("cc_s0_out", [2, 128, 520])

        run_g = (DBG in (None, "g", "s")) and not SKIP_G
        run_attn = DBG in (None, "s", "y", "S")
        run_ph2 = DBG != "S"
        for tc4 in range(4) if run_g else []:
            t0 = tc4 * 512
            ln_stats_chunk(t0, 1e-3, None)
            k.bcast_row(PB[:, 0:512], ROWS[0:1, 512:1024], ins=["ROWS"], outs=["PB"])
            k.bcast_row(PC[:, 0:512], ROWS[0:1, 1024:1536], ins=["ROWS"], outs=["PC"])
            for dc in range(8):
                k.ve(lambda h, dc=dc, t0=t0: h.tensor_tensor(
                    SQB[:, dc * 512:(dc + 1) * 512],
                    XNB[:, dc * TC + t0: dc * TC + t0 + 512], PB[:, 0:512], ALU.mult),
                    ins=["XNB", "PB"], outs=["SQB"])
            for dc in range(8):
                k.ve(lambda h, dc=dc: h.tensor_tensor(
                    SQB[:, dc * 512:(dc + 1) * 512],
                    SQB[:, dc * 512:(dc + 1) * 512], PC[:, 0:512], ALU.add),
                    ins=["SQB", "PC"], outs=["SQB"])
            for dc in range(8):
                k.ve(lambda h, dc=dc, t0=t0: h.tensor_scalar(
                    XL[:, dc * TC + t0: dc * TC + t0 + 512],
                    SQB[:, dc * 512:(dc + 1) * 512],
                    LNG[:, dc:dc + 1], LNB[:, dc:dc + 1], ALU.mult, ALU.add),
                    ins=["SQB", "LNG", "LNB"], outs=["XL"])
        if run_g:
            for q in range(8):
                k.ve(lambda h, q=q: h.reduce_max(COLS[:, 1:2], XL[:, q * TC:(q + 1) * TC],
                                                 AX.X, apply_absolute_value=True),
                     ins=["XL"], outs=["COLS"])
                if q == 0:
                    k.ve(lambda h: h.tensor_copy(COLS[:, 0:1], COLS[:, 1:2]),
                         ins=["COLS"], outs=["COLS"])
                else:
                    k.ve(lambda h: h.tensor_tensor(COLS[:, 0:1], COLS[:, 0:1],
                                                   COLS[:, 1:2], ALU.max),
                         ins=["COLS"], outs=["COLS"])
            k.ve(lambda h: h.memset(XST[:, 0:128], 0.0), outs=["XST"])
            k.ve(lambda h: h.tensor_copy(XST[:, 0:1], COLS[:, 0:1]),
                 ins=["COLS"], outs=["XST"])
            k.tr(PB[:, 0:128], XST[:, 0:128], ins=["XST"], outs=["PB"])
            k.ve(lambda h: h.reduce_max(SCL[0:1, 0:1], PB[0:1, 0:128], AX.X),
                 ins=["PB"], outs=["SCL"])
            k.ve(lambda h: h.memset(ROWS[0:1, 0:128], 0.0), outs=["ROWS"])
            k.ve(lambda h: h.tensor_copy(ROWS[0:1, 0:1], SCL[0:1, 0:1]),
                 ins=["SCL"], outs=["ROWS"])
            k.dma("ge", ap["cc_gam_in"], ROWS[0:1, 0:128], ins=["ROWS"], outs=["d:cc_gam_in"])
            k.cc("AllReduce", ALU.max, ALL8, ap["cc_gam_in"], ap["cc_gam_out"],
                 ins=["d:cc_gam_in"], outs=["d:cc_gam_out"])
            k.dma("ge", SCL[0:1, 0:128], ap["cc_gam_out"], ins=["d:cc_gam_out"], outs=["SCL"])
            for t8 in range(8):
                wsb, wsn = (WS0, "WS0") if t8 % 2 == 0 else (WS1, "WS1")
                k.dma("sy", wsb[:], ap["w_ffn0"][t8 * 128:(t8 + 1) * 128, :],
                      ins=["d:w_ffn0"], outs=[wsn])
                k.ve(lambda h, wsb=wsb: h.reduce_sum(COLS[:, 2:3], wsb[:], AX.X,
                                                     apply_absolute_value=True),
                     ins=[wsn], outs=["COLS"])
                if t8 == 0:
                    k.ve(lambda h: h.tensor_copy(COLS[:, 3:4], COLS[:, 2:3]),
                         ins=["COLS"], outs=["COLS"])
                else:
                    k.ve(lambda h: h.tensor_tensor(COLS[:, 3:4], COLS[:, 3:4],
                                                   COLS[:, 2:3], ALU.add),
                         ins=["COLS"], outs=["COLS"])
            k.ve(lambda h: h.memset(XST[:, 0:128], 0.0), outs=["XST"])
            k.ve(lambda h: h.tensor_copy(XST[:, 0:1], COLS[:, 3:4]),
                 ins=["COLS"], outs=["XST"])
            k.tr(PB[:, 0:128], XST[:, 0:128], ins=["XST"], outs=["PB"])
            k.ve(lambda h: h.reduce_sum(SCL[0:1, 1:2], PB[0:1, 0:128], AX.X),
                 ins=["PB"], outs=["SCL"])
            k.ve(lambda h: h.tensor_scalar(SCL[0:1, 2:3], SCL[0:1, 1:2], 1.0 / (D * D), 1e-5,
                                           ALU.mult, ALU.add), ins=["SCL"], outs=["SCL"])
            k.ve(lambda h: h.reciprocal(SCL[0:1, 2:3], SCL[0:1, 2:3]), ins=["SCL"], outs=["SCL"])
            k.bcast_row(PC[:, 0:1], SCL[0:1, 2:3], ins=["SCL"], outs=["PC"])
            k.ve(lambda h: h.tensor_copy(COLS[:, 4:5], PC[:, 0:1]), ins=["PC"], outs=["COLS"])
            for t8 in range(8):
                wsb, wsn = (WS0, "WS0") if t8 % 2 == 0 else (WS1, "WS1")
                k.dma("sy", wsb[:], ap["w_ffn0"][t8 * 128:(t8 + 1) * 128, :],
                      ins=["d:w_ffn0"], outs=[wsn])
                k.ve(lambda h, wsb=wsb: h.tensor_scalar(
                    wsb[:], wsb[:], COLS[:, 4:5], None, ALU.mult),
                    ins=[wsn, "COLS"], outs=[wsn])
                k.ve(lambda h, wsb=wsb: h.tensor_copy(WQI[:], wsb[:]), ins=[wsn], outs=["WQI"])
                k.ve(lambda h, t8=t8: h.tensor_scalar(
                    WQ0[:, t8 * 1024:(t8 + 1) * 1024], WQI[:], -1.0, 1.0,
                    ALU.max, ALU.min), ins=["WQI"], outs=["WQ0"])
            k.ve(lambda h: h.reciprocal(SCL[0:1, 3:4], SCL[0:1, 0:1]), ins=["SCL"], outs=["SCL"])
            k.ve(lambda h: h.tensor_scalar_mul(SCL[0:1, 3:4], SCL[0:1, 3:4], 128.0),
                 ins=["SCL"], outs=["SCL"])
            k.bcast_row(PC[:, 0:1], SCL[0:1, 3:4], ins=["SCL"], outs=["PC"])
            k.ve(lambda h: h.tensor_copy(COLS[:, 5:6], PC[:, 0:1]), ins=["PC"], outs=["COLS"])
            k.ve(lambda h: h.tensor_scalar_mul(SCL[0:1, 4:5], SCL[0:1, 1:2],
                                               1.0 / (D * D * 128.0)), ins=["SCL"], outs=["SCL"])
            k.ve(lambda h: h.tensor_tensor(SCL[0:1, 4:5], SCL[0:1, 4:5], SCL[0:1, 0:1], ALU.mult),
                 ins=["SCL"], outs=["SCL"])
            k.bcast_row(PC[:, 0:1], SCL[0:1, 4:5], ins=["SCL"], outs=["PC"])
            k.ve(lambda h: h.tensor_copy(COLS[:, 6:7], PC[:, 0:1]), ins=["PC"], outs=["COLS"])
            for tc4 in range(4):
                t0 = tc4 * 512
                for dc in range(8):
                    k.ve(lambda h, dc=dc, t0=t0: h.tensor_scalar(
                        ROWS[:, 0:512], XL[:, dc * TC + t0: dc * TC + t0 + 512],
                        COLS[:, 5:6], None, ALU.mult), ins=["XL", "COLS"], outs=["ROWS"])
                    k.ve(lambda h, dc=dc: h.tensor_scalar(
                        XQ0[:, dc * 512:(dc + 1) * 512], ROWS[:, 0:512],
                        -(128.0 - 1e-5), 128.0 - 1e-5, ALU.max, ALU.min),
                        ins=["ROWS"], outs=["XQ0"])
                for mc in range(8):
                    for dc in range(8):
                        k.mm(PD[:, 0:512], WQ0[:, dc * 1024 + mc * 128: dc * 1024 + mc * 128 + 128],
                             XQ0[:, dc * 512:(dc + 1) * 512],
                             ins=["WQ0", "XQ0"], outs=["PD"], start=(dc == 0), stop=(dc == 7))
                    k.se(lambda h, mc=mc: h.activation(
                        SQB[:, mc * 512:(mc + 1) * 512], PD[:, 0:512], ACT.Silu,
                        scale=COLS[:, 6:7]), ins=["PD", "COLS"], outs=["SQB"])
                k.dma("ge", ap["g_sp"].rearrange("p (a t) -> p a t", t=TC)[:, :, t0:t0 + 512],
                      SQB[:].rearrange("p (a t) -> p a t", t=512), ins=["SQB"], outs=["d:g_sp"])

        if DBG == "g":
            dump_bf16_to_xr("g_sp")
        p.barrier()
        QF = k.sbuf("QF", [128, 4096], BF16, scope=l0)
        KF = k.sbuf("KF", [128, 4096], BF16, scope=l0)
        KFF = k.sbuf("KFF", [128, 512], F32, scope=l0)
        VFF = k.sbuf("VFF", [128, 512], F32, scope=l0)
        KT = k.sbuf("KT", [128, 4096], BF16, scope=l0)
        VT = k.sbuf("VT", [128, 4352], BF16, scope=l0)
        PMT = k.sbuf("PMT", [128, 256], BF16, scope=l0)
        SSB = k.sbuf("SSB", [128, 520], F32, scope=l0)
        SSBB = k.sbuf("SSBB", [128, 520], BF16, scope=l0)
        SIN = k.sbuf("SIN", [128, 520], F32, scope=l0)
        SRUN = k.sbuf("SRUN", [128, 520], F32, scope=l0)
        YNUM = k.sbuf("YNUM", [128, 128], F32, scope=l0)
        YSC = k.sbuf("YSC", [128, 4096], BF16, scope=l0)
        FMT = k.sbuf("FMT", [128, 512], F32, scope=l0)
        FMT2 = k.sbuf("FMT2", [128, 512], F32, scope=l0)

        k.ve(lambda h: h.memset(VT[:], 1.0), outs=["VT"])

        def fmap_into(dst, dstn, dst_sl, src_psum, pn):
            k.ve(lambda h: h.tensor_scalar_min(FMT[:, 0:512], src_psum, 0.0),
                 ins=[pn], outs=["FMT"])
            k.se(lambda h: h.activation(FMT2[:, 0:512], FMT[:, 0:512], ACT.Exp),
                 ins=["FMT"], outs=["FMT2"])
            k.p.emit("ve", lambda h: h.scalar_tensor_tensor(
                dst[dst_sl], src_psum, 0.0, FMT2[:, 0:512], ALU.max, ALU.add),
                ins=[pn, "FMT2"], outs=[dstn])

        def project_qkv(sc):
            t0 = sc * 512
            for hp in range(8):
                for wi, col0 in enumerate((0, 1024, 2048)):
                    wb, wbn, nk = load_w_block("w_attn0", col0 + hp * 128, 128,
                                               idx=hp * 3 + wi)
                    for dc in range(8):
                        k.mm(PA[:, 0:512], wb[:, dc * 128:(dc + 1) * 128],
                             XNB[:, dc * TC + t0:dc * TC + t0 + 512],
                             ins=[wbn, "XNB"], outs=["PA"],
                             start=(dc == 0), stop=(dc == 7))
                    if col0 == 0:
                        fmap_into(QF, "QF", np.s_[:, hp * 512:(hp + 1) * 512],
                                  PA[:, 0:512], "PA")
                    elif col0 == 1024:
                        fmap_into(KFF, "KFF", np.s_[:, 0:512], PA[:, 0:512], "PA")
                        k.ve(lambda h, hp=hp: h.tensor_copy(
                            KF[:, hp * 512:(hp + 1) * 512], KFF[:]),
                            ins=["KFF"], outs=["KF"])
                        for sub in range(4):
                            k.tr(PA[:, 512:640], KFF[:, sub * 128:(sub + 1) * 128],
                                 ins=["KFF"], outs=["PA"])
                            k.ve(lambda h, hp=hp, sub=sub: h.tensor_copy(
                                KT[:, sub * 1024 + hp * 128: sub * 1024 + (hp + 1) * 128],
                                PA[:, 512:640]), ins=["PA"], outs=["KT"])
                    else:
                        k.ve(lambda h: h.tensor_copy(VFF[:], PA[:, 0:512]),
                             ins=["PA"], outs=["VFF"])
                        for sub in range(4):
                            k.tr(PA[:, 512:640], VFF[:, sub * 128:(sub + 1) * 128],
                                 ins=["VFF"], outs=["PA"])
                            base = sub * 1088 + hp * 136
                            k.ve(lambda h, base=base: h.tensor_copy(
                                VT[:, base: base + 64], PA[:, 512:576]),
                                ins=["PA"], outs=["VT"])
                            k.ve(lambda h, base=base: h.tensor_copy(
                                VT[:, base + 68: base + 132], PA[:, 576:640]),
                                ins=["PA"], outs=["VT"])

        def s_update(j):
            for hp in range(8):
                bank, bn = (PD, "PD") if hp < 4 else (PE, "PE")
                off = (hp % 4) * 65
                base = j * 1088 + hp * 136
                k.mm(bank[0:64, off:off + 65],
                     KT[:, j * 1024 + hp * 128: j * 1024 + hp * 128 + 64],
                     VT[:, base: base + 65],
                     ins=["KT", "VT"], outs=[bn], start=True, stop=True)
                k.mm(bank[64:128, off:off + 65],
                     KT[:, j * 1024 + hp * 128 + 64: j * 1024 + hp * 128 + 128],
                     VT[:, base + 68: base + 133],
                     ins=["KT", "VT"], outs=[bn], start=True, stop=True)
            k.ve(lambda h: h.tensor_tensor(SRUN[:, 0:260], SRUN[:, 0:260],
                                           PD[:, 0:260], ALU.add), ins=["SRUN", "PD"],
                 outs=["SRUN"])
            k.ve(lambda h: h.tensor_tensor(SRUN[:, 260:520], SRUN[:, 260:520],
                                           PE[:, 0:260], ALU.add), ins=["SRUN", "PE"],
                 outs=["SRUN"])

        if run_attn:
            k.ve(lambda h: h.memset(SRUN[:], 0.0), outs=["SRUN"])
            for sc in range(4):
                project_qkv(sc)
                for j in range(4):
                    s_update(j)
            if DBG == "S":
                k.dma("ge", ap["xr"].rearrange("p (a t) -> p a t", t=TC)[:, 0, 0:520],
                      SRUN[:], ins=["SRUN"], outs=["d:xr"])
            k.dma("ge", ap["cc_s0_in"], SRUN[:], ins=["SRUN"], outs=["d:cc_s0_in"])
            k.cc("AllGather", ALU.bypass, PAIRS, ap["cc_s0_in"], ap["cc_s0_out"],
                 ins=["d:cc_s0_in"], outs=["d:cc_s0_out"])
            k.dma("ge", SIN[:], ap["cc_s0_out"][0], ins=["d:cc_s0_out"], outs=["SIN"])
            k.ve(lambda h: h.tensor_scalar(SIN[:], SIN[:], HSEL[:, 0:1], None, ALU.mult),
                 ins=["SIN", "HSEL"], outs=["SIN"])

            k.ve(lambda h: h.memset(SRUN[:], 0.0), outs=["SRUN"])
            for sc in range(4) if run_ph2 else []:
                project_qkv(sc)
                for j in range(4):
                    k.ve(lambda h: h.tensor_tensor(SSB[:], SIN[:], SRUN[:], ALU.add),
                         ins=["SIN", "SRUN"], outs=["SSB"])
                    k.ve(lambda h: h.tensor_copy(SSBB[:], SSB[:]), ins=["SSB"], outs=["SSBB"])
                    for hp in range(8):
                        qs = np.s_[:, hp * 512 + j * 128: hp * 512 + (j + 1) * 128]
                        c0 = (hp % 4) * 65 + (0 if hp < 4 else 260)
                        k.mm(PA[:, hp * 256: hp * 256 + 128],
                             KF[:, hp * 512 + j * 128: hp * 512 + j * 128 + 128][0:64, :],
                             QF[qs][0:64, :], ins=["KF", "QF"], outs=["PA"])
                        k.mm(PA[:, hp * 256 + 128: hp * 256 + 256],
                             KF[:, hp * 512 + j * 128: hp * 512 + j * 128 + 128][64:128, :],
                             QF[qs][64:128, :], ins=["KF", "QF"], outs=["PA"])
                        k.ve(lambda h, hp=hp: h.tensor_tensor(
                            PMT[:, 0:128], PA[:, hp * 256: hp * 256 + 128], TRIB[:], ALU.mult),
                            ins=["PA", "TRIB"], outs=["PMT"])
                        k.ve(lambda h, hp=hp: h.tensor_tensor(
                            PMT[:, 128:256], PA[:, hp * 256 + 128: hp * 256 + 256],
                            TRIB[:], ALU.mult), ins=["PA", "TRIB"], outs=["PMT"])
                        base = j * 1088 + hp * 136
                        k.mm(PB[0:64, 0:128], VT[:, base: base + 64], PMT[:, 0:128],
                             ins=["VT", "PMT"], outs=["PB"], start=True, stop=False)
                        k.mm(PB[64:128, 0:128], VT[:, base + 68: base + 132], PMT[:, 128:256],
                             ins=["VT", "PMT"], outs=["PB"], start=True, stop=False)
                        k.mm(PB[0:64, 0:128], SSBB[0:64, c0:c0 + 64], QF[qs][0:64, :],
                             ins=["SSBB", "QF"], outs=["PB"], start=False, stop=True)
                        k.mm(PB[64:128, 0:128], SSBB[64:128, c0:c0 + 64], QF[qs][64:128, :],
                             ins=["SSBB", "QF"], outs=["PB"], start=False, stop=True)
                        k.mm(PC[0:1, 0:128], VT[:, base + 64: base + 65], PMT[:, 0:128],
                             ins=["VT", "PMT"], outs=["PC"], start=True, stop=False)
                        k.mm(PC[0:1, 128:256], VT[:, base + 132: base + 133], PMT[:, 128:256],
                             ins=["VT", "PMT"], outs=["PC"], start=False, stop=False)
                        k.mm(PC[0:1, 0:128], SSBB[0:64, c0 + 64:c0 + 65], QF[qs][0:64, :],
                             ins=["SSBB", "QF"], outs=["PC"], start=False, stop=True)
                        k.mm(PC[0:1, 128:256], SSBB[64:128, c0 + 64:c0 + 65], QF[qs][64:128, :],
                             ins=["SSBB", "QF"], outs=["PC"], start=False, stop=True)
                        k.ve(lambda h: h.tensor_copy(YNUM[:, 0:128], PB[:, 0:128]),
                             ins=["PB"], outs=["YNUM"])
                        k.ve(lambda h: h.tensor_scalar_add(ROWS[0:1, 0:256],
                                                           PC[0:1, 0:256], 1e-6),
                             ins=["PC"], outs=["ROWS"])
                        k.ve(lambda h: h.reciprocal(ROWS[0:1, 0:256], ROWS[0:1, 0:256]),
                             ins=["ROWS"], outs=["ROWS"])
                        k.bcast_row(PC[:, 256:384], ROWS[0:1, 0:128], ins=["ROWS"], outs=["PC"])
                        k.bcast_row(PC[:, 384:512], ROWS[0:1, 128:256], ins=["ROWS"], outs=["PC"])
                        k.ve(lambda h, hp=hp, j=j: h.tensor_tensor(
                            YSC[0:64, hp * 512 + j * 128: hp * 512 + (j + 1) * 128],
                            YNUM[0:64, 0:128], PC[0:64, 256:384], ALU.mult),
                            ins=["YNUM", "PC"], outs=["YSC"])
                        k.ve(lambda h, hp=hp, j=j: h.tensor_tensor(
                            YSC[64:128, hp * 512 + j * 128: hp * 512 + (j + 1) * 128],
                            YNUM[64:128, 0:128], PC[64:128, 384:512], ALU.mult),
                            ins=["YNUM", "PC"], outs=["YSC"])
                    s_update(j)
                if DBG == "y":
                    k.dma("ge", ap["g_sp"].rearrange("p (a t) -> p a t", t=TC)
                          [:, :, sc * 512:(sc + 1) * 512],
                          YSC[:].rearrange("p (a t) -> p a t", t=512),
                          ins=["YSC"], outs=["d:g_sp"])
                for mc in range(8):
                    wb, wbn, nk = load_w_block("w_proj0", mc * 128, 128, idx=mc)
                    for dc in range(8):
                        k.mm(PB[:, 0:512], wb[:, dc * 128:(dc + 1) * 128],
                             YSC[:, dc * 512:(dc + 1) * 512],
                             ins=[wbn, "YSC"], outs=["PB"], start=(dc == 0), stop=(dc == 7))
                    k.ve(lambda h, mc=mc: h.tensor_copy(SQB[:, mc * 512:(mc + 1) * 512],
                                                        PB[:, 0:512]), ins=["PB"], outs=["SQB"])
                k.dma("ge", ap["s_sp"].rearrange("p (a t) -> p a t", t=TC)
                      [:, :, sc * 512:(sc + 1) * 512],
                      SQB[:].rearrange("p (a t) -> p a t", t=512), ins=["SQB"], outs=["d:s_sp"])
        if DBG == "s":
            dump_bf16_to_xr("s_sp")
        if DBG == "y":
            dump_bf16_to_xr("g_sp")
        if DBG is None:
            emit_combine()
        p.barrier()
        l0.close()


    if NLAYERS >= 2:
        emit_layernorm()
        if DBG == "xn1":
            dump_xnb_to_xr()
        run_g1 = DBG in (None, "g1", "s1")
        run_attn1 = DBG in (None, "s1")
        l1 = ExitStack()
        QX = k.sbuf("QX", [128, 16384], BF16, scope=l1)
        YF = k.sbuf("YF", [128, 16384], BF16, scope=l1)
        QNM = k.sbuf("QNM", [128, 16], F32, scope=l1)
        KNM = k.sbuf("KNM", [128, 16], F32, scope=l1)
        NRM = k.sbuf("NRM", [128, 16], F32, scope=l1)
        MCOL = k.sbuf("MCOL", [128, 16], F32, scope=l1)
        XLS = k.sbuf("XLS", [128, 8], F32, scope=l1)
        XPB = k.sbuf("XPB", [128, 8], F32, scope=l1)
        MUR = k.sbuf("MUR", [128, 8], F32, scope=l1)
        MUK = k.sbuf("MUK", [128, 8], F32, scope=l1)
        FCB = k.sbuf("FCB", [128, 8], F32, scope=l1)
        KFB = k.sbuf("KFB", [128, 128], BF16, scope=l1)
        l1p = ExitStack()
        WQKVB = k.sbuf("WQKVB", [128, 4096], BF16, scope=l1p)
        BIAST = k.sbuf("BIAST", [128, 3072], BF16, scope=l1p)
        QT0 = k.sbuf("QT0", [128, 512], F32, scope=l1p)
        RT = k.sbuf("RT", [128, 512], F32, scope=l1p)
        RTB = k.sbuf("RTB", [128, 512], BF16, scope=l1p)
        FMP = k.sbuf("FMP", [128, 512], F32, scope=l1p)
        RCO = k.sbuf("RCO", [128, 512], F32, scope=l1p)
        RSI = k.sbuf("RSI", [128, 512], F32, scope=l1p)
        k.dram("cc_xl_in", [128, 8])
        k.dram("cc_xl_out", [2, 128, 8])
        k.dram("exch1_in", [128, 32768], BF16)
        k.dram("exch1_out", [2, 128, 32768], BF16)
        k.dma("sy", MUR[:], ap["mu_r1"], ins=["d:mu_r1"], outs=["MUR"])
        k.dma("sy", MUK[:], ap["mu_k1"], ins=["d:mu_k1"], outs=["MUK"])
        k.dma("sy", FCB[:], ap["b_fc1"], ins=["d:b_fc1"], outs=["FCB"])
        k.dma("sy", RCO[:].rearrange("p (a c) -> p a c", c=32),
              ap["rope_cos"].rearrange("(a p) c -> p a c", p=128),
              ins=["d:rope_cos"], outs=["RCO"])
        k.dma("sy", RSI[:].rearrange("p (a c) -> p a c", c=32),
              ap["rope_sin"].rearrange("(a p) c -> p a c", p=128),
              ins=["d:rope_sin"], outs=["RSI"])

        for dc in range(8):
            k.ve(lambda h, dc=dc: h.tensor_copy(XLS[:, dc:dc + 1],
                                                XNB[:, dc * TC + TC - 1: dc * TC + TC]),
                 ins=["XNB"], outs=["XLS"])
        k.dma("ge", ap["cc_xl_in"], XLS[:], ins=["XLS"], outs=["d:cc_xl_in"])
        k.cc("AllGather", ALU.bypass, PAIRS, ap["cc_xl_in"], ap["cc_xl_out"],
             ins=["d:cc_xl_in"], outs=["d:cc_xl_out"])
        k.dma("ge", XPB[:], ap["cc_xl_out"][0], ins=["d:cc_xl_out"], outs=["XPB"])
        k.ve(lambda h: h.tensor_scalar(XPB[:], XPB[:], HSEL[:, 0:1], None, ALU.mult),
             ins=["XPB", "HSEL"], outs=["XPB"])

        k.ve(lambda h: h.memset(QNM[:], 0.0), outs=["QNM"])
        k.ve(lambda h: h.memset(KNM[:], 0.0), outs=["KNM"])
        for cc6 in range(6):
            k.dma("sy", ROWS[0:1, 0:512], ap["b_qkv1r"][:, cc6 * 512:(cc6 + 1) * 512],
                  ins=["d:b_qkv1r"], outs=["ROWS"])
            k.bcast_row(PA[:, 0:512], ROWS[0:1, 0:512],
                        ins=["ROWS"], outs=["PA"])
            k.ve(lambda h, cc6=cc6: h.tensor_copy(BIAST[:, cc6 * 512:(cc6 + 1) * 512],
                                                  PA[:, 0:512]), ins=["PA"], outs=["BIAST"])

        for cc6 in range(6):
            iq = cc6 < 2
            ik = 2 <= cc6 < 4
            for dc in range(8):
                wsb, wsn = (WS0, "WS0") if dc % 2 == 0 else (WS1, "WS1")
                k.dma("sy", wsb[:, 0:512],
                      ap["w_qkv1"][dc * 128:(dc + 1) * 128, cc6 * 512:(cc6 + 1) * 512],
                      ins=["d:w_qkv1"], outs=[wsn])
                k.ge(lambda h, wsb=wsb, dc=dc: h.tensor_copy(
                    WQKVB[:, dc * 512:(dc + 1) * 512], wsb[:, 0:512]),
                    ins=[wsn], outs=["WQKVB"])
            for tt in range(16):
                for dc in range(8):
                    k.mm(PB[:, 0:512],
                         XNB[:, dc * TC + tt * 128: dc * TC + (tt + 1) * 128],
                         WQKVB[:, dc * 512:(dc + 1) * 512],
                         ins=["XNB", "WQKVB"], outs=["PB"],
                         start=(dc == 0), stop=(dc == 7))
                k.ve(lambda h, cc6=cc6: h.tensor_tensor(
                    QT0[:], PB[:, 0:512], BIAST[:, cc6 * 512:(cc6 + 1) * 512], ALU.add),
                    ins=["PB", "BIAST"], outs=["QT0"])
                if iq or ik:
                    k.ve(lambda h: h.tensor_tensor(RT[:], QT0[:], QT0[:], ALU.mult),
                         ins=["QT0"], outs=["RT"])
                    dstn, dst = ("QNM", QNM) if iq else ("KNM", KNM)
                    k.ve(lambda h: h.tensor_reduce(
                        NRM[:, 0:8], RT[:].rearrange("p (a b) -> p a b", b=64),
                        AX.X, ALU.add), ins=["RT"], outs=["NRM"])
                    h8 = (cc6 % 2) * 8
                    k.ve(lambda h, dst=dst, h8=h8: h.tensor_tensor(
                        dst[:, h8:h8 + 8], dst[:, h8:h8 + 8], NRM[:, 0:8], ALU.max),
                        ins=["NRM", dstn], outs=[dstn])
                    qe = QT0[:].rearrange("p (a two c) -> p a two c", two=32, c=2)[:, :, :, 0]
                    qo = QT0[:].rearrange("p (a two c) -> p a two c", two=32, c=2)[:, :, :, 1]
                    cosb = RCO[:, tt * 32:(tt + 1) * 32].rearrange(
                        "p (o c) -> p o c", o=1).broadcast_to([128, 8, 32])
                    sinb = RSI[:, tt * 32:(tt + 1) * 32].rearrange(
                        "p (o c) -> p o c", o=1).broadcast_to([128, 8, 32])
                    re = RT[:].rearrange("p (a two c) -> p a two c", two=32, c=2)[:, :, :, 0]
                    ro = RT[:].rearrange("p (a two c) -> p a two c", two=32, c=2)[:, :, :, 1]
                    k.ve(lambda h, qe=qe, cosb=cosb: h.tensor_tensor(
                        FMP[:, 0:256].rearrange("p (a c) -> p a c", c=32), qe, cosb, ALU.mult),
                        ins=["QT0", "RCO"], outs=["FMP"])
                    k.p.emit("ve", lambda h, qo=qo, sinb=sinb, re=re: h.scalar_tensor_tensor(
                        re, qo, -1.0, sinb, ALU.mult, ALU.mult), ins=["QT0", "RSI"], outs=["RT"])
                    k.ve(lambda h, re=re: h.tensor_tensor(
                        re, re, FMP[:, 0:256].rearrange("p (a c) -> p a c", c=32), ALU.add),
                        ins=["RT", "FMP"], outs=["RT"])
                    k.ve(lambda h, qe=qe, sinb=sinb: h.tensor_tensor(
                        FMP[:, 0:256].rearrange("p (a c) -> p a c", c=32), qe, sinb, ALU.mult),
                        ins=["QT0", "RSI"], outs=["FMP"])
                    k.ve(lambda h, qo=qo, cosb=cosb, ro=ro: h.tensor_tensor(
                        ro, qo, cosb, ALU.mult), ins=["QT0", "RCO"], outs=["RT"])
                    k.ve(lambda h, ro=ro: h.tensor_tensor(
                        ro, ro, FMP[:, 0:256].rearrange("p (a c) -> p a c", c=32), ALU.add),
                        ins=["RT", "FMP"], outs=["RT"])
                    for pb4 in range(4):
                        pair = (cc6 % 2) * 4 + pb4
                        k.tr(PE[:, 0:128], RT[:, pb4 * 128:(pb4 + 1) * 128],
                             ins=["RT"], outs=["PE"])
                        if iq:
                            k.ve(lambda h, pair=pair, tt=tt: h.tensor_copy(
                                QX[:, pair * 2048 + tt * 128: pair * 2048 + (tt + 1) * 128],
                                PE[:, 0:128]), ins=["PE"], outs=["QX"])
                        else:
                            k.ve(lambda h: h.tensor_copy(KFB[:], PE[:, 0:128]),
                                 ins=["PE"], outs=["KFB"])
                            k.dma("ge", ap["exch1_in"][:, pair * 2048 + tt * 128:
                                                       pair * 2048 + (tt + 1) * 128],
                                  KFB[:], ins=["KFB"], outs=["d:exch1_in"])
                else:
                    k.ve(lambda h: h.tensor_copy(RTB[:], QT0[:]), ins=["QT0"], outs=["RTB"])
                    k.dma("ge", ap["exch1_in"][:, 16384 + tt * 1024 + (cc6 - 4) * 512:
                                               16384 + tt * 1024 + (cc6 - 3) * 512],
                          RTB[:], ins=["RTB"], outs=["d:exch1_in"])
        p.barrier()
        l1p.close()
        k.cc("AllGather", ALU.bypass, PAIRS, ap["exch1_in"], ap["exch1_out"],
             ins=["d:exch1_in"], outs=["d:exch1_out"])
        l1g = ExitStack()
        PREV = k.sbuf("PREV", [128, 512], BF16, scope=l1g)
        D1 = k.sbuf("D1", [128, 512], F32, scope=l1g)
        XRB = k.sbuf("XRB", [128, 4096], BF16, scope=l1g)
        XKB = k.sbuf("XKB", [128, 4096], BF16, scope=l1g)
        RB = k.sbuf("RB", [128, 4096], BF16, scope=l1g)
        KA = k.sbuf("KA", [128, 4096], BF16, scope=l1g)
        FM = k.sbuf("FM", [128, 512], F32, scope=l1g)
        for dst, (which, t_) in enumerate((("QNM", QNM), ("KNM", KNM))):
            k.ve(lambda h: h.memset(XST[:, 0:128], 0.0), outs=["XST"])
            k.ve(lambda h, t_=t_: h.tensor_copy(XST[:, 0:16], t_[:]), ins=[which], outs=["XST"])
            k.tr(PE[:, 0:128], XST[:, 0:128], ins=["XST"], outs=["PE"])
            k.ve(lambda h: h.reduce_max(XST2[:, 0:1], PE[:, 0:128], AX.X),
                 ins=["PE"], outs=["XST2"])
            k.ve(lambda h: h.memset(XST[:, 0:128], 0.0), outs=["XST"])
            k.ve(lambda h: h.tensor_copy(XST[:, 0:1], XST2[:, 0:1]), ins=["XST2"], outs=["XST"])
            k.tr(PE[:, 0:128], XST[:, 0:128], ins=["XST"], outs=["PE"])
            k.ve(lambda h, dst=dst: h.tensor_copy(
                ROWS[0:1, 512 + dst * 16: 512 + dst * 16 + 16], PE[0:1, 0:16]),
                ins=["PE"], outs=["ROWS"])
        k.ve(lambda h: h.tensor_tensor(ROWS[0:1, 544:560], ROWS[0:1, 512:528],
                                       ROWS[0:1, 528:544], ALU.mult), ins=["ROWS"], outs=["ROWS"])
        k.se(lambda h: h.activation(ROWS[0:1, 544:560], ROWS[0:1, 544:560], ACT.Sqrt),
             ins=["ROWS"], outs=["ROWS"])
        k.ve(lambda h: h.tensor_scalar_mul(ROWS[0:1, 544:560], ROWS[0:1, 544:560], -0.125),
             ins=["ROWS"], outs=["ROWS"])
        for hd in range(16):
            k.bcast_row(PE[:, 0:1], ROWS[0:1, 544 + hd:545 + hd], ins=["ROWS"], outs=["PE"])
            k.ve(lambda h, hd=hd: h.tensor_copy(MCOL[:, hd:hd + 1], PE[:, 0:1]),
                 ins=["PE"], outs=["MCOL"])

        for tc4 in range(4) if run_g1 else []:
            t0 = tc4 * 512
            for dc in range(8):
                if tc4 == 0:
                    k.ve(lambda h, dc=dc: h.tensor_copy(
                        PREV[:, 1:512], XNB[:, dc * TC: dc * TC + 511]),
                        ins=["XNB"], outs=["PREV"])
                    k.ve(lambda h, dc=dc: h.tensor_copy(
                        PREV[:, 0:1], XPB[:, dc:dc + 1]), ins=["XPB"], outs=["PREV"])
                else:
                    k.ve(lambda h, dc=dc, t0=t0: h.tensor_copy(
                        PREV[:], XNB[:, dc * TC + t0 - 1: dc * TC + t0 + 511]),
                        ins=["XNB"], outs=["PREV"])
                k.ve(lambda h, dc=dc, t0=t0: h.tensor_tensor(
                    D1[:], XNB[:, dc * TC + t0: dc * TC + t0 + 512], PREV[:], ALU.subtract),
                    ins=["XNB", "PREV"], outs=["D1"])
                k.p.emit("ve", lambda h, dc=dc: h.scalar_tensor_tensor(
                    XRB[:, dc * 512:(dc + 1) * 512], D1[:], MUR[:, dc:dc + 1], PREV[:],
                    ALU.mult, ALU.add), ins=["D1", "MUR", "PREV"], outs=["XRB"])
                k.p.emit("ve", lambda h, dc=dc: h.scalar_tensor_tensor(
                    XKB[:, dc * 512:(dc + 1) * 512], D1[:], MUK[:, dc:dc + 1], PREV[:],
                    ALU.mult, ALU.add), ins=["D1", "MUK", "PREV"], outs=["XKB"])
            for mc in range(8):
                wb, wbn, nk = load_w_block("w_r1", mc * 128, 128, idx=mc)
                for dc in range(8):
                    k.mm(PB[:, 0:512], wb[:, dc * 128:(dc + 1) * 128],
                         XRB[:, dc * 512:(dc + 1) * 512],
                         ins=[wbn, "XRB"], outs=["PB"], start=(dc == 0), stop=(dc == 7))
                k.se(lambda h, mc=mc: h.activation(RB[:, mc * 512:(mc + 1) * 512],
                                                   PB[:, 0:512], ACT.Sigmoid),
                     ins=["PB"], outs=["RB"])
            for mc in range(8):
                wb, wbn, nk = load_w_block("w_k1", mc * 128, 128, idx=mc)
                for dc in range(8):
                    k.mm(PB[:, 0:512], wb[:, dc * 128:(dc + 1) * 128],
                         XKB[:, dc * 512:(dc + 1) * 512],
                         ins=[wbn, "XKB"], outs=["PB"], start=(dc == 0), stop=(dc == 7))
                k.se(lambda h: h.activation(FM[:], PB[:, 0:512], ACT.Relu),
                     ins=["PB"], outs=["FM"])
                k.se(lambda h, mc=mc: h.activation(KA[:, mc * 512:(mc + 1) * 512],
                                                   FM[:], ACT.Square), ins=["FM"], outs=["KA"])
            for mc in range(8):
                wb, wbn, nk = load_w_block("w_v1", mc * 128, 128, idx=mc)
                for dc in range(8):
                    k.mm(PB[:, 0:512], wb[:, dc * 128:(dc + 1) * 128],
                         KA[:, dc * 512:(dc + 1) * 512],
                         ins=[wbn, "KA"], outs=["PB"], start=(dc == 0), stop=(dc == 7))
                k.ve(lambda h, mc=mc: h.tensor_tensor(
                    FM[:], PB[:, 0:512], RB[:, mc * 512:(mc + 1) * 512], ALU.mult),
                    ins=["PB", "RB"], outs=["FM"])
                k.se(lambda h, mc=mc: h.activation(SQB[:, mc * 512:(mc + 1) * 512],
                                                   FM[:], ACT.Silu), ins=["FM"], outs=["SQB"])
            k.dma("ge", ap["g_sp"].rearrange("p (a t) -> p a t", t=TC)[:, :, t0:t0 + 512],
                  SQB[:].rearrange("p (a t) -> p a t", t=512), ins=["SQB"], outs=["d:g_sp"])
        p.barrier()
        l1g.close()
        l1c = ExitStack()
        KPH = k.sbuf("KPH", [128, 4096], BF16, scope=l1c)
        VPH = k.sbuf("VPH", [128, 4096], BF16, scope=l1c)
        PTB = k.sbuf("PTB", [128, 1024], BF16, scope=l1c)

        if DBG == "g1":
            dump_bf16_to_xr("g_sp")
        for pair in range(8) if run_attn1 else []:
            for half in range(2):
                k.dma("sy", KPH[:, half * 2048:(half + 1) * 2048],
                      ap["exch1_out"][half][:, pair * 2048:(pair + 1) * 2048],
                      ins=["d:exch1_out"], outs=["KPH"])
                k.dma("sy", VPH[:, half * 2048:(half + 1) * 2048]
                      .rearrange("p (a b) -> p a b", b=128),
                      ap["exch1_out"][half][:, 16384:32768]
                      .rearrange("p (a b) -> p a b", b=1024)[:, :, pair * 128:(pair + 1) * 128],
                      ins=["d:exch1_out"], outs=["VPH"])
            for qt in range(4):
                q0 = pair * 2048 + qt * 512
                for sch in range(32):
                    bank, bn = (PA[:, 0:512], "PA") if sch % 2 == 0 else (PA[:, 512:1024], "PA")
                    k.mm(bank, KPH[0:64, sch * 128:(sch + 1) * 128],
                         QX[0:64, q0:q0 + 512], ins=["KPH", "QX"], outs=[bn])
                    k.p.emit("se", lambda h, bank=bank, pair=pair: h.activation(
                        PTB[:, 0:512], bank, ACT.Exp, bias=MCOL[:, 2 * pair:2 * pair + 1],
                        scale=0.125), ins=[bn, "MCOL"], outs=["PTB"])
                    k.mm(bank, KPH[64:128, sch * 128:(sch + 1) * 128],
                         QX[64:128, q0:q0 + 512], ins=["KPH", "QX"], outs=[bn])
                    k.p.emit("se", lambda h, bank=bank, pair=pair: h.activation(
                        PTB[:, 512:1024], bank, ACT.Exp, bias=MCOL[:, 2 * pair + 1:2 * pair + 2],
                        scale=0.125), ins=[bn, "MCOL"], outs=["PTB"])
                    k.mm(PB[0:64, 0:512], VPH[:, sch * 128: sch * 128 + 64], PTB[:, 0:512],
                         ins=["VPH", "PTB"], outs=["PB"], start=(sch == 0), stop=(sch == 31))
                    k.mm(PB[64:128, 0:512], VPH[:, sch * 128 + 64: sch * 128 + 128],
                         PTB[:, 512:1024],
                         ins=["VPH", "PTB"], outs=["PB"], start=(sch == 0), stop=(sch == 31))
                    k.mm(PC[0:1, 0:512], ONES[:], PTB[:, 0:512],
                         ins=["ONESB", "PTB"], outs=["PC"], start=(sch == 0), stop=(sch == 31))
                    k.mm(PD[0:1, 0:512], ONES[:], PTB[:, 512:1024],
                         ins=["ONESB", "PTB"], outs=["PD"], start=(sch == 0), stop=(sch == 31))
                k.ve(lambda h: h.tensor_copy(ROWS[0:1, 0:512], PC[0:1, 0:512]),
                     ins=["PC"], outs=["ROWS"])
                k.ve(lambda h: h.reciprocal(ROWS[0:1, 0:512], ROWS[0:1, 0:512]),
                     ins=["ROWS"], outs=["ROWS"])
                k.ve(lambda h: h.tensor_copy(ROWS[0:1, 512:1024], PD[0:1, 0:512]),
                     ins=["PD"], outs=["ROWS"])
                k.ve(lambda h: h.reciprocal(ROWS[0:1, 512:1024], ROWS[0:1, 512:1024]),
                     ins=["ROWS"], outs=["ROWS"])
                k.bcast_row(PA[:, 1024:1536], ROWS[0:1, 0:512], ins=["ROWS"], outs=["PA"])
                k.bcast_row(PA[:, 1536:2048], ROWS[0:1, 512:1024], ins=["ROWS"], outs=["PA"])
                k.ve(lambda h: h.tensor_copy(XST[:], PB[0:128, 0:512]),
                     ins=["PB"], outs=["XST"])
                k.ve(lambda h, pair=pair, q0=q0: h.tensor_tensor(
                    YF[0:64, q0:q0 + 512], XST[0:64, 0:512], PA[0:64, 1024:1536], ALU.mult),
                    ins=["XST", "PA"], outs=["YF"])
                k.ve(lambda h, pair=pair, q0=q0: h.tensor_tensor(
                    YF[64:128, q0:q0 + 512], XST[64:128, 0:512], PA[64:128, 1536:2048], ALU.mult),
                    ins=["XST", "PA"], outs=["YF"])
        for tc4 in range(4) if run_attn1 else []:
            t0 = tc4 * 512
            for mc in range(8):
                wb, wbn, nk = load_w_block("w_fc1", mc * 128, 128, idx=mc)
                for dc in range(8):
                    k.mm(PB[:, 0:512], wb[:, dc * 128:(dc + 1) * 128],
                         YF[:, dc * 2048 + t0: dc * 2048 + t0 + 512],
                         ins=[wbn, "YF"], outs=["PB"], start=(dc == 0), stop=(dc == 7))
                k.ve(lambda h, mc=mc: h.tensor_scalar(
                    SQB[:, mc * 512:(mc + 1) * 512], PB[:, 0:512],
                    FCB[:, mc:mc + 1], None, ALU.add), ins=["PB", "FCB"], outs=["SQB"])
            k.dma("ge", ap["s_sp"].rearrange("p (a t) -> p a t", t=TC)[:, :, t0:t0 + 512],
                  SQB[:].rearrange("p (a t) -> p a t", t=512), ins=["SQB"], outs=["d:s_sp"])
        if DBG == "s1":
            dump_bf16_to_xr("s_sp")
        if DBG is None:
            emit_combine()
        p.barrier()
        l1c.close()
        l1.close()


    if NLAYERS >= 3:
        emit_layernorm()
        run_g2 = DBG in (None, "g2", "s2")
        run_attn2 = DBG in (None, "s2")
        l2 = ExitStack()
        KV2 = k.sbuf("KV2", [128, 16], F32, scope=l2)
        SCG = k.sbuf("SCG", [128, 22], F32, scope=l2)
        SCU = k.sbuf("SCU", [128, 22], F32, scope=l2)
        SCD = k.sbuf("SCD", [128, 8], F32, scope=l2)
        CLP = k.sbuf("CLP", [1, 2], F32, scope=l2)
        COLX = k.sbuf("COLX", [128, 8], F32, scope=l2)
        QT2 = k.sbuf("QT2", [128, 512], F32, scope=l2)
        KT2 = k.sbuf("KT2", [128, 512], BF16, scope=l2)
        VT2 = k.sbuf("VT2", [128, 512], F32, scope=l2)
        FM2 = k.sbuf("FM2", [128, 512], F32, scope=l2)
        FM3 = k.sbuf("FM3", [128, 512], F32, scope=l2)
        BQ2 = k.sbuf("BQ2", [128, 8], F32, scope=l2)
        BK2 = k.sbuf("BK2", [128, 8], F32, scope=l2)
        BV2 = k.sbuf("BV2", [128, 8], F32, scope=l2)
        BO2 = k.sbuf("BO2", [128, 8], F32, scope=l2)
        HSG = k.sbuf("HSG", [128, 512], BF16, scope=l2)
        HIQ = k.sbuf("HIQ", [128, 512], I32, scope=l2)
        COLS2 = k.sbuf("COLS2", [128, 16], F32, scope=l2)
        l2q = ExitStack()
        RECB = k.sbuf("RECB", [128, 2816], F32, scope=l2q)
        WTQ = k.sbuf("WTQ", [128, 2816], F32, scope=l2q)
        WTQI = k.sbuf("WTQI", [128, 2816], I32, scope=l2q)
        WTQB = k.sbuf("WTQB", [128, 2816], BF16, scope=l2q)
        k.dram("wgq_sp", [D, 2816], BF16)
        k.dram("wuq_sp", [D, 2816], BF16)
        k.dram("wdq_sp", [2816, D], BF16)
        k.dram("h_sp", [128, 22 * TC], BF16)
        k.dram("rowb", [128, 32])
        k.dram("cc_gm_in", [1, 128])
        k.dram("cc_gm_out", [1, 128])
        k.dram("cc_kv_in", [128, 16])
        k.dram("cc_kv_out", [128, 16])
        k.dram("cc_hm_in", [1, 128])
        k.dram("cc_hm_out", [1, 128])
        for nm, t_ in (("bq2", BQ2), ("bk2", BK2), ("bv2", BV2), ("bo2", BO2)):
            k.dma("sy", t_[:], ap[nm], ins=["d:" + nm], outs=[nm.upper()])
        k.dma("sy", CLP[:], ap["clip2"], ins=["d:clip2"], outs=["CLP"])
        k.se(lambda h: h.activation(CLP[:], CLP[:], ACT.Sigmoid), ins=["CLP"], outs=["CLP"])

        def absmax_to_scalar(col_ap, col_name, dst_sl):
            k.ve(lambda h: h.memset(XST[:, 0:128], 0.0), outs=["XST"])
            k.ve(lambda h: h.tensor_copy(XST[:, 0:col_ap.shape[1]], col_ap),
                 ins=[col_name], outs=["XST"])
            k.tr(PE[:, 0:128], XST[:, 0:128], ins=["XST"], outs=["PE"])
            k.ve(lambda h: h.reduce_max(XST2[:, 0:1], PE[:, 0:128], AX.X),
                 ins=["PE"], outs=["XST2"])
            k.ve(lambda h: h.memset(XST[:, 0:128], 0.0), outs=["XST"])
            k.ve(lambda h: h.tensor_copy(XST[:, 0:1], XST2[:, 0:1]), ins=["XST2"], outs=["XST"])
            k.tr(PE[:, 0:128], XST[:, 0:128], ins=["XST"], outs=["PE"])
            k.ve(lambda h: h.memset(ROWS[0:1, 0:128], 0.0), outs=["ROWS"])
            k.ve(lambda h: h.reduce_max(ROWS[0:1, 0:1], PE[0:1, 0:128], AX.X),
                 ins=["PE"], outs=["ROWS"])

        def quant_weights(asis, trans, spill, nrow_tiles, ncols, scol, scoln):
            for t8 in range(nrow_tiles):
                k.dma("sy", WTQ[:, 0:ncols], ap[asis][t8 * 128:(t8 + 1) * 128, :],
                      ins=["d:" + asis], outs=["WTQ"])
                k.ve(lambda h, t8=t8, ncols=ncols: h.reduce_max(
                    scol[:, t8:t8 + 1], WTQ[:, 0:ncols], AX.X, apply_absolute_value=True),
                    ins=["WTQ"], outs=[scoln])
            k.ve(lambda h, n=nrow_tiles: h.tensor_scalar_mul(scol[:, 0:n], scol[:, 0:n],
                                                             1.0 / 127.0),
                 ins=[scoln], outs=[scoln])
            k.dma("ge", ap["rowb"][:, 0:nrow_tiles], scol[:, 0:nrow_tiles],
                  ins=[scoln], outs=["d:rowb"])
            nq = nrow_tiles * 128
            for cc in range((nq + 511) // 512):
                w = min(512, nq - cc * 512)
                for j in range((w + 127) // 128):
                    t8 = cc * 4 + j
                    k.p.emit("sy", lambda h, j=j, t8=t8: h.dma_start(
                        out=ROWS[0:1, j * 128:(j + 1) * 128],
                        in_=ap["rowb"].rearrange("p a -> a p")[t8:t8 + 1, :],
                        allow_slow_non_contiguous=True),
                        ins=["d:rowb"], outs=["ROWS"], dma=True)
                k.ve(lambda h, w=w: h.reciprocal(ROWS[0:1, 0:w], ROWS[0:1, 0:w]),
                     ins=["ROWS"], outs=["ROWS"])
                k.bcast_row(PB[:, 0:w], ROWS[0:1, 0:w], ins=["ROWS"], outs=["PB"])
                k.ve(lambda h, cc=cc, w=w: h.tensor_copy(RECB[:, cc * 512:cc * 512 + w],
                                                         PB[:, 0:w]), ins=["PB"], outs=["RECB"])
            ntile = (ap[trans].shape[0]) // 128
            for t8 in range(ntile):
                k.dma("sy", WTQ[:, 0:nq], ap[trans][t8 * 128:(t8 + 1) * 128, :],
                      ins=["d:" + trans], outs=["WTQ"])
                k.ve(lambda h, nq=nq: h.tensor_tensor(WTQ[:, 0:nq], WTQ[:, 0:nq],
                                                      RECB[:, 0:nq], ALU.mult),
                     ins=["WTQ", "RECB"], outs=["WTQ"])
                k.ve(lambda h, nq=nq: h.tensor_copy(WTQI[:, 0:nq], WTQ[:, 0:nq]),
                     ins=["WTQ"], outs=["WTQI"])
                k.ve(lambda h, nq=nq: h.tensor_scalar(WTQB[:, 0:nq], WTQI[:, 0:nq],
                                                      -128.0, 127.0, ALU.max, ALU.min),
                     ins=["WTQI"], outs=["WTQB"])
                k.dma("ge", ap[spill][t8 * 128:(t8 + 1) * 128, :], WTQB[:, 0:nq],
                      ins=["WTQB"], outs=["d:" + spill])

        if run_g2:
            quant_weights("wg2", "wg2T", "wgq_sp", 22, 1024, SCG, "SCG")
            quant_weights("wu2", "wu2T", "wuq_sp", 22, 1024, SCU, "SCU")
            quant_weights("wd2", "wd2T", "wdq_sp", 8, 2816, SCD, "SCD")

            for q in range(8):
                k.ve(lambda h, q=q: h.reduce_max(COLS2[:, 1:2], XNB[:, q * TC:(q + 1) * TC],
                                                 AX.X, apply_absolute_value=True),
                     ins=["XNB"], outs=["COLS2"])
                if q == 0:
                    k.ve(lambda h: h.tensor_copy(COLS2[:, 0:1], COLS2[:, 1:2]),
                         ins=["COLS2"], outs=["COLS2"])
                else:
                    k.ve(lambda h: h.tensor_tensor(COLS2[:, 0:1], COLS2[:, 0:1],
                                                   COLS2[:, 1:2], ALU.max),
                         ins=["COLS2"], outs=["COLS2"])
            absmax_to_scalar(COLS2[:, 0:1], "COLS2", np.s_[0:128])
            k.dma("ge", ap["cc_gm_in"], ROWS[0:1, 0:128], ins=["ROWS"], outs=["d:cc_gm_in"])
            k.cc("AllReduce", ALU.max, ALL8, ap["cc_gm_in"], ap["cc_gm_out"],
                 ins=["d:cc_gm_in"], outs=["d:cc_gm_out"])

        p.barrier()
        l2q.close()
        l2a = ExitStack()
        QF2 = k.sbuf("QF2", [128, 16384], BF16, scope=l2a)
        if run_attn2:
            k.ve(lambda h: h.memset(KV2[:], 0.0), outs=["KV2"])
            for tc4 in range(4):
                t0 = tc4 * 512
                for mc in range(8):
                    for wname, widx in (("wq2", 0), ("wk2", 1), ("wv2", 2)):
                        wb, wbn, nk = load_w_block(wname, mc * 128, 128, idx=mc * 3 + widx)
                        bank = PB if widx == 0 else (PC if widx == 1 else PD)
                        bn = "PB" if widx == 0 else ("PC" if widx == 1 else "PD")
                        for dc in range(8):
                            k.mm(bank[:, 0:512], wb[:, dc * 128:(dc + 1) * 128],
                                 XNB[:, dc * TC + t0: dc * TC + t0 + 512],
                                 ins=[wbn, "XNB"], outs=[bn],
                                 start=(dc == 0), stop=(dc == 7))
                    k.ve(lambda h, mc=mc: h.tensor_scalar(
                        QT2[:], PB[:, 0:512], BQ2[:, mc:mc + 1], None, ALU.add),
                        ins=["PB", "BQ2"], outs=["QT2"])
                    k.ve(lambda h: h.tensor_scalar_min(FM2[:], QT2[:], 0.0),
                         ins=["QT2"], outs=["FM2"])
                    k.se(lambda h: h.activation(FM3[:], FM2[:], ACT.Exp),
                         ins=["FM2"], outs=["FM3"])
                    k.p.emit("ve", lambda h, mc=mc, t0=t0: h.scalar_tensor_tensor(
                        QF2[:, mc * TC + t0: mc * TC + t0 + 512], QT2[:], 0.0, FM3[:],
                        ALU.max, ALU.add), ins=["QT2", "FM3"], outs=["QF2"])
                    k.ve(lambda h, mc=mc: h.tensor_scalar(
                        QT2[:], PC[:, 0:512], BK2[:, mc:mc + 1], None, ALU.add),
                        ins=["PC", "BK2"], outs=["QT2"])
                    k.ve(lambda h: h.tensor_scalar_min(FM2[:], QT2[:], 0.0),
                         ins=["QT2"], outs=["FM2"])
                    k.se(lambda h: h.activation(FM3[:], FM2[:], ACT.Exp),
                         ins=["FM2"], outs=["FM3"])
                    k.p.emit("ve", lambda h: h.scalar_tensor_tensor(
                        KT2[:], QT2[:], 0.0, FM3[:], ALU.max, ALU.add),
                        ins=["QT2", "FM3"], outs=["KT2"])
                    k.ve(lambda h, mc=mc: h.tensor_scalar(
                        VT2[:], PD[:, 0:512], BV2[:, mc:mc + 1], None, ALU.add),
                        ins=["PD", "BV2"], outs=["VT2"])
                    k.ve(lambda h: h.tensor_tensor(FM2[:], KT2[:], VT2[:], ALU.mult),
                         ins=["KT2", "VT2"], outs=["FM2"])
                    k.ve(lambda h: h.reduce_sum(COLS2[:, 2:3], FM2[:], AX.X),
                         ins=["FM2"], outs=["COLS2"])
                    k.ve(lambda h, mc=mc: h.tensor_tensor(KV2[:, mc:mc + 1], KV2[:, mc:mc + 1],
                                                          COLS2[:, 2:3], ALU.add),
                         ins=["COLS2", "KV2"], outs=["KV2"])
                    k.ve(lambda h: h.reduce_sum(COLS2[:, 2:3], KT2[:], AX.X),
                         ins=["KT2"], outs=["COLS2"])
                    k.ve(lambda h, mc=mc: h.tensor_tensor(KV2[:, 8 + mc:9 + mc],
                                                          KV2[:, 8 + mc:9 + mc],
                                                          COLS2[:, 2:3], ALU.add),
                         ins=["COLS2", "KV2"], outs=["KV2"])
            k.dma("ge", ap["cc_kv_in"], KV2[:], ins=["KV2"], outs=["d:cc_kv_in"])
            k.cc("AllReduce", ALU.add, PAIRS, ap["cc_kv_in"], ap["cc_kv_out"],
                 ins=["d:cc_kv_in"], outs=["d:cc_kv_out"])
            k.dma("sy", KV2[:], ap["cc_kv_out"], ins=["d:cc_kv_out"], outs=["KV2"])
            VF2 = k.sbuf("VF2", [128, 16384], BF16, scope=l2a)
            for tc4 in range(4):
                t0 = tc4 * 512
                for mc in range(8):
                    k.ve(lambda h, mc=mc, t0=t0: h.tensor_scalar(
                        FM2[:], QF2[:, mc * TC + t0: mc * TC + t0 + 512],
                        KV2[:, 8 + mc:9 + mc], 1e-6, ALU.mult, ALU.add),
                        ins=["QF2", "KV2"], outs=["FM2"])
                    k.ve(lambda h: h.reciprocal(FM2[:], FM2[:]), ins=["FM2"], outs=["FM2"])
                    k.ve(lambda h, mc=mc, t0=t0: h.tensor_scalar(
                        FM3[:], QF2[:, mc * TC + t0: mc * TC + t0 + 512],
                        KV2[:, mc:mc + 1], None, ALU.mult),
                        ins=["QF2", "KV2"], outs=["FM3"])
                    k.ve(lambda h, mc=mc, t0=t0: h.tensor_tensor(
                        VF2[:, mc * TC + t0: mc * TC + t0 + 512], FM3[:], FM2[:], ALU.mult),
                        ins=["FM3", "FM2"], outs=["VF2"])
            for tc4 in range(4):
                t0 = tc4 * 512
                for mc in range(8):
                    wb, wbn, nk = load_w_block("wo2", mc * 128, 128, idx=mc)
                    for dc in range(8):
                        k.mm(PB[:, 0:512], wb[:, dc * 128:(dc + 1) * 128],
                             VF2[:, dc * TC + t0: dc * TC + t0 + 512],
                             ins=[wbn, "VF2"], outs=["PB"], start=(dc == 0), stop=(dc == 7))
                    k.se(lambda h, mc=mc: h.activation(
                        SQB[:, mc * 512:(mc + 1) * 512], PB[:, 0:512], ACT.Gelu_apprx_tanh,
                        bias=BO2[:, mc:mc + 1]), ins=["PB", "BO2"], outs=["SQB"])
                k.dma("ge", ap["s_sp"].rearrange("p (a t) -> p a t", t=TC)[:, :, t0:t0 + 512],
                      SQB[:].rearrange("p (a t) -> p a t", t=512), ins=["SQB"], outs=["d:s_sp"])
        if DBG == "s2":
            dump_bf16_to_xr("s_sp")

        p.barrier()
        l2a.close()
        l2f = ExitStack()
        XQ2 = k.sbuf("XQ2", [128, 16384], BF16, scope=l2f)
        HQF = k.sbuf("HQF", [128, 22 * 512], BF16, scope=l2f)
        if run_g2:
            k.dma("sy", ROWS[0:1, 0:128], ap["cc_gm_out"], ins=["d:cc_gm_out"], outs=["ROWS"])
            k.ve(lambda h: h.tensor_tensor(ROWS[0:1, 1:2], ROWS[0:1, 0:1], CLP[0:1, 0:1],
                                           ALU.mult), ins=["ROWS", "CLP"], outs=["ROWS"])
            k.ve(lambda h: h.tensor_scalar_mul(ROWS[0:1, 1:2], ROWS[0:1, 1:2], 1.0 / 127.0),
                 ins=["ROWS"], outs=["ROWS"])
            k.ve(lambda h: h.reciprocal(ROWS[0:1, 2:3], ROWS[0:1, 1:2]),
                 ins=["ROWS"], outs=["ROWS"])
            k.bcast_row(PB[:, 0:1], ROWS[0:1, 1:2], ins=["ROWS"], outs=["PB"])
            k.ve(lambda h: h.tensor_copy(COLX[:, 0:1], PB[:, 0:1]), ins=["PB"], outs=["COLX"])
            k.bcast_row(PB[:, 0:1], ROWS[0:1, 2:3], ins=["ROWS"], outs=["PB"])
            k.ve(lambda h: h.tensor_copy(COLX[:, 1:2], PB[:, 0:1]), ins=["PB"], outs=["COLX"])
            for q in range(8):
                for tc4 in range(4):
                    sl = np.s_[:, q * TC + tc4 * 512: q * TC + (tc4 + 1) * 512]
                    k.ve(lambda h, sl=sl: h.tensor_scalar(
                        FM2[:], XNB[sl], COLX[:, 1:2], None, ALU.mult),
                        ins=["XNB", "COLX"], outs=["FM2"])
                    k.ve(lambda h: h.tensor_copy(HIQ[:], FM2[:]), ins=["FM2"], outs=["HIQ"])
                    k.ve(lambda h, sl=sl: h.tensor_scalar(
                        XQ2[sl], HIQ[:], -128.0, 127.0, ALU.max, ALU.min),
                        ins=["HIQ"], outs=["XQ2"])
            k.ve(lambda h: h.memset(COLS2[:, 4:5], 0.0), outs=["COLS2"])
            for tc4 in range(4):
                t0 = tc4 * 512
                for ic in range(22):
                    for wname, widx in (("wgq_sp", 0), ("wuq_sp", 1)):
                        wbb, wbn = (WB0, "WB0") if widx == 0 else (WB1, "WB1")
                        k.dma("sy", wbb[:, 0:1024].rearrange("p (a m) -> p a m", m=128),
                              ap[wname].rearrange("(a p) m -> p a m", p=128)
                              [:, :, ic * 128:(ic + 1) * 128],
                              ins=["d:" + wname], outs=[wbn])
                        bank, bn = (PB, "PB") if widx == 0 else (PC, "PC")
                        for dc in range(8):
                            k.mm(bank[:, 0:512], wbb[:, dc * 128:(dc + 1) * 128],
                                 XQ2[:, dc * TC + t0: dc * TC + t0 + 512],
                                 ins=[wbn, "XQ2"], outs=[bn],
                                 start=(dc == 0), stop=(dc == 7))
                    k.ve(lambda h, ic=ic: h.tensor_tensor(
                        COLS2[:, 2:3], SCG[:, ic:ic + 1], COLX[:, 0:1], ALU.mult),
                        ins=["SCG", "COLX"], outs=["COLS2"])
                    k.se(lambda h: h.activation(HSG[:], PB[:, 0:512], ACT.Silu,
                                                scale=COLS2[:, 2:3]),
                         ins=["PB", "COLS2"], outs=["HSG"])
                    k.ve(lambda h, ic=ic: h.tensor_tensor(
                        COLS2[:, 3:4], SCU[:, ic:ic + 1], COLX[:, 0:1], ALU.mult),
                        ins=["SCU", "COLX"], outs=["COLS2"])
                    k.ve(lambda h: h.tensor_tensor(FM2[:], HSG[:], PC[:, 0:512], ALU.mult),
                         ins=["HSG", "PC"], outs=["FM2"])
                    k.ve(lambda h: h.tensor_scalar(FM3[:], FM2[:], COLS2[:, 3:4], None,
                                                   ALU.mult), ins=["FM2", "COLS2"], outs=["FM3"])
                    k.ve(lambda h: h.tensor_copy(HSG[:], FM3[:]), ins=["FM3"], outs=["HSG"])
                    k.dma("ge", ap["h_sp"].rearrange("p (a t) -> p a t", t=TC)
                          [:, ic, t0:t0 + 512], HSG[:], ins=["HSG"], outs=["d:h_sp"])
                    k.ve(lambda h: h.reduce_max(COLS2[:, 5:6], FM3[:], AX.X,
                                                apply_absolute_value=True),
                         ins=["FM3"], outs=["COLS2"])
                    k.ve(lambda h: h.tensor_tensor(COLS2[:, 4:5], COLS2[:, 4:5],
                                                   COLS2[:, 5:6], ALU.max),
                         ins=["COLS2"], outs=["COLS2"])
            absmax_to_scalar(COLS2[:, 4:5], "COLS2", np.s_[0:128])
            k.dma("ge", ap["cc_hm_in"], ROWS[0:1, 0:128], ins=["ROWS"], outs=["d:cc_hm_in"])
            k.cc("AllReduce", ALU.max, ALL8, ap["cc_hm_in"], ap["cc_hm_out"],
                 ins=["d:cc_hm_in"], outs=["d:cc_hm_out"])
            k.dma("sy", ROWS[0:1, 0:128], ap["cc_hm_out"], ins=["d:cc_hm_out"], outs=["ROWS"])
            k.ve(lambda h: h.tensor_tensor(ROWS[0:1, 1:2], ROWS[0:1, 0:1], CLP[0:1, 1:2],
                                           ALU.mult), ins=["ROWS", "CLP"], outs=["ROWS"])
            k.ve(lambda h: h.tensor_scalar_mul(ROWS[0:1, 1:2], ROWS[0:1, 1:2], 1.0 / 127.0),
                 ins=["ROWS"], outs=["ROWS"])
            k.ve(lambda h: h.reciprocal(ROWS[0:1, 2:3], ROWS[0:1, 1:2]),
                 ins=["ROWS"], outs=["ROWS"])
            k.bcast_row(PB[:, 0:1], ROWS[0:1, 1:2], ins=["ROWS"], outs=["PB"])
            k.ve(lambda h: h.tensor_copy(COLX[:, 2:3], PB[:, 0:1]), ins=["PB"], outs=["COLX"])
            k.bcast_row(PB[:, 0:1], ROWS[0:1, 2:3], ins=["ROWS"], outs=["PB"])
            k.ve(lambda h: h.tensor_copy(COLX[:, 3:4], PB[:, 0:1]), ins=["PB"], outs=["COLX"])
            for tc4 in range(4):
                t0 = tc4 * 512
                for ic in range(22):
                    k.dma("sy", HSG[:], ap["h_sp"].rearrange("p (a t) -> p a t", t=TC)
                          [:, ic, t0:t0 + 512], ins=["d:h_sp"], outs=["HSG"])
                    k.ve(lambda h: h.tensor_scalar(FM2[:], HSG[:], COLX[:, 3:4], None,
                                                   ALU.mult), ins=["HSG", "COLX"], outs=["FM2"])
                    k.ve(lambda h: h.tensor_copy(HIQ[:], FM2[:]), ins=["FM2"], outs=["HIQ"])
                    k.ve(lambda h, ic=ic: h.tensor_scalar(
                        HQF[:, ic * 512:(ic + 1) * 512], HIQ[:], -128.0, 127.0,
                        ALU.max, ALU.min), ins=["HIQ"], outs=["HQF"])
                for mc in range(8):
                    for ic in range(22):
                        wbb, wbn = (WB0, "WB0") if ic % 2 == 0 else (WB1, "WB1")
                        k.dma("sy", wbb[:, 0:128],
                              ap["wdq_sp"][ic * 128:(ic + 1) * 128, mc * 128:(mc + 1) * 128],
                              ins=["d:wdq_sp"], outs=[wbn])
                        k.mm(PB[:, 0:512], wbb[:, 0:128],
                             HQF[:, ic * 512:(ic + 1) * 512],
                             ins=[wbn, "HQF"], outs=["PB"],
                             start=(ic == 0), stop=(ic == 21))
                    k.ve(lambda h, mc=mc: h.tensor_tensor(
                        COLS2[:, 2:3], SCD[:, mc:mc + 1], COLX[:, 2:3], ALU.mult),
                        ins=["SCD", "COLX"], outs=["COLS2"])
                    k.se(lambda h, mc=mc: h.activation(
                        SQB[:, mc * 512:(mc + 1) * 512], PB[:, 0:512], ACT.Silu,
                        scale=COLS2[:, 2:3]), ins=["PB", "COLS2"], outs=["SQB"])
                k.dma("ge", ap["g_sp"].rearrange("p (a t) -> p a t", t=TC)[:, :, t0:t0 + 512],
                      SQB[:].rearrange("p (a t) -> p a t", t=512), ins=["SQB"], outs=["d:g_sp"])
        if DBG == "g2":
            dump_bf16_to_xr("g_sp")
        if DBG is None:
            emit_combine()
        p.barrier()
        l2f.close()
        l2.close()


    if NLAYERS >= 4:
        emit_layernorm()
        if DBG == "xn3":
            dump_xnb_to_xr()
        run_g3 = DBG in (None, "g3", "s3")
        run_attn3 = DBG in (None, "s3", "m1", "m1xc", "m1y", "m1z", "m1xzb", "m1xcb", "m1yg", "m1xmr")
        l3 = ExitStack()
        NW3 = k.sbuf("NW3", [128, 16], F32, scope=l3)
        CW3 = k.sbuf("CW3", [128, 128], F32, scope=l3)
        CB3 = k.sbuf("CB3", [128, 32], F32, scope=l3)
        DTB3 = k.sbuf("DTB3", [128, 32], F32, scope=l3)
        AL3 = k.sbuf("AL3", [128, 512], F32, scope=l3)
        DP3 = k.sbuf("DP3", [128, 32], F32, scope=l3)
        B13 = k.sbuf("B13", [128, 32], F32, scope=l3)
        B23 = k.sbuf("B23", [128, 8], F32, scope=l3)
        XT3 = k.sbuf("XT3", [128, 24], F32, scope=l3)
        HINC = k.sbuf("HINC", [128, 256], F32, scope=l3)
        k.dram("xi_sp", [128, 8 * TC])
        k.dram("xz_sp", [128, 16 * TC], BF16)
        k.dram("xc_sp", [128, 16 * TC], BF16)
        k.dram("dc_sp", [128, 16 * TC], BF16)
        k.dram("y_sp", [128, 16 * TC], BF16)
        k.dram("cbc_sp", [128, 4 * 8192], BF16)
        k.dram("dbg_sp", [128, 16 * TC], BF16)
        k.dram("cc_t3_in", [128, 128])
        k.dram("cc_t3_out", [2, 128, 128])
        k.dram("cc_st_in", [128, 256])
        k.dram("cc_st_out", [2, 128, 256])
        for nm, t_ in (("nw3", NW3), ("cw3", CW3), ("cb3", CB3), ("dtb3", DTB3),
                       ("al3", AL3), ("dp3", DP3)):
            src = ap[nm]
            w = src.shape[2]
            k.dma("sy", t_[:, 0:w], src[0], ins=["d:" + nm], outs=[nm.upper()])
            k.dma("sy", t_[:, w:2 * w], src[1], ins=["d:" + nm], outs=[nm.upper()])
        k.se(lambda h: h.activation(AL3[:], AL3[:], ACT.Exp), ins=["AL3"], outs=["AL3"])
        k.ve(lambda h: h.tensor_scalar_mul(AL3[:], AL3[:], -1.0), ins=["AL3"], outs=["AL3"])
        k.dma("sy", B13[:], ap["b13"], ins=["d:b13"], outs=["B13"])
        k.dma("sy", B23[:], ap["b23"], ins=["d:b23"], outs=["B23"])

        if run_g3:
            l3g = ExitStack()
            H13 = k.sbuf("H13", [128, 32 * 512], BF16, scope=l3g)
            for tc4 in range(4):
                t0 = tc4 * 512
                for mf in range(32):
                    wb, wbn, nk = load_w_block("w13", mf * 128, 128, idx=mf)
                    for dc in range(8):
                        k.mm(PB[:, 0:512], wb[:, dc * 128:(dc + 1) * 128],
                             XNB[:, dc * TC + t0: dc * TC + t0 + 512],
                             ins=[wbn, "XNB"], outs=["PB"], start=(dc == 0), stop=(dc == 7))
                    k.se(lambda h, mf=mf: h.activation(
                        H13[:, mf * 512:(mf + 1) * 512], PB[:, 0:512], ACT.Gelu_apprx_tanh,
                        bias=B13[:, mf:mf + 1]), ins=["PB", "B13"], outs=["H13"])
                for mc in range(8):
                    for ff in range(32):
                        wsb, wsn = (WS0, "WS0") if ff % 2 == 0 else (WS1, "WS1")
                        wbb, wbn = (WB0, "WB0") if ff % 2 == 0 else (WB1, "WB1")
                        k.dma("sy", wsb[:, 0:128],
                              ap["w23"][ff * 128:(ff + 1) * 128, mc * 128:(mc + 1) * 128],
                              ins=["d:w23"], outs=[wsn])
                        k.ge(lambda h, wbb=wbb, wsb=wsb: h.tensor_copy(
                            wbb[:, 0:128], wsb[:, 0:128]), ins=[wsn], outs=[wbn])
                        k.mm(PC[:, 0:512], wbb[:, 0:128],
                             H13[:, ff * 512:(ff + 1) * 512],
                             ins=[wbn, "H13"], outs=["PC"], start=(ff == 0), stop=(ff == 31))
                    k.se(lambda h, mc=mc: h.activation(
                        SQB[:, mc * 512:(mc + 1) * 512], PC[:, 0:512], ACT.Silu,
                        bias=B23[:, mc:mc + 1]), ins=["PC", "B23"], outs=["SQB"])
                k.dma("ge", ap["g_sp"].rearrange("p (a t) -> p a t", t=TC)[:, :, t0:t0 + 512],
                      SQB[:].rearrange("p (a t) -> p a t", t=512), ins=["SQB"], outs=["d:g_sp"])
            p.barrier()
            l3g.close()
        if DBG == "g3":
            dump_bf16_to_xr("g_sp")

        if run_attn3:
            for q in range(8):
                for tc4 in range(4):
                    st, stn = (XST, "XST") if tc4 % 2 == 0 else (XST2, "XST2")
                    k.ve(lambda h, st=st, q=q, tc4=tc4: h.tensor_copy(
                        st[:], XNB[:, q * TC + tc4 * 512: q * TC + (tc4 + 1) * 512]),
                        ins=["XNB"], outs=[stn])
                    k.dma("ge", ap["xi_sp"].rearrange("p (a t) -> p a t", t=TC)
                          [:, q, tc4 * 512:(tc4 + 1) * 512], st[:], ins=[stn], outs=["d:xi_sp"])

            TCK = 256
            NCH = TC // TCK
            l3m = ExitStack()
            XMR = k.sbuf("XMR", [128, 8 * TCK], BF16, scope=l3m)
            XSC = k.sbuf("XSC", [128, 8 * TCK], F32, scope=l3m)
            XIC = k.sbuf("XIC", [128, TCK], F32, scope=l3m)
            XZB = k.sbuf("XZB", [128, 16 * TCK], BF16, scope=l3m)
            XCB = k.sbuf("XCB", [128, 16 * TCK], BF16, scope=l3m)
            UALL = k.sbuf("UALL", [128, 16 * TCK], BF16, scope=l3m)
            TAIL = k.sbuf("TAIL", [128, 48], BF16, scope=l3m)
            XZT = k.sbuf("XZT", [128, 48], F32, scope=l3m)
            DBC = k.sbuf("DBC", [128, TCK], F32, scope=l3m)
            DBCB = k.sbuf("DBCB", [128, TCK], BF16, scope=l3m)
            DLT = k.sbuf("DLT", [128, 16 * TCK], BF16, scope=l3m)
            DCC = k.sbuf("DCC", [128, 16 * TCK], BF16, scope=l3m)
            DCAR = k.sbuf("DCAR", [128, 17], F32, scope=l3m)
            BBALL = k.sbuf("BBALL", [128, 16 * TCK], BF16, scope=l3m)
            CCALL = k.sbuf("CCALL", [128, 16 * TCK], BF16, scope=l3m)
            DAT = k.sbuf("DAT", [128, TCK], F32, scope=l3m)
            BXT = k.sbuf("BXT", [128, TCK], BF16, scope=l3m)
            HS1 = k.sbuf("HS1", [128, TCK], F32, scope=l3m)
            CARRY = k.sbuf("CARRY", [128, 256], F32, scope=l3m)
            YMI = k.sbuf("YMI", [128, 16 * TCK], BF16, scope=l3m)
            YLB = k.sbuf("YLB", [128, TCK], F32, scope=l3m)
            YLBB = k.sbuf("YLBB", [128, TCK], BF16, scope=l3m)
            YT = k.sbuf("YT", [128, TCK], F32, scope=l3m)
            ZSL = k.sbuf("ZSL", [128, TCK], F32, scope=l3m)
            YGALL = k.sbuf("YGALL", [128, 16 * TCK], BF16, scope=l3m)

            def mamba_layer(li):
                c16 = li * 16
                for dc in range(8):
                    k.dma("sy", XST2[:, dc * 3:dc * 3 + 3],
                          ap["xi_sp"].rearrange("p (a t) -> p a t", t=TC)[:, dc, 2045:2048],
                          ins=["d:xi_sp"], outs=["XST2"])
                k.ve(lambda h: h.memset(XST2[:, 24:128], 0.0), outs=["XST2"])
                k.dma("ge", ap["cc_t3_in"], XST2[:, 0:128], ins=["XST2"], outs=["d:cc_t3_in"])
                k.cc("AllGather", ALU.bypass, PAIRS, ap["cc_t3_in"], ap["cc_t3_out"],
                     ins=["d:cc_t3_in"], outs=["d:cc_t3_out"])
                k.dma("sy", XT3[:, 0:24], ap["cc_t3_out"][0][:, 0:24],
                      ins=["d:cc_t3_out"], outs=["XT3"])
                k.ve(lambda h: h.tensor_scalar(XT3[:], XT3[:], HSEL[:, 0:1], None, ALU.mult),
                     ins=["XT3", "HSEL"], outs=["XT3"])
                for tc4 in range(NCH):
                    t0 = tc4 * TCK
                    for dc in range(8):
                        k.dma("sy", XIC[:], ap["xi_sp"].rearrange("p (a t) -> p a t", t=TC)
                              [:, dc, t0:t0 + TCK], ins=["d:xi_sp"], outs=["XIC"])
                        k.ve(lambda h: h.tensor_tensor(XST[:, 0:TCK], XIC[:], XIC[:],
                                                       ALU.mult), ins=["XIC"], outs=["XST"])
                        k.ve(lambda h: h.tensor_copy(SQB[:, 0:TCK], XST[:, 0:TCK]),
                             ins=["XST"], outs=["SQB"])
                        k.mm(PB[0:1, 0:TCK], ONES[:], SQB[:, 0:TCK],
                             ins=["ONESB", "SQB"], outs=["PB"],
                             start=(dc == 0), stop=(dc == 7))
                        k.ve(lambda h, dc=dc: h.tensor_scalar(
                            XSC[:, dc * TCK:(dc + 1) * TCK], XIC[:],
                            NW3[:, li * 8 + dc:li * 8 + dc + 1],
                            None, ALU.mult), ins=["XIC", "NW3"], outs=["XSC"])
                    k.ve(lambda h: h.tensor_scalar(ROWS[0:1, 0:TCK], PB[0:1, 0:TCK],
                                                   1.0 / D, 1e-5, ALU.mult, ALU.add),
                         ins=["PB"], outs=["ROWS"])
                    k.ve(lambda h: h.reciprocal(ROWS[0:1, 0:TCK], ROWS[0:1, 0:TCK]),
                         ins=["ROWS"], outs=["ROWS"])
                    k.se(lambda h: h.activation(ROWS[0:1, 0:TCK], ROWS[0:1, 0:TCK], ACT.Sqrt),
                         ins=["ROWS"], outs=["ROWS"])
                    k.bcast_row(PC[:, 0:TCK], ROWS[0:1, 0:TCK], ins=["ROWS"], outs=["PC"])
                    for dc in range(8):
                        k.ve(lambda h, dc=dc: h.tensor_tensor(
                            XMR[:, dc * TCK:(dc + 1) * TCK], XSC[:, dc * TCK:(dc + 1) * TCK],
                            PC[:, 0:TCK], ALU.mult), ins=["XSC", "PC"], outs=["XMR"])
                    if DBG == "m1xmr":
                        k.dma("ge", ap["dbg_sp"].rearrange("p (a t) -> p a t", t=TC)
                              [:, 0:8, t0:t0 + TCK], XMR[:].rearrange("p (a t) -> p a t", t=TCK),
                              ins=["XMR"], outs=["d:dbg_sp"])
                    if tc4 == 0:
                        for dc in range(8):
                            k.ve(lambda h, dc=dc: h.tensor_tensor(
                                XST[:, dc * 3:dc * 3 + 3], XT3[:, dc * 3:dc * 3 + 3],
                                XT3[:, dc * 3:dc * 3 + 3], ALU.mult),
                                ins=["XT3"], outs=["XST"])
                        k.ve(lambda h: h.tensor_copy(SQB[:, 0:24], XST[:, 0:24]),
                             ins=["XST"], outs=["SQB"])
                        for dc in range(8):
                            k.mm(PD[0:1, dc * 3:dc * 3 + 3] if False else PD[0:1, 0:3],
                                 ONES[:], SQB[:, dc * 3:dc * 3 + 3],
                                 ins=["ONESB", "SQB"], outs=["PD"],
                                 start=(dc == 0), stop=(dc == 7))
                        k.ve(lambda h: h.tensor_scalar(ROWS[0:1, TCK:TCK + 3], PD[0:1, 0:3],
                                                       1.0 / D, 1e-5, ALU.mult, ALU.add),
                             ins=["PD"], outs=["ROWS"])
                        k.ve(lambda h: h.reciprocal(ROWS[0:1, TCK:TCK + 3], ROWS[0:1, TCK:TCK + 3]),
                             ins=["ROWS"], outs=["ROWS"])
                        k.se(lambda h: h.activation(ROWS[0:1, TCK:TCK + 3], ROWS[0:1, TCK:TCK + 3],
                                                    ACT.Sqrt), ins=["ROWS"], outs=["ROWS"])
                        k.bcast_row(PD[:, 4:7], ROWS[0:1, TCK:TCK + 3], ins=["ROWS"], outs=["PD"])
                        for dc in range(8):
                            k.p.emit("ve", lambda h, dc=dc: h.scalar_tensor_tensor(
                                XST2[:, dc * 3:dc * 3 + 3], XT3[:, dc * 3:dc * 3 + 3],
                                NW3[:, li * 8 + dc:li * 8 + dc + 1], PD[:, 4:7],
                                ALU.mult, ALU.mult), ins=["XT3", "NW3", "PD"], outs=["XST2"])
                        k.ve(lambda h: h.tensor_copy(GST[:, 0:24], XST2[:, 0:24]),
                             ins=["XST2"], outs=["GST"])
                    for mz in range(32):
                        wb, wbn, nk = load_w_block("inw3_l%d" % li, mz * 128, 128, idx=mz)
                        for dc in range(8):
                            k.mm(PB[:, 0:TCK], wb[:, dc * 128:(dc + 1) * 128],
                                 XMR[:, dc * TCK:(dc + 1) * TCK],
                                 ins=[wbn, "XMR"], outs=["PB"], start=(dc == 0), stop=(dc == 7))
                        if mz < 16:
                            k.ve(lambda h, mz=mz: h.tensor_copy(
                                XZB[:, mz * TCK:(mz + 1) * TCK], PB[:, 0:TCK]),
                                ins=["PB"], outs=["XZB"])
                        else:
                            k.ve(lambda h: h.tensor_copy(DBCB[:, 0:TCK], PB[:, 0:TCK]),
                                 ins=["PB"], outs=["DBCB"])
                            k.dma("sy", ap["xz_sp"].rearrange("p (a t) -> p a t", t=TC)
                                  [:, mz - 16, t0:t0 + TCK], DBCB[:, 0:TCK],
                                  ins=["DBCB"], outs=["d:xz_sp"])
                        if tc4 == 0 and mz < 16:
                            for dc in range(8):
                                k.mm(PC[:, 0:3], wb[:, dc * 128:(dc + 1) * 128],
                                     GST[:, dc * 3:dc * 3 + 3],
                                     ins=[wbn, "GST"], outs=["PC"],
                                     start=(dc == 0), stop=(dc == 7))
                            k.ve(lambda h, mz=mz: h.tensor_copy(
                                XZT[:, mz * 3:(mz + 1) * 3], PC[:, 0:3]),
                                ins=["PC"], outs=["XZT"])
                    if tc4 == 0:
                        k.ve(lambda h: h.tensor_scalar(TAIL[:], XZT[:], HSEL[:, 0:1],
                                                       None, ALU.mult),
                             ins=["XZT", "HSEL"], outs=["TAIL"])
                    for eh in range(16):
                        sl = np.s_[:, eh * TCK:(eh + 1) * TCK]
                        cwc = lambda kk, eh=eh: CW3[:, li * 64 + eh * 4 + kk: li * 64 + eh * 4 + kk + 1]
                        k.ve(lambda h, eh=eh, cwc=cwc: h.tensor_scalar(
                            XST[:, 3:TCK], XZB[:, eh * TCK: eh * TCK + TCK - 3], cwc(0),
                            None, ALU.mult), ins=["XZB", "CW3"], outs=["XST"])
                        k.ve(lambda h, eh=eh, cwc=cwc: h.tensor_scalar(
                            XST[:, 0:3], TAIL[:, eh * 3:(eh + 1) * 3], cwc(0),
                            None, ALU.mult), ins=["TAIL", "CW3"], outs=["XST"])
                        for kk in (1, 2):
                            k.p.emit("ve", lambda h, eh=eh, kk=kk, cwc=cwc: h.scalar_tensor_tensor(
                                XST[:, kk:TCK], XZB[:, eh * TCK: eh * TCK + TCK - kk],
                                cwc(3 - kk), XST[:, kk:TCK], ALU.mult, ALU.add),
                                ins=["XZB", "CW3", "XST"], outs=["XST"])
                            k.p.emit("ve", lambda h, eh=eh, kk=kk, cwc=cwc: h.scalar_tensor_tensor(
                                XST[:, 0:kk], TAIL[:, eh * 3 + 3 - kk: eh * 3 + 3], cwc(3 - kk),
                                XST[:, 0:kk], ALU.mult, ALU.add),
                                ins=["TAIL", "CW3", "XST"], outs=["XST"])
                        k.p.emit("ve", lambda h, sl=sl, cwc=cwc: h.scalar_tensor_tensor(
                            XST[:, 0:TCK], XZB[sl], cwc(3), XST[:, 0:TCK], ALU.mult, ALU.add),
                            ins=["XZB", "CW3", "XST"], outs=["XST"])
                        k.se(lambda h, sl=sl, eh=eh: h.activation(
                            XCB[sl], XST[:, 0:TCK], ACT.Silu,
                            bias=CB3[:, c16 + eh:c16 + eh + 1]), ins=["XST", "CB3"], outs=["XCB"])
                        k.ve(lambda h, eh=eh: h.tensor_copy(
                            TAIL[:, eh * 3:(eh + 1) * 3],
                            XZB[:, eh * TCK + TCK - 3: eh * TCK + TCK]), ins=["XZB"], outs=["TAIL"])
                    k.dma("ge", ap["xc_sp"].rearrange("p (a t) -> p a t", t=TC)
                          [:, :, t0:t0 + TCK], XCB[:].rearrange("p (a t) -> p a t", t=TCK),
                          ins=["XCB"], outs=["d:xc_sp"])
                    if DBG == "m1xzb":
                        k.dma("ge", ap["y_sp"].rearrange("p (a t) -> p a t", t=TC)
                              [:, :, t0:t0 + TCK], XZB[:].rearrange("p (a t) -> p a t", t=TCK),
                              ins=["XZB"], outs=["d:y_sp"])
                    if DBG == "m1xcb":
                        k.dma("ge", ap["y_sp"].rearrange("p (a t) -> p a t", t=TC)
                              [:, :, t0:t0 + TCK], XCB[:].rearrange("p (a t) -> p a t", t=TCK),
                              ins=["XCB"], outs=["d:y_sp"])
                    for eh in range(16):
                        wsb, wsn = (WS0, "WS0") if eh % 2 == 0 else (WS1, "WS1")
                        wbb, wbn = (WB0, "WB0") if eh % 2 == 0 else (WB1, "WB1")
                        k.dma("sy", wsb[:, 0:96],
                              ap["xpw3_l%d" % li][eh * 128:(eh + 1) * 128, :],
                              ins=["d:xpw3_l%d" % li], outs=[wsn])
                        k.ge(lambda h, wbb=wbb, wsb=wsb: h.tensor_copy(
                            wbb[:, 0:96], wsb[:, 0:96]), ins=[wsn], outs=[wbn])
                        k.mm(PD[0:96, 0:TCK], wbb[:, 0:96], XCB[:, eh * TCK:(eh + 1) * TCK],
                             ins=[wbn, "XCB"], outs=["PD"], start=(eh == 0), stop=(eh == 15))
                    k.ve(lambda h: h.tensor_copy(DBC[0:96, 0:TCK], PD[0:96, 0:TCK]),
                         ins=["PD"], outs=["DBC"])
                    k.ve(lambda h: h.tensor_copy(DBCB[0:64, 0:TCK], DBC[0:64, 0:TCK]),
                         ins=["DBC"], outs=["DBCB"])
                    for eh in range(16):
                        wsb, wsn = (WS0, "WS0") if eh % 2 == 0 else (WS1, "WS1")
                        wbb, wbn = (WB0, "WB0") if eh % 2 == 0 else (WB1, "WB1")
                        k.dma("sy", wsb[0:64, 0:128],
                              ap["dtw3_l%d" % li][:, eh * 128:(eh + 1) * 128],
                              ins=["d:dtw3_l%d" % li], outs=[wsn])
                        k.ge(lambda h, wbb=wbb, wsb=wsb: h.tensor_copy(
                            wbb[0:64, 0:128], wsb[0:64, 0:128]), ins=[wsn], outs=[wbn])
                        k.mm(PC[:, 0:TCK], wbb[0:64, 0:128], DBCB[0:64, 0:TCK],
                             ins=[wbn, "DBCB"], outs=["PC"])
                        k.ve(lambda h, eh=eh: h.tensor_scalar(
                            XST[:, 0:TCK], PC[:, 0:TCK], DTB3[:, c16 + eh:c16 + eh + 1],
                            None, ALU.add), ins=["PC", "DTB3"], outs=["XST"])
                        k.se(lambda h: h.activation(ZSL[:], XST[:, 0:TCK], ACT.Abs),
                             ins=["XST"], outs=["ZSL"])
                        k.se(lambda h: h.activation(ZSL[:], ZSL[:], ACT.Exp, scale=-1.0),
                             ins=["ZSL"], outs=["ZSL"])
                        k.ve(lambda h: h.tensor_scalar_add(ZSL[:], ZSL[:], 1.0),
                             ins=["ZSL"], outs=["ZSL"])
                        k.se(lambda h: h.activation(ZSL[:], ZSL[:], ACT.Ln),
                             ins=["ZSL"], outs=["ZSL"])
                        k.p.emit("ve", lambda h, eh=eh: h.scalar_tensor_tensor(
                            DLT[:, eh * TCK:(eh + 1) * TCK], XST[:, 0:TCK], 0.0, ZSL[:],
                            ALU.max, ALU.add), ins=["XST", "ZSL"], outs=["DLT"])
                        k.ve(lambda h, eh=eh: h.tensor_tensor(
                            UALL[:, eh * TCK:(eh + 1) * TCK], DLT[:, eh * TCK:(eh + 1) * TCK],
                            XCB[:, eh * TCK:(eh + 1) * TCK], ALU.mult),
                            ins=["DLT", "XCB"], outs=["UALL"])
                        if tc4 == 0:
                            k.ve(lambda h, eh=eh: h.memset(DCAR[:, eh:eh + 1], 0.0),
                                 outs=["DCAR"])
                        k.p.emit("ve", lambda h, eh=eh: h.tensor_tensor_scan(
                            DCC[:, eh * TCK:(eh + 1) * TCK], DLT[:, eh * TCK:(eh + 1) * TCK],
                            DLT[:, eh * TCK:(eh + 1) * TCK], DCAR[:, eh:eh + 1],
                            ALU.add, ALU.bypass), ins=["DLT", "DCAR"], outs=["DCC"])
                        k.ve(lambda h, eh=eh: h.reduce_sum(
                            DCAR[:, 16:17], DLT[:, eh * TCK:(eh + 1) * TCK], AX.X),
                            ins=["DLT"], outs=["DCAR"])
                        k.ve(lambda h, eh=eh: h.tensor_tensor(
                            DCAR[:, eh:eh + 1], DCAR[:, eh:eh + 1], DCAR[:, 16:17], ALU.add),
                            ins=["DCAR"], outs=["DCAR"])
                    k.dma("ge", ap["dc_sp"].rearrange("p (a t) -> p a t", t=TC)
                          [:, :, t0:t0 + TCK], DCC[:].rearrange("p (a t) -> p a t", t=TCK),
                          ins=["DCC"], outs=["d:dc_sp"])
                    for n in range(16):
                        for src_row, dst, dstn in ((64 + n, BBALL, "BBALL"),
                                                   (80 + n, CCALL, "CCALL")):
                            k.dma("ge", ROWS[0:1, 0:TCK], DBC[src_row:src_row + 1, 0:TCK],
                                  ins=["DBC"], outs=["ROWS"])
                            k.bcast_row(PB[:, 0:TCK], ROWS[0:1, 0:TCK],
                                        ins=["ROWS"], outs=["PB"])
                            k.ve(lambda h, n=n, dst=dst: h.tensor_copy(
                                dst[:, n * TCK:(n + 1) * TCK], PB[:, 0:TCK]),
                                ins=["PB"], outs=[dstn])
                    k.dma("ge", ap["cbc_sp"][:, tc4 * 16 * TCK:(tc4 + 1) * 16 * TCK], CCALL[:],
                          ins=["CCALL"], outs=["d:cbc_sp"])
                    if tc4 == 0:
                        k.ve(lambda h: h.memset(CARRY[:], 0.0), outs=["CARRY"])
                    for eh in range(16):
                        for n in range(16):
                            ac = li * 256 + eh * 16 + n
                            k.se(lambda h, eh=eh, ac=ac: h.activation(
                                DAT[:], DLT[:, eh * TCK:(eh + 1) * TCK], ACT.Exp,
                                scale=AL3[:, ac:ac + 1]), ins=["DLT", "AL3"], outs=["DAT"])
                            k.ve(lambda h, eh=eh, n=n: h.tensor_tensor(
                                BXT[:], UALL[:, eh * TCK:(eh + 1) * TCK],
                                BBALL[:, n * TCK:(n + 1) * TCK], ALU.mult),
                                ins=["UALL", "BBALL"], outs=["BXT"])
                            cidx = eh * 16 + n
                            k.p.emit("ve", lambda h, cidx=cidx: h.tensor_tensor_scan(
                                HS1[:], DAT[:], BXT[:], CARRY[:, cidx:cidx + 1],
                                ALU.mult, ALU.add), ins=["DAT", "BXT", "CARRY"], outs=["HS1"])
                            k.ge(lambda h, cidx=cidx: h.tensor_copy(
                                CARRY[:, cidx:cidx + 1], HS1[:, TCK - 1:TCK]),
                                ins=["HS1"], outs=["CARRY"])
                            k.ve(lambda h, n=n: h.tensor_tensor(
                                YMI[:].rearrange("p (t n) -> p t n", n=16)[:, :, n],
                                HS1[:], CCALL[:, n * TCK:(n + 1) * TCK], ALU.mult),
                                ins=["HS1", "CCALL"], outs=["YMI"])
                        k.ve(lambda h: h.tensor_reduce(
                            YLB[:], YMI[:].rearrange("p (t n) -> p t n", n=16),
                            AX.X, ALU.add), ins=["YMI"], outs=["YLB"])
                        k.ve(lambda h: h.tensor_copy(YLBB[:], YLB[:]), ins=["YLB"], outs=["YLBB"])
                        if DBG not in ("m1xzb", "m1xcb"):
                            k.dma("ge", ap["y_sp"].rearrange("p (a t) -> p a t", t=TC)
                                  [:, eh, t0:t0 + TCK], YLBB[:], ins=["YLBB"], outs=["d:y_sp"])
                k.dma("ge", ap["cc_st_in"], CARRY[:], ins=["CARRY"], outs=["d:cc_st_in"])
                k.cc("AllGather", ALU.bypass, PAIRS, ap["cc_st_in"], ap["cc_st_out"],
                     ins=["d:cc_st_in"], outs=["d:cc_st_out"])
                k.dma("sy", HINC[:], ap["cc_st_out"][0], ins=["d:cc_st_out"], outs=["HINC"])
                k.ve(lambda h: h.tensor_scalar(HINC[:], HINC[:], HSEL[:, 0:1], None, ALU.mult),
                     ins=["HINC", "HSEL"], outs=["HINC"])
                for tc4 in range(NCH):
                    t0 = tc4 * TCK
                    k.dma("sy", CCALL[:], ap["cbc_sp"][:, tc4 * 16 * TCK:(tc4 + 1) * 16 * TCK],
                          ins=["d:cbc_sp"], outs=["CCALL"])
                    k.dma("sy", DCC[:].rearrange("p (a t) -> p a t", t=TCK),
                          ap["dc_sp"].rearrange("p (a t) -> p a t", t=TC)[:, :, t0:t0 + TCK],
                          ins=["d:dc_sp"], outs=["DCC"])
                    k.dma("sy", XCB[:].rearrange("p (a t) -> p a t", t=TCK),
                          ap["xc_sp"].rearrange("p (a t) -> p a t", t=TC)[:, :, t0:t0 + TCK],
                          ins=["d:xc_sp"], outs=["XCB"])
                    k.dma("sy", XZB[:].rearrange("p (a t) -> p a t", t=TCK),
                          ap["xz_sp"].rearrange("p (a t) -> p a t", t=TC)[:, :, t0:t0 + TCK],
                          ins=["d:xz_sp"], outs=["XZB"])
                    for eh in range(16):
                        for n in range(16):
                            ac = li * 256 + eh * 16 + n
                            cidx = eh * 16 + n
                            k.se(lambda h, eh=eh, ac=ac: h.activation(
                                DAT[:], DCC[:, eh * TCK:(eh + 1) * TCK], ACT.Exp,
                                scale=AL3[:, ac:ac + 1]), ins=["DCC", "AL3"], outs=["DAT"])
                            k.p.emit("ve", lambda h, cidx=cidx, n=n: h.scalar_tensor_tensor(
                                YMI[:].rearrange("p (t n) -> p t n", n=16)[:, :, n],
                                DAT[:], HINC[:, cidx:cidx + 1],
                                CCALL[:, n * TCK:(n + 1) * TCK], ALU.mult, ALU.mult),
                                ins=["DAT", "HINC", "CCALL"], outs=["YMI"])
                        k.ve(lambda h: h.tensor_reduce(
                            YLB[:], YMI[:].rearrange("p (t n) -> p t n", n=16),
                            AX.X, ALU.add), ins=["YMI"], outs=["YLB"])
                        k.dma("ge", XST[:, 0:TCK],
                              ap["y_sp"].rearrange("p (a t) -> p a t", t=TC)
                              [:, eh, t0:t0 + TCK], ins=["d:y_sp"], outs=["XST"])
                        k.ve(lambda h: h.tensor_tensor(YT[:], XST[:, 0:TCK], YLB[:], ALU.add),
                             ins=["XST", "YLB"], outs=["YT"])
                        k.p.emit("ve", lambda h, eh=eh: h.scalar_tensor_tensor(
                            YT[:], XCB[:, eh * TCK:(eh + 1) * TCK],
                            DP3[:, c16 + eh:c16 + eh + 1], YT[:], ALU.mult, ALU.add),
                            ins=["XCB", "DP3", "YT"], outs=["YT"])
                        k.se(lambda h, eh=eh: h.activation(
                            ZSL[:], XZB[:, eh * TCK:(eh + 1) * TCK], ACT.Silu),
                            ins=["XZB"], outs=["ZSL"])
                        k.ve(lambda h, eh=eh: h.tensor_tensor(
                            YGALL[:, eh * TCK:(eh + 1) * TCK], YT[:], ZSL[:], ALU.mult),
                            ins=["YT", "ZSL"], outs=["YGALL"])
                    if DBG == "m1yg":
                        k.dma("ge", ap["dbg_sp"].rearrange("p (a t) -> p a t", t=TC)
                              [:, :, t0:t0 + TCK], YGALL[:].rearrange("p (a t) -> p a t", t=TCK),
                              ins=["YGALL"], outs=["d:dbg_sp"])
                    for mc in range(8):
                        for eh in range(16):
                            wsb, wsn = (WS0, "WS0") if eh % 2 == 0 else (WS1, "WS1")
                            wbb, wbn = (WB0, "WB0") if eh % 2 == 0 else (WB1, "WB1")
                            k.dma("sy", wsb[:, 0:128],
                                  ap["opw3_l%d" % li][eh * 128:(eh + 1) * 128,
                                                      mc * 128:(mc + 1) * 128],
                                  ins=["d:opw3_l%d" % li], outs=[wsn])
                            k.ge(lambda h, wbb=wbb, wsb=wsb: h.tensor_copy(
                                wbb[:, 0:128], wsb[:, 0:128]), ins=[wsn], outs=[wbn])
                            k.mm(PB[:, 0:TCK], wbb[:, 0:128],
                                 YGALL[:, eh * TCK:(eh + 1) * TCK],
                                 ins=[wbn, "YGALL"], outs=["PB"],
                                 start=(eh == 0), stop=(eh == 15))
                        k.dma("sy", XIC[:], ap["xi_sp"].rearrange("p (a t) -> p a t", t=TC)
                              [:, mc, t0:t0 + TCK], ins=["d:xi_sp"], outs=["XIC"])
                        k.ve(lambda h: h.tensor_tensor(XIC[:], XIC[:], PB[:, 0:TCK], ALU.add),
                             ins=["PB", "XIC"], outs=["XIC"])
                        k.dma("ge", ap["xi_sp"].rearrange("p (a t) -> p a t", t=TC)
                              [:, mc, t0:t0 + TCK], XIC[:], ins=["XIC"], outs=["d:xi_sp"])

            for li in range(1 if DBG in ("m1", "m1xc", "m1y", "m1z", "m1xzb", "m1xcb", "m1yg", "m1xmr") else 2):
                mamba_layer(li)
            p.barrier()
            l3m.close()
        if DBG in ("m1xc", "m1y", "m1z", "m1xzb", "m1xcb", "m1yg", "m1xmr"):
            src = {"m1xc": "xc_sp", "m1y": "y_sp", "m1z": "xz_sp", "m1xzb": "y_sp", "m1xcb": "y_sp", "m1yg": "dbg_sp", "m1xmr": "dbg_sp"}[DBG]
            for dc in range(8):
                for q in range(4):
                    st, stn = (XST, "XST") if q % 2 == 0 else (XST2, "XST2")
                    k.dma("ge", st[:], ap[src].rearrange("p (a t) -> p a t", t=TC)
                          [:, dc, q * 512:(q + 1) * 512], ins=["d:" + src], outs=[stn])
                    k.dma("ge", ap["xr"].rearrange("p (a t) -> p a t", t=TC)
                          [:, dc, q * 512:(q + 1) * 512], st[:], ins=[stn], outs=["d:xr"])
        if DBG == "m1":
            for dc in range(8):
                for q in range(4):
                    st, stn = (XST, "XST") if q % 2 == 0 else (XST2, "XST2")
                    k.dma("sy", st[:], ap["xi_sp"].rearrange("p (a t) -> p a t", t=TC)
                          [:, dc, q * 512:(q + 1) * 512], ins=["d:xi_sp"], outs=[stn])
                    k.dma("ge", ap["xr"].rearrange("p (a t) -> p a t", t=TC)
                          [:, dc, q * 512:(q + 1) * 512], st[:], ins=[stn], outs=["d:xr"])
        if DBG == "s3":
            for dc in range(8):
                for q in range(4):
                    st, stn = (XST, "XST") if q % 2 == 0 else (XST2, "XST2")
                    k.dma("sy", st[:], ap["xi_sp"].rearrange("p (a t) -> p a t", t=TC)
                          [:, dc, q * 512:(q + 1) * 512], ins=["d:xi_sp"], outs=[stn])
                    k.dma("ge", ap["xr"].rearrange("p (a t) -> p a t", t=TC)
                          [:, dc, q * 512:(q + 1) * 512], st[:], ins=[stn], outs=["d:xr"])
        if DBG is None:
            for dc in range(8):
                k.dma("sy", GST[:], ap["g_sp"].rearrange("p (a t) -> p a t", t=TC)[:, dc],
                      ins=["d:g_sp"], outs=["GST"])
                for q in range(4):
                    st, stn = (XST, "XST") if q % 2 == 0 else (XST2, "XST2")
                    k.dma("sy", st[:], ap["xi_sp"].rearrange("p (a t) -> p a t", t=TC)
                          [:, dc, q * 512:(q + 1) * 512], ins=["d:xi_sp"], outs=[stn])
                    k.ve(lambda h, st=st, q=q: h.tensor_tensor(
                        PRODB[:, 0:512], st[:], GST[:, q * 512:(q + 1) * 512], ALU.mult),
                        ins=[stn, "GST"], outs=["PRODB"])
                    st2, stn2 = (XST2, "XST2") if q % 2 == 0 else (XST, "XST")
                    k.dma("sy", st2[:], ap["xr"].rearrange("p (a t) -> p a t", t=TC)
                          [:, dc, q * 512:(q + 1) * 512], ins=["d:xr"], outs=[stn2])
                    k.ve(lambda h, st2=st2: h.tensor_tensor(
                        st2[:], st2[:], PRODB[:, 0:512], ALU.add),
                        ins=[stn2, "PRODB"], outs=[stn2])
                    k.dma("ge", ap["xr"].rearrange("p (a t) -> p a t", t=TC)
                          [:, dc, q * 512:(q + 1) * 512], st2[:], ins=[stn2], outs=["d:xr"])
        p.barrier()
        l3.close()

    p.barrier()
    for tt in range(16):
        tok, tokn = (TOK, "TOK") if tt % 2 == 0 else (TOK2, "TOK2")
        st, stn = (XST, "XST") if tt % 2 == 0 else (XST2, "XST2")
        for dc in range(8):
            k.dma("sy", st[:, 0:128],
                  ap["xr"].rearrange("p (a t) -> p a t", t=TC)
                  [:, dc, tt * 128:(tt + 1) * 128], ins=["d:xr"], outs=[stn])
            bank, bn = (PB, "PB") if dc % 2 == 0 else (PC, "PC")
            k.tr(bank[:, 0:128], st[:, 0:128], ins=[stn], outs=[bn])
            k.ve(lambda h, tok=tok, bank=bank, dc=dc: h.tensor_copy(
                tok[:, dc * 128:(dc + 1) * 128], bank[:, 0:128]), ins=[bn], outs=[tokn])
        k.dma("ge", ap["y"][tt * 128:(tt + 1) * 128, :], tok[:], ins=[tokn], outs=["d:y"])
    p.finish()

    with nc.Block() as block:
        @block.tensor
        def _(h):
            for t in p.thunks["te"]:
                t(h)

        @block.vector
        def _(h):
            for t in p.thunks["ve"]:
                t(h)

        @block.scalar
        def _(h):
            for t in p.thunks["se"]:
                t(h)

        @block.gpsimd
        def _(h):
            for t in p.thunks["ge"]:
                t(h)

        @block.sync
        def _(h):
            for t in p.thunks["sy"]:
                t(h)
    return nc


_CACHE = {}


def _get_nc():
    if NLAYERS not in _CACHE:
        stk = ExitStack()
        nc = bass.Bass("TRN2", target_bir_lowering=False, debug=False)
        _build(nc, stk)
        _CACHE[NLAYERS] = nc
    return _CACHE[NLAYERS]


def kernel(**inputs):
    per_core = _host_prep(inputs)
    nc = _get_nc()
    res = run_bass_kernel_spmd(nc, per_core, core_ids=list(range(NCORES)))
    out = np.empty((4, 4096, 1024), np.float32)
    for c in range(NCORES):
        b, h = c // 2, c % 2
        out[b, h * TC:(h + 1) * TC, :] = res.results[c]["y"]
    return out


# version 05
# speedup vs baseline: 1.2298x; 1.2298x over previous
import traceback
import numpy as np
from contextlib import ExitStack

import concourse.bass as bass
import concourse.mybir as mybir
from concourse.bass_utils import run_bass_kernel_spmd

F32 = mybir.dt.float32
BF16 = mybir.dt.bfloat16
I32 = mybir.dt.int32
ALU = mybir.AluOpType
ACT = mybir.ActivationFunctionType
AX = mybir.AxisListType

NCORES = 8
TC = 2048
D = 1024
NLAYERS = 4
SKIP_G = False
DBG = None

ENGINES = ("te", "ve", "se", "ge", "sy")
RQ = 8
QSEMS = tuple(f"{e}q{r}" for e in ENGINES for r in range(RQ))
ALL_SEMS = ENGINES + QSEMS + ("cc",)


class Prog:
    def __init__(self, nc):
        self.nc = nc
        self.thunks = {e: [] for e in ENGINES}
        self.counts = {s: 0 for s in ALL_SEMS}
        self.clock = {e: {s: 0 for s in ALL_SEMS} for e in ENGINES}
        self.dirty = {e: set() for e in ENGINES}
        self.qseq = {e: 0 for e in ENGINES}
        self.bufs = {}
        self.sems = {}

    def reg(self, name, ready=False):
        self.bufs[name] = {"w": ("sy", 0) if ready else None, "r": []}
        return name

    def emit(self, eng, fn, ins=(), outs=(), dma=False, cc=False):
        ins = [i for i in ins if i]
        outs = [o for o in outs if o]
        waits = {}
        drain = False

        def need(src, val):
            nonlocal drain
            if src == eng:
                return
            if self.clock[eng].get(src, 0) < val:
                waits[src] = max(waits.get(src, 0), val)

        for b in ins:
            st = self.bufs[b]
            if st["w"]:
                src, val = st["w"]
                if src == eng and b in self.dirty[eng]:
                    drain = True
                need(src, val)
        for b in outs:
            st = self.bufs[b]
            for src, val in ([st["w"]] if st["w"] else []) + st["r"]:
                if src == eng and b in self.dirty[eng]:
                    drain = True
                need(src, val)
        if cc:
            sem_key, step = "cc", 1
        elif dma:
            i = self.qseq[eng]
            self.qseq[eng] += 1
            sem_key, step = f"{eng}q{i % RQ}", 16
            if i >= RQ:
                gen_prev = 16 * (i // RQ)
                if self.clock[eng].get(sem_key, 0) < gen_prev:
                    waits[sem_key] = max(waits.get(sem_key, 0), gen_prev)
        else:
            sem_key, step = eng, 1
        self.counts[sem_key] += step
        val_after = self.counts[sem_key]
        waits_l = sorted(waits.items())
        for src, val in waits_l:
            self.clock[eng][src] = val
        sems = self.sems

        tb = traceback.extract_stack(limit=8)

        def thunk(h, waits_l=waits_l, drain=drain, fn=fn, sem_key=sem_key, step=step):
            for src, val in waits_l:
                h.wait_ge(sems[src], val)
            if drain:
                h.drain()
            try:
                fn(h).then_inc(sems[sem_key], step)
            except Exception:
                print("EMIT SITE:")
                for fr in tb:
                    print("  ", fr.filename, fr.lineno, fr.line)
                raise

        self.thunks[eng].append(thunk)
        if drain:
            self.dirty[eng].clear()
        for b in ins:
            self.bufs[b]["r"].append((sem_key, val_after))
            self.dirty[eng].add(b)
        for b in outs:
            self.bufs[b]["w"] = (sem_key, val_after)
            self.bufs[b]["r"] = []
            self.dirty[eng].add(b)
        return val_after

    def barrier(self):
        snapq = dict(self.counts)
        for e in ENGINES:
            self.counts[e] += 1
        snape = dict(self.counts)
        sems = self.sems
        for e in ENGINES:
            def thunk(h, e=e):
                try:
                    h.drain()
                except Exception:
                    pass
                h.sem_inc(sems[e], 1)
                for s in ALL_SEMS:
                    tgt = snape[s] if s in ENGINES else snapq[s]
                    if tgt > 0 and s != e:
                        h.wait_ge(sems[s], tgt)
            self.thunks[e].append(thunk)
        for e in ENGINES:
            for s in ALL_SEMS:
                self.clock[e][s] = max(self.clock[e][s],
                                       snape[s] if s in ENGINES else snapq[s])
            self.dirty[e].clear()
        for b in self.bufs.values():
            if b["w"] is not None:
                b["r"] = []

    def finish(self):
        snap = dict(self.counts)
        sems = self.sems

        def thunk(h):
            for s in ALL_SEMS:
                if snap[s] > 0:
                    h.wait_ge(sems[s], snap[s])
        self.thunks["sy"].append(thunk)


def _col(v, rows=128):
    v = np.ascontiguousarray(np.asarray(v, np.float32))
    return np.ascontiguousarray(v.reshape(v.shape[0] // rows, rows).T)


def _host_prep(inputs):
    x_full = np.asarray(inputs["L0_attn_x"], np.float32)
    per_core = []
    eye = np.eye(128, dtype=np.float32)
    tri = np.triu(np.ones((128, 128), np.float32))
    inv_freq = 1.0 / 10000.0 ** (np.arange(0, 64, 2, dtype=np.float64) / 64.0)
    for c in range(NCORES):
        b, h = c // 2, c % 2
        m = {"x": np.ascontiguousarray(x_full[b, h * TC:(h + 1) * TC, :]),
             "eye": eye, "tri": tri,
             "hsel": np.full((128, 1), float(h), np.float32)}
        pos = np.arange(h * TC, (h + 1) * TC, dtype=np.float64)
        fr = pos[:, None] * inv_freq[None, :]
        m["rope_cos"] = np.cos(fr).astype(np.float32)
        m["rope_sin"] = np.sin(fr).astype(np.float32)
        if NLAYERS >= 1:
            m["w_attn0"] = np.asarray(inputs["L0_attn_w_attn"], np.float32)
            m["w_proj0"] = np.asarray(inputs["L0_attn_w_proj"], np.float32)
            m["w_ffn0"] = np.asarray(inputs["L0_ffn_weight"], np.float32)
            m["ln_g0"] = _col(inputs["L0_ffn_ln_gamma"])
            m["ln_b0"] = _col(inputs["L0_ffn_ln_beta"])
        if NLAYERS >= 2:
            m["w_qkv1"] = np.asarray(inputs["L1_attn_w_qkv"], np.float32)
            m["b_qkv1"] = _col(inputs["L1_attn_b_qkv"])
            m["b_qkv1r"] = np.asarray(inputs["L1_attn_b_qkv"], np.float32).reshape(1, 3072)
            m["w_fc1"] = np.asarray(inputs["L1_attn_w_fc"], np.float32)
            m["b_fc1"] = _col(inputs["L1_attn_b_fc"])
            m["w_r1"] = np.ascontiguousarray(np.asarray(inputs["L1_ffn_W_r"], np.float32).T)
            m["w_k1"] = np.ascontiguousarray(np.asarray(inputs["L1_ffn_W_k"], np.float32).T)
            m["w_v1"] = np.ascontiguousarray(np.asarray(inputs["L1_ffn_W_v"], np.float32).T)
            m["mu_r1"] = _col(inputs["L1_ffn_mu_r"])
            m["mu_k1"] = _col(inputs["L1_ffn_mu_k"])
        if NLAYERS >= 3:
            for nm, key in (("wq2", "L2_attn_Wq"), ("wk2", "L2_attn_Wk"),
                            ("wv2", "L2_attn_Wv"), ("wo2", "L2_attn_Wo")):
                m[nm] = np.asarray(inputs[key], np.float32)
            for nm, key in (("bq2", "L2_attn_bq"), ("bk2", "L2_attn_bk"),
                            ("bv2", "L2_attn_bv"), ("bo2", "L2_attn_bo")):
                m[nm] = _col(inputs[key])
            m["clip2"] = np.array([[inputs["L2_ffn_clip_x"][0],
                                    inputs["L2_ffn_clip_down"][0]]], np.float32)
            wg = np.asarray(inputs["L2_ffn_w_gate"], np.float32)
            wu = np.asarray(inputs["L2_ffn_w_up"], np.float32)
            wd = np.asarray(inputs["L2_ffn_w_down"], np.float32)
            m["wg2"], m["wu2"], m["wd2"] = wg, wu, wd
            m["wg2T"] = np.ascontiguousarray(wg.T)
            m["wu2T"] = np.ascontiguousarray(wu.T)
            m["wd2T"] = np.ascontiguousarray(wd.T)
        if NLAYERS >= 4:
            nw = np.asarray(inputs["L3_attn_norm_w"], np.float32)
            m["nw3"] = np.stack([_col(nw[0]), _col(nw[1])])
            ipw = np.asarray(inputs["L3_attn_in_proj_w"], np.float32)
            m["inw3_l0"] = np.ascontiguousarray(ipw[0].T)
            m["inw3_l1"] = np.ascontiguousarray(ipw[1].T)
            cw = np.asarray(inputs["L3_attn_conv_w"], np.float32)
            m["cw3"] = np.ascontiguousarray(
                cw.reshape(2, 16, 128, 4).transpose(0, 2, 1, 3).reshape(2, 128, 64))
            cb = np.asarray(inputs["L3_attn_conv_b"], np.float32)
            m["cb3"] = np.stack([_col(cb[0]), _col(cb[1])])
            xpw = np.asarray(inputs["L3_attn_x_proj_w"], np.float32)
            m["xpw3_l0"] = np.ascontiguousarray(xpw[0].T)
            m["xpw3_l1"] = np.ascontiguousarray(xpw[1].T)
            dtw = np.asarray(inputs["L3_attn_dt_proj_w"], np.float32)
            m["dtw3_l0"] = np.ascontiguousarray(dtw[0].T)
            m["dtw3_l1"] = np.ascontiguousarray(dtw[1].T)
            dtb = np.asarray(inputs["L3_attn_dt_proj_b"], np.float32)
            m["dtb3"] = np.stack([_col(dtb[0]), _col(dtb[1])])
            al = np.asarray(inputs["L3_attn_A_log"], np.float32)
            m["al3"] = np.ascontiguousarray(
                al.reshape(2, 16, 128, 16).transpose(0, 2, 1, 3).reshape(2, 128, 256))
            dp = np.asarray(inputs["L3_attn_D_param"], np.float32)
            m["dp3"] = np.stack([_col(dp[0]), _col(dp[1])])
            opw = np.asarray(inputs["L3_attn_out_proj_w"], np.float32)
            m["opw3_l0"] = np.ascontiguousarray(opw[0].T)
            m["opw3_l1"] = np.ascontiguousarray(opw[1].T)
            m["w13"] = np.ascontiguousarray(np.asarray(inputs["L3_ffn_w1"], np.float32)[c])
            m["b13"] = _col(np.asarray(inputs["L3_ffn_b1"], np.float32)[c])
            m["w23"] = np.ascontiguousarray(np.asarray(inputs["L3_ffn_w2"], np.float32)[c])
            m["b23"] = _col(np.asarray(inputs["L3_ffn_b2"], np.float32)[c])
        per_core.append(m)
    return per_core


class KB:
    def __init__(self, nc, stk):
        self.nc = nc
        self.stk = stk
        self.p = Prog(nc)
        self.ap = {}
        self.sb = {}
        for s in ALL_SEMS:
            self.p.sems[s] = stk.enter_context(nc.semaphore(s))

    def din(self, name, shape, dt=F32):
        self.ap[name] = self.nc.dram_tensor(name, list(shape), dt, kind="ExternalInput").ap()
        self.p.reg("d:" + name, ready=True)
        return self.ap[name]

    def dout(self, name, shape, dt=F32):
        self.ap[name] = self.nc.dram_tensor(name, list(shape), dt, kind="ExternalOutput").ap()
        self.p.reg("d:" + name)
        return self.ap[name]

    def dram(self, name, shape, dt=F32):
        self.ap[name] = self.nc.dram_tensor(name, list(shape), dt).ap()
        self.p.reg("d:" + name)
        return self.ap[name]

    def sbuf(self, name, shape, dt=F32, scope=None):
        t = (scope or self.stk).enter_context(self.nc.sbuf_tensor(name, list(shape), dt))
        self.sb[name] = t
        self.p.reg(name)
        return t

    def psum(self, name, shape, dt=F32, scope=None):
        t = (scope or self.stk).enter_context(self.nc.psum_tensor(name, list(shape), dt))
        self.sb[name] = t
        self.p.reg(name)
        return t

    def dma(self, eng, dst, src, ins, outs):
        self.p.emit(eng, lambda h: h.dma_start(out=dst, in_=src), ins=ins, outs=outs, dma=True)

    def mm(self, out, lhsT, rhs, ins, outs, start=True, stop=True):
        self.p.emit("te", lambda h: h.matmul(out, lhsT, rhs, start=start, stop=stop),
                    ins=ins, outs=outs)

    def tr(self, out, in_, ins, outs, bf=False):
        eye = self.sb["EYEB" if bf else "EYE"]
        nm = "EYEB" if bf else "EYE"
        self.p.emit("te", lambda h: h.transpose(out, in_, eye[:]),
                    ins=list(ins) + [nm], outs=outs)

    def ve(self, fn, ins=(), outs=()):
        self.p.emit("ve", fn, ins=ins, outs=outs)

    def se(self, fn, ins=(), outs=()):
        self.p.emit("se", fn, ins=ins, outs=outs)

    def ge(self, fn, ins=(), outs=()):
        self.p.emit("ge", fn, ins=ins, outs=outs)

    def cc(self, kind, op, groups, in_ap, out_ap, ins, outs):
        self.p.emit("ge", lambda h: h.collective_compute(
            kind, op, replica_groups=groups, ins=[in_ap], outs=[out_ap]),
            ins=ins, outs=outs, cc=True)

    def bcast_row(self, psum_slice, row_ap, ins, outs):
        self.mm(psum_slice, self.sb["ONESROW"][:], row_ap,
                ins=list(ins) + ["ONESROW"], outs=outs)


PAIRS = [[0, 1], [2, 3], [4, 5], [6, 7]]
ALL8 = [list(range(8))]


def _build(nc, stk):
    k = KB(nc, stk)
    p = k.p
    ap = k.ap

    k.din("x", [TC, D])
    k.din("eye", [128, 128])
    k.din("tri", [128, 128])
    k.din("rope_cos", [TC, 32])
    k.din("rope_sin", [TC, 32])
    k.din("hsel", [128, 1])
    k.dout("y", [TC, D])

    if NLAYERS >= 1:
        k.din("w_attn0", [D, 3 * D]); k.din("w_proj0", [D, D]); k.din("w_ffn0", [D, D])
        k.din("ln_g0", [128, 8]); k.din("ln_b0", [128, 8])
    if NLAYERS >= 2:
        k.din("w_qkv1", [D, 3 * D]); k.din("b_qkv1", [128, 24]); k.din("b_qkv1r", [1, 3 * D])
        k.din("w_fc1", [D, D]); k.din("b_fc1", [128, 8])
        k.din("w_r1", [D, D]); k.din("w_k1", [D, D]); k.din("w_v1", [D, D])
        k.din("mu_r1", [128, 8]); k.din("mu_k1", [128, 8])
    if NLAYERS >= 3:
        for nm in ("wq2", "wk2", "wv2", "wo2"):
            k.din(nm, [D, D])
        for nm in ("bq2", "bk2", "bv2", "bo2"):
            k.din(nm, [128, 8])
        k.din("clip2", [1, 2])
        k.din("wg2", [2816, D]); k.din("wu2", [2816, D]); k.din("wd2", [D, 2816])
        k.din("wg2T", [D, 2816]); k.din("wu2T", [D, 2816]); k.din("wd2T", [2816, D])
    if NLAYERS >= 4:
        k.din("nw3", [2, 128, 8])
        k.din("inw3_l0", [D, 4096]); k.din("inw3_l1", [D, 4096])
        k.din("cw3", [2, 128, 64]); k.din("cb3", [2, 128, 16])
        k.din("xpw3_l0", [2048, 96]); k.din("xpw3_l1", [2048, 96])
        k.din("dtw3_l0", [64, 2048]); k.din("dtw3_l1", [64, 2048])
        k.din("dtb3", [2, 128, 16])
        k.din("al3", [2, 128, 256]); k.din("dp3", [2, 128, 16])
        k.din("opw3_l0", [2048, D]); k.din("opw3_l1", [2048, D])
        k.din("w13", [D, 4096]); k.din("b13", [128, 32])
        k.din("w23", [4096, D]); k.din("b23", [128, 8])

    k.dram("xr", [128, 8 * TC])
    k.dram("s_sp", [128, 8 * TC], BF16)
    k.dram("g_sp", [128, 8 * TC], BF16)

    XNB = k.sbuf("XNB", [128, 8 * TC], BF16)
    ONES = k.sbuf("ONESB", [128, 1], BF16)
    ONESF = k.sbuf("ONESF", [128, 1], F32)
    ONESROW = k.sbuf("ONESROW", [1, 128], F32)
    EYE = k.sbuf("EYE", [128, 128], F32)
    EYEB = k.sbuf("EYEB", [128, 128], BF16)
    TRIF = k.sbuf("TRIF", [128, 128], F32)
    TRIB = k.sbuf("TRIB", [128, 128], BF16)
    HSEL = k.sbuf("HSEL", [128, 1], F32)
    ROWS = k.sbuf("ROWS", [128, 1536], F32)
    SQB = k.sbuf("SQB", [128, 4096], BF16)
    XST = k.sbuf("XST", [128, 512], F32)
    XST2 = k.sbuf("XST2", [128, 512], F32)
    PRODB = k.sbuf("PRODB", [128, 512], F32)
    GST = k.sbuf("GST", [128, 2048], BF16)
    SST = k.sbuf("SST", [128, 2048], BF16)
    TOK = k.sbuf("TOK", [128, 1024], F32)
    TOK2 = k.sbuf("TOK2", [128, 1024], F32)
    WS0 = k.sbuf("WS0", [128, 1024], F32)
    WS1 = k.sbuf("WS1", [128, 1024], F32)
    WB0 = k.sbuf("WB0", [128, 1024], BF16)
    WB1 = k.sbuf("WB1", [128, 1024], BF16)

    PA = k.psum("PA", [128, 2048], F32)
    PB = k.psum("PB", [128, 512], F32)
    PC = k.psum("PC", [128, 512], F32)
    PD = k.psum("PD", [128, 512], F32)
    PE = k.psum("PE", [128, 512], F32)

    k.ve(lambda h: h.memset(ONES[:], 1.0), outs=["ONESB"])
    k.ve(lambda h: h.memset(ONESF[:], 1.0), outs=["ONESF"])
    k.ve(lambda h: h.memset(ONESROW[:], 1.0), outs=["ONESROW"])
    k.dma("sy", EYE[:], ap["eye"], ins=["d:eye"], outs=["EYE"])
    k.dma("sy", HSEL[:], ap["hsel"], ins=["d:hsel"], outs=["HSEL"])
    k.dma("sy", TRIF[:], ap["tri"], ins=["d:tri"], outs=["TRIF"])
    k.ve(lambda h: h.tensor_copy(TRIB[:], TRIF[:]), ins=["TRIF"], outs=["TRIB"])
    k.ve(lambda h: h.tensor_copy(EYEB[:], EYE[:]), ins=["EYE"], outs=["EYEB"])

    for tt in range(16):
        tok, tokn = (TOK, "TOK") if tt % 2 == 0 else (TOK2, "TOK2")
        k.dma("sy", tok[:], ap["x"][tt * 128:(tt + 1) * 128, :], ins=["d:x"], outs=[tokn])
        st, stn = (XST, "XST") if tt % 2 == 0 else (XST2, "XST2")
        for dc in range(8):
            bank, bn = (PB, "PB") if dc % 2 == 0 else (PC, "PC")
            k.tr(bank[:, 0:128], tok[:, dc * 128:(dc + 1) * 128], ins=[tokn], outs=[bn])
            k.ve(lambda h, st=st, bank=bank, dc=dc: h.tensor_copy(
                st[:, 0:128], bank[:, 0:128]), ins=[bn], outs=[stn])
            k.dma("ge", ap["xr"].rearrange("p (a t) -> p a t", t=TC)[:, dc:dc + 1,
                                                                    tt * 128:(tt + 1) * 128],
                  st[:, 0:128][:, None, :], ins=[stn], outs=["d:xr"])

    def ln_stats_chunk(t0, eps, into_rows):
        for dc in range(8):
            k.mm(PB[0:1, 0:512], ONES[:], XNB[:, dc * TC + t0: dc * TC + t0 + 512],
                 ins=["ONESB", "XNB"], outs=["PB"], start=(dc == 0), stop=(dc == 7))
        for dc in range(8):
            k.se(lambda h, dc=dc: h.activation(
                SQB[:, dc * 512:(dc + 1) * 512],
                XNB[:, dc * TC + t0: dc * TC + t0 + 512], ACT.Square),
                ins=["XNB"], outs=["SQB"])
        for dc in range(8):
            k.mm(PC[0:1, 0:512], ONES[:], SQB[:, dc * 512:(dc + 1) * 512],
                 ins=["ONESB", "SQB"], outs=["PC"], start=(dc == 0), stop=(dc == 7))
        A = ROWS[0:1, 0:512]
        Bq = ROWS[0:1, 512:1024]
        Cq = ROWS[0:1, 1024:1536]
        k.ve(lambda h: h.tensor_scalar_mul(A, PB[0:1, 0:512], 1.0 / D),
             ins=["PB"], outs=["ROWS"])
        k.ve(lambda h: h.tensor_scalar_mul(Bq, PC[0:1, 0:512], 1.0 / D),
             ins=["PC"], outs=["ROWS"])
        k.ve(lambda h: h.tensor_tensor(Cq, A, A, ALU.mult), ins=["ROWS"], outs=["ROWS"])
        k.ve(lambda h: h.tensor_tensor(Bq, Bq, Cq, ALU.subtract), ins=["ROWS"], outs=["ROWS"])
        k.ve(lambda h, eps=eps: h.tensor_scalar_add(Bq, Bq, eps), ins=["ROWS"], outs=["ROWS"])
        k.ve(lambda h: h.reciprocal(Bq, Bq), ins=["ROWS"], outs=["ROWS"])
        k.se(lambda h: h.activation(Bq, Bq, ACT.Sqrt), ins=["ROWS"], outs=["ROWS"])
        k.ve(lambda h: h.tensor_tensor(Cq, A, Bq, ALU.mult), ins=["ROWS"], outs=["ROWS"])
        k.ve(lambda h: h.tensor_scalar_mul(Cq, Cq, -1.0), ins=["ROWS"], outs=["ROWS"])

    def emit_layernorm(eps=1e-5):
        for dc in range(8):
            for tc4 in range(4):
                st, stn = (XST, "XST") if (dc * 4 + tc4) % 2 == 0 else (XST2, "XST2")
                k.dma("sy", st[:],
                      ap["xr"].rearrange("p (a t) -> p a t", t=TC)
                      [:, dc, tc4 * 512:(tc4 + 1) * 512], ins=["d:xr"], outs=[stn])
                k.ve(lambda h, st=st, dc=dc, tc4=tc4: h.tensor_copy(
                    XNB[:, dc * TC + tc4 * 512: dc * TC + (tc4 + 1) * 512], st[:]),
                    ins=[stn], outs=["XNB"])
        for tc4 in range(4):
            t0 = tc4 * 512
            ln_stats_chunk(t0, eps, None)
            k.bcast_row(PB[:, 0:512], ROWS[0:1, 512:1024], ins=["ROWS"], outs=["PB"])
            k.bcast_row(PC[:, 0:512], ROWS[0:1, 1024:1536], ins=["ROWS"], outs=["PC"])
            for dc in range(8):
                k.ve(lambda h, dc=dc, t0=t0: h.tensor_tensor(
                    SQB[:, dc * 512:(dc + 1) * 512],
                    XNB[:, dc * TC + t0: dc * TC + t0 + 512], PB[:, 0:512], ALU.mult),
                    ins=["XNB", "PB"], outs=["SQB"])
            for dc in range(8):
                k.ve(lambda h, dc=dc, t0=t0: h.tensor_tensor(
                    XNB[:, dc * TC + t0: dc * TC + t0 + 512],
                    SQB[:, dc * 512:(dc + 1) * 512], PC[:, 0:512], ALU.add),
                    ins=["SQB", "PC"], outs=["XNB"])

    def emit_combine():
        for dc in range(8):
            k.dma("sy", SST[:], ap["s_sp"].rearrange("p (a t) -> p a t", t=TC)[:, dc],
                  ins=["d:s_sp"], outs=["SST"])
            k.dma("sy", GST[:], ap["g_sp"].rearrange("p (a t) -> p a t", t=TC)[:, dc],
                  ins=["d:g_sp"], outs=["GST"])
            for q in range(4):
                st, stn = (XST, "XST") if q % 2 == 0 else (XST2, "XST2")
                k.dma("sy", st[:],
                      ap["xr"].rearrange("p (a t) -> p a t", t=TC)
                      [:, dc, q * 512:(q + 1) * 512], ins=["d:xr"], outs=[stn])
                k.ve(lambda h, st=st, q=q: h.tensor_tensor(
                    PRODB[:, 0:512], SST[:, q * 512:(q + 1) * 512],
                    GST[:, q * 512:(q + 1) * 512], ALU.mult),
                    ins=["SST", "GST"], outs=["PRODB"])
                k.ve(lambda h, st=st: h.tensor_tensor(
                    st[:], st[:], PRODB[:, 0:512], ALU.add), ins=[stn, "PRODB"], outs=[stn])
                k.dma("ge", ap["xr"].rearrange("p (a t) -> p a t", t=TC)
                      [:, dc, q * 512:(q + 1) * 512], st[:], ins=[stn], outs=["d:xr"])

    def load_w_block(src_name, m0, mwid, idx, kdim=1024):
        wbb, wbn = (WB0, "WB0") if idx % 2 == 0 else (WB1, "WB1")
        nk = kdim // 128
        src = ap[src_name].rearrange("(a p) m -> p a m", p=128)
        k.dma("ge", wbb[:, 0:nk * mwid].rearrange("p (a m) -> p a m", m=mwid),
              src[:, :, m0:m0 + mwid], ins=["d:" + src_name], outs=[wbn])
        return wbb, wbn, nk

    def dump_bf16_to_xr(src_ap_name):
        for dc in range(8):
            k.dma("sy", GST[:], ap[src_ap_name].rearrange("p (a t) -> p a t", t=TC)[:, dc],
                  ins=["d:" + src_ap_name], outs=["GST"])
            for q in range(4):
                st, stn = (XST, "XST") if q % 2 == 0 else (XST2, "XST2")
                k.ve(lambda h, st=st, q=q: h.tensor_copy(st[:], GST[:, q * 512:(q + 1) * 512]),
                     ins=["GST"], outs=[stn])
                k.dma("ge", ap["xr"].rearrange("p (a t) -> p a t", t=TC)
                      [:, dc, q * 512:(q + 1) * 512], st[:], ins=[stn], outs=["d:xr"])

    def dump_xnb_to_xr():
        for dc in range(8):
            for q in range(4):
                st, stn = (XST, "XST") if q % 2 == 0 else (XST2, "XST2")
                k.ve(lambda h, st=st, dc=dc, q=q: h.tensor_copy(
                    st[:], XNB[:, dc * TC + q * 512: dc * TC + (q + 1) * 512]),
                    ins=["XNB"], outs=[stn])
                k.dma("ge", ap["xr"].rearrange("p (a t) -> p a t", t=TC)
                      [:, dc, q * 512:(q + 1) * 512], st[:], ins=[stn], outs=["d:xr"])

    if NLAYERS >= 1:
        emit_layernorm()
        if DBG == "xn":
            dump_xnb_to_xr()
        l0 = ExitStack()
        XL = k.sbuf("XL", [128, 8 * TC], BF16, scope=l0)
        WQ0 = k.sbuf("WQ0", [128, 8192], BF16, scope=l0)
        WQI = k.sbuf("WQI", [128, 1024], I32, scope=l0)
        XQ0 = k.sbuf("XQ0", [128, 4096], BF16, scope=l0)
        COLS = k.sbuf("COLS", [128, 16], F32, scope=l0)
        SCL = k.sbuf("SCL", [1, 128], F32, scope=l0)
        LNG = k.sbuf("LNG", [128, 8], F32, scope=l0)
        LNB = k.sbuf("LNB", [128, 8], F32, scope=l0)
        k.dma("sy", LNG[:], ap["ln_g0"], ins=["d:ln_g0"], outs=["LNG"])
        k.dma("sy", LNB[:], ap["ln_b0"], ins=["d:ln_b0"], outs=["LNB"])
        k.dram("cc_gam_in", [1, 128])
        k.dram("cc_gam_out", [1, 128])
        k.dram("cc_s0_in", [128, 520])
        k.dram("cc_s0_out", [2, 128, 520])

        run_g = (DBG in (None, "g", "s")) and not SKIP_G
        run_attn = DBG in (None, "s", "y", "S")
        run_ph2 = DBG != "S"
        for tc4 in range(4) if run_g else []:
            t0 = tc4 * 512
            ln_stats_chunk(t0, 1e-3, None)
            k.bcast_row(PB[:, 0:512], ROWS[0:1, 512:1024], ins=["ROWS"], outs=["PB"])
            k.bcast_row(PC[:, 0:512], ROWS[0:1, 1024:1536], ins=["ROWS"], outs=["PC"])
            for dc in range(8):
                k.ve(lambda h, dc=dc, t0=t0: h.tensor_tensor(
                    SQB[:, dc * 512:(dc + 1) * 512],
                    XNB[:, dc * TC + t0: dc * TC + t0 + 512], PB[:, 0:512], ALU.mult),
                    ins=["XNB", "PB"], outs=["SQB"])
            for dc in range(8):
                k.ve(lambda h, dc=dc: h.tensor_tensor(
                    SQB[:, dc * 512:(dc + 1) * 512],
                    SQB[:, dc * 512:(dc + 1) * 512], PC[:, 0:512], ALU.add),
                    ins=["SQB", "PC"], outs=["SQB"])
            for dc in range(8):
                k.ve(lambda h, dc=dc, t0=t0: h.tensor_scalar(
                    XL[:, dc * TC + t0: dc * TC + t0 + 512],
                    SQB[:, dc * 512:(dc + 1) * 512],
                    LNG[:, dc:dc + 1], LNB[:, dc:dc + 1], ALU.mult, ALU.add),
                    ins=["SQB", "LNG", "LNB"], outs=["XL"])
        if run_g:
            for q in range(8):
                k.ve(lambda h, q=q: h.reduce_max(COLS[:, 1:2], XL[:, q * TC:(q + 1) * TC],
                                                 AX.X, apply_absolute_value=True),
                     ins=["XL"], outs=["COLS"])
                if q == 0:
                    k.ve(lambda h: h.tensor_copy(COLS[:, 0:1], COLS[:, 1:2]),
                         ins=["COLS"], outs=["COLS"])
                else:
                    k.ve(lambda h: h.tensor_tensor(COLS[:, 0:1], COLS[:, 0:1],
                                                   COLS[:, 1:2], ALU.max),
                         ins=["COLS"], outs=["COLS"])
            k.ve(lambda h: h.memset(XST[:, 0:128], 0.0), outs=["XST"])
            k.ve(lambda h: h.tensor_copy(XST[:, 0:1], COLS[:, 0:1]),
                 ins=["COLS"], outs=["XST"])
            k.tr(PB[:, 0:128], XST[:, 0:128], ins=["XST"], outs=["PB"])
            k.ve(lambda h: h.reduce_max(SCL[0:1, 0:1], PB[0:1, 0:128], AX.X),
                 ins=["PB"], outs=["SCL"])
            k.ve(lambda h: h.memset(ROWS[0:1, 0:128], 0.0), outs=["ROWS"])
            k.ve(lambda h: h.tensor_copy(ROWS[0:1, 0:1], SCL[0:1, 0:1]),
                 ins=["SCL"], outs=["ROWS"])
            k.dma("ge", ap["cc_gam_in"], ROWS[0:1, 0:128], ins=["ROWS"], outs=["d:cc_gam_in"])
            k.cc("AllReduce", ALU.max, ALL8, ap["cc_gam_in"], ap["cc_gam_out"],
                 ins=["d:cc_gam_in"], outs=["d:cc_gam_out"])
            k.dma("ge", SCL[0:1, 0:128], ap["cc_gam_out"], ins=["d:cc_gam_out"], outs=["SCL"])
            for t8 in range(8):
                wsb, wsn = (WS0, "WS0") if t8 % 2 == 0 else (WS1, "WS1")
                k.dma("sy", wsb[:], ap["w_ffn0"][t8 * 128:(t8 + 1) * 128, :],
                      ins=["d:w_ffn0"], outs=[wsn])
                k.ve(lambda h, wsb=wsb: h.reduce_sum(COLS[:, 2:3], wsb[:], AX.X,
                                                     apply_absolute_value=True),
                     ins=[wsn], outs=["COLS"])
                if t8 == 0:
                    k.ve(lambda h: h.tensor_copy(COLS[:, 3:4], COLS[:, 2:3]),
                         ins=["COLS"], outs=["COLS"])
                else:
                    k.ve(lambda h: h.tensor_tensor(COLS[:, 3:4], COLS[:, 3:4],
                                                   COLS[:, 2:3], ALU.add),
                         ins=["COLS"], outs=["COLS"])
            k.ve(lambda h: h.memset(XST[:, 0:128], 0.0), outs=["XST"])
            k.ve(lambda h: h.tensor_copy(XST[:, 0:1], COLS[:, 3:4]),
                 ins=["COLS"], outs=["XST"])
            k.tr(PB[:, 0:128], XST[:, 0:128], ins=["XST"], outs=["PB"])
            k.ve(lambda h: h.reduce_sum(SCL[0:1, 1:2], PB[0:1, 0:128], AX.X),
                 ins=["PB"], outs=["SCL"])
            k.ve(lambda h: h.tensor_scalar(SCL[0:1, 2:3], SCL[0:1, 1:2], 1.0 / (D * D), 1e-5,
                                           ALU.mult, ALU.add), ins=["SCL"], outs=["SCL"])
            k.ve(lambda h: h.reciprocal(SCL[0:1, 2:3], SCL[0:1, 2:3]), ins=["SCL"], outs=["SCL"])
            k.bcast_row(PC[:, 0:1], SCL[0:1, 2:3], ins=["SCL"], outs=["PC"])
            k.ve(lambda h: h.tensor_copy(COLS[:, 4:5], PC[:, 0:1]), ins=["PC"], outs=["COLS"])
            for t8 in range(8):
                wsb, wsn = (WS0, "WS0") if t8 % 2 == 0 else (WS1, "WS1")
                k.dma("sy", wsb[:], ap["w_ffn0"][t8 * 128:(t8 + 1) * 128, :],
                      ins=["d:w_ffn0"], outs=[wsn])
                k.ve(lambda h, wsb=wsb: h.tensor_scalar(
                    wsb[:], wsb[:], COLS[:, 4:5], None, ALU.mult),
                    ins=[wsn, "COLS"], outs=[wsn])
                k.ve(lambda h, wsb=wsb: h.tensor_copy(WQI[:], wsb[:]), ins=[wsn], outs=["WQI"])
                k.ve(lambda h, t8=t8: h.tensor_scalar(
                    WQ0[:, t8 * 1024:(t8 + 1) * 1024], WQI[:], -1.0, 1.0,
                    ALU.max, ALU.min), ins=["WQI"], outs=["WQ0"])
            k.ve(lambda h: h.reciprocal(SCL[0:1, 3:4], SCL[0:1, 0:1]), ins=["SCL"], outs=["SCL"])
            k.ve(lambda h: h.tensor_scalar_mul(SCL[0:1, 3:4], SCL[0:1, 3:4], 128.0),
                 ins=["SCL"], outs=["SCL"])
            k.bcast_row(PC[:, 0:1], SCL[0:1, 3:4], ins=["SCL"], outs=["PC"])
            k.ve(lambda h: h.tensor_copy(COLS[:, 5:6], PC[:, 0:1]), ins=["PC"], outs=["COLS"])
            k.ve(lambda h: h.tensor_scalar_mul(SCL[0:1, 4:5], SCL[0:1, 1:2],
                                               1.0 / (D * D * 128.0)), ins=["SCL"], outs=["SCL"])
            k.ve(lambda h: h.tensor_tensor(SCL[0:1, 4:5], SCL[0:1, 4:5], SCL[0:1, 0:1], ALU.mult),
                 ins=["SCL"], outs=["SCL"])
            k.bcast_row(PC[:, 0:1], SCL[0:1, 4:5], ins=["SCL"], outs=["PC"])
            k.ve(lambda h: h.tensor_copy(COLS[:, 6:7], PC[:, 0:1]), ins=["PC"], outs=["COLS"])
            for tc4 in range(4):
                t0 = tc4 * 512
                for dc in range(8):
                    k.ve(lambda h, dc=dc, t0=t0: h.tensor_scalar(
                        ROWS[:, 0:512], XL[:, dc * TC + t0: dc * TC + t0 + 512],
                        COLS[:, 5:6], None, ALU.mult), ins=["XL", "COLS"], outs=["ROWS"])
                    k.ve(lambda h, dc=dc: h.tensor_scalar(
                        XQ0[:, dc * 512:(dc + 1) * 512], ROWS[:, 0:512],
                        -(128.0 - 1e-5), 128.0 - 1e-5, ALU.max, ALU.min),
                        ins=["ROWS"], outs=["XQ0"])
                for mc in range(8):
                    for dc in range(8):
                        k.mm(PD[:, 0:512], WQ0[:, dc * 1024 + mc * 128: dc * 1024 + mc * 128 + 128],
                             XQ0[:, dc * 512:(dc + 1) * 512],
                             ins=["WQ0", "XQ0"], outs=["PD"], start=(dc == 0), stop=(dc == 7))
                    k.se(lambda h, mc=mc: h.activation(
                        SQB[:, mc * 512:(mc + 1) * 512], PD[:, 0:512], ACT.Silu,
                        scale=COLS[:, 6:7]), ins=["PD", "COLS"], outs=["SQB"])
                k.dma("ge", ap["g_sp"].rearrange("p (a t) -> p a t", t=TC)[:, :, t0:t0 + 512],
                      SQB[:].rearrange("p (a t) -> p a t", t=512), ins=["SQB"], outs=["d:g_sp"])

        if DBG == "g":
            dump_bf16_to_xr("g_sp")
        p.barrier()
        QF = k.sbuf("QF", [128, 4096], BF16, scope=l0)
        KF = k.sbuf("KF", [128, 4096], BF16, scope=l0)
        KFF = k.sbuf("KFF", [128, 512], F32, scope=l0)
        VFF = k.sbuf("VFF", [128, 512], F32, scope=l0)
        KT = k.sbuf("KT", [128, 4096], BF16, scope=l0)
        VT = k.sbuf("VT", [128, 4352], BF16, scope=l0)
        PMT = k.sbuf("PMT", [128, 256], BF16, scope=l0)
        SSB = k.sbuf("SSB", [128, 520], F32, scope=l0)
        SSBB = k.sbuf("SSBB", [128, 520], BF16, scope=l0)
        SIN = k.sbuf("SIN", [128, 520], F32, scope=l0)
        SRUN = k.sbuf("SRUN", [128, 520], F32, scope=l0)
        YNUM = k.sbuf("YNUM", [128, 128], F32, scope=l0)
        YSC = k.sbuf("YSC", [128, 4096], BF16, scope=l0)
        FMT = k.sbuf("FMT", [128, 512], F32, scope=l0)
        FMT2 = k.sbuf("FMT2", [128, 512], F32, scope=l0)

        k.ve(lambda h: h.memset(VT[:], 1.0), outs=["VT"])

        def fmap_into(dst, dstn, dst_sl, src_psum, pn):
            k.ve(lambda h: h.tensor_scalar_min(FMT[:, 0:512], src_psum, 0.0),
                 ins=[pn], outs=["FMT"])
            k.se(lambda h: h.activation(FMT2[:, 0:512], FMT[:, 0:512], ACT.Exp),
                 ins=["FMT"], outs=["FMT2"])
            k.p.emit("ve", lambda h: h.scalar_tensor_tensor(
                dst[dst_sl], src_psum, 0.0, FMT2[:, 0:512], ALU.max, ALU.add),
                ins=[pn, "FMT2"], outs=[dstn])

        def project_qkv(sc):
            t0 = sc * 512
            for hp in range(8):
                for wi, col0 in enumerate((0, 1024, 2048)):
                    wb, wbn, nk = load_w_block("w_attn0", col0 + hp * 128, 128,
                                               idx=hp * 3 + wi)
                    for dc in range(8):
                        k.mm(PA[:, 0:512], wb[:, dc * 128:(dc + 1) * 128],
                             XNB[:, dc * TC + t0:dc * TC + t0 + 512],
                             ins=[wbn, "XNB"], outs=["PA"],
                             start=(dc == 0), stop=(dc == 7))
                    if col0 == 0:
                        fmap_into(QF, "QF", np.s_[:, hp * 512:(hp + 1) * 512],
                                  PA[:, 0:512], "PA")
                    elif col0 == 1024:
                        fmap_into(KFF, "KFF", np.s_[:, 0:512], PA[:, 0:512], "PA")
                        k.ve(lambda h, hp=hp: h.tensor_copy(
                            KF[:, hp * 512:(hp + 1) * 512], KFF[:]),
                            ins=["KFF"], outs=["KF"])
                        for sub in range(4):
                            k.tr(PA[:, 512:640], KFF[:, sub * 128:(sub + 1) * 128],
                                 ins=["KFF"], outs=["PA"])
                            k.ve(lambda h, hp=hp, sub=sub: h.tensor_copy(
                                KT[:, sub * 1024 + hp * 128: sub * 1024 + (hp + 1) * 128],
                                PA[:, 512:640]), ins=["PA"], outs=["KT"])
                    else:
                        k.ve(lambda h: h.tensor_copy(VFF[:], PA[:, 0:512]),
                             ins=["PA"], outs=["VFF"])
                        for sub in range(4):
                            k.tr(PA[:, 512:640], VFF[:, sub * 128:(sub + 1) * 128],
                                 ins=["VFF"], outs=["PA"])
                            base = sub * 1088 + hp * 136
                            k.ve(lambda h, base=base: h.tensor_copy(
                                VT[:, base: base + 64], PA[:, 512:576]),
                                ins=["PA"], outs=["VT"])
                            k.ve(lambda h, base=base: h.tensor_copy(
                                VT[:, base + 68: base + 132], PA[:, 576:640]),
                                ins=["PA"], outs=["VT"])

        def s_update(j):
            for hp in range(8):
                bank, bn = (PD, "PD") if hp < 4 else (PE, "PE")
                off = (hp % 4) * 65
                base = j * 1088 + hp * 136
                k.mm(bank[0:64, off:off + 65],
                     KT[:, j * 1024 + hp * 128: j * 1024 + hp * 128 + 64],
                     VT[:, base: base + 65],
                     ins=["KT", "VT"], outs=[bn], start=True, stop=True)
                k.mm(bank[64:128, off:off + 65],
                     KT[:, j * 1024 + hp * 128 + 64: j * 1024 + hp * 128 + 128],
                     VT[:, base + 68: base + 133],
                     ins=["KT", "VT"], outs=[bn], start=True, stop=True)
            k.ve(lambda h: h.tensor_tensor(SRUN[:, 0:260], SRUN[:, 0:260],
                                           PD[:, 0:260], ALU.add), ins=["SRUN", "PD"],
                 outs=["SRUN"])
            k.ve(lambda h: h.tensor_tensor(SRUN[:, 260:520], SRUN[:, 260:520],
                                           PE[:, 0:260], ALU.add), ins=["SRUN", "PE"],
                 outs=["SRUN"])

        if run_attn:
            k.ve(lambda h: h.memset(SRUN[:], 0.0), outs=["SRUN"])
            for sc in range(4):
                project_qkv(sc)
                for j in range(4):
                    s_update(j)
            if DBG == "S":
                k.dma("ge", ap["xr"].rearrange("p (a t) -> p a t", t=TC)[:, 0, 0:520],
                      SRUN[:], ins=["SRUN"], outs=["d:xr"])
            k.dma("ge", ap["cc_s0_in"], SRUN[:], ins=["SRUN"], outs=["d:cc_s0_in"])
            k.cc("AllGather", ALU.bypass, PAIRS, ap["cc_s0_in"], ap["cc_s0_out"],
                 ins=["d:cc_s0_in"], outs=["d:cc_s0_out"])
            k.dma("ge", SIN[:], ap["cc_s0_out"][0], ins=["d:cc_s0_out"], outs=["SIN"])
            k.ve(lambda h: h.tensor_scalar(SIN[:], SIN[:], HSEL[:, 0:1], None, ALU.mult),
                 ins=["SIN", "HSEL"], outs=["SIN"])

            k.ve(lambda h: h.memset(SRUN[:], 0.0), outs=["SRUN"])
            for sc in range(4) if run_ph2 else []:
                project_qkv(sc)
                for j in range(4):
                    k.ve(lambda h: h.tensor_tensor(SSB[:], SIN[:], SRUN[:], ALU.add),
                         ins=["SIN", "SRUN"], outs=["SSB"])
                    k.ve(lambda h: h.tensor_copy(SSBB[:], SSB[:]), ins=["SSB"], outs=["SSBB"])
                    for hp in range(8):
                        qs = np.s_[:, hp * 512 + j * 128: hp * 512 + (j + 1) * 128]
                        c0 = (hp % 4) * 65 + (0 if hp < 4 else 260)
                        k.mm(PA[:, hp * 256: hp * 256 + 128],
                             KF[:, hp * 512 + j * 128: hp * 512 + j * 128 + 128][0:64, :],
                             QF[qs][0:64, :], ins=["KF", "QF"], outs=["PA"])
                        k.mm(PA[:, hp * 256 + 128: hp * 256 + 256],
                             KF[:, hp * 512 + j * 128: hp * 512 + j * 128 + 128][64:128, :],
                             QF[qs][64:128, :], ins=["KF", "QF"], outs=["PA"])
                        k.ve(lambda h, hp=hp: h.tensor_tensor(
                            PMT[:, 0:128], PA[:, hp * 256: hp * 256 + 128], TRIB[:], ALU.mult),
                            ins=["PA", "TRIB"], outs=["PMT"])
                        k.ve(lambda h, hp=hp: h.tensor_tensor(
                            PMT[:, 128:256], PA[:, hp * 256 + 128: hp * 256 + 256],
                            TRIB[:], ALU.mult), ins=["PA", "TRIB"], outs=["PMT"])
                        base = j * 1088 + hp * 136
                        k.mm(PB[0:64, 0:128], VT[:, base: base + 64], PMT[:, 0:128],
                             ins=["VT", "PMT"], outs=["PB"], start=True, stop=False)
                        k.mm(PB[64:128, 0:128], VT[:, base + 68: base + 132], PMT[:, 128:256],
                             ins=["VT", "PMT"], outs=["PB"], start=True, stop=False)
                        k.mm(PB[0:64, 0:128], SSBB[0:64, c0:c0 + 64], QF[qs][0:64, :],
                             ins=["SSBB", "QF"], outs=["PB"], start=False, stop=True)
                        k.mm(PB[64:128, 0:128], SSBB[64:128, c0:c0 + 64], QF[qs][64:128, :],
                             ins=["SSBB", "QF"], outs=["PB"], start=False, stop=True)
                        k.mm(PC[0:1, 0:128], VT[:, base + 64: base + 65], PMT[:, 0:128],
                             ins=["VT", "PMT"], outs=["PC"], start=True, stop=False)
                        k.mm(PC[0:1, 128:256], VT[:, base + 132: base + 133], PMT[:, 128:256],
                             ins=["VT", "PMT"], outs=["PC"], start=False, stop=False)
                        k.mm(PC[0:1, 0:128], SSBB[0:64, c0 + 64:c0 + 65], QF[qs][0:64, :],
                             ins=["SSBB", "QF"], outs=["PC"], start=False, stop=True)
                        k.mm(PC[0:1, 128:256], SSBB[64:128, c0 + 64:c0 + 65], QF[qs][64:128, :],
                             ins=["SSBB", "QF"], outs=["PC"], start=False, stop=True)
                        k.ve(lambda h: h.tensor_copy(YNUM[:, 0:128], PB[:, 0:128]),
                             ins=["PB"], outs=["YNUM"])
                        k.ve(lambda h: h.tensor_scalar_add(ROWS[0:1, 0:256],
                                                           PC[0:1, 0:256], 1e-6),
                             ins=["PC"], outs=["ROWS"])
                        k.ve(lambda h: h.reciprocal(ROWS[0:1, 0:256], ROWS[0:1, 0:256]),
                             ins=["ROWS"], outs=["ROWS"])
                        k.bcast_row(PC[:, 256:384], ROWS[0:1, 0:128], ins=["ROWS"], outs=["PC"])
                        k.bcast_row(PC[:, 384:512], ROWS[0:1, 128:256], ins=["ROWS"], outs=["PC"])
                        k.ve(lambda h, hp=hp, j=j: h.tensor_tensor(
                            YSC[0:64, hp * 512 + j * 128: hp * 512 + (j + 1) * 128],
                            YNUM[0:64, 0:128], PC[0:64, 256:384], ALU.mult),
                            ins=["YNUM", "PC"], outs=["YSC"])
                        k.ve(lambda h, hp=hp, j=j: h.tensor_tensor(
                            YSC[64:128, hp * 512 + j * 128: hp * 512 + (j + 1) * 128],
                            YNUM[64:128, 0:128], PC[64:128, 384:512], ALU.mult),
                            ins=["YNUM", "PC"], outs=["YSC"])
                    s_update(j)
                if DBG == "y":
                    k.dma("ge", ap["g_sp"].rearrange("p (a t) -> p a t", t=TC)
                          [:, :, sc * 512:(sc + 1) * 512],
                          YSC[:].rearrange("p (a t) -> p a t", t=512),
                          ins=["YSC"], outs=["d:g_sp"])
                for mc in range(8):
                    wb, wbn, nk = load_w_block("w_proj0", mc * 128, 128, idx=mc)
                    for dc in range(8):
                        k.mm(PB[:, 0:512], wb[:, dc * 128:(dc + 1) * 128],
                             YSC[:, dc * 512:(dc + 1) * 512],
                             ins=[wbn, "YSC"], outs=["PB"], start=(dc == 0), stop=(dc == 7))
                    k.ve(lambda h, mc=mc: h.tensor_copy(SQB[:, mc * 512:(mc + 1) * 512],
                                                        PB[:, 0:512]), ins=["PB"], outs=["SQB"])
                k.dma("ge", ap["s_sp"].rearrange("p (a t) -> p a t", t=TC)
                      [:, :, sc * 512:(sc + 1) * 512],
                      SQB[:].rearrange("p (a t) -> p a t", t=512), ins=["SQB"], outs=["d:s_sp"])
        if DBG == "s":
            dump_bf16_to_xr("s_sp")
        if DBG == "y":
            dump_bf16_to_xr("g_sp")
        if DBG is None:
            emit_combine()
        p.barrier()
        l0.close()


    if NLAYERS >= 2:
        emit_layernorm()
        if DBG == "xn1":
            dump_xnb_to_xr()
        run_g1 = DBG in (None, "g1", "s1")
        run_attn1 = DBG in (None, "s1")
        l1 = ExitStack()
        QX = k.sbuf("QX", [128, 16384], BF16, scope=l1)
        YF = k.sbuf("YF", [128, 16384], BF16, scope=l1)
        QNM = k.sbuf("QNM", [128, 16], F32, scope=l1)
        KNM = k.sbuf("KNM", [128, 16], F32, scope=l1)
        NRM = k.sbuf("NRM", [128, 16], F32, scope=l1)
        MCOL = k.sbuf("MCOL", [128, 16], F32, scope=l1)
        XLS = k.sbuf("XLS", [128, 8], F32, scope=l1)
        XPB = k.sbuf("XPB", [128, 8], F32, scope=l1)
        MUR = k.sbuf("MUR", [128, 8], F32, scope=l1)
        MUK = k.sbuf("MUK", [128, 8], F32, scope=l1)
        FCB = k.sbuf("FCB", [128, 8], F32, scope=l1)
        KFB = k.sbuf("KFB", [128, 128], BF16, scope=l1)
        l1p = ExitStack()
        WQKVB = k.sbuf("WQKVB", [128, 4096], BF16, scope=l1p)
        BIAST = k.sbuf("BIAST", [128, 3072], BF16, scope=l1p)
        QT0 = k.sbuf("QT0", [128, 512], F32, scope=l1p)
        RT = k.sbuf("RT", [128, 512], F32, scope=l1p)
        RTB = k.sbuf("RTB", [128, 512], BF16, scope=l1p)
        FMP = k.sbuf("FMP", [128, 512], F32, scope=l1p)
        RCO = k.sbuf("RCO", [128, 512], F32, scope=l1p)
        RSI = k.sbuf("RSI", [128, 512], F32, scope=l1p)
        k.dram("cc_xl_in", [128, 8])
        k.dram("cc_xl_out", [2, 128, 8])
        k.dram("exch1_in", [128, 32768], BF16)
        k.dram("exch1_out", [2, 128, 32768], BF16)
        k.dma("sy", MUR[:], ap["mu_r1"], ins=["d:mu_r1"], outs=["MUR"])
        k.dma("sy", MUK[:], ap["mu_k1"], ins=["d:mu_k1"], outs=["MUK"])
        k.dma("sy", FCB[:], ap["b_fc1"], ins=["d:b_fc1"], outs=["FCB"])
        k.dma("sy", RCO[:].rearrange("p (a c) -> p a c", c=32),
              ap["rope_cos"].rearrange("(a p) c -> p a c", p=128),
              ins=["d:rope_cos"], outs=["RCO"])
        k.dma("sy", RSI[:].rearrange("p (a c) -> p a c", c=32),
              ap["rope_sin"].rearrange("(a p) c -> p a c", p=128),
              ins=["d:rope_sin"], outs=["RSI"])

        for dc in range(8):
            k.ve(lambda h, dc=dc: h.tensor_copy(XLS[:, dc:dc + 1],
                                                XNB[:, dc * TC + TC - 1: dc * TC + TC]),
                 ins=["XNB"], outs=["XLS"])
        k.dma("ge", ap["cc_xl_in"], XLS[:], ins=["XLS"], outs=["d:cc_xl_in"])
        k.cc("AllGather", ALU.bypass, PAIRS, ap["cc_xl_in"], ap["cc_xl_out"],
             ins=["d:cc_xl_in"], outs=["d:cc_xl_out"])
        k.dma("ge", XPB[:], ap["cc_xl_out"][0], ins=["d:cc_xl_out"], outs=["XPB"])
        k.ve(lambda h: h.tensor_scalar(XPB[:], XPB[:], HSEL[:, 0:1], None, ALU.mult),
             ins=["XPB", "HSEL"], outs=["XPB"])

        k.ve(lambda h: h.memset(QNM[:], 0.0), outs=["QNM"])
        k.ve(lambda h: h.memset(KNM[:], 0.0), outs=["KNM"])
        for cc6 in range(6):
            k.dma("sy", ROWS[0:1, 0:512], ap["b_qkv1r"][:, cc6 * 512:(cc6 + 1) * 512],
                  ins=["d:b_qkv1r"], outs=["ROWS"])
            k.bcast_row(PA[:, 0:512], ROWS[0:1, 0:512],
                        ins=["ROWS"], outs=["PA"])
            k.ve(lambda h, cc6=cc6: h.tensor_copy(BIAST[:, cc6 * 512:(cc6 + 1) * 512],
                                                  PA[:, 0:512]), ins=["PA"], outs=["BIAST"])

        for cc6 in range(6):
            iq = cc6 < 2
            ik = 2 <= cc6 < 4
            for dc in range(8):
                k.dma("ge", WQKVB[:, dc * 512:(dc + 1) * 512],
                      ap["w_qkv1"][dc * 128:(dc + 1) * 128, cc6 * 512:(cc6 + 1) * 512],
                      ins=["d:w_qkv1"], outs=["WQKVB"])
            for tt in range(16):
                for dc in range(8):
                    k.mm(PB[:, 0:512],
                         XNB[:, dc * TC + tt * 128: dc * TC + (tt + 1) * 128],
                         WQKVB[:, dc * 512:(dc + 1) * 512],
                         ins=["XNB", "WQKVB"], outs=["PB"],
                         start=(dc == 0), stop=(dc == 7))
                k.ve(lambda h, cc6=cc6: h.tensor_tensor(
                    QT0[:], PB[:, 0:512], BIAST[:, cc6 * 512:(cc6 + 1) * 512], ALU.add),
                    ins=["PB", "BIAST"], outs=["QT0"])
                if iq or ik:
                    k.ve(lambda h: h.tensor_tensor(RT[:], QT0[:], QT0[:], ALU.mult),
                         ins=["QT0"], outs=["RT"])
                    dstn, dst = ("QNM", QNM) if iq else ("KNM", KNM)
                    k.ve(lambda h: h.tensor_reduce(
                        NRM[:, 0:8], RT[:].rearrange("p (a b) -> p a b", b=64),
                        AX.X, ALU.add), ins=["RT"], outs=["NRM"])
                    h8 = (cc6 % 2) * 8
                    k.ve(lambda h, dst=dst, h8=h8: h.tensor_tensor(
                        dst[:, h8:h8 + 8], dst[:, h8:h8 + 8], NRM[:, 0:8], ALU.max),
                        ins=["NRM", dstn], outs=[dstn])
                    qe = QT0[:].rearrange("p (a two c) -> p a two c", two=32, c=2)[:, :, :, 0]
                    qo = QT0[:].rearrange("p (a two c) -> p a two c", two=32, c=2)[:, :, :, 1]
                    cosb = RCO[:, tt * 32:(tt + 1) * 32].rearrange(
                        "p (o c) -> p o c", o=1).broadcast_to([128, 8, 32])
                    sinb = RSI[:, tt * 32:(tt + 1) * 32].rearrange(
                        "p (o c) -> p o c", o=1).broadcast_to([128, 8, 32])
                    re = RT[:].rearrange("p (a two c) -> p a two c", two=32, c=2)[:, :, :, 0]
                    ro = RT[:].rearrange("p (a two c) -> p a two c", two=32, c=2)[:, :, :, 1]
                    k.ve(lambda h, qe=qe, cosb=cosb: h.tensor_tensor(
                        FMP[:, 0:256].rearrange("p (a c) -> p a c", c=32), qe, cosb, ALU.mult),
                        ins=["QT0", "RCO"], outs=["FMP"])
                    k.p.emit("ve", lambda h, qo=qo, sinb=sinb, re=re: h.scalar_tensor_tensor(
                        re, qo, -1.0, sinb, ALU.mult, ALU.mult), ins=["QT0", "RSI"], outs=["RT"])
                    k.ve(lambda h, re=re: h.tensor_tensor(
                        re, re, FMP[:, 0:256].rearrange("p (a c) -> p a c", c=32), ALU.add),
                        ins=["RT", "FMP"], outs=["RT"])
                    k.ve(lambda h, qe=qe, sinb=sinb: h.tensor_tensor(
                        FMP[:, 0:256].rearrange("p (a c) -> p a c", c=32), qe, sinb, ALU.mult),
                        ins=["QT0", "RSI"], outs=["FMP"])
                    k.ve(lambda h, qo=qo, cosb=cosb, ro=ro: h.tensor_tensor(
                        ro, qo, cosb, ALU.mult), ins=["QT0", "RCO"], outs=["RT"])
                    k.ve(lambda h, ro=ro: h.tensor_tensor(
                        ro, ro, FMP[:, 0:256].rearrange("p (a c) -> p a c", c=32), ALU.add),
                        ins=["RT", "FMP"], outs=["RT"])
                    for pb4 in range(4):
                        pair = (cc6 % 2) * 4 + pb4
                        k.tr(PE[:, 0:128], RT[:, pb4 * 128:(pb4 + 1) * 128],
                             ins=["RT"], outs=["PE"])
                        if iq:
                            k.ve(lambda h, pair=pair, tt=tt: h.tensor_copy(
                                QX[:, pair * 2048 + tt * 128: pair * 2048 + (tt + 1) * 128],
                                PE[:, 0:128]), ins=["PE"], outs=["QX"])
                        else:
                            k.ve(lambda h: h.tensor_copy(KFB[:], PE[:, 0:128]),
                                 ins=["PE"], outs=["KFB"])
                            k.dma("ge", ap["exch1_in"][:, pair * 2048 + tt * 128:
                                                       pair * 2048 + (tt + 1) * 128],
                                  KFB[:], ins=["KFB"], outs=["d:exch1_in"])
                else:
                    k.ve(lambda h: h.tensor_copy(RTB[:], QT0[:]), ins=["QT0"], outs=["RTB"])
                    k.dma("ge", ap["exch1_in"][:, 16384 + tt * 1024 + (cc6 - 4) * 512:
                                               16384 + tt * 1024 + (cc6 - 3) * 512],
                          RTB[:], ins=["RTB"], outs=["d:exch1_in"])
        p.barrier()
        l1p.close()
        k.cc("AllGather", ALU.bypass, PAIRS, ap["exch1_in"], ap["exch1_out"],
             ins=["d:exch1_in"], outs=["d:exch1_out"])
        l1g = ExitStack()
        PREV = k.sbuf("PREV", [128, 512], BF16, scope=l1g)
        D1 = k.sbuf("D1", [128, 512], F32, scope=l1g)
        XRB = k.sbuf("XRB", [128, 4096], BF16, scope=l1g)
        XKB = k.sbuf("XKB", [128, 4096], BF16, scope=l1g)
        RB = k.sbuf("RB", [128, 4096], BF16, scope=l1g)
        KA = k.sbuf("KA", [128, 4096], BF16, scope=l1g)
        FM = k.sbuf("FM", [128, 512], F32, scope=l1g)
        for dst, (which, t_) in enumerate((("QNM", QNM), ("KNM", KNM))):
            k.ve(lambda h: h.memset(XST[:, 0:128], 0.0), outs=["XST"])
            k.ve(lambda h, t_=t_: h.tensor_copy(XST[:, 0:16], t_[:]), ins=[which], outs=["XST"])
            k.tr(PE[:, 0:128], XST[:, 0:128], ins=["XST"], outs=["PE"])
            k.ve(lambda h: h.reduce_max(XST2[:, 0:1], PE[:, 0:128], AX.X),
                 ins=["PE"], outs=["XST2"])
            k.ve(lambda h: h.memset(XST[:, 0:128], 0.0), outs=["XST"])
            k.ve(lambda h: h.tensor_copy(XST[:, 0:1], XST2[:, 0:1]), ins=["XST2"], outs=["XST"])
            k.tr(PE[:, 0:128], XST[:, 0:128], ins=["XST"], outs=["PE"])
            k.ve(lambda h, dst=dst: h.tensor_copy(
                ROWS[0:1, 512 + dst * 16: 512 + dst * 16 + 16], PE[0:1, 0:16]),
                ins=["PE"], outs=["ROWS"])
        k.ve(lambda h: h.tensor_tensor(ROWS[0:1, 544:560], ROWS[0:1, 512:528],
                                       ROWS[0:1, 528:544], ALU.mult), ins=["ROWS"], outs=["ROWS"])
        k.se(lambda h: h.activation(ROWS[0:1, 544:560], ROWS[0:1, 544:560], ACT.Sqrt),
             ins=["ROWS"], outs=["ROWS"])
        k.ve(lambda h: h.tensor_scalar_mul(ROWS[0:1, 544:560], ROWS[0:1, 544:560], -0.125),
             ins=["ROWS"], outs=["ROWS"])
        for hd in range(16):
            k.bcast_row(PE[:, 0:1], ROWS[0:1, 544 + hd:545 + hd], ins=["ROWS"], outs=["PE"])
            k.ve(lambda h, hd=hd: h.tensor_copy(MCOL[:, hd:hd + 1], PE[:, 0:1]),
                 ins=["PE"], outs=["MCOL"])

        for tc4 in range(4) if run_g1 else []:
            t0 = tc4 * 512
            for dc in range(8):
                if tc4 == 0:
                    k.ve(lambda h, dc=dc: h.tensor_copy(
                        PREV[:, 1:512], XNB[:, dc * TC: dc * TC + 511]),
                        ins=["XNB"], outs=["PREV"])
                    k.ve(lambda h, dc=dc: h.tensor_copy(
                        PREV[:, 0:1], XPB[:, dc:dc + 1]), ins=["XPB"], outs=["PREV"])
                else:
                    k.ve(lambda h, dc=dc, t0=t0: h.tensor_copy(
                        PREV[:], XNB[:, dc * TC + t0 - 1: dc * TC + t0 + 511]),
                        ins=["XNB"], outs=["PREV"])
                k.ve(lambda h, dc=dc, t0=t0: h.tensor_tensor(
                    D1[:], XNB[:, dc * TC + t0: dc * TC + t0 + 512], PREV[:], ALU.subtract),
                    ins=["XNB", "PREV"], outs=["D1"])
                k.p.emit("ve", lambda h, dc=dc: h.scalar_tensor_tensor(
                    XRB[:, dc * 512:(dc + 1) * 512], D1[:], MUR[:, dc:dc + 1], PREV[:],
                    ALU.mult, ALU.add), ins=["D1", "MUR", "PREV"], outs=["XRB"])
                k.p.emit("ve", lambda h, dc=dc: h.scalar_tensor_tensor(
                    XKB[:, dc * 512:(dc + 1) * 512], D1[:], MUK[:, dc:dc + 1], PREV[:],
                    ALU.mult, ALU.add), ins=["D1", "MUK", "PREV"], outs=["XKB"])
            for mc in range(8):
                wb, wbn, nk = load_w_block("w_r1", mc * 128, 128, idx=mc)
                for dc in range(8):
                    k.mm(PB[:, 0:512], wb[:, dc * 128:(dc + 1) * 128],
                         XRB[:, dc * 512:(dc + 1) * 512],
                         ins=[wbn, "XRB"], outs=["PB"], start=(dc == 0), stop=(dc == 7))
                k.se(lambda h, mc=mc: h.activation(RB[:, mc * 512:(mc + 1) * 512],
                                                   PB[:, 0:512], ACT.Sigmoid),
                     ins=["PB"], outs=["RB"])
            for mc in range(8):
                wb, wbn, nk = load_w_block("w_k1", mc * 128, 128, idx=mc)
                for dc in range(8):
                    k.mm(PB[:, 0:512], wb[:, dc * 128:(dc + 1) * 128],
                         XKB[:, dc * 512:(dc + 1) * 512],
                         ins=[wbn, "XKB"], outs=["PB"], start=(dc == 0), stop=(dc == 7))
                k.se(lambda h: h.activation(FM[:], PB[:, 0:512], ACT.Relu),
                     ins=["PB"], outs=["FM"])
                k.se(lambda h, mc=mc: h.activation(KA[:, mc * 512:(mc + 1) * 512],
                                                   FM[:], ACT.Square), ins=["FM"], outs=["KA"])
            for mc in range(8):
                wb, wbn, nk = load_w_block("w_v1", mc * 128, 128, idx=mc)
                for dc in range(8):
                    k.mm(PB[:, 0:512], wb[:, dc * 128:(dc + 1) * 128],
                         KA[:, dc * 512:(dc + 1) * 512],
                         ins=[wbn, "KA"], outs=["PB"], start=(dc == 0), stop=(dc == 7))
                k.ve(lambda h, mc=mc: h.tensor_tensor(
                    FM[:], PB[:, 0:512], RB[:, mc * 512:(mc + 1) * 512], ALU.mult),
                    ins=["PB", "RB"], outs=["FM"])
                k.se(lambda h, mc=mc: h.activation(SQB[:, mc * 512:(mc + 1) * 512],
                                                   FM[:], ACT.Silu), ins=["FM"], outs=["SQB"])
            k.dma("ge", ap["g_sp"].rearrange("p (a t) -> p a t", t=TC)[:, :, t0:t0 + 512],
                  SQB[:].rearrange("p (a t) -> p a t", t=512), ins=["SQB"], outs=["d:g_sp"])
        p.barrier()
        l1g.close()
        l1c = ExitStack()
        KPH = k.sbuf("KPH", [128, 4096], BF16, scope=l1c)
        VPH = k.sbuf("VPH", [128, 4096], BF16, scope=l1c)
        PTB = k.sbuf("PTB", [128, 1024], BF16, scope=l1c)

        if DBG == "g1":
            dump_bf16_to_xr("g_sp")
        for pair in range(8) if run_attn1 else []:
            for half in range(2):
                k.dma("sy", KPH[:, half * 2048:(half + 1) * 2048],
                      ap["exch1_out"][half][:, pair * 2048:(pair + 1) * 2048],
                      ins=["d:exch1_out"], outs=["KPH"])
                k.dma("sy", VPH[:, half * 2048:(half + 1) * 2048]
                      .rearrange("p (a b) -> p a b", b=128),
                      ap["exch1_out"][half][:, 16384:32768]
                      .rearrange("p (a b) -> p a b", b=1024)[:, :, pair * 128:(pair + 1) * 128],
                      ins=["d:exch1_out"], outs=["VPH"])
            for qt in range(4):
                q0 = pair * 2048 + qt * 512
                for sch in range(32):
                    bank, bn = (PA[:, 0:512], "PA") if sch % 2 == 0 else (PA[:, 512:1024], "PA")
                    k.mm(bank, KPH[0:64, sch * 128:(sch + 1) * 128],
                         QX[0:64, q0:q0 + 512], ins=["KPH", "QX"], outs=[bn])
                    k.p.emit("se", lambda h, bank=bank, pair=pair: h.activation(
                        PTB[:, 0:512], bank, ACT.Exp, bias=MCOL[:, 2 * pair:2 * pair + 1],
                        scale=0.125), ins=[bn, "MCOL"], outs=["PTB"])
                    k.mm(bank, KPH[64:128, sch * 128:(sch + 1) * 128],
                         QX[64:128, q0:q0 + 512], ins=["KPH", "QX"], outs=[bn])
                    k.p.emit("se", lambda h, bank=bank, pair=pair: h.activation(
                        PTB[:, 512:1024], bank, ACT.Exp, bias=MCOL[:, 2 * pair + 1:2 * pair + 2],
                        scale=0.125), ins=[bn, "MCOL"], outs=["PTB"])
                    k.mm(PB[0:64, 0:512], VPH[:, sch * 128: sch * 128 + 64], PTB[:, 0:512],
                         ins=["VPH", "PTB"], outs=["PB"], start=(sch == 0), stop=(sch == 31))
                    k.mm(PB[64:128, 0:512], VPH[:, sch * 128 + 64: sch * 128 + 128],
                         PTB[:, 512:1024],
                         ins=["VPH", "PTB"], outs=["PB"], start=(sch == 0), stop=(sch == 31))
                    k.mm(PC[0:1, 0:512], ONES[:], PTB[:, 0:512],
                         ins=["ONESB", "PTB"], outs=["PC"], start=(sch == 0), stop=(sch == 31))
                    k.mm(PD[0:1, 0:512], ONES[:], PTB[:, 512:1024],
                         ins=["ONESB", "PTB"], outs=["PD"], start=(sch == 0), stop=(sch == 31))
                k.ve(lambda h: h.tensor_copy(ROWS[0:1, 0:512], PC[0:1, 0:512]),
                     ins=["PC"], outs=["ROWS"])
                k.ve(lambda h: h.reciprocal(ROWS[0:1, 0:512], ROWS[0:1, 0:512]),
                     ins=["ROWS"], outs=["ROWS"])
                k.ve(lambda h: h.tensor_copy(ROWS[0:1, 512:1024], PD[0:1, 0:512]),
                     ins=["PD"], outs=["ROWS"])
                k.ve(lambda h: h.reciprocal(ROWS[0:1, 512:1024], ROWS[0:1, 512:1024]),
                     ins=["ROWS"], outs=["ROWS"])
                k.bcast_row(PA[:, 1024:1536], ROWS[0:1, 0:512], ins=["ROWS"], outs=["PA"])
                k.bcast_row(PA[:, 1536:2048], ROWS[0:1, 512:1024], ins=["ROWS"], outs=["PA"])
                k.ve(lambda h: h.tensor_copy(XST[:], PB[0:128, 0:512]),
                     ins=["PB"], outs=["XST"])
                k.ve(lambda h, pair=pair, q0=q0: h.tensor_tensor(
                    YF[0:64, q0:q0 + 512], XST[0:64, 0:512], PA[0:64, 1024:1536], ALU.mult),
                    ins=["XST", "PA"], outs=["YF"])
                k.ve(lambda h, pair=pair, q0=q0: h.tensor_tensor(
                    YF[64:128, q0:q0 + 512], XST[64:128, 0:512], PA[64:128, 1536:2048], ALU.mult),
                    ins=["XST", "PA"], outs=["YF"])
        for tc4 in range(4) if run_attn1 else []:
            t0 = tc4 * 512
            for mc in range(8):
                wb, wbn, nk = load_w_block("w_fc1", mc * 128, 128, idx=mc)
                for dc in range(8):
                    k.mm(PB[:, 0:512], wb[:, dc * 128:(dc + 1) * 128],
                         YF[:, dc * 2048 + t0: dc * 2048 + t0 + 512],
                         ins=[wbn, "YF"], outs=["PB"], start=(dc == 0), stop=(dc == 7))
                k.ve(lambda h, mc=mc: h.tensor_scalar(
                    SQB[:, mc * 512:(mc + 1) * 512], PB[:, 0:512],
                    FCB[:, mc:mc + 1], None, ALU.add), ins=["PB", "FCB"], outs=["SQB"])
            k.dma("ge", ap["s_sp"].rearrange("p (a t) -> p a t", t=TC)[:, :, t0:t0 + 512],
                  SQB[:].rearrange("p (a t) -> p a t", t=512), ins=["SQB"], outs=["d:s_sp"])
        if DBG == "s1":
            dump_bf16_to_xr("s_sp")
        if DBG is None:
            emit_combine()
        p.barrier()
        l1c.close()
        l1.close()


    if NLAYERS >= 3:
        emit_layernorm()
        run_g2 = DBG in (None, "g2", "s2")
        run_attn2 = DBG in (None, "s2")
        l2 = ExitStack()
        KV2 = k.sbuf("KV2", [128, 16], F32, scope=l2)
        SCG = k.sbuf("SCG", [128, 22], F32, scope=l2)
        SCU = k.sbuf("SCU", [128, 22], F32, scope=l2)
        SCD = k.sbuf("SCD", [128, 8], F32, scope=l2)
        CLP = k.sbuf("CLP", [1, 2], F32, scope=l2)
        COLX = k.sbuf("COLX", [128, 8], F32, scope=l2)
        QT2 = k.sbuf("QT2", [128, 512], F32, scope=l2)
        KT2 = k.sbuf("KT2", [128, 512], BF16, scope=l2)
        VT2 = k.sbuf("VT2", [128, 512], F32, scope=l2)
        FM2 = k.sbuf("FM2", [128, 512], F32, scope=l2)
        FM3 = k.sbuf("FM3", [128, 512], F32, scope=l2)
        BQ2 = k.sbuf("BQ2", [128, 8], F32, scope=l2)
        BK2 = k.sbuf("BK2", [128, 8], F32, scope=l2)
        BV2 = k.sbuf("BV2", [128, 8], F32, scope=l2)
        BO2 = k.sbuf("BO2", [128, 8], F32, scope=l2)
        HSG = k.sbuf("HSG", [128, 512], BF16, scope=l2)
        HIQ = k.sbuf("HIQ", [128, 512], I32, scope=l2)
        COLS2 = k.sbuf("COLS2", [128, 16], F32, scope=l2)
        l2q = ExitStack()
        RECB = k.sbuf("RECB", [128, 2816], F32, scope=l2q)
        WTQ = k.sbuf("WTQ", [128, 2816], F32, scope=l2q)
        WTQI = k.sbuf("WTQI", [128, 2816], I32, scope=l2q)
        WTQB = k.sbuf("WTQB", [128, 2816], BF16, scope=l2q)
        k.dram("wgq_sp", [D, 2816], BF16)
        k.dram("wuq_sp", [D, 2816], BF16)
        k.dram("wdq_sp", [2816, D], BF16)
        k.dram("h_sp", [128, 22 * TC], BF16)
        k.dram("rowb", [128, 32])
        k.dram("cc_gm_in", [1, 128])
        k.dram("cc_gm_out", [1, 128])
        k.dram("cc_kv_in", [128, 16])
        k.dram("cc_kv_out", [128, 16])
        k.dram("cc_hm_in", [1, 128])
        k.dram("cc_hm_out", [1, 128])
        for nm, t_ in (("bq2", BQ2), ("bk2", BK2), ("bv2", BV2), ("bo2", BO2)):
            k.dma("sy", t_[:], ap[nm], ins=["d:" + nm], outs=[nm.upper()])
        k.dma("sy", CLP[:], ap["clip2"], ins=["d:clip2"], outs=["CLP"])
        k.se(lambda h: h.activation(CLP[:], CLP[:], ACT.Sigmoid), ins=["CLP"], outs=["CLP"])

        def absmax_to_scalar(col_ap, col_name, dst_sl):
            k.ve(lambda h: h.memset(XST[:, 0:128], 0.0), outs=["XST"])
            k.ve(lambda h: h.tensor_copy(XST[:, 0:col_ap.shape[1]], col_ap),
                 ins=[col_name], outs=["XST"])
            k.tr(PE[:, 0:128], XST[:, 0:128], ins=["XST"], outs=["PE"])
            k.ve(lambda h: h.reduce_max(XST2[:, 0:1], PE[:, 0:128], AX.X),
                 ins=["PE"], outs=["XST2"])
            k.ve(lambda h: h.memset(XST[:, 0:128], 0.0), outs=["XST"])
            k.ve(lambda h: h.tensor_copy(XST[:, 0:1], XST2[:, 0:1]), ins=["XST2"], outs=["XST"])
            k.tr(PE[:, 0:128], XST[:, 0:128], ins=["XST"], outs=["PE"])
            k.ve(lambda h: h.memset(ROWS[0:1, 0:128], 0.0), outs=["ROWS"])
            k.ve(lambda h: h.reduce_max(ROWS[0:1, 0:1], PE[0:1, 0:128], AX.X),
                 ins=["PE"], outs=["ROWS"])

        def quant_weights(asis, trans, spill, nrow_tiles, ncols, scol, scoln):
            for t8 in range(nrow_tiles):
                k.dma("sy", WTQ[:, 0:ncols], ap[asis][t8 * 128:(t8 + 1) * 128, :],
                      ins=["d:" + asis], outs=["WTQ"])
                k.ve(lambda h, t8=t8, ncols=ncols: h.reduce_max(
                    scol[:, t8:t8 + 1], WTQ[:, 0:ncols], AX.X, apply_absolute_value=True),
                    ins=["WTQ"], outs=[scoln])
            k.ve(lambda h, n=nrow_tiles: h.tensor_scalar_mul(scol[:, 0:n], scol[:, 0:n],
                                                             1.0 / 127.0),
                 ins=[scoln], outs=[scoln])
            k.dma("ge", ap["rowb"][:, 0:nrow_tiles], scol[:, 0:nrow_tiles],
                  ins=[scoln], outs=["d:rowb"])
            nq = nrow_tiles * 128
            for cc in range((nq + 511) // 512):
                w = min(512, nq - cc * 512)
                for j in range((w + 127) // 128):
                    t8 = cc * 4 + j
                    k.p.emit("sy", lambda h, j=j, t8=t8: h.dma_start(
                        out=ROWS[0:1, j * 128:(j + 1) * 128],
                        in_=ap["rowb"].rearrange("p a -> a p")[t8:t8 + 1, :],
                        allow_slow_non_contiguous=True),
                        ins=["d:rowb"], outs=["ROWS"], dma=True)
                k.ve(lambda h, w=w: h.reciprocal(ROWS[0:1, 0:w], ROWS[0:1, 0:w]),
                     ins=["ROWS"], outs=["ROWS"])
                k.bcast_row(PB[:, 0:w], ROWS[0:1, 0:w], ins=["ROWS"], outs=["PB"])
                k.ve(lambda h, cc=cc, w=w: h.tensor_copy(RECB[:, cc * 512:cc * 512 + w],
                                                         PB[:, 0:w]), ins=["PB"], outs=["RECB"])
            ntile = (ap[trans].shape[0]) // 128
            for t8 in range(ntile):
                k.dma("sy", WTQ[:, 0:nq], ap[trans][t8 * 128:(t8 + 1) * 128, :],
                      ins=["d:" + trans], outs=["WTQ"])
                k.ve(lambda h, nq=nq: h.tensor_tensor(WTQ[:, 0:nq], WTQ[:, 0:nq],
                                                      RECB[:, 0:nq], ALU.mult),
                     ins=["WTQ", "RECB"], outs=["WTQ"])
                k.ve(lambda h, nq=nq: h.tensor_copy(WTQI[:, 0:nq], WTQ[:, 0:nq]),
                     ins=["WTQ"], outs=["WTQI"])
                k.ve(lambda h, nq=nq: h.tensor_scalar(WTQB[:, 0:nq], WTQI[:, 0:nq],
                                                      -128.0, 127.0, ALU.max, ALU.min),
                     ins=["WTQI"], outs=["WTQB"])
                k.dma("ge", ap[spill][t8 * 128:(t8 + 1) * 128, :], WTQB[:, 0:nq],
                      ins=["WTQB"], outs=["d:" + spill])

        if run_g2:
            quant_weights("wg2", "wg2T", "wgq_sp", 22, 1024, SCG, "SCG")
            quant_weights("wu2", "wu2T", "wuq_sp", 22, 1024, SCU, "SCU")
            quant_weights("wd2", "wd2T", "wdq_sp", 8, 2816, SCD, "SCD")

            for q in range(8):
                k.ve(lambda h, q=q: h.reduce_max(COLS2[:, 1:2], XNB[:, q * TC:(q + 1) * TC],
                                                 AX.X, apply_absolute_value=True),
                     ins=["XNB"], outs=["COLS2"])
                if q == 0:
                    k.ve(lambda h: h.tensor_copy(COLS2[:, 0:1], COLS2[:, 1:2]),
                         ins=["COLS2"], outs=["COLS2"])
                else:
                    k.ve(lambda h: h.tensor_tensor(COLS2[:, 0:1], COLS2[:, 0:1],
                                                   COLS2[:, 1:2], ALU.max),
                         ins=["COLS2"], outs=["COLS2"])
            absmax_to_scalar(COLS2[:, 0:1], "COLS2", np.s_[0:128])
            k.dma("ge", ap["cc_gm_in"], ROWS[0:1, 0:128], ins=["ROWS"], outs=["d:cc_gm_in"])
            k.cc("AllReduce", ALU.max, ALL8, ap["cc_gm_in"], ap["cc_gm_out"],
                 ins=["d:cc_gm_in"], outs=["d:cc_gm_out"])

        p.barrier()
        l2q.close()
        l2a = ExitStack()
        QF2 = k.sbuf("QF2", [128, 16384], BF16, scope=l2a)
        if run_attn2:
            k.ve(lambda h: h.memset(KV2[:], 0.0), outs=["KV2"])
            for tc4 in range(4):
                t0 = tc4 * 512
                for mc in range(8):
                    for wname, widx in (("wq2", 0), ("wk2", 1), ("wv2", 2)):
                        wb, wbn, nk = load_w_block(wname, mc * 128, 128, idx=mc * 3 + widx)
                        bank = PB if widx == 0 else (PC if widx == 1 else PD)
                        bn = "PB" if widx == 0 else ("PC" if widx == 1 else "PD")
                        for dc in range(8):
                            k.mm(bank[:, 0:512], wb[:, dc * 128:(dc + 1) * 128],
                                 XNB[:, dc * TC + t0: dc * TC + t0 + 512],
                                 ins=[wbn, "XNB"], outs=[bn],
                                 start=(dc == 0), stop=(dc == 7))
                    k.ve(lambda h, mc=mc: h.tensor_scalar(
                        QT2[:], PB[:, 0:512], BQ2[:, mc:mc + 1], None, ALU.add),
                        ins=["PB", "BQ2"], outs=["QT2"])
                    k.ve(lambda h: h.tensor_scalar_min(FM2[:], QT2[:], 0.0),
                         ins=["QT2"], outs=["FM2"])
                    k.se(lambda h: h.activation(FM3[:], FM2[:], ACT.Exp),
                         ins=["FM2"], outs=["FM3"])
                    k.p.emit("ve", lambda h, mc=mc, t0=t0: h.scalar_tensor_tensor(
                        QF2[:, mc * TC + t0: mc * TC + t0 + 512], QT2[:], 0.0, FM3[:],
                        ALU.max, ALU.add), ins=["QT2", "FM3"], outs=["QF2"])
                    k.ve(lambda h, mc=mc: h.tensor_scalar(
                        QT2[:], PC[:, 0:512], BK2[:, mc:mc + 1], None, ALU.add),
                        ins=["PC", "BK2"], outs=["QT2"])
                    k.ve(lambda h: h.tensor_scalar_min(FM2[:], QT2[:], 0.0),
                         ins=["QT2"], outs=["FM2"])
                    k.se(lambda h: h.activation(FM3[:], FM2[:], ACT.Exp),
                         ins=["FM2"], outs=["FM3"])
                    k.p.emit("ve", lambda h: h.scalar_tensor_tensor(
                        KT2[:], QT2[:], 0.0, FM3[:], ALU.max, ALU.add),
                        ins=["QT2", "FM3"], outs=["KT2"])
                    k.ve(lambda h, mc=mc: h.tensor_scalar(
                        VT2[:], PD[:, 0:512], BV2[:, mc:mc + 1], None, ALU.add),
                        ins=["PD", "BV2"], outs=["VT2"])
                    k.ve(lambda h: h.tensor_tensor(FM2[:], KT2[:], VT2[:], ALU.mult),
                         ins=["KT2", "VT2"], outs=["FM2"])
                    k.ve(lambda h: h.reduce_sum(COLS2[:, 2:3], FM2[:], AX.X),
                         ins=["FM2"], outs=["COLS2"])
                    k.ve(lambda h, mc=mc: h.tensor_tensor(KV2[:, mc:mc + 1], KV2[:, mc:mc + 1],
                                                          COLS2[:, 2:3], ALU.add),
                         ins=["COLS2", "KV2"], outs=["KV2"])
                    k.ve(lambda h: h.reduce_sum(COLS2[:, 2:3], KT2[:], AX.X),
                         ins=["KT2"], outs=["COLS2"])
                    k.ve(lambda h, mc=mc: h.tensor_tensor(KV2[:, 8 + mc:9 + mc],
                                                          KV2[:, 8 + mc:9 + mc],
                                                          COLS2[:, 2:3], ALU.add),
                         ins=["COLS2", "KV2"], outs=["KV2"])
            k.dma("ge", ap["cc_kv_in"], KV2[:], ins=["KV2"], outs=["d:cc_kv_in"])
            k.cc("AllReduce", ALU.add, PAIRS, ap["cc_kv_in"], ap["cc_kv_out"],
                 ins=["d:cc_kv_in"], outs=["d:cc_kv_out"])
            k.dma("sy", KV2[:], ap["cc_kv_out"], ins=["d:cc_kv_out"], outs=["KV2"])
            VF2 = k.sbuf("VF2", [128, 16384], BF16, scope=l2a)
            for tc4 in range(4):
                t0 = tc4 * 512
                for mc in range(8):
                    k.ve(lambda h, mc=mc, t0=t0: h.tensor_scalar(
                        FM2[:], QF2[:, mc * TC + t0: mc * TC + t0 + 512],
                        KV2[:, 8 + mc:9 + mc], 1e-6, ALU.mult, ALU.add),
                        ins=["QF2", "KV2"], outs=["FM2"])
                    k.ve(lambda h: h.reciprocal(FM2[:], FM2[:]), ins=["FM2"], outs=["FM2"])
                    k.ve(lambda h, mc=mc, t0=t0: h.tensor_scalar(
                        FM3[:], QF2[:, mc * TC + t0: mc * TC + t0 + 512],
                        KV2[:, mc:mc + 1], None, ALU.mult),
                        ins=["QF2", "KV2"], outs=["FM3"])
                    k.ve(lambda h, mc=mc, t0=t0: h.tensor_tensor(
                        VF2[:, mc * TC + t0: mc * TC + t0 + 512], FM3[:], FM2[:], ALU.mult),
                        ins=["FM3", "FM2"], outs=["VF2"])
            for tc4 in range(4):
                t0 = tc4 * 512
                for mc in range(8):
                    wb, wbn, nk = load_w_block("wo2", mc * 128, 128, idx=mc)
                    for dc in range(8):
                        k.mm(PB[:, 0:512], wb[:, dc * 128:(dc + 1) * 128],
                             VF2[:, dc * TC + t0: dc * TC + t0 + 512],
                             ins=[wbn, "VF2"], outs=["PB"], start=(dc == 0), stop=(dc == 7))
                    k.se(lambda h, mc=mc: h.activation(
                        SQB[:, mc * 512:(mc + 1) * 512], PB[:, 0:512], ACT.Gelu_apprx_tanh,
                        bias=BO2[:, mc:mc + 1]), ins=["PB", "BO2"], outs=["SQB"])
                k.dma("ge", ap["s_sp"].rearrange("p (a t) -> p a t", t=TC)[:, :, t0:t0 + 512],
                      SQB[:].rearrange("p (a t) -> p a t", t=512), ins=["SQB"], outs=["d:s_sp"])
        if DBG == "s2":
            dump_bf16_to_xr("s_sp")

        p.barrier()
        l2a.close()
        l2f = ExitStack()
        XQ2 = k.sbuf("XQ2", [128, 16384], BF16, scope=l2f)
        HQF = k.sbuf("HQF", [128, 22 * 512], BF16, scope=l2f)
        if run_g2:
            k.dma("sy", ROWS[0:1, 0:128], ap["cc_gm_out"], ins=["d:cc_gm_out"], outs=["ROWS"])
            k.ve(lambda h: h.tensor_tensor(ROWS[0:1, 1:2], ROWS[0:1, 0:1], CLP[0:1, 0:1],
                                           ALU.mult), ins=["ROWS", "CLP"], outs=["ROWS"])
            k.ve(lambda h: h.tensor_scalar_mul(ROWS[0:1, 1:2], ROWS[0:1, 1:2], 1.0 / 127.0),
                 ins=["ROWS"], outs=["ROWS"])
            k.ve(lambda h: h.reciprocal(ROWS[0:1, 2:3], ROWS[0:1, 1:2]),
                 ins=["ROWS"], outs=["ROWS"])
            k.bcast_row(PB[:, 0:1], ROWS[0:1, 1:2], ins=["ROWS"], outs=["PB"])
            k.ve(lambda h: h.tensor_copy(COLX[:, 0:1], PB[:, 0:1]), ins=["PB"], outs=["COLX"])
            k.bcast_row(PB[:, 0:1], ROWS[0:1, 2:3], ins=["ROWS"], outs=["PB"])
            k.ve(lambda h: h.tensor_copy(COLX[:, 1:2], PB[:, 0:1]), ins=["PB"], outs=["COLX"])
            for q in range(8):
                for tc4 in range(4):
                    sl = np.s_[:, q * TC + tc4 * 512: q * TC + (tc4 + 1) * 512]
                    k.ve(lambda h, sl=sl: h.tensor_scalar(
                        FM2[:], XNB[sl], COLX[:, 1:2], None, ALU.mult),
                        ins=["XNB", "COLX"], outs=["FM2"])
                    k.ve(lambda h: h.tensor_copy(HIQ[:], FM2[:]), ins=["FM2"], outs=["HIQ"])
                    k.ve(lambda h, sl=sl: h.tensor_scalar(
                        XQ2[sl], HIQ[:], -128.0, 127.0, ALU.max, ALU.min),
                        ins=["HIQ"], outs=["XQ2"])
            k.ve(lambda h: h.memset(COLS2[:, 4:5], 0.0), outs=["COLS2"])
            for tc4 in range(4):
                t0 = tc4 * 512
                for ic in range(22):
                    for wname, widx in (("wgq_sp", 0), ("wuq_sp", 1)):
                        wbb, wbn = (WB0, "WB0") if widx == 0 else (WB1, "WB1")
                        k.dma("sy", wbb[:, 0:1024].rearrange("p (a m) -> p a m", m=128),
                              ap[wname].rearrange("(a p) m -> p a m", p=128)
                              [:, :, ic * 128:(ic + 1) * 128],
                              ins=["d:" + wname], outs=[wbn])
                        bank, bn = (PB, "PB") if widx == 0 else (PC, "PC")
                        for dc in range(8):
                            k.mm(bank[:, 0:512], wbb[:, dc * 128:(dc + 1) * 128],
                                 XQ2[:, dc * TC + t0: dc * TC + t0 + 512],
                                 ins=[wbn, "XQ2"], outs=[bn],
                                 start=(dc == 0), stop=(dc == 7))
                    k.ve(lambda h, ic=ic: h.tensor_tensor(
                        COLS2[:, 2:3], SCG[:, ic:ic + 1], COLX[:, 0:1], ALU.mult),
                        ins=["SCG", "COLX"], outs=["COLS2"])
                    k.se(lambda h: h.activation(HSG[:], PB[:, 0:512], ACT.Silu,
                                                scale=COLS2[:, 2:3]),
                         ins=["PB", "COLS2"], outs=["HSG"])
                    k.ve(lambda h, ic=ic: h.tensor_tensor(
                        COLS2[:, 3:4], SCU[:, ic:ic + 1], COLX[:, 0:1], ALU.mult),
                        ins=["SCU", "COLX"], outs=["COLS2"])
                    k.ve(lambda h: h.tensor_tensor(FM2[:], HSG[:], PC[:, 0:512], ALU.mult),
                         ins=["HSG", "PC"], outs=["FM2"])
                    k.ve(lambda h: h.tensor_scalar(FM3[:], FM2[:], COLS2[:, 3:4], None,
                                                   ALU.mult), ins=["FM2", "COLS2"], outs=["FM3"])
                    k.ve(lambda h: h.tensor_copy(HSG[:], FM3[:]), ins=["FM3"], outs=["HSG"])
                    k.dma("ge", ap["h_sp"].rearrange("p (a t) -> p a t", t=TC)
                          [:, ic, t0:t0 + 512], HSG[:], ins=["HSG"], outs=["d:h_sp"])
                    k.ve(lambda h: h.reduce_max(COLS2[:, 5:6], FM3[:], AX.X,
                                                apply_absolute_value=True),
                         ins=["FM3"], outs=["COLS2"])
                    k.ve(lambda h: h.tensor_tensor(COLS2[:, 4:5], COLS2[:, 4:5],
                                                   COLS2[:, 5:6], ALU.max),
                         ins=["COLS2"], outs=["COLS2"])
            absmax_to_scalar(COLS2[:, 4:5], "COLS2", np.s_[0:128])
            k.dma("ge", ap["cc_hm_in"], ROWS[0:1, 0:128], ins=["ROWS"], outs=["d:cc_hm_in"])
            k.cc("AllReduce", ALU.max, ALL8, ap["cc_hm_in"], ap["cc_hm_out"],
                 ins=["d:cc_hm_in"], outs=["d:cc_hm_out"])
            k.dma("sy", ROWS[0:1, 0:128], ap["cc_hm_out"], ins=["d:cc_hm_out"], outs=["ROWS"])
            k.ve(lambda h: h.tensor_tensor(ROWS[0:1, 1:2], ROWS[0:1, 0:1], CLP[0:1, 1:2],
                                           ALU.mult), ins=["ROWS", "CLP"], outs=["ROWS"])
            k.ve(lambda h: h.tensor_scalar_mul(ROWS[0:1, 1:2], ROWS[0:1, 1:2], 1.0 / 127.0),
                 ins=["ROWS"], outs=["ROWS"])
            k.ve(lambda h: h.reciprocal(ROWS[0:1, 2:3], ROWS[0:1, 1:2]),
                 ins=["ROWS"], outs=["ROWS"])
            k.bcast_row(PB[:, 0:1], ROWS[0:1, 1:2], ins=["ROWS"], outs=["PB"])
            k.ve(lambda h: h.tensor_copy(COLX[:, 2:3], PB[:, 0:1]), ins=["PB"], outs=["COLX"])
            k.bcast_row(PB[:, 0:1], ROWS[0:1, 2:3], ins=["ROWS"], outs=["PB"])
            k.ve(lambda h: h.tensor_copy(COLX[:, 3:4], PB[:, 0:1]), ins=["PB"], outs=["COLX"])
            for tc4 in range(4):
                t0 = tc4 * 512
                for ic in range(22):
                    k.dma("sy", HSG[:], ap["h_sp"].rearrange("p (a t) -> p a t", t=TC)
                          [:, ic, t0:t0 + 512], ins=["d:h_sp"], outs=["HSG"])
                    k.ve(lambda h: h.tensor_scalar(FM2[:], HSG[:], COLX[:, 3:4], None,
                                                   ALU.mult), ins=["HSG", "COLX"], outs=["FM2"])
                    k.ve(lambda h: h.tensor_copy(HIQ[:], FM2[:]), ins=["FM2"], outs=["HIQ"])
                    k.ve(lambda h, ic=ic: h.tensor_scalar(
                        HQF[:, ic * 512:(ic + 1) * 512], HIQ[:], -128.0, 127.0,
                        ALU.max, ALU.min), ins=["HIQ"], outs=["HQF"])
                for mc in range(8):
                    for ic in range(22):
                        wbb, wbn = (WB0, "WB0") if ic % 2 == 0 else (WB1, "WB1")
                        k.dma("sy", wbb[:, 0:128],
                              ap["wdq_sp"][ic * 128:(ic + 1) * 128, mc * 128:(mc + 1) * 128],
                              ins=["d:wdq_sp"], outs=[wbn])
                        k.mm(PB[:, 0:512], wbb[:, 0:128],
                             HQF[:, ic * 512:(ic + 1) * 512],
                             ins=[wbn, "HQF"], outs=["PB"],
                             start=(ic == 0), stop=(ic == 21))
                    k.ve(lambda h, mc=mc: h.tensor_tensor(
                        COLS2[:, 2:3], SCD[:, mc:mc + 1], COLX[:, 2:3], ALU.mult),
                        ins=["SCD", "COLX"], outs=["COLS2"])
                    k.se(lambda h, mc=mc: h.activation(
                        SQB[:, mc * 512:(mc + 1) * 512], PB[:, 0:512], ACT.Silu,
                        scale=COLS2[:, 2:3]), ins=["PB", "COLS2"], outs=["SQB"])
                k.dma("ge", ap["g_sp"].rearrange("p (a t) -> p a t", t=TC)[:, :, t0:t0 + 512],
                      SQB[:].rearrange("p (a t) -> p a t", t=512), ins=["SQB"], outs=["d:g_sp"])
        if DBG == "g2":
            dump_bf16_to_xr("g_sp")
        if DBG is None:
            emit_combine()
        p.barrier()
        l2f.close()
        l2.close()


    if NLAYERS >= 4:
        emit_layernorm()
        if DBG == "xn3":
            dump_xnb_to_xr()
        run_g3 = DBG in (None, "g3", "s3")
        run_attn3 = DBG in (None, "s3", "m1", "m1xc", "m1y", "m1z", "m1xzb", "m1xcb", "m1yg", "m1xmr")
        l3 = ExitStack()
        NW3 = k.sbuf("NW3", [128, 16], F32, scope=l3)
        CW3 = k.sbuf("CW3", [128, 128], F32, scope=l3)
        CB3 = k.sbuf("CB3", [128, 32], F32, scope=l3)
        DTB3 = k.sbuf("DTB3", [128, 32], F32, scope=l3)
        AL3 = k.sbuf("AL3", [128, 512], F32, scope=l3)
        DP3 = k.sbuf("DP3", [128, 32], F32, scope=l3)
        B13 = k.sbuf("B13", [128, 32], F32, scope=l3)
        B23 = k.sbuf("B23", [128, 8], F32, scope=l3)
        XT3 = k.sbuf("XT3", [128, 24], F32, scope=l3)
        HINC = k.sbuf("HINC", [128, 256], F32, scope=l3)
        k.dram("xi_sp", [128, 8 * TC])
        k.dram("xz_sp", [128, 16 * TC], BF16)
        k.dram("xc_sp", [128, 16 * TC], BF16)
        k.dram("dc_sp", [128, 16 * TC], BF16)
        k.dram("y_sp", [128, 16 * TC], BF16)
        k.dram("cbc_sp", [128, 4 * 8192], BF16)
        k.dram("dbg_sp", [128, 16 * TC], BF16)
        k.dram("cc_t3_in", [128, 128])
        k.dram("cc_t3_out", [2, 128, 128])
        k.dram("cc_st_in", [128, 256])
        k.dram("cc_st_out", [2, 128, 256])
        for nm, t_ in (("nw3", NW3), ("cw3", CW3), ("cb3", CB3), ("dtb3", DTB3),
                       ("al3", AL3), ("dp3", DP3)):
            src = ap[nm]
            w = src.shape[2]
            k.dma("sy", t_[:, 0:w], src[0], ins=["d:" + nm], outs=[nm.upper()])
            k.dma("sy", t_[:, w:2 * w], src[1], ins=["d:" + nm], outs=[nm.upper()])
        k.se(lambda h: h.activation(AL3[:], AL3[:], ACT.Exp), ins=["AL3"], outs=["AL3"])
        k.ve(lambda h: h.tensor_scalar_mul(AL3[:], AL3[:], -1.0), ins=["AL3"], outs=["AL3"])
        k.dma("sy", B13[:], ap["b13"], ins=["d:b13"], outs=["B13"])
        k.dma("sy", B23[:], ap["b23"], ins=["d:b23"], outs=["B23"])

        if run_g3:
            l3g = ExitStack()
            H13 = k.sbuf("H13", [128, 32 * 512], BF16, scope=l3g)
            for tc4 in range(4):
                t0 = tc4 * 512
                for mf in range(32):
                    wb, wbn, nk = load_w_block("w13", mf * 128, 128, idx=mf)
                    for dc in range(8):
                        k.mm(PB[:, 0:512], wb[:, dc * 128:(dc + 1) * 128],
                             XNB[:, dc * TC + t0: dc * TC + t0 + 512],
                             ins=[wbn, "XNB"], outs=["PB"], start=(dc == 0), stop=(dc == 7))
                    k.se(lambda h, mf=mf: h.activation(
                        H13[:, mf * 512:(mf + 1) * 512], PB[:, 0:512], ACT.Gelu_apprx_tanh,
                        bias=B13[:, mf:mf + 1]), ins=["PB", "B13"], outs=["H13"])
                for mc in range(8):
                    for ff in range(32):
                        wbb, wbn = (WB0, "WB0") if ff % 2 == 0 else (WB1, "WB1")
                        k.dma("ge", wbb[:, 0:128],
                              ap["w23"][ff * 128:(ff + 1) * 128, mc * 128:(mc + 1) * 128],
                              ins=["d:w23"], outs=[wbn])
                        k.mm(PC[:, 0:512], wbb[:, 0:128],
                             H13[:, ff * 512:(ff + 1) * 512],
                             ins=[wbn, "H13"], outs=["PC"], start=(ff == 0), stop=(ff == 31))
                    k.se(lambda h, mc=mc: h.activation(
                        SQB[:, mc * 512:(mc + 1) * 512], PC[:, 0:512], ACT.Silu,
                        bias=B23[:, mc:mc + 1]), ins=["PC", "B23"], outs=["SQB"])
                k.dma("ge", ap["g_sp"].rearrange("p (a t) -> p a t", t=TC)[:, :, t0:t0 + 512],
                      SQB[:].rearrange("p (a t) -> p a t", t=512), ins=["SQB"], outs=["d:g_sp"])
            p.barrier()
            l3g.close()
        if DBG == "g3":
            dump_bf16_to_xr("g_sp")

        if run_attn3:
            for q in range(8):
                for tc4 in range(4):
                    st, stn = (XST, "XST") if tc4 % 2 == 0 else (XST2, "XST2")
                    k.ve(lambda h, st=st, q=q, tc4=tc4: h.tensor_copy(
                        st[:], XNB[:, q * TC + tc4 * 512: q * TC + (tc4 + 1) * 512]),
                        ins=["XNB"], outs=[stn])
                    k.dma("ge", ap["xi_sp"].rearrange("p (a t) -> p a t", t=TC)
                          [:, q, tc4 * 512:(tc4 + 1) * 512], st[:], ins=[stn], outs=["d:xi_sp"])

            TCK = 256
            NCH = TC // TCK
            l3m = ExitStack()
            XMR = k.sbuf("XMR", [128, 8 * TCK], BF16, scope=l3m)
            XSC = k.sbuf("XSC", [128, 8 * TCK], F32, scope=l3m)
            XIC = k.sbuf("XIC", [128, TCK], F32, scope=l3m)
            XZB = k.sbuf("XZB", [128, 16 * TCK], BF16, scope=l3m)
            XCB = k.sbuf("XCB", [128, 16 * TCK], BF16, scope=l3m)
            UALL = k.sbuf("UALL", [128, 16 * TCK], BF16, scope=l3m)
            TAIL = k.sbuf("TAIL", [128, 48], BF16, scope=l3m)
            XZT = k.sbuf("XZT", [128, 48], F32, scope=l3m)
            DBC = k.sbuf("DBC", [128, TCK], F32, scope=l3m)
            DBCB = k.sbuf("DBCB", [128, TCK], BF16, scope=l3m)
            DLT = k.sbuf("DLT", [128, 16 * TCK], BF16, scope=l3m)
            DCC = k.sbuf("DCC", [128, 16 * TCK], BF16, scope=l3m)
            DCAR = k.sbuf("DCAR", [128, 17], F32, scope=l3m)
            BBALL = k.sbuf("BBALL", [128, 16 * TCK], BF16, scope=l3m)
            CCALL = k.sbuf("CCALL", [128, 16 * TCK], BF16, scope=l3m)
            DAT = k.sbuf("DAT", [128, TCK], F32, scope=l3m)
            DAT2 = k.sbuf("DAT2", [128, TCK], F32, scope=l3m)
            BXALL = k.sbuf("BXALL", [128, 16 * TCK], BF16, scope=l3m)
            HSALL = k.sbuf("HSALL", [128, 16 * TCK], F32, scope=l3m)
            DATB = k.sbuf("DATB", [128, TCK], BF16, scope=l3m)
            DATB2 = k.sbuf("DATB2", [128, TCK], BF16, scope=l3m)
            CARRY = k.sbuf("CARRY", [128, 256], F32, scope=l3m)
            YLBB = k.sbuf("YLBB", [128, TCK], BF16, scope=l3m)
            YT = k.sbuf("YT", [128, TCK], F32, scope=l3m)
            ZSL = k.sbuf("ZSL", [128, TCK], F32, scope=l3m)
            YGALL = k.sbuf("YGALL", [128, 16 * TCK], BF16, scope=l3m)

            def mamba_layer(li):
                c16 = li * 16
                for dc in range(8):
                    k.dma("sy", XST2[:, dc * 3:dc * 3 + 3],
                          ap["xi_sp"].rearrange("p (a t) -> p a t", t=TC)[:, dc, 2045:2048],
                          ins=["d:xi_sp"], outs=["XST2"])
                k.ve(lambda h: h.memset(XST2[:, 24:128], 0.0), outs=["XST2"])
                k.dma("ge", ap["cc_t3_in"], XST2[:, 0:128], ins=["XST2"], outs=["d:cc_t3_in"])
                k.cc("AllGather", ALU.bypass, PAIRS, ap["cc_t3_in"], ap["cc_t3_out"],
                     ins=["d:cc_t3_in"], outs=["d:cc_t3_out"])
                k.dma("sy", XT3[:, 0:24], ap["cc_t3_out"][0][:, 0:24],
                      ins=["d:cc_t3_out"], outs=["XT3"])
                k.ve(lambda h: h.tensor_scalar(XT3[:], XT3[:], HSEL[:, 0:1], None, ALU.mult),
                     ins=["XT3", "HSEL"], outs=["XT3"])
                for tc4 in range(NCH):
                    t0 = tc4 * TCK
                    for dc in range(8):
                        k.dma("sy", XIC[:], ap["xi_sp"].rearrange("p (a t) -> p a t", t=TC)
                              [:, dc, t0:t0 + TCK], ins=["d:xi_sp"], outs=["XIC"])
                        k.ve(lambda h: h.tensor_tensor(XST[:, 0:TCK], XIC[:], XIC[:],
                                                       ALU.mult), ins=["XIC"], outs=["XST"])
                        k.ve(lambda h: h.tensor_copy(SQB[:, 0:TCK], XST[:, 0:TCK]),
                             ins=["XST"], outs=["SQB"])
                        k.mm(PB[0:1, 0:TCK], ONES[:], SQB[:, 0:TCK],
                             ins=["ONESB", "SQB"], outs=["PB"],
                             start=(dc == 0), stop=(dc == 7))
                        k.ve(lambda h, dc=dc: h.tensor_scalar(
                            XSC[:, dc * TCK:(dc + 1) * TCK], XIC[:],
                            NW3[:, li * 8 + dc:li * 8 + dc + 1],
                            None, ALU.mult), ins=["XIC", "NW3"], outs=["XSC"])
                    k.ve(lambda h: h.tensor_scalar(ROWS[0:1, 0:TCK], PB[0:1, 0:TCK],
                                                   1.0 / D, 1e-5, ALU.mult, ALU.add),
                         ins=["PB"], outs=["ROWS"])
                    k.ve(lambda h: h.reciprocal(ROWS[0:1, 0:TCK], ROWS[0:1, 0:TCK]),
                         ins=["ROWS"], outs=["ROWS"])
                    k.se(lambda h: h.activation(ROWS[0:1, 0:TCK], ROWS[0:1, 0:TCK], ACT.Sqrt),
                         ins=["ROWS"], outs=["ROWS"])
                    k.bcast_row(PC[:, 0:TCK], ROWS[0:1, 0:TCK], ins=["ROWS"], outs=["PC"])
                    for dc in range(8):
                        k.ve(lambda h, dc=dc: h.tensor_tensor(
                            XMR[:, dc * TCK:(dc + 1) * TCK], XSC[:, dc * TCK:(dc + 1) * TCK],
                            PC[:, 0:TCK], ALU.mult), ins=["XSC", "PC"], outs=["XMR"])
                    if DBG == "m1xmr":
                        k.dma("ge", ap["dbg_sp"].rearrange("p (a t) -> p a t", t=TC)
                              [:, 0:8, t0:t0 + TCK], XMR[:].rearrange("p (a t) -> p a t", t=TCK),
                              ins=["XMR"], outs=["d:dbg_sp"])
                    if tc4 == 0:
                        for dc in range(8):
                            k.ve(lambda h, dc=dc: h.tensor_tensor(
                                XST[:, dc * 3:dc * 3 + 3], XT3[:, dc * 3:dc * 3 + 3],
                                XT3[:, dc * 3:dc * 3 + 3], ALU.mult),
                                ins=["XT3"], outs=["XST"])
                        k.ve(lambda h: h.tensor_copy(SQB[:, 0:24], XST[:, 0:24]),
                             ins=["XST"], outs=["SQB"])
                        for dc in range(8):
                            k.mm(PD[0:1, dc * 3:dc * 3 + 3] if False else PD[0:1, 0:3],
                                 ONES[:], SQB[:, dc * 3:dc * 3 + 3],
                                 ins=["ONESB", "SQB"], outs=["PD"],
                                 start=(dc == 0), stop=(dc == 7))
                        k.ve(lambda h: h.tensor_scalar(ROWS[0:1, TCK:TCK + 3], PD[0:1, 0:3],
                                                       1.0 / D, 1e-5, ALU.mult, ALU.add),
                             ins=["PD"], outs=["ROWS"])
                        k.ve(lambda h: h.reciprocal(ROWS[0:1, TCK:TCK + 3], ROWS[0:1, TCK:TCK + 3]),
                             ins=["ROWS"], outs=["ROWS"])
                        k.se(lambda h: h.activation(ROWS[0:1, TCK:TCK + 3], ROWS[0:1, TCK:TCK + 3],
                                                    ACT.Sqrt), ins=["ROWS"], outs=["ROWS"])
                        k.bcast_row(PD[:, 4:7], ROWS[0:1, TCK:TCK + 3], ins=["ROWS"], outs=["PD"])
                        for dc in range(8):
                            k.p.emit("ve", lambda h, dc=dc: h.scalar_tensor_tensor(
                                XST2[:, dc * 3:dc * 3 + 3], XT3[:, dc * 3:dc * 3 + 3],
                                NW3[:, li * 8 + dc:li * 8 + dc + 1], PD[:, 4:7],
                                ALU.mult, ALU.mult), ins=["XT3", "NW3", "PD"], outs=["XST2"])
                        k.ve(lambda h: h.tensor_copy(GST[:, 0:24], XST2[:, 0:24]),
                             ins=["XST2"], outs=["GST"])
                    for mz in range(32):
                        wb, wbn, nk = load_w_block("inw3_l%d" % li, mz * 128, 128, idx=mz)
                        for dc in range(8):
                            k.mm(PB[:, 0:TCK], wb[:, dc * 128:(dc + 1) * 128],
                                 XMR[:, dc * TCK:(dc + 1) * TCK],
                                 ins=[wbn, "XMR"], outs=["PB"], start=(dc == 0), stop=(dc == 7))
                        if mz < 16:
                            k.ve(lambda h, mz=mz: h.tensor_copy(
                                XZB[:, mz * TCK:(mz + 1) * TCK], PB[:, 0:TCK]),
                                ins=["PB"], outs=["XZB"])
                        else:
                            k.ve(lambda h: h.tensor_copy(DBCB[:, 0:TCK], PB[:, 0:TCK]),
                                 ins=["PB"], outs=["DBCB"])
                            k.dma("sy", ap["xz_sp"].rearrange("p (a t) -> p a t", t=TC)
                                  [:, mz - 16, t0:t0 + TCK], DBCB[:, 0:TCK],
                                  ins=["DBCB"], outs=["d:xz_sp"])
                        if tc4 == 0 and mz < 16:
                            for dc in range(8):
                                k.mm(PC[:, 0:3], wb[:, dc * 128:(dc + 1) * 128],
                                     GST[:, dc * 3:dc * 3 + 3],
                                     ins=[wbn, "GST"], outs=["PC"],
                                     start=(dc == 0), stop=(dc == 7))
                            k.ve(lambda h, mz=mz: h.tensor_copy(
                                XZT[:, mz * 3:(mz + 1) * 3], PC[:, 0:3]),
                                ins=["PC"], outs=["XZT"])
                    if tc4 == 0:
                        k.ve(lambda h: h.tensor_scalar(TAIL[:], XZT[:], HSEL[:, 0:1],
                                                       None, ALU.mult),
                             ins=["XZT", "HSEL"], outs=["TAIL"])
                    for eh in range(16):
                        sl = np.s_[:, eh * TCK:(eh + 1) * TCK]
                        cwc = lambda kk, eh=eh: CW3[:, li * 64 + eh * 4 + kk: li * 64 + eh * 4 + kk + 1]
                        k.ve(lambda h, eh=eh, cwc=cwc: h.tensor_scalar(
                            XST[:, 3:TCK], XZB[:, eh * TCK: eh * TCK + TCK - 3], cwc(0),
                            None, ALU.mult), ins=["XZB", "CW3"], outs=["XST"])
                        k.ve(lambda h, eh=eh, cwc=cwc: h.tensor_scalar(
                            XST[:, 0:3], TAIL[:, eh * 3:(eh + 1) * 3], cwc(0),
                            None, ALU.mult), ins=["TAIL", "CW3"], outs=["XST"])
                        for kk in (1, 2):
                            k.p.emit("ve", lambda h, eh=eh, kk=kk, cwc=cwc: h.scalar_tensor_tensor(
                                XST[:, kk:TCK], XZB[:, eh * TCK: eh * TCK + TCK - kk],
                                cwc(3 - kk), XST[:, kk:TCK], ALU.mult, ALU.add),
                                ins=["XZB", "CW3", "XST"], outs=["XST"])
                            k.p.emit("ve", lambda h, eh=eh, kk=kk, cwc=cwc: h.scalar_tensor_tensor(
                                XST[:, 0:kk], TAIL[:, eh * 3 + 3 - kk: eh * 3 + 3], cwc(3 - kk),
                                XST[:, 0:kk], ALU.mult, ALU.add),
                                ins=["TAIL", "CW3", "XST"], outs=["XST"])
                        k.p.emit("ve", lambda h, sl=sl, cwc=cwc: h.scalar_tensor_tensor(
                            XST[:, 0:TCK], XZB[sl], cwc(3), XST[:, 0:TCK], ALU.mult, ALU.add),
                            ins=["XZB", "CW3", "XST"], outs=["XST"])
                        k.se(lambda h, sl=sl, eh=eh: h.activation(
                            XCB[sl], XST[:, 0:TCK], ACT.Silu,
                            bias=CB3[:, c16 + eh:c16 + eh + 1]), ins=["XST", "CB3"], outs=["XCB"])
                        k.ve(lambda h, eh=eh: h.tensor_copy(
                            TAIL[:, eh * 3:(eh + 1) * 3],
                            XZB[:, eh * TCK + TCK - 3: eh * TCK + TCK]), ins=["XZB"], outs=["TAIL"])
                    k.dma("ge", ap["xc_sp"].rearrange("p (a t) -> p a t", t=TC)
                          [:, :, t0:t0 + TCK], XCB[:].rearrange("p (a t) -> p a t", t=TCK),
                          ins=["XCB"], outs=["d:xc_sp"])
                    if DBG == "m1xzb":
                        k.dma("ge", ap["y_sp"].rearrange("p (a t) -> p a t", t=TC)
                              [:, :, t0:t0 + TCK], XZB[:].rearrange("p (a t) -> p a t", t=TCK),
                              ins=["XZB"], outs=["d:y_sp"])
                    if DBG == "m1xcb":
                        k.dma("ge", ap["y_sp"].rearrange("p (a t) -> p a t", t=TC)
                              [:, :, t0:t0 + TCK], XCB[:].rearrange("p (a t) -> p a t", t=TCK),
                              ins=["XCB"], outs=["d:y_sp"])
                    for eh in range(16):
                        wbb, wbn = (WB0, "WB0") if eh % 2 == 0 else (WB1, "WB1")
                        k.dma("ge", wbb[:, 0:96],
                              ap["xpw3_l%d" % li][eh * 128:(eh + 1) * 128, :],
                              ins=["d:xpw3_l%d" % li], outs=[wbn])
                        k.mm(PD[0:96, 0:TCK], wbb[:, 0:96], XCB[:, eh * TCK:(eh + 1) * TCK],
                             ins=[wbn, "XCB"], outs=["PD"], start=(eh == 0), stop=(eh == 15))
                    k.ve(lambda h: h.tensor_copy(DBC[0:96, 0:TCK], PD[0:96, 0:TCK]),
                         ins=["PD"], outs=["DBC"])
                    k.ve(lambda h: h.tensor_copy(DBCB[0:64, 0:TCK], DBC[0:64, 0:TCK]),
                         ins=["DBC"], outs=["DBCB"])
                    for eh in range(16):
                        wbb, wbn = (WB0, "WB0") if eh % 2 == 0 else (WB1, "WB1")
                        k.dma("ge", wbb[0:64, 0:128],
                              ap["dtw3_l%d" % li][:, eh * 128:(eh + 1) * 128],
                              ins=["d:dtw3_l%d" % li], outs=[wbn])
                        k.mm(PC[:, 0:TCK], wbb[0:64, 0:128], DBCB[0:64, 0:TCK],
                             ins=[wbn, "DBCB"], outs=["PC"])
                        k.ve(lambda h, eh=eh: h.tensor_scalar(
                            XST[:, 0:TCK], PC[:, 0:TCK], DTB3[:, c16 + eh:c16 + eh + 1],
                            None, ALU.add), ins=["PC", "DTB3"], outs=["XST"])
                        k.se(lambda h: h.activation(ZSL[:], XST[:, 0:TCK], ACT.Abs),
                             ins=["XST"], outs=["ZSL"])
                        k.se(lambda h: h.activation(ZSL[:], ZSL[:], ACT.Exp, scale=-1.0),
                             ins=["ZSL"], outs=["ZSL"])
                        k.ve(lambda h: h.tensor_scalar_add(ZSL[:], ZSL[:], 1.0),
                             ins=["ZSL"], outs=["ZSL"])
                        k.se(lambda h: h.activation(ZSL[:], ZSL[:], ACT.Ln),
                             ins=["ZSL"], outs=["ZSL"])
                        k.p.emit("ve", lambda h, eh=eh: h.scalar_tensor_tensor(
                            DLT[:, eh * TCK:(eh + 1) * TCK], XST[:, 0:TCK], 0.0, ZSL[:],
                            ALU.max, ALU.add), ins=["XST", "ZSL"], outs=["DLT"])
                        k.ve(lambda h, eh=eh: h.tensor_tensor(
                            UALL[:, eh * TCK:(eh + 1) * TCK], DLT[:, eh * TCK:(eh + 1) * TCK],
                            XCB[:, eh * TCK:(eh + 1) * TCK], ALU.mult),
                            ins=["DLT", "XCB"], outs=["UALL"])
                        if tc4 == 0:
                            k.ve(lambda h, eh=eh: h.memset(DCAR[:, eh:eh + 1], 0.0),
                                 outs=["DCAR"])
                        k.p.emit("ve", lambda h, eh=eh: h.tensor_tensor_scan(
                            DCC[:, eh * TCK:(eh + 1) * TCK], DLT[:, eh * TCK:(eh + 1) * TCK],
                            DLT[:, eh * TCK:(eh + 1) * TCK], DCAR[:, eh:eh + 1],
                            ALU.add, ALU.bypass), ins=["DLT", "DCAR"], outs=["DCC"])
                        k.ve(lambda h, eh=eh: h.reduce_sum(
                            DCAR[:, 16:17], DLT[:, eh * TCK:(eh + 1) * TCK], AX.X),
                            ins=["DLT"], outs=["DCAR"])
                        k.ve(lambda h, eh=eh: h.tensor_tensor(
                            DCAR[:, eh:eh + 1], DCAR[:, eh:eh + 1], DCAR[:, 16:17], ALU.add),
                            ins=["DCAR"], outs=["DCAR"])
                    k.dma("ge", ap["dc_sp"].rearrange("p (a t) -> p a t", t=TC)
                          [:, :, t0:t0 + TCK], DCC[:].rearrange("p (a t) -> p a t", t=TCK),
                          ins=["DCC"], outs=["d:dc_sp"])
                    for n in range(16):
                        for src_row, dst, dstn in ((64 + n, BBALL, "BBALL"),
                                                   (80 + n, CCALL, "CCALL")):
                            k.dma("ge", ROWS[0:1, 0:TCK], DBC[src_row:src_row + 1, 0:TCK],
                                  ins=["DBC"], outs=["ROWS"])
                            k.bcast_row(PB[:, 0:TCK], ROWS[0:1, 0:TCK],
                                        ins=["ROWS"], outs=["PB"])
                            k.ve(lambda h, n=n, dst=dst: h.tensor_copy(
                                dst[:, n * TCK:(n + 1) * TCK], PB[:, 0:TCK]),
                                ins=["PB"], outs=[dstn])
                    k.dma("ge", ap["cbc_sp"][:, tc4 * 16 * TCK:(tc4 + 1) * 16 * TCK], CCALL[:],
                          ins=["CCALL"], outs=["d:cbc_sp"])
                    if tc4 == 0:
                        k.ve(lambda h: h.memset(CARRY[:], 0.0), outs=["CARRY"])
                    for eh in range(16):
                        k.ve(lambda h, eh=eh: h.tensor_tensor(
                            BXALL[:].rearrange("p (n t) -> p n t", t=TCK),
                            UALL[:, eh * TCK:(eh + 1) * TCK]
                                .rearrange("p (o t) -> p o t", o=1)
                                .broadcast_to([128, 16, TCK]),
                            BBALL[:].rearrange("p (n t) -> p n t", t=TCK), ALU.mult),
                            ins=["UALL", "BBALL"], outs=["BXALL"])
                        for n in range(16):
                            ac = li * 256 + eh * 16 + n
                            cidx = eh * 16 + n
                            da, dan = (DAT, "DAT") if n % 2 == 0 else (DAT2, "DAT2")
                            k.se(lambda h, eh=eh, ac=ac, da=da: h.activation(
                                da[:], DLT[:, eh * TCK:(eh + 1) * TCK], ACT.Exp,
                                scale=AL3[:, ac:ac + 1]), ins=["DLT", "AL3"], outs=[dan])
                            k.p.emit("ve", lambda h, cidx=cidx, n=n, da=da: h.tensor_tensor_scan(
                                HSALL[:, n * TCK:(n + 1) * TCK], da[:],
                                BXALL[:, n * TCK:(n + 1) * TCK], CARRY[:, cidx:cidx + 1],
                                ALU.mult, ALU.add), ins=[dan, "BXALL", "CARRY"], outs=["HSALL"])
                            k.ve(lambda h, cidx=cidx, n=n: h.tensor_copy(
                                CARRY[:, cidx:cidx + 1],
                                HSALL[:, n * TCK + TCK - 1:n * TCK + TCK]),
                                ins=["HSALL"], outs=["CARRY"])
                        k.ve(lambda h: h.tensor_tensor(HSALL[:], HSALL[:], CCALL[:], ALU.mult),
                             ins=["HSALL", "CCALL"], outs=["HSALL"])
                        for lvl in (8, 4, 2, 1):
                            k.ve(lambda h, lvl=lvl: h.tensor_tensor(
                                HSALL[:, 0:lvl * TCK], HSALL[:, 0:lvl * TCK],
                                HSALL[:, lvl * TCK:2 * lvl * TCK], ALU.add),
                                ins=["HSALL"], outs=["HSALL"])
                        k.ve(lambda h: h.tensor_copy(YLBB[:], HSALL[:, 0:TCK]),
                             ins=["HSALL"], outs=["YLBB"])
                        if DBG not in ("m1xzb", "m1xcb"):
                            k.dma("ge", ap["y_sp"].rearrange("p (a t) -> p a t", t=TC)
                                  [:, eh, t0:t0 + TCK], YLBB[:], ins=["YLBB"], outs=["d:y_sp"])
                k.dma("ge", ap["cc_st_in"], CARRY[:], ins=["CARRY"], outs=["d:cc_st_in"])
                k.cc("AllGather", ALU.bypass, PAIRS, ap["cc_st_in"], ap["cc_st_out"],
                     ins=["d:cc_st_in"], outs=["d:cc_st_out"])
                k.dma("sy", HINC[:], ap["cc_st_out"][0], ins=["d:cc_st_out"], outs=["HINC"])
                k.ve(lambda h: h.tensor_scalar(HINC[:], HINC[:], HSEL[:, 0:1], None, ALU.mult),
                     ins=["HINC", "HSEL"], outs=["HINC"])
                for tc4 in range(NCH):
                    t0 = tc4 * TCK
                    k.dma("sy", CCALL[:], ap["cbc_sp"][:, tc4 * 16 * TCK:(tc4 + 1) * 16 * TCK],
                          ins=["d:cbc_sp"], outs=["CCALL"])
                    k.dma("sy", DCC[:].rearrange("p (a t) -> p a t", t=TCK),
                          ap["dc_sp"].rearrange("p (a t) -> p a t", t=TC)[:, :, t0:t0 + TCK],
                          ins=["d:dc_sp"], outs=["DCC"])
                    k.dma("sy", XCB[:].rearrange("p (a t) -> p a t", t=TCK),
                          ap["xc_sp"].rearrange("p (a t) -> p a t", t=TC)[:, :, t0:t0 + TCK],
                          ins=["d:xc_sp"], outs=["XCB"])
                    k.dma("sy", XZB[:].rearrange("p (a t) -> p a t", t=TCK),
                          ap["xz_sp"].rearrange("p (a t) -> p a t", t=TC)[:, :, t0:t0 + TCK],
                          ins=["d:xz_sp"], outs=["XZB"])
                    for eh in range(16):
                        for n in range(16):
                            ac = li * 256 + eh * 16 + n
                            cidx = eh * 16 + n
                            da, dan = (DATB, "DATB") if n % 2 == 0 else (DATB2, "DATB2")
                            k.se(lambda h, eh=eh, ac=ac, da=da: h.activation(
                                da[:], DCC[:, eh * TCK:(eh + 1) * TCK], ACT.Exp,
                                scale=AL3[:, ac:ac + 1]), ins=["DCC", "AL3"], outs=[dan])
                            k.p.emit("ve", lambda h, cidx=cidx, n=n, da=da: h.scalar_tensor_tensor(
                                BXALL[:, n * TCK:(n + 1) * TCK],
                                da[:], HINC[:, cidx:cidx + 1],
                                CCALL[:, n * TCK:(n + 1) * TCK], ALU.mult, ALU.mult),
                                ins=[dan, "HINC", "CCALL"], outs=["BXALL"])
                        k.ve(lambda h: h.tensor_tensor(
                            HSALL[:, 0:8 * TCK], BXALL[:, 0:8 * TCK],
                            BXALL[:, 8 * TCK:16 * TCK], ALU.add),
                            ins=["BXALL"], outs=["HSALL"])
                        for lvl in (4, 2, 1):
                            k.ve(lambda h, lvl=lvl: h.tensor_tensor(
                                HSALL[:, 0:lvl * TCK], HSALL[:, 0:lvl * TCK],
                                HSALL[:, lvl * TCK:2 * lvl * TCK], ALU.add),
                                ins=["HSALL"], outs=["HSALL"])
                        k.dma("ge", XST[:, 0:TCK],
                              ap["y_sp"].rearrange("p (a t) -> p a t", t=TC)
                              [:, eh, t0:t0 + TCK], ins=["d:y_sp"], outs=["XST"])
                        k.ve(lambda h: h.tensor_tensor(YT[:], XST[:, 0:TCK], HSALL[:, 0:TCK],
                                                       ALU.add),
                             ins=["XST", "HSALL"], outs=["YT"])
                        k.p.emit("ve", lambda h, eh=eh: h.scalar_tensor_tensor(
                            YT[:], XCB[:, eh * TCK:(eh + 1) * TCK],
                            DP3[:, c16 + eh:c16 + eh + 1], YT[:], ALU.mult, ALU.add),
                            ins=["XCB", "DP3", "YT"], outs=["YT"])
                        k.se(lambda h, eh=eh: h.activation(
                            ZSL[:], XZB[:, eh * TCK:(eh + 1) * TCK], ACT.Silu),
                            ins=["XZB"], outs=["ZSL"])
                        k.ve(lambda h, eh=eh: h.tensor_tensor(
                            YGALL[:, eh * TCK:(eh + 1) * TCK], YT[:], ZSL[:], ALU.mult),
                            ins=["YT", "ZSL"], outs=["YGALL"])
                    if DBG == "m1yg":
                        k.dma("ge", ap["dbg_sp"].rearrange("p (a t) -> p a t", t=TC)
                              [:, :, t0:t0 + TCK], YGALL[:].rearrange("p (a t) -> p a t", t=TCK),
                              ins=["YGALL"], outs=["d:dbg_sp"])
                    for mc in range(8):
                        for eh in range(16):
                            wbb, wbn = (WB0, "WB0") if eh % 2 == 0 else (WB1, "WB1")
                            k.dma("ge", wbb[:, 0:128],
                                  ap["opw3_l%d" % li][eh * 128:(eh + 1) * 128,
                                                      mc * 128:(mc + 1) * 128],
                                  ins=["d:opw3_l%d" % li], outs=[wbn])
                            k.mm(PB[:, 0:TCK], wbb[:, 0:128],
                                 YGALL[:, eh * TCK:(eh + 1) * TCK],
                                 ins=[wbn, "YGALL"], outs=["PB"],
                                 start=(eh == 0), stop=(eh == 15))
                        k.dma("sy", XIC[:], ap["xi_sp"].rearrange("p (a t) -> p a t", t=TC)
                              [:, mc, t0:t0 + TCK], ins=["d:xi_sp"], outs=["XIC"])
                        k.ve(lambda h: h.tensor_tensor(XIC[:], XIC[:], PB[:, 0:TCK], ALU.add),
                             ins=["PB", "XIC"], outs=["XIC"])
                        k.dma("ge", ap["xi_sp"].rearrange("p (a t) -> p a t", t=TC)
                              [:, mc, t0:t0 + TCK], XIC[:], ins=["XIC"], outs=["d:xi_sp"])

            for li in range(1 if DBG in ("m1", "m1xc", "m1y", "m1z", "m1xzb", "m1xcb", "m1yg", "m1xmr") else 2):
                mamba_layer(li)
            p.barrier()
            l3m.close()
        if DBG in ("m1xc", "m1y", "m1z", "m1xzb", "m1xcb", "m1yg", "m1xmr"):
            src = {"m1xc": "xc_sp", "m1y": "y_sp", "m1z": "xz_sp", "m1xzb": "y_sp", "m1xcb": "y_sp", "m1yg": "dbg_sp", "m1xmr": "dbg_sp"}[DBG]
            for dc in range(8):
                for q in range(4):
                    st, stn = (XST, "XST") if q % 2 == 0 else (XST2, "XST2")
                    k.dma("ge", st[:], ap[src].rearrange("p (a t) -> p a t", t=TC)
                          [:, dc, q * 512:(q + 1) * 512], ins=["d:" + src], outs=[stn])
                    k.dma("ge", ap["xr"].rearrange("p (a t) -> p a t", t=TC)
                          [:, dc, q * 512:(q + 1) * 512], st[:], ins=[stn], outs=["d:xr"])
        if DBG == "m1":
            for dc in range(8):
                for q in range(4):
                    st, stn = (XST, "XST") if q % 2 == 0 else (XST2, "XST2")
                    k.dma("sy", st[:], ap["xi_sp"].rearrange("p (a t) -> p a t", t=TC)
                          [:, dc, q * 512:(q + 1) * 512], ins=["d:xi_sp"], outs=[stn])
                    k.dma("ge", ap["xr"].rearrange("p (a t) -> p a t", t=TC)
                          [:, dc, q * 512:(q + 1) * 512], st[:], ins=[stn], outs=["d:xr"])
        if DBG == "s3":
            for dc in range(8):
                for q in range(4):
                    st, stn = (XST, "XST") if q % 2 == 0 else (XST2, "XST2")
                    k.dma("sy", st[:], ap["xi_sp"].rearrange("p (a t) -> p a t", t=TC)
                          [:, dc, q * 512:(q + 1) * 512], ins=["d:xi_sp"], outs=[stn])
                    k.dma("ge", ap["xr"].rearrange("p (a t) -> p a t", t=TC)
                          [:, dc, q * 512:(q + 1) * 512], st[:], ins=[stn], outs=["d:xr"])
        if DBG is None:
            for dc in range(8):
                k.dma("sy", GST[:], ap["g_sp"].rearrange("p (a t) -> p a t", t=TC)[:, dc],
                      ins=["d:g_sp"], outs=["GST"])
                for q in range(4):
                    st, stn = (XST, "XST") if q % 2 == 0 else (XST2, "XST2")
                    k.dma("sy", st[:], ap["xi_sp"].rearrange("p (a t) -> p a t", t=TC)
                          [:, dc, q * 512:(q + 1) * 512], ins=["d:xi_sp"], outs=[stn])
                    k.ve(lambda h, st=st, q=q: h.tensor_tensor(
                        PRODB[:, 0:512], st[:], GST[:, q * 512:(q + 1) * 512], ALU.mult),
                        ins=[stn, "GST"], outs=["PRODB"])
                    st2, stn2 = (XST2, "XST2") if q % 2 == 0 else (XST, "XST")
                    k.dma("sy", st2[:], ap["xr"].rearrange("p (a t) -> p a t", t=TC)
                          [:, dc, q * 512:(q + 1) * 512], ins=["d:xr"], outs=[stn2])
                    k.ve(lambda h, st2=st2: h.tensor_tensor(
                        st2[:], st2[:], PRODB[:, 0:512], ALU.add),
                        ins=[stn2, "PRODB"], outs=[stn2])
                    k.dma("ge", ap["xr"].rearrange("p (a t) -> p a t", t=TC)
                          [:, dc, q * 512:(q + 1) * 512], st2[:], ins=[stn2], outs=["d:xr"])
        p.barrier()
        l3.close()

    p.barrier()
    for tt in range(16):
        tok, tokn = (TOK, "TOK") if tt % 2 == 0 else (TOK2, "TOK2")
        st, stn = (XST, "XST") if tt % 2 == 0 else (XST2, "XST2")
        for dc in range(8):
            k.dma("sy", st[:, 0:128],
                  ap["xr"].rearrange("p (a t) -> p a t", t=TC)
                  [:, dc, tt * 128:(tt + 1) * 128], ins=["d:xr"], outs=[stn])
            bank, bn = (PB, "PB") if dc % 2 == 0 else (PC, "PC")
            k.tr(bank[:, 0:128], st[:, 0:128], ins=[stn], outs=[bn])
            k.ve(lambda h, tok=tok, bank=bank, dc=dc: h.tensor_copy(
                tok[:, dc * 128:(dc + 1) * 128], bank[:, 0:128]), ins=[bn], outs=[tokn])
        k.dma("ge", ap["y"][tt * 128:(tt + 1) * 128, :], tok[:], ins=[tokn], outs=["d:y"])
    p.finish()

    with nc.Block() as block:
        @block.tensor
        def _(h):
            for t in p.thunks["te"]:
                t(h)

        @block.vector
        def _(h):
            for t in p.thunks["ve"]:
                t(h)

        @block.scalar
        def _(h):
            for t in p.thunks["se"]:
                t(h)

        @block.gpsimd
        def _(h):
            for t in p.thunks["ge"]:
                t(h)

        @block.sync
        def _(h):
            for t in p.thunks["sy"]:
                t(h)
    return nc


_CACHE = {}


def _get_nc():
    if NLAYERS not in _CACHE:
        stk = ExitStack()
        nc = bass.Bass("TRN2", target_bir_lowering=False, debug=False)
        _build(nc, stk)
        _CACHE[NLAYERS] = nc
    return _CACHE[NLAYERS]


def kernel(**inputs):
    per_core = _host_prep(inputs)
    nc = _get_nc()
    res = run_bass_kernel_spmd(nc, per_core, core_ids=list(range(NCORES)))
    out = np.empty((4, 4096, 1024), np.float32)
    for c in range(NCORES):
        b, h = c // 2, c % 2
        out[b, h * TC:(h + 1) * TC, :] = res.results[c]["y"]
    return out


# version 08
# speedup vs baseline: 1.2668x; 1.0301x over previous
import traceback
import numpy as np
from contextlib import ExitStack

import concourse.bass as bass
import concourse.mybir as mybir
from concourse.bass_utils import run_bass_kernel_spmd

F32 = mybir.dt.float32
BF16 = mybir.dt.bfloat16
I32 = mybir.dt.int32
ALU = mybir.AluOpType
ACT = mybir.ActivationFunctionType
AX = mybir.AxisListType

NCORES = 8
TC = 2048
D = 1024
NLAYERS = 4
SKIP_G = False
DBG = None

ENGINES = ("te", "ve", "se", "ge", "sy")
RQ = 8
QSEMS = tuple(f"{e}q{r}" for e in ENGINES for r in range(RQ))
ALL_SEMS = ENGINES + QSEMS + ("cc",)


class Prog:
    def __init__(self, nc):
        self.nc = nc
        self.thunks = {e: [] for e in ENGINES}
        self.counts = {s: 0 for s in ALL_SEMS}
        self.clock = {e: {s: 0 for s in ALL_SEMS} for e in ENGINES}
        self.dirty = {e: set() for e in ENGINES}
        self.qseq = {e: 0 for e in ENGINES}
        self.bufs = {}
        self.sems = {}

    def reg(self, name, ready=False):
        self.bufs[name] = {"w": ("sy", 0) if ready else None, "r": []}
        return name

    def emit(self, eng, fn, ins=(), outs=(), dma=False, cc=False):
        ins = [i for i in ins if i]
        outs = [o for o in outs if o]
        waits = {}
        drain = False

        def need(src, val):
            nonlocal drain
            if src == eng:
                return
            if self.clock[eng].get(src, 0) < val:
                waits[src] = max(waits.get(src, 0), val)

        for b in ins:
            st = self.bufs[b]
            if st["w"]:
                src, val = st["w"]
                if src == eng and b in self.dirty[eng]:
                    drain = True
                need(src, val)
        for b in outs:
            st = self.bufs[b]
            for src, val in ([st["w"]] if st["w"] else []) + st["r"]:
                if src == eng and b in self.dirty[eng]:
                    drain = True
                need(src, val)
        if cc:
            sem_key, step = "cc", 1
        elif dma:
            i = self.qseq[eng]
            self.qseq[eng] += 1
            sem_key, step = f"{eng}q{i % RQ}", 16
            if i >= RQ:
                gen_prev = 16 * (i // RQ)
                if self.clock[eng].get(sem_key, 0) < gen_prev:
                    waits[sem_key] = max(waits.get(sem_key, 0), gen_prev)
        else:
            sem_key, step = eng, 1
        self.counts[sem_key] += step
        val_after = self.counts[sem_key]
        waits_l = sorted(waits.items())
        for src, val in waits_l:
            self.clock[eng][src] = val
        sems = self.sems

        tb = traceback.extract_stack(limit=8)

        def thunk(h, waits_l=waits_l, drain=drain, fn=fn, sem_key=sem_key, step=step):
            for src, val in waits_l:
                h.wait_ge(sems[src], val)
            if drain:
                h.drain()
            try:
                fn(h).then_inc(sems[sem_key], step)
            except Exception:
                print("EMIT SITE:")
                for fr in tb:
                    print("  ", fr.filename, fr.lineno, fr.line)
                raise

        self.thunks[eng].append(thunk)
        if drain:
            self.dirty[eng].clear()
        for b in ins:
            self.bufs[b]["r"].append((sem_key, val_after))
            self.dirty[eng].add(b)
        for b in outs:
            self.bufs[b]["w"] = (sem_key, val_after)
            self.bufs[b]["r"] = []
            self.dirty[eng].add(b)
        return val_after

    def barrier(self):
        snapq = dict(self.counts)
        for e in ENGINES:
            self.counts[e] += 1
        snape = dict(self.counts)
        sems = self.sems
        for e in ENGINES:
            def thunk(h, e=e):
                try:
                    h.drain()
                except Exception:
                    pass
                h.sem_inc(sems[e], 1)
                for s in ALL_SEMS:
                    tgt = snape[s] if s in ENGINES else snapq[s]
                    if tgt > 0 and s != e:
                        h.wait_ge(sems[s], tgt)
            self.thunks[e].append(thunk)
        for e in ENGINES:
            for s in ALL_SEMS:
                self.clock[e][s] = max(self.clock[e][s],
                                       snape[s] if s in ENGINES else snapq[s])
            self.dirty[e].clear()
        for b in self.bufs.values():
            if b["w"] is not None:
                b["r"] = []

    def finish(self):
        snap = dict(self.counts)
        sems = self.sems

        def thunk(h):
            for s in ALL_SEMS:
                if snap[s] > 0:
                    h.wait_ge(sems[s], snap[s])
        self.thunks["sy"].append(thunk)


def _col(v, rows=128):
    v = np.ascontiguousarray(np.asarray(v, np.float32))
    return np.ascontiguousarray(v.reshape(v.shape[0] // rows, rows).T)


def _host_prep(inputs):
    x_full = np.asarray(inputs["L0_attn_x"], np.float32)
    per_core = []
    eye = np.eye(128, dtype=np.float32)
    tri = np.triu(np.ones((128, 128), np.float32))
    inv_freq = 1.0 / 10000.0 ** (np.arange(0, 64, 2, dtype=np.float64) / 64.0)
    for c in range(NCORES):
        b, h = c // 2, c % 2
        m = {"x": np.ascontiguousarray(x_full[b, h * TC:(h + 1) * TC, :]),
             "eye": eye, "tri": tri,
             "hsel": np.full((128, 1), float(h), np.float32)}
        pos = np.arange(h * TC, (h + 1) * TC, dtype=np.float64)
        fr = pos[:, None] * inv_freq[None, :]
        m["rope_cos"] = np.cos(fr).astype(np.float32)
        m["rope_sin"] = np.sin(fr).astype(np.float32)
        if NLAYERS >= 1:
            m["w_attn0"] = np.asarray(inputs["L0_attn_w_attn"], np.float32)
            m["w_proj0"] = np.asarray(inputs["L0_attn_w_proj"], np.float32)
            m["w_ffn0"] = np.asarray(inputs["L0_ffn_weight"], np.float32)
            m["ln_g0"] = _col(inputs["L0_ffn_ln_gamma"])
            m["ln_b0"] = _col(inputs["L0_ffn_ln_beta"])
        if NLAYERS >= 2:
            m["w_qkv1"] = np.asarray(inputs["L1_attn_w_qkv"], np.float32)
            m["b_qkv1"] = _col(inputs["L1_attn_b_qkv"])
            m["b_qkv1r"] = np.asarray(inputs["L1_attn_b_qkv"], np.float32).reshape(1, 3072)
            m["w_fc1"] = np.asarray(inputs["L1_attn_w_fc"], np.float32)
            m["b_fc1"] = _col(inputs["L1_attn_b_fc"])
            m["w_r1"] = np.ascontiguousarray(np.asarray(inputs["L1_ffn_W_r"], np.float32).T)
            m["w_k1"] = np.ascontiguousarray(np.asarray(inputs["L1_ffn_W_k"], np.float32).T)
            m["w_v1"] = np.ascontiguousarray(np.asarray(inputs["L1_ffn_W_v"], np.float32).T)
            m["mu_r1"] = _col(inputs["L1_ffn_mu_r"])
            m["mu_k1"] = _col(inputs["L1_ffn_mu_k"])
        if NLAYERS >= 3:
            for nm, key in (("wq2", "L2_attn_Wq"), ("wk2", "L2_attn_Wk"),
                            ("wv2", "L2_attn_Wv"), ("wo2", "L2_attn_Wo")):
                m[nm] = np.asarray(inputs[key], np.float32)
            for nm, key in (("bq2", "L2_attn_bq"), ("bk2", "L2_attn_bk"),
                            ("bv2", "L2_attn_bv"), ("bo2", "L2_attn_bo")):
                m[nm] = _col(inputs[key])
            m["clip2"] = np.array([[inputs["L2_ffn_clip_x"][0],
                                    inputs["L2_ffn_clip_down"][0]]], np.float32)
            wg = np.asarray(inputs["L2_ffn_w_gate"], np.float32)
            wu = np.asarray(inputs["L2_ffn_w_up"], np.float32)
            wd = np.asarray(inputs["L2_ffn_w_down"], np.float32)
            m["wg2"], m["wu2"], m["wd2"] = wg, wu, wd
            m["wg2T"] = np.ascontiguousarray(wg.T)
            m["wu2T"] = np.ascontiguousarray(wu.T)
            m["wd2T"] = np.ascontiguousarray(wd.T)
        if NLAYERS >= 4:
            nw = np.asarray(inputs["L3_attn_norm_w"], np.float32)
            m["nw3"] = np.stack([_col(nw[0]), _col(nw[1])])
            ipw = np.asarray(inputs["L3_attn_in_proj_w"], np.float32)
            m["inw3_l0"] = np.ascontiguousarray(ipw[0].T)
            m["inw3_l1"] = np.ascontiguousarray(ipw[1].T)
            cw = np.asarray(inputs["L3_attn_conv_w"], np.float32)
            m["cw3"] = np.ascontiguousarray(
                cw.reshape(2, 16, 128, 4).transpose(0, 2, 1, 3).reshape(2, 128, 64))
            cb = np.asarray(inputs["L3_attn_conv_b"], np.float32)
            m["cb3"] = np.stack([_col(cb[0]), _col(cb[1])])
            xpw = np.asarray(inputs["L3_attn_x_proj_w"], np.float32)
            m["xpw3_l0"] = np.ascontiguousarray(xpw[0].T)
            m["xpw3_l1"] = np.ascontiguousarray(xpw[1].T)
            dtw = np.asarray(inputs["L3_attn_dt_proj_w"], np.float32)
            m["dtw3_l0"] = np.ascontiguousarray(dtw[0].T)
            m["dtw3_l1"] = np.ascontiguousarray(dtw[1].T)
            dtb = np.asarray(inputs["L3_attn_dt_proj_b"], np.float32)
            m["dtb3"] = np.stack([_col(dtb[0]), _col(dtb[1])])
            al = np.asarray(inputs["L3_attn_A_log"], np.float32)
            m["al3"] = np.ascontiguousarray(
                al.reshape(2, 16, 128, 16).transpose(0, 2, 1, 3).reshape(2, 128, 256))
            dp = np.asarray(inputs["L3_attn_D_param"], np.float32)
            m["dp3"] = np.stack([_col(dp[0]), _col(dp[1])])
            opw = np.asarray(inputs["L3_attn_out_proj_w"], np.float32)
            m["opw3_l0"] = np.ascontiguousarray(opw[0].T)
            m["opw3_l1"] = np.ascontiguousarray(opw[1].T)
            m["w13"] = np.ascontiguousarray(np.asarray(inputs["L3_ffn_w1"], np.float32)[c])
            m["b13"] = _col(np.asarray(inputs["L3_ffn_b1"], np.float32)[c])
            m["w23"] = np.ascontiguousarray(np.asarray(inputs["L3_ffn_w2"], np.float32)[c])
            m["b23"] = _col(np.asarray(inputs["L3_ffn_b2"], np.float32)[c])
        per_core.append(m)
    return per_core


class KB:
    def __init__(self, nc, stk):
        self.nc = nc
        self.stk = stk
        self.p = Prog(nc)
        self.ap = {}
        self.sb = {}
        for s in ALL_SEMS:
            self.p.sems[s] = stk.enter_context(nc.semaphore(s))

    def din(self, name, shape, dt=F32):
        self.ap[name] = self.nc.dram_tensor(name, list(shape), dt, kind="ExternalInput").ap()
        self.p.reg("d:" + name, ready=True)
        return self.ap[name]

    def dout(self, name, shape, dt=F32):
        self.ap[name] = self.nc.dram_tensor(name, list(shape), dt, kind="ExternalOutput").ap()
        self.p.reg("d:" + name)
        return self.ap[name]

    def dram(self, name, shape, dt=F32):
        self.ap[name] = self.nc.dram_tensor(name, list(shape), dt).ap()
        self.p.reg("d:" + name)
        return self.ap[name]

    def sbuf(self, name, shape, dt=F32, scope=None):
        t = (scope or self.stk).enter_context(self.nc.sbuf_tensor(name, list(shape), dt))
        self.sb[name] = t
        self.p.reg(name)
        return t

    def psum(self, name, shape, dt=F32, scope=None):
        t = (scope or self.stk).enter_context(self.nc.psum_tensor(name, list(shape), dt))
        self.sb[name] = t
        self.p.reg(name)
        return t

    def dma(self, eng, dst, src, ins, outs):
        self.p.emit(eng, lambda h: h.dma_start(out=dst, in_=src), ins=ins, outs=outs, dma=True)

    def mm(self, out, lhsT, rhs, ins, outs, start=True, stop=True):
        self.p.emit("te", lambda h: h.matmul(out, lhsT, rhs, start=start, stop=stop),
                    ins=ins, outs=outs)

    def tr(self, out, in_, ins, outs, bf=False):
        eye = self.sb["EYEB" if bf else "EYE"]
        nm = "EYEB" if bf else "EYE"
        self.p.emit("te", lambda h: h.transpose(out, in_, eye[:]),
                    ins=list(ins) + [nm], outs=outs)

    def ve(self, fn, ins=(), outs=()):
        self.p.emit("ve", fn, ins=ins, outs=outs)

    def se(self, fn, ins=(), outs=()):
        self.p.emit("se", fn, ins=ins, outs=outs)

    def ge(self, fn, ins=(), outs=()):
        self.p.emit("ge", fn, ins=ins, outs=outs)

    def cc(self, kind, op, groups, in_ap, out_ap, ins, outs):
        self.p.emit("ge", lambda h: h.collective_compute(
            kind, op, replica_groups=groups, ins=[in_ap], outs=[out_ap]),
            ins=ins, outs=outs, cc=True)

    def bcast_row(self, psum_slice, row_ap, ins, outs):
        self.mm(psum_slice, self.sb["ONESROW"][:], row_ap,
                ins=list(ins) + ["ONESROW"], outs=outs)


PAIRS = [[0, 1], [2, 3], [4, 5], [6, 7]]
ALL8 = [list(range(8))]


def _build(nc, stk):
    k = KB(nc, stk)
    p = k.p
    ap = k.ap

    k.din("x", [TC, D])
    k.din("eye", [128, 128])
    k.din("tri", [128, 128])
    k.din("rope_cos", [TC, 32])
    k.din("rope_sin", [TC, 32])
    k.din("hsel", [128, 1])
    k.dout("y", [TC, D])

    if NLAYERS >= 1:
        k.din("w_attn0", [D, 3 * D]); k.din("w_proj0", [D, D]); k.din("w_ffn0", [D, D])
        k.din("ln_g0", [128, 8]); k.din("ln_b0", [128, 8])
    if NLAYERS >= 2:
        k.din("w_qkv1", [D, 3 * D]); k.din("b_qkv1", [128, 24]); k.din("b_qkv1r", [1, 3 * D])
        k.din("w_fc1", [D, D]); k.din("b_fc1", [128, 8])
        k.din("w_r1", [D, D]); k.din("w_k1", [D, D]); k.din("w_v1", [D, D])
        k.din("mu_r1", [128, 8]); k.din("mu_k1", [128, 8])
    if NLAYERS >= 3:
        for nm in ("wq2", "wk2", "wv2", "wo2"):
            k.din(nm, [D, D])
        for nm in ("bq2", "bk2", "bv2", "bo2"):
            k.din(nm, [128, 8])
        k.din("clip2", [1, 2])
        k.din("wg2", [2816, D]); k.din("wu2", [2816, D]); k.din("wd2", [D, 2816])
        k.din("wg2T", [D, 2816]); k.din("wu2T", [D, 2816]); k.din("wd2T", [2816, D])
    if NLAYERS >= 4:
        k.din("nw3", [2, 128, 8])
        k.din("inw3_l0", [D, 4096]); k.din("inw3_l1", [D, 4096])
        k.din("cw3", [2, 128, 64]); k.din("cb3", [2, 128, 16])
        k.din("xpw3_l0", [2048, 96]); k.din("xpw3_l1", [2048, 96])
        k.din("dtw3_l0", [64, 2048]); k.din("dtw3_l1", [64, 2048])
        k.din("dtb3", [2, 128, 16])
        k.din("al3", [2, 128, 256]); k.din("dp3", [2, 128, 16])
        k.din("opw3_l0", [2048, D]); k.din("opw3_l1", [2048, D])
        k.din("w13", [D, 4096]); k.din("b13", [128, 32])
        k.din("w23", [4096, D]); k.din("b23", [128, 8])

    k.dram("xr", [128, 8 * TC])
    k.dram("s_sp", [128, 8 * TC], BF16)
    k.dram("g_sp", [128, 8 * TC], BF16)

    XNB = k.sbuf("XNB", [128, 8 * TC], BF16)
    ONES = k.sbuf("ONESB", [128, 1], BF16)
    ONESF = k.sbuf("ONESF", [128, 1], F32)
    ONESROW = k.sbuf("ONESROW", [1, 128], F32)
    EYE = k.sbuf("EYE", [128, 128], F32)
    EYEB = k.sbuf("EYEB", [128, 128], BF16)
    TRIF = k.sbuf("TRIF", [128, 128], F32)
    TRIB = k.sbuf("TRIB", [128, 128], BF16)
    HSEL = k.sbuf("HSEL", [128, 1], F32)
    ROWS = k.sbuf("ROWS", [128, 1536], F32)
    SQB = k.sbuf("SQB", [128, 4096], BF16)
    XST = k.sbuf("XST", [128, 512], F32)
    XST2 = k.sbuf("XST2", [128, 512], F32)
    PRODB = k.sbuf("PRODB", [128, 512], F32)
    GST = k.sbuf("GST", [128, 2048], BF16)
    SST = k.sbuf("SST", [128, 2048], BF16)
    TOK = k.sbuf("TOK", [128, 1024], F32)
    TOK2 = k.sbuf("TOK2", [128, 1024], F32)
    WS0 = k.sbuf("WS0", [128, 1024], F32)
    WS1 = k.sbuf("WS1", [128, 1024], F32)
    WB0 = k.sbuf("WB0", [128, 1024], BF16)
    WB1 = k.sbuf("WB1", [128, 1024], BF16)

    PA = k.psum("PA", [128, 2048], F32)
    PB = k.psum("PB", [128, 512], F32)
    PC = k.psum("PC", [128, 512], F32)
    PD = k.psum("PD", [128, 512], F32)
    PE = k.psum("PE", [128, 512], F32)

    k.ve(lambda h: h.memset(ONES[:], 1.0), outs=["ONESB"])
    k.ve(lambda h: h.memset(ONESF[:], 1.0), outs=["ONESF"])
    k.ve(lambda h: h.memset(ONESROW[:], 1.0), outs=["ONESROW"])
    k.dma("sy", EYE[:], ap["eye"], ins=["d:eye"], outs=["EYE"])
    k.dma("sy", HSEL[:], ap["hsel"], ins=["d:hsel"], outs=["HSEL"])
    k.dma("sy", TRIF[:], ap["tri"], ins=["d:tri"], outs=["TRIF"])
    k.ve(lambda h: h.tensor_copy(TRIB[:], TRIF[:]), ins=["TRIF"], outs=["TRIB"])
    k.ve(lambda h: h.tensor_copy(EYEB[:], EYE[:]), ins=["EYE"], outs=["EYEB"])

    for tt in range(16):
        tok, tokn = (TOK, "TOK") if tt % 2 == 0 else (TOK2, "TOK2")
        k.dma("sy", tok[:], ap["x"][tt * 128:(tt + 1) * 128, :], ins=["d:x"], outs=[tokn])
        st, stn = (XST, "XST") if tt % 2 == 0 else (XST2, "XST2")
        for dc in range(8):
            bank, bn = (PB, "PB") if dc % 2 == 0 else (PC, "PC")
            k.tr(bank[:, 0:128], tok[:, dc * 128:(dc + 1) * 128], ins=[tokn], outs=[bn])
            k.ve(lambda h, st=st, bank=bank, dc=dc: h.tensor_copy(
                st[:, 0:128], bank[:, 0:128]), ins=[bn], outs=[stn])
            k.dma("ge", ap["xr"].rearrange("p (a t) -> p a t", t=TC)[:, dc:dc + 1,
                                                                    tt * 128:(tt + 1) * 128],
                  st[:, 0:128][:, None, :], ins=[stn], outs=["d:xr"])

    def ln_stats_chunk(t0, eps, into_rows):
        for dc in range(8):
            k.mm(PB[0:1, 0:512], ONES[:], XNB[:, dc * TC + t0: dc * TC + t0 + 512],
                 ins=["ONESB", "XNB"], outs=["PB"], start=(dc == 0), stop=(dc == 7))
        for dc in range(8):
            k.se(lambda h, dc=dc: h.activation(
                SQB[:, dc * 512:(dc + 1) * 512],
                XNB[:, dc * TC + t0: dc * TC + t0 + 512], ACT.Square),
                ins=["XNB"], outs=["SQB"])
        for dc in range(8):
            k.mm(PC[0:1, 0:512], ONES[:], SQB[:, dc * 512:(dc + 1) * 512],
                 ins=["ONESB", "SQB"], outs=["PC"], start=(dc == 0), stop=(dc == 7))
        A = ROWS[0:1, 0:512]
        Bq = ROWS[0:1, 512:1024]
        Cq = ROWS[0:1, 1024:1536]
        k.ve(lambda h: h.tensor_scalar_mul(A, PB[0:1, 0:512], 1.0 / D),
             ins=["PB"], outs=["ROWS"])
        k.ve(lambda h: h.tensor_scalar_mul(Bq, PC[0:1, 0:512], 1.0 / D),
             ins=["PC"], outs=["ROWS"])
        k.ve(lambda h: h.tensor_tensor(Cq, A, A, ALU.mult), ins=["ROWS"], outs=["ROWS"])
        k.ve(lambda h: h.tensor_tensor(Bq, Bq, Cq, ALU.subtract), ins=["ROWS"], outs=["ROWS"])
        k.ve(lambda h, eps=eps: h.tensor_scalar_add(Bq, Bq, eps), ins=["ROWS"], outs=["ROWS"])
        k.ve(lambda h: h.reciprocal(Bq, Bq), ins=["ROWS"], outs=["ROWS"])
        k.se(lambda h: h.activation(Bq, Bq, ACT.Sqrt), ins=["ROWS"], outs=["ROWS"])
        k.ve(lambda h: h.tensor_tensor(Cq, A, Bq, ALU.mult), ins=["ROWS"], outs=["ROWS"])
        k.ve(lambda h: h.tensor_scalar_mul(Cq, Cq, -1.0), ins=["ROWS"], outs=["ROWS"])

    def emit_layernorm(eps=1e-5):
        for dc in range(8):
            for tc4 in range(4):
                st, stn = (XST, "XST") if (dc * 4 + tc4) % 2 == 0 else (XST2, "XST2")
                k.dma("sy", st[:],
                      ap["xr"].rearrange("p (a t) -> p a t", t=TC)
                      [:, dc, tc4 * 512:(tc4 + 1) * 512], ins=["d:xr"], outs=[stn])
                k.ve(lambda h, st=st, dc=dc, tc4=tc4: h.tensor_copy(
                    XNB[:, dc * TC + tc4 * 512: dc * TC + (tc4 + 1) * 512], st[:]),
                    ins=[stn], outs=["XNB"])
        for tc4 in range(4):
            t0 = tc4 * 512
            ln_stats_chunk(t0, eps, None)
            k.bcast_row(PB[:, 0:512], ROWS[0:1, 512:1024], ins=["ROWS"], outs=["PB"])
            k.bcast_row(PC[:, 0:512], ROWS[0:1, 1024:1536], ins=["ROWS"], outs=["PC"])
            for dc in range(8):
                k.ve(lambda h, dc=dc, t0=t0: h.tensor_tensor(
                    SQB[:, dc * 512:(dc + 1) * 512],
                    XNB[:, dc * TC + t0: dc * TC + t0 + 512], PB[:, 0:512], ALU.mult),
                    ins=["XNB", "PB"], outs=["SQB"])
            for dc in range(8):
                k.ve(lambda h, dc=dc, t0=t0: h.tensor_tensor(
                    XNB[:, dc * TC + t0: dc * TC + t0 + 512],
                    SQB[:, dc * 512:(dc + 1) * 512], PC[:, 0:512], ALU.add),
                    ins=["SQB", "PC"], outs=["XNB"])

    def emit_combine():
        for dc in range(8):
            k.dma("sy", SST[:], ap["s_sp"].rearrange("p (a t) -> p a t", t=TC)[:, dc],
                  ins=["d:s_sp"], outs=["SST"])
            k.dma("sy", GST[:], ap["g_sp"].rearrange("p (a t) -> p a t", t=TC)[:, dc],
                  ins=["d:g_sp"], outs=["GST"])
            for q in range(4):
                st, stn = (XST, "XST") if q % 2 == 0 else (XST2, "XST2")
                k.dma("sy", st[:],
                      ap["xr"].rearrange("p (a t) -> p a t", t=TC)
                      [:, dc, q * 512:(q + 1) * 512], ins=["d:xr"], outs=[stn])
                k.ve(lambda h, st=st, q=q: h.tensor_tensor(
                    PRODB[:, 0:512], SST[:, q * 512:(q + 1) * 512],
                    GST[:, q * 512:(q + 1) * 512], ALU.mult),
                    ins=["SST", "GST"], outs=["PRODB"])
                k.ve(lambda h, st=st: h.tensor_tensor(
                    st[:], st[:], PRODB[:, 0:512], ALU.add), ins=[stn, "PRODB"], outs=[stn])
                k.dma("ge", ap["xr"].rearrange("p (a t) -> p a t", t=TC)
                      [:, dc, q * 512:(q + 1) * 512], st[:], ins=[stn], outs=["d:xr"])

    def load_w_block(src_name, m0, mwid, idx, kdim=1024):
        wbb, wbn = (WB0, "WB0") if idx % 2 == 0 else (WB1, "WB1")
        nk = kdim // 128
        src = ap[src_name].rearrange("(a p) m -> p a m", p=128)
        k.dma("ge", wbb[:, 0:nk * mwid].rearrange("p (a m) -> p a m", m=mwid),
              src[:, :, m0:m0 + mwid], ins=["d:" + src_name], outs=[wbn])
        return wbb, wbn, nk

    def dump_bf16_to_xr(src_ap_name):
        for dc in range(8):
            k.dma("sy", GST[:], ap[src_ap_name].rearrange("p (a t) -> p a t", t=TC)[:, dc],
                  ins=["d:" + src_ap_name], outs=["GST"])
            for q in range(4):
                st, stn = (XST, "XST") if q % 2 == 0 else (XST2, "XST2")
                k.ve(lambda h, st=st, q=q: h.tensor_copy(st[:], GST[:, q * 512:(q + 1) * 512]),
                     ins=["GST"], outs=[stn])
                k.dma("ge", ap["xr"].rearrange("p (a t) -> p a t", t=TC)
                      [:, dc, q * 512:(q + 1) * 512], st[:], ins=[stn], outs=["d:xr"])

    def dump_xnb_to_xr():
        for dc in range(8):
            for q in range(4):
                st, stn = (XST, "XST") if q % 2 == 0 else (XST2, "XST2")
                k.ve(lambda h, st=st, dc=dc, q=q: h.tensor_copy(
                    st[:], XNB[:, dc * TC + q * 512: dc * TC + (q + 1) * 512]),
                    ins=["XNB"], outs=[stn])
                k.dma("ge", ap["xr"].rearrange("p (a t) -> p a t", t=TC)
                      [:, dc, q * 512:(q + 1) * 512], st[:], ins=[stn], outs=["d:xr"])

    if NLAYERS >= 1:
        emit_layernorm()
        if DBG == "xn":
            dump_xnb_to_xr()
        l0 = ExitStack()
        XL = k.sbuf("XL", [128, 8 * TC], BF16, scope=l0)
        WQ0 = k.sbuf("WQ0", [128, 8192], BF16, scope=l0)
        WQI = k.sbuf("WQI", [128, 1024], I32, scope=l0)
        XQ0 = k.sbuf("XQ0", [128, 4096], BF16, scope=l0)
        COLS = k.sbuf("COLS", [128, 16], F32, scope=l0)
        SCL = k.sbuf("SCL", [1, 128], F32, scope=l0)
        LNG = k.sbuf("LNG", [128, 8], F32, scope=l0)
        LNB = k.sbuf("LNB", [128, 8], F32, scope=l0)
        k.dma("sy", LNG[:], ap["ln_g0"], ins=["d:ln_g0"], outs=["LNG"])
        k.dma("sy", LNB[:], ap["ln_b0"], ins=["d:ln_b0"], outs=["LNB"])
        k.dram("cc_gam_in", [1, 128])
        k.dram("cc_gam_out", [1, 128])
        k.dram("cc_s0_in", [128, 520])
        k.dram("cc_s0_out", [2, 128, 520])

        run_g = (DBG in (None, "g", "s")) and not SKIP_G
        run_attn = DBG in (None, "s", "y", "S")
        run_ph2 = DBG != "S"
        for tc4 in range(4) if run_g else []:
            t0 = tc4 * 512
            ln_stats_chunk(t0, 1e-3, None)
            k.bcast_row(PB[:, 0:512], ROWS[0:1, 512:1024], ins=["ROWS"], outs=["PB"])
            k.bcast_row(PC[:, 0:512], ROWS[0:1, 1024:1536], ins=["ROWS"], outs=["PC"])
            for dc in range(8):
                k.ve(lambda h, dc=dc, t0=t0: h.tensor_tensor(
                    SQB[:, dc * 512:(dc + 1) * 512],
                    XNB[:, dc * TC + t0: dc * TC + t0 + 512], PB[:, 0:512], ALU.mult),
                    ins=["XNB", "PB"], outs=["SQB"])
            for dc in range(8):
                k.ve(lambda h, dc=dc: h.tensor_tensor(
                    SQB[:, dc * 512:(dc + 1) * 512],
                    SQB[:, dc * 512:(dc + 1) * 512], PC[:, 0:512], ALU.add),
                    ins=["SQB", "PC"], outs=["SQB"])
            for dc in range(8):
                k.ve(lambda h, dc=dc, t0=t0: h.tensor_scalar(
                    XL[:, dc * TC + t0: dc * TC + t0 + 512],
                    SQB[:, dc * 512:(dc + 1) * 512],
                    LNG[:, dc:dc + 1], LNB[:, dc:dc + 1], ALU.mult, ALU.add),
                    ins=["SQB", "LNG", "LNB"], outs=["XL"])
        if run_g:
            for q in range(8):
                k.ve(lambda h, q=q: h.reduce_max(COLS[:, 1:2], XL[:, q * TC:(q + 1) * TC],
                                                 AX.X, apply_absolute_value=True),
                     ins=["XL"], outs=["COLS"])
                if q == 0:
                    k.ve(lambda h: h.tensor_copy(COLS[:, 0:1], COLS[:, 1:2]),
                         ins=["COLS"], outs=["COLS"])
                else:
                    k.ve(lambda h: h.tensor_tensor(COLS[:, 0:1], COLS[:, 0:1],
                                                   COLS[:, 1:2], ALU.max),
                         ins=["COLS"], outs=["COLS"])
            k.ve(lambda h: h.memset(XST[:, 0:128], 0.0), outs=["XST"])
            k.ve(lambda h: h.tensor_copy(XST[:, 0:1], COLS[:, 0:1]),
                 ins=["COLS"], outs=["XST"])
            k.tr(PB[:, 0:128], XST[:, 0:128], ins=["XST"], outs=["PB"])
            k.ve(lambda h: h.reduce_max(SCL[0:1, 0:1], PB[0:1, 0:128], AX.X),
                 ins=["PB"], outs=["SCL"])
            k.ve(lambda h: h.memset(ROWS[0:1, 0:128], 0.0), outs=["ROWS"])
            k.ve(lambda h: h.tensor_copy(ROWS[0:1, 0:1], SCL[0:1, 0:1]),
                 ins=["SCL"], outs=["ROWS"])
            k.dma("ge", ap["cc_gam_in"], ROWS[0:1, 0:128], ins=["ROWS"], outs=["d:cc_gam_in"])
            k.cc("AllReduce", ALU.max, ALL8, ap["cc_gam_in"], ap["cc_gam_out"],
                 ins=["d:cc_gam_in"], outs=["d:cc_gam_out"])
            k.dma("ge", SCL[0:1, 0:128], ap["cc_gam_out"], ins=["d:cc_gam_out"], outs=["SCL"])
            for t8 in range(8):
                wsb, wsn = (WS0, "WS0") if t8 % 2 == 0 else (WS1, "WS1")
                k.dma("sy", wsb[:], ap["w_ffn0"][t8 * 128:(t8 + 1) * 128, :],
                      ins=["d:w_ffn0"], outs=[wsn])
                k.ve(lambda h, wsb=wsb: h.reduce_sum(COLS[:, 2:3], wsb[:], AX.X,
                                                     apply_absolute_value=True),
                     ins=[wsn], outs=["COLS"])
                if t8 == 0:
                    k.ve(lambda h: h.tensor_copy(COLS[:, 3:4], COLS[:, 2:3]),
                         ins=["COLS"], outs=["COLS"])
                else:
                    k.ve(lambda h: h.tensor_tensor(COLS[:, 3:4], COLS[:, 3:4],
                                                   COLS[:, 2:3], ALU.add),
                         ins=["COLS"], outs=["COLS"])
            k.ve(lambda h: h.memset(XST[:, 0:128], 0.0), outs=["XST"])
            k.ve(lambda h: h.tensor_copy(XST[:, 0:1], COLS[:, 3:4]),
                 ins=["COLS"], outs=["XST"])
            k.tr(PB[:, 0:128], XST[:, 0:128], ins=["XST"], outs=["PB"])
            k.ve(lambda h: h.reduce_sum(SCL[0:1, 1:2], PB[0:1, 0:128], AX.X),
                 ins=["PB"], outs=["SCL"])
            k.ve(lambda h: h.tensor_scalar(SCL[0:1, 2:3], SCL[0:1, 1:2], 1.0 / (D * D), 1e-5,
                                           ALU.mult, ALU.add), ins=["SCL"], outs=["SCL"])
            k.ve(lambda h: h.reciprocal(SCL[0:1, 2:3], SCL[0:1, 2:3]), ins=["SCL"], outs=["SCL"])
            k.bcast_row(PC[:, 0:1], SCL[0:1, 2:3], ins=["SCL"], outs=["PC"])
            k.ve(lambda h: h.tensor_copy(COLS[:, 4:5], PC[:, 0:1]), ins=["PC"], outs=["COLS"])
            for t8 in range(8):
                wsb, wsn = (WS0, "WS0") if t8 % 2 == 0 else (WS1, "WS1")
                k.dma("sy", wsb[:], ap["w_ffn0"][t8 * 128:(t8 + 1) * 128, :],
                      ins=["d:w_ffn0"], outs=[wsn])
                k.ve(lambda h, wsb=wsb: h.tensor_scalar(
                    wsb[:], wsb[:], COLS[:, 4:5], None, ALU.mult),
                    ins=[wsn, "COLS"], outs=[wsn])
                k.ve(lambda h, wsb=wsb: h.tensor_copy(WQI[:], wsb[:]), ins=[wsn], outs=["WQI"])
                k.ve(lambda h, t8=t8: h.tensor_scalar(
                    WQ0[:, t8 * 1024:(t8 + 1) * 1024], WQI[:], -1.0, 1.0,
                    ALU.max, ALU.min), ins=["WQI"], outs=["WQ0"])
            k.ve(lambda h: h.reciprocal(SCL[0:1, 3:4], SCL[0:1, 0:1]), ins=["SCL"], outs=["SCL"])
            k.ve(lambda h: h.tensor_scalar_mul(SCL[0:1, 3:4], SCL[0:1, 3:4], 128.0),
                 ins=["SCL"], outs=["SCL"])
            k.bcast_row(PC[:, 0:1], SCL[0:1, 3:4], ins=["SCL"], outs=["PC"])
            k.ve(lambda h: h.tensor_copy(COLS[:, 5:6], PC[:, 0:1]), ins=["PC"], outs=["COLS"])
            k.ve(lambda h: h.tensor_scalar_mul(SCL[0:1, 4:5], SCL[0:1, 1:2],
                                               1.0 / (D * D * 128.0)), ins=["SCL"], outs=["SCL"])
            k.ve(lambda h: h.tensor_tensor(SCL[0:1, 4:5], SCL[0:1, 4:5], SCL[0:1, 0:1], ALU.mult),
                 ins=["SCL"], outs=["SCL"])
            k.bcast_row(PC[:, 0:1], SCL[0:1, 4:5], ins=["SCL"], outs=["PC"])
            k.ve(lambda h: h.tensor_copy(COLS[:, 6:7], PC[:, 0:1]), ins=["PC"], outs=["COLS"])
            for tc4 in range(4):
                t0 = tc4 * 512
                for dc in range(8):
                    k.ve(lambda h, dc=dc, t0=t0: h.tensor_scalar(
                        ROWS[:, 0:512], XL[:, dc * TC + t0: dc * TC + t0 + 512],
                        COLS[:, 5:6], None, ALU.mult), ins=["XL", "COLS"], outs=["ROWS"])
                    k.ve(lambda h, dc=dc: h.tensor_scalar(
                        XQ0[:, dc * 512:(dc + 1) * 512], ROWS[:, 0:512],
                        -(128.0 - 1e-5), 128.0 - 1e-5, ALU.max, ALU.min),
                        ins=["ROWS"], outs=["XQ0"])
                for mc in range(8):
                    for dc in range(8):
                        k.mm(PD[:, 0:512], WQ0[:, dc * 1024 + mc * 128: dc * 1024 + mc * 128 + 128],
                             XQ0[:, dc * 512:(dc + 1) * 512],
                             ins=["WQ0", "XQ0"], outs=["PD"], start=(dc == 0), stop=(dc == 7))
                    k.se(lambda h, mc=mc: h.activation(
                        SQB[:, mc * 512:(mc + 1) * 512], PD[:, 0:512], ACT.Silu,
                        scale=COLS[:, 6:7]), ins=["PD", "COLS"], outs=["SQB"])
                k.dma("ge", ap["g_sp"].rearrange("p (a t) -> p a t", t=TC)[:, :, t0:t0 + 512],
                      SQB[:].rearrange("p (a t) -> p a t", t=512), ins=["SQB"], outs=["d:g_sp"])

        if DBG == "g":
            dump_bf16_to_xr("g_sp")
        p.barrier()
        QF = k.sbuf("QF", [128, 4096], BF16, scope=l0)
        KF = k.sbuf("KF", [128, 4096], BF16, scope=l0)
        KFF = k.sbuf("KFF", [128, 512], F32, scope=l0)
        VFF = k.sbuf("VFF", [128, 512], F32, scope=l0)
        KT = k.sbuf("KT", [128, 4096], BF16, scope=l0)
        VT = k.sbuf("VT", [128, 4352], BF16, scope=l0)
        PMT = k.sbuf("PMT", [128, 256], BF16, scope=l0)
        SSB = k.sbuf("SSB", [128, 520], F32, scope=l0)
        SSBB = k.sbuf("SSBB", [128, 520], BF16, scope=l0)
        SIN = k.sbuf("SIN", [128, 520], F32, scope=l0)
        SRUN = k.sbuf("SRUN", [128, 520], F32, scope=l0)
        YNUM = k.sbuf("YNUM", [128, 128], F32, scope=l0)
        YSC = k.sbuf("YSC", [128, 4096], BF16, scope=l0)
        FMT = k.sbuf("FMT", [128, 512], F32, scope=l0)
        FMT2 = k.sbuf("FMT2", [128, 512], F32, scope=l0)

        k.ve(lambda h: h.memset(VT[:], 1.0), outs=["VT"])

        def fmap_into(dst, dstn, dst_sl, src_psum, pn):
            k.ve(lambda h: h.tensor_scalar_min(FMT[:, 0:512], src_psum, 0.0),
                 ins=[pn], outs=["FMT"])
            k.se(lambda h: h.activation(FMT2[:, 0:512], FMT[:, 0:512], ACT.Exp),
                 ins=["FMT"], outs=["FMT2"])
            k.p.emit("ve", lambda h: h.scalar_tensor_tensor(
                dst[dst_sl], src_psum, 0.0, FMT2[:, 0:512], ALU.max, ALU.add),
                ins=[pn, "FMT2"], outs=[dstn])

        def project_qkv(sc):
            t0 = sc * 512
            for hp in range(8):
                for wi, col0 in enumerate((0, 1024, 2048)):
                    wb, wbn, nk = load_w_block("w_attn0", col0 + hp * 128, 128,
                                               idx=hp * 3 + wi)
                    ev = (hp * 3 + wi) % 2 == 0
                    pa, pan = (PA, "PA") if ev else (PB, "PB")
                    tb, tbn, toff = (PA, "PA", 512) if ev else (PC, "PC", 0)
                    for dc in range(8):
                        k.mm(pa[:, 0:512], wb[:, dc * 128:(dc + 1) * 128],
                             XNB[:, dc * TC + t0:dc * TC + t0 + 512],
                             ins=[wbn, "XNB"], outs=[pan],
                             start=(dc == 0), stop=(dc == 7))
                    if col0 == 0:
                        fmap_into(QF, "QF", np.s_[:, hp * 512:(hp + 1) * 512],
                                  pa[:, 0:512], pan)
                    elif col0 == 1024:
                        fmap_into(KFF, "KFF", np.s_[:, 0:512], pa[:, 0:512], pan)
                        k.ve(lambda h, hp=hp: h.tensor_copy(
                            KF[:, hp * 512:(hp + 1) * 512], KFF[:]),
                            ins=["KFF"], outs=["KF"])
                        for sub in range(4):
                            k.tr(tb[:, toff:toff + 128], KFF[:, sub * 128:(sub + 1) * 128],
                                 ins=["KFF"], outs=[tbn])
                            k.ve(lambda h, hp=hp, sub=sub, tb=tb, toff=toff: h.tensor_copy(
                                KT[:, sub * 1024 + hp * 128: sub * 1024 + (hp + 1) * 128],
                                tb[:, toff:toff + 128]), ins=[tbn], outs=["KT"])
                    else:
                        k.ve(lambda h, pa=pa: h.tensor_copy(VFF[:], pa[:, 0:512]),
                             ins=[pan], outs=["VFF"])
                        for sub in range(4):
                            k.tr(tb[:, toff:toff + 128], VFF[:, sub * 128:(sub + 1) * 128],
                                 ins=["VFF"], outs=[tbn])
                            base = sub * 1088 + hp * 136
                            k.ve(lambda h, base=base, tb=tb, toff=toff: h.tensor_copy(
                                VT[:, base: base + 64], tb[:, toff:toff + 64]),
                                ins=[tbn], outs=["VT"])
                            k.ve(lambda h, base=base, tb=tb, toff=toff: h.tensor_copy(
                                VT[:, base + 68: base + 132], tb[:, toff + 64:toff + 128]),
                                ins=[tbn], outs=["VT"])

        def s_update(j):
            for hp in range(8):
                bank, bn = (PD, "PD") if hp < 4 else (PE, "PE")
                off = (hp % 4) * 65
                base = j * 1088 + hp * 136
                k.mm(bank[0:64, off:off + 65],
                     KT[:, j * 1024 + hp * 128: j * 1024 + hp * 128 + 64],
                     VT[:, base: base + 65],
                     ins=["KT", "VT"], outs=[bn], start=True, stop=True)
                k.mm(bank[64:128, off:off + 65],
                     KT[:, j * 1024 + hp * 128 + 64: j * 1024 + hp * 128 + 128],
                     VT[:, base + 68: base + 133],
                     ins=["KT", "VT"], outs=[bn], start=True, stop=True)
            k.ve(lambda h: h.tensor_tensor(SRUN[:, 0:260], SRUN[:, 0:260],
                                           PD[:, 0:260], ALU.add), ins=["SRUN", "PD"],
                 outs=["SRUN"])
            k.ve(lambda h: h.tensor_tensor(SRUN[:, 260:520], SRUN[:, 260:520],
                                           PE[:, 0:260], ALU.add), ins=["SRUN", "PE"],
                 outs=["SRUN"])

        if run_attn:
            k.ve(lambda h: h.memset(SRUN[:], 0.0), outs=["SRUN"])
            for sc in range(4):
                project_qkv(sc)
                for j in range(4):
                    s_update(j)
            if DBG == "S":
                k.dma("ge", ap["xr"].rearrange("p (a t) -> p a t", t=TC)[:, 0, 0:520],
                      SRUN[:], ins=["SRUN"], outs=["d:xr"])
            k.dma("ge", ap["cc_s0_in"], SRUN[:], ins=["SRUN"], outs=["d:cc_s0_in"])
            k.cc("AllGather", ALU.bypass, PAIRS, ap["cc_s0_in"], ap["cc_s0_out"],
                 ins=["d:cc_s0_in"], outs=["d:cc_s0_out"])
            k.dma("ge", SIN[:], ap["cc_s0_out"][0], ins=["d:cc_s0_out"], outs=["SIN"])
            k.ve(lambda h: h.tensor_scalar(SIN[:], SIN[:], HSEL[:, 0:1], None, ALU.mult),
                 ins=["SIN", "HSEL"], outs=["SIN"])

            k.ve(lambda h: h.memset(SRUN[:], 0.0), outs=["SRUN"])
            for sc in range(4) if run_ph2 else []:
                project_qkv(sc)
                for j in range(4):
                    k.ve(lambda h: h.tensor_tensor(SSB[:], SIN[:], SRUN[:], ALU.add),
                         ins=["SIN", "SRUN"], outs=["SSB"])
                    k.ve(lambda h: h.tensor_copy(SSBB[:], SSB[:]), ins=["SSB"], outs=["SSBB"])
                    for hp in range(8):
                        qs = np.s_[:, hp * 512 + j * 128: hp * 512 + (j + 1) * 128]
                        c0 = (hp % 4) * 65 + (0 if hp < 4 else 260)
                        k.mm(PA[:, hp * 256: hp * 256 + 128],
                             KF[:, hp * 512 + j * 128: hp * 512 + j * 128 + 128][0:64, :],
                             QF[qs][0:64, :], ins=["KF", "QF"], outs=["PA"])
                        k.mm(PA[:, hp * 256 + 128: hp * 256 + 256],
                             KF[:, hp * 512 + j * 128: hp * 512 + j * 128 + 128][64:128, :],
                             QF[qs][64:128, :], ins=["KF", "QF"], outs=["PA"])
                        k.ve(lambda h, hp=hp: h.tensor_tensor(
                            PMT[:, 0:128], PA[:, hp * 256: hp * 256 + 128], TRIB[:], ALU.mult),
                            ins=["PA", "TRIB"], outs=["PMT"])
                        k.ve(lambda h, hp=hp: h.tensor_tensor(
                            PMT[:, 128:256], PA[:, hp * 256 + 128: hp * 256 + 256],
                            TRIB[:], ALU.mult), ins=["PA", "TRIB"], outs=["PMT"])
                        base = j * 1088 + hp * 136
                        k.mm(PB[0:64, 0:128], VT[:, base: base + 64], PMT[:, 0:128],
                             ins=["VT", "PMT"], outs=["PB"], start=True, stop=False)
                        k.mm(PB[64:128, 0:128], VT[:, base + 68: base + 132], PMT[:, 128:256],
                             ins=["VT", "PMT"], outs=["PB"], start=True, stop=False)
                        k.mm(PB[0:64, 0:128], SSBB[0:64, c0:c0 + 64], QF[qs][0:64, :],
                             ins=["SSBB", "QF"], outs=["PB"], start=False, stop=True)
                        k.mm(PB[64:128, 0:128], SSBB[64:128, c0:c0 + 64], QF[qs][64:128, :],
                             ins=["SSBB", "QF"], outs=["PB"], start=False, stop=True)
                        k.mm(PC[0:1, 0:128], VT[:, base + 64: base + 65], PMT[:, 0:128],
                             ins=["VT", "PMT"], outs=["PC"], start=True, stop=False)
                        k.mm(PC[0:1, 128:256], VT[:, base + 132: base + 133], PMT[:, 128:256],
                             ins=["VT", "PMT"], outs=["PC"], start=False, stop=False)
                        k.mm(PC[0:1, 0:128], SSBB[0:64, c0 + 64:c0 + 65], QF[qs][0:64, :],
                             ins=["SSBB", "QF"], outs=["PC"], start=False, stop=True)
                        k.mm(PC[0:1, 128:256], SSBB[64:128, c0 + 64:c0 + 65], QF[qs][64:128, :],
                             ins=["SSBB", "QF"], outs=["PC"], start=False, stop=True)
                        k.ve(lambda h: h.tensor_copy(YNUM[:, 0:128], PB[:, 0:128]),
                             ins=["PB"], outs=["YNUM"])
                        k.ve(lambda h: h.tensor_scalar_add(ROWS[0:1, 0:256],
                                                           PC[0:1, 0:256], 1e-6),
                             ins=["PC"], outs=["ROWS"])
                        k.ve(lambda h: h.reciprocal(ROWS[0:1, 0:256], ROWS[0:1, 0:256]),
                             ins=["ROWS"], outs=["ROWS"])
                        k.bcast_row(PC[:, 256:384], ROWS[0:1, 0:128], ins=["ROWS"], outs=["PC"])
                        k.bcast_row(PC[:, 384:512], ROWS[0:1, 128:256], ins=["ROWS"], outs=["PC"])
                        k.ve(lambda h, hp=hp, j=j: h.tensor_tensor(
                            YSC[0:64, hp * 512 + j * 128: hp * 512 + (j + 1) * 128],
                            YNUM[0:64, 0:128], PC[0:64, 256:384], ALU.mult),
                            ins=["YNUM", "PC"], outs=["YSC"])
                        k.ve(lambda h, hp=hp, j=j: h.tensor_tensor(
                            YSC[64:128, hp * 512 + j * 128: hp * 512 + (j + 1) * 128],
                            YNUM[64:128, 0:128], PC[64:128, 384:512], ALU.mult),
                            ins=["YNUM", "PC"], outs=["YSC"])
                    s_update(j)
                if DBG == "y":
                    k.dma("ge", ap["g_sp"].rearrange("p (a t) -> p a t", t=TC)
                          [:, :, sc * 512:(sc + 1) * 512],
                          YSC[:].rearrange("p (a t) -> p a t", t=512),
                          ins=["YSC"], outs=["d:g_sp"])
                for mc in range(8):
                    wb, wbn, nk = load_w_block("w_proj0", mc * 128, 128, idx=mc)
                    for dc in range(8):
                        k.mm(PB[:, 0:512], wb[:, dc * 128:(dc + 1) * 128],
                             YSC[:, dc * 512:(dc + 1) * 512],
                             ins=[wbn, "YSC"], outs=["PB"], start=(dc == 0), stop=(dc == 7))
                    k.ve(lambda h, mc=mc: h.tensor_copy(SQB[:, mc * 512:(mc + 1) * 512],
                                                        PB[:, 0:512]), ins=["PB"], outs=["SQB"])
                k.dma("ge", ap["s_sp"].rearrange("p (a t) -> p a t", t=TC)
                      [:, :, sc * 512:(sc + 1) * 512],
                      SQB[:].rearrange("p (a t) -> p a t", t=512), ins=["SQB"], outs=["d:s_sp"])
        if DBG == "s":
            dump_bf16_to_xr("s_sp")
        if DBG == "y":
            dump_bf16_to_xr("g_sp")
        if DBG is None:
            emit_combine()
        p.barrier()
        l0.close()


    if NLAYERS >= 2:
        emit_layernorm()
        if DBG == "xn1":
            dump_xnb_to_xr()
        run_g1 = DBG in (None, "g1", "s1")
        run_attn1 = DBG in (None, "s1")
        l1 = ExitStack()
        QX = k.sbuf("QX", [128, 16384], BF16, scope=l1)
        YF = k.sbuf("YF", [128, 16384], BF16, scope=l1)
        QNM = k.sbuf("QNM", [128, 16], F32, scope=l1)
        KNM = k.sbuf("KNM", [128, 16], F32, scope=l1)
        NRM = k.sbuf("NRM", [128, 16], F32, scope=l1)
        MCOL = k.sbuf("MCOL", [128, 16], F32, scope=l1)
        XLS = k.sbuf("XLS", [128, 8], F32, scope=l1)
        XPB = k.sbuf("XPB", [128, 8], F32, scope=l1)
        MUR = k.sbuf("MUR", [128, 8], F32, scope=l1)
        MUK = k.sbuf("MUK", [128, 8], F32, scope=l1)
        FCB = k.sbuf("FCB", [128, 8], F32, scope=l1)
        KFB = k.sbuf("KFB", [128, 128], BF16, scope=l1)
        l1p = ExitStack()
        WQKVB = k.sbuf("WQKVB", [128, 4096], BF16, scope=l1p)
        BIAST = k.sbuf("BIAST", [128, 3072], BF16, scope=l1p)
        QT0 = k.sbuf("QT0", [128, 512], F32, scope=l1p)
        RT = k.sbuf("RT", [128, 512], F32, scope=l1p)
        RTB = k.sbuf("RTB", [128, 512], BF16, scope=l1p)
        FMP = k.sbuf("FMP", [128, 512], F32, scope=l1p)
        RCO = k.sbuf("RCO", [128, 512], F32, scope=l1p)
        RSI = k.sbuf("RSI", [128, 512], F32, scope=l1p)
        k.dram("cc_xl_in", [128, 8])
        k.dram("cc_xl_out", [2, 128, 8])
        k.dram("exch1_in", [128, 32768], BF16)
        k.dram("exch1_out", [2, 128, 32768], BF16)
        k.dma("sy", MUR[:], ap["mu_r1"], ins=["d:mu_r1"], outs=["MUR"])
        k.dma("sy", MUK[:], ap["mu_k1"], ins=["d:mu_k1"], outs=["MUK"])
        k.dma("sy", FCB[:], ap["b_fc1"], ins=["d:b_fc1"], outs=["FCB"])
        k.dma("sy", RCO[:].rearrange("p (a c) -> p a c", c=32),
              ap["rope_cos"].rearrange("(a p) c -> p a c", p=128),
              ins=["d:rope_cos"], outs=["RCO"])
        k.dma("sy", RSI[:].rearrange("p (a c) -> p a c", c=32),
              ap["rope_sin"].rearrange("(a p) c -> p a c", p=128),
              ins=["d:rope_sin"], outs=["RSI"])

        for dc in range(8):
            k.ve(lambda h, dc=dc: h.tensor_copy(XLS[:, dc:dc + 1],
                                                XNB[:, dc * TC + TC - 1: dc * TC + TC]),
                 ins=["XNB"], outs=["XLS"])
        k.dma("ge", ap["cc_xl_in"], XLS[:], ins=["XLS"], outs=["d:cc_xl_in"])
        k.cc("AllGather", ALU.bypass, PAIRS, ap["cc_xl_in"], ap["cc_xl_out"],
             ins=["d:cc_xl_in"], outs=["d:cc_xl_out"])
        k.dma("ge", XPB[:], ap["cc_xl_out"][0], ins=["d:cc_xl_out"], outs=["XPB"])
        k.ve(lambda h: h.tensor_scalar(XPB[:], XPB[:], HSEL[:, 0:1], None, ALU.mult),
             ins=["XPB", "HSEL"], outs=["XPB"])

        k.ve(lambda h: h.memset(QNM[:], 0.0), outs=["QNM"])
        k.ve(lambda h: h.memset(KNM[:], 0.0), outs=["KNM"])
        for cc6 in range(6):
            k.dma("sy", ROWS[0:1, 0:512], ap["b_qkv1r"][:, cc6 * 512:(cc6 + 1) * 512],
                  ins=["d:b_qkv1r"], outs=["ROWS"])
            k.bcast_row(PA[:, 0:512], ROWS[0:1, 0:512],
                        ins=["ROWS"], outs=["PA"])
            k.ve(lambda h, cc6=cc6: h.tensor_copy(BIAST[:, cc6 * 512:(cc6 + 1) * 512],
                                                  PA[:, 0:512]), ins=["PA"], outs=["BIAST"])

        for cc6 in range(6):
            iq = cc6 < 2
            ik = 2 <= cc6 < 4
            for dc in range(8):
                k.dma("ge", WQKVB[:, dc * 512:(dc + 1) * 512],
                      ap["w_qkv1"][dc * 128:(dc + 1) * 128, cc6 * 512:(cc6 + 1) * 512],
                      ins=["d:w_qkv1"], outs=["WQKVB"])
            for tt in range(16):
                for dc in range(8):
                    k.mm(PB[:, 0:512],
                         XNB[:, dc * TC + tt * 128: dc * TC + (tt + 1) * 128],
                         WQKVB[:, dc * 512:(dc + 1) * 512],
                         ins=["XNB", "WQKVB"], outs=["PB"],
                         start=(dc == 0), stop=(dc == 7))
                k.ve(lambda h, cc6=cc6: h.tensor_tensor(
                    QT0[:], PB[:, 0:512], BIAST[:, cc6 * 512:(cc6 + 1) * 512], ALU.add),
                    ins=["PB", "BIAST"], outs=["QT0"])
                if iq or ik:
                    k.ve(lambda h: h.tensor_tensor(RT[:], QT0[:], QT0[:], ALU.mult),
                         ins=["QT0"], outs=["RT"])
                    dstn, dst = ("QNM", QNM) if iq else ("KNM", KNM)
                    k.ve(lambda h: h.tensor_reduce(
                        NRM[:, 0:8], RT[:].rearrange("p (a b) -> p a b", b=64),
                        AX.X, ALU.add), ins=["RT"], outs=["NRM"])
                    h8 = (cc6 % 2) * 8
                    k.ve(lambda h, dst=dst, h8=h8: h.tensor_tensor(
                        dst[:, h8:h8 + 8], dst[:, h8:h8 + 8], NRM[:, 0:8], ALU.max),
                        ins=["NRM", dstn], outs=[dstn])
                    qe = QT0[:].rearrange("p (a two c) -> p a two c", two=32, c=2)[:, :, :, 0]
                    qo = QT0[:].rearrange("p (a two c) -> p a two c", two=32, c=2)[:, :, :, 1]
                    cosb = RCO[:, tt * 32:(tt + 1) * 32].rearrange(
                        "p (o c) -> p o c", o=1).broadcast_to([128, 8, 32])
                    sinb = RSI[:, tt * 32:(tt + 1) * 32].rearrange(
                        "p (o c) -> p o c", o=1).broadcast_to([128, 8, 32])
                    re = RT[:].rearrange("p (a two c) -> p a two c", two=32, c=2)[:, :, :, 0]
                    ro = RT[:].rearrange("p (a two c) -> p a two c", two=32, c=2)[:, :, :, 1]
                    k.ve(lambda h, qe=qe, cosb=cosb: h.tensor_tensor(
                        FMP[:, 0:256].rearrange("p (a c) -> p a c", c=32), qe, cosb, ALU.mult),
                        ins=["QT0", "RCO"], outs=["FMP"])
                    k.p.emit("ve", lambda h, qo=qo, sinb=sinb, re=re: h.scalar_tensor_tensor(
                        re, qo, -1.0, sinb, ALU.mult, ALU.mult), ins=["QT0", "RSI"], outs=["RT"])
                    k.ve(lambda h, re=re: h.tensor_tensor(
                        re, re, FMP[:, 0:256].rearrange("p (a c) -> p a c", c=32), ALU.add),
                        ins=["RT", "FMP"], outs=["RT"])
                    k.ve(lambda h, qe=qe, sinb=sinb: h.tensor_tensor(
                        FMP[:, 0:256].rearrange("p (a c) -> p a c", c=32), qe, sinb, ALU.mult),
                        ins=["QT0", "RSI"], outs=["FMP"])
                    k.ve(lambda h, qo=qo, cosb=cosb, ro=ro: h.tensor_tensor(
                        ro, qo, cosb, ALU.mult), ins=["QT0", "RCO"], outs=["RT"])
                    k.ve(lambda h, ro=ro: h.tensor_tensor(
                        ro, ro, FMP[:, 0:256].rearrange("p (a c) -> p a c", c=32), ALU.add),
                        ins=["RT", "FMP"], outs=["RT"])
                    for pb4 in range(4):
                        pair = (cc6 % 2) * 4 + pb4
                        k.tr(PE[:, 0:128], RT[:, pb4 * 128:(pb4 + 1) * 128],
                             ins=["RT"], outs=["PE"])
                        if iq:
                            k.ve(lambda h, pair=pair, tt=tt: h.tensor_copy(
                                QX[:, pair * 2048 + tt * 128: pair * 2048 + (tt + 1) * 128],
                                PE[:, 0:128]), ins=["PE"], outs=["QX"])
                        else:
                            k.ve(lambda h: h.tensor_copy(KFB[:], PE[:, 0:128]),
                                 ins=["PE"], outs=["KFB"])
                            k.dma("ge", ap["exch1_in"][:, pair * 2048 + tt * 128:
                                                       pair * 2048 + (tt + 1) * 128],
                                  KFB[:], ins=["KFB"], outs=["d:exch1_in"])
                else:
                    k.ve(lambda h: h.tensor_copy(RTB[:], QT0[:]), ins=["QT0"], outs=["RTB"])
                    k.dma("ge", ap["exch1_in"][:, 16384 + tt * 1024 + (cc6 - 4) * 512:
                                               16384 + tt * 1024 + (cc6 - 3) * 512],
                          RTB[:], ins=["RTB"], outs=["d:exch1_in"])
        p.barrier()
        l1p.close()
        k.cc("AllGather", ALU.bypass, PAIRS, ap["exch1_in"], ap["exch1_out"],
             ins=["d:exch1_in"], outs=["d:exch1_out"])
        l1g = ExitStack()
        PREV = k.sbuf("PREV", [128, 512], BF16, scope=l1g)
        D1 = k.sbuf("D1", [128, 512], F32, scope=l1g)
        XRB = k.sbuf("XRB", [128, 4096], BF16, scope=l1g)
        XKB = k.sbuf("XKB", [128, 4096], BF16, scope=l1g)
        RB = k.sbuf("RB", [128, 4096], BF16, scope=l1g)
        KA = k.sbuf("KA", [128, 4096], BF16, scope=l1g)
        FM = k.sbuf("FM", [128, 512], F32, scope=l1g)
        for dst, (which, t_) in enumerate((("QNM", QNM), ("KNM", KNM))):
            k.ve(lambda h: h.memset(XST[:, 0:128], 0.0), outs=["XST"])
            k.ve(lambda h, t_=t_: h.tensor_copy(XST[:, 0:16], t_[:]), ins=[which], outs=["XST"])
            k.tr(PE[:, 0:128], XST[:, 0:128], ins=["XST"], outs=["PE"])
            k.ve(lambda h: h.reduce_max(XST2[:, 0:1], PE[:, 0:128], AX.X),
                 ins=["PE"], outs=["XST2"])
            k.ve(lambda h: h.memset(XST[:, 0:128], 0.0), outs=["XST"])
            k.ve(lambda h: h.tensor_copy(XST[:, 0:1], XST2[:, 0:1]), ins=["XST2"], outs=["XST"])
            k.tr(PE[:, 0:128], XST[:, 0:128], ins=["XST"], outs=["PE"])
            k.ve(lambda h, dst=dst: h.tensor_copy(
                ROWS[0:1, 512 + dst * 16: 512 + dst * 16 + 16], PE[0:1, 0:16]),
                ins=["PE"], outs=["ROWS"])
        k.ve(lambda h: h.tensor_tensor(ROWS[0:1, 544:560], ROWS[0:1, 512:528],
                                       ROWS[0:1, 528:544], ALU.mult), ins=["ROWS"], outs=["ROWS"])
        k.se(lambda h: h.activation(ROWS[0:1, 544:560], ROWS[0:1, 544:560], ACT.Sqrt),
             ins=["ROWS"], outs=["ROWS"])
        k.ve(lambda h: h.tensor_scalar_mul(ROWS[0:1, 544:560], ROWS[0:1, 544:560], -0.125),
             ins=["ROWS"], outs=["ROWS"])
        for hd in range(16):
            k.bcast_row(PE[:, 0:1], ROWS[0:1, 544 + hd:545 + hd], ins=["ROWS"], outs=["PE"])
            k.ve(lambda h, hd=hd: h.tensor_copy(MCOL[:, hd:hd + 1], PE[:, 0:1]),
                 ins=["PE"], outs=["MCOL"])

        for tc4 in range(4) if run_g1 else []:
            t0 = tc4 * 512
            for dc in range(8):
                if tc4 == 0:
                    k.ve(lambda h, dc=dc: h.tensor_copy(
                        PREV[:, 1:512], XNB[:, dc * TC: dc * TC + 511]),
                        ins=["XNB"], outs=["PREV"])
                    k.ve(lambda h, dc=dc: h.tensor_copy(
                        PREV[:, 0:1], XPB[:, dc:dc + 1]), ins=["XPB"], outs=["PREV"])
                else:
                    k.ve(lambda h, dc=dc, t0=t0: h.tensor_copy(
                        PREV[:], XNB[:, dc * TC + t0 - 1: dc * TC + t0 + 511]),
                        ins=["XNB"], outs=["PREV"])
                k.ve(lambda h, dc=dc, t0=t0: h.tensor_tensor(
                    D1[:], XNB[:, dc * TC + t0: dc * TC + t0 + 512], PREV[:], ALU.subtract),
                    ins=["XNB", "PREV"], outs=["D1"])
                k.p.emit("ve", lambda h, dc=dc: h.scalar_tensor_tensor(
                    XRB[:, dc * 512:(dc + 1) * 512], D1[:], MUR[:, dc:dc + 1], PREV[:],
                    ALU.mult, ALU.add), ins=["D1", "MUR", "PREV"], outs=["XRB"])
                k.p.emit("ve", lambda h, dc=dc: h.scalar_tensor_tensor(
                    XKB[:, dc * 512:(dc + 1) * 512], D1[:], MUK[:, dc:dc + 1], PREV[:],
                    ALU.mult, ALU.add), ins=["D1", "MUK", "PREV"], outs=["XKB"])
            for mc in range(8):
                wb, wbn, nk = load_w_block("w_r1", mc * 128, 128, idx=mc)
                for dc in range(8):
                    k.mm(PB[:, 0:512], wb[:, dc * 128:(dc + 1) * 128],
                         XRB[:, dc * 512:(dc + 1) * 512],
                         ins=[wbn, "XRB"], outs=["PB"], start=(dc == 0), stop=(dc == 7))
                k.se(lambda h, mc=mc: h.activation(RB[:, mc * 512:(mc + 1) * 512],
                                                   PB[:, 0:512], ACT.Sigmoid),
                     ins=["PB"], outs=["RB"])
            for mc in range(8):
                wb, wbn, nk = load_w_block("w_k1", mc * 128, 128, idx=mc)
                for dc in range(8):
                    k.mm(PB[:, 0:512], wb[:, dc * 128:(dc + 1) * 128],
                         XKB[:, dc * 512:(dc + 1) * 512],
                         ins=[wbn, "XKB"], outs=["PB"], start=(dc == 0), stop=(dc == 7))
                k.se(lambda h: h.activation(FM[:], PB[:, 0:512], ACT.Relu),
                     ins=["PB"], outs=["FM"])
                k.se(lambda h, mc=mc: h.activation(KA[:, mc * 512:(mc + 1) * 512],
                                                   FM[:], ACT.Square), ins=["FM"], outs=["KA"])
            for mc in range(8):
                wb, wbn, nk = load_w_block("w_v1", mc * 128, 128, idx=mc)
                for dc in range(8):
                    k.mm(PB[:, 0:512], wb[:, dc * 128:(dc + 1) * 128],
                         KA[:, dc * 512:(dc + 1) * 512],
                         ins=[wbn, "KA"], outs=["PB"], start=(dc == 0), stop=(dc == 7))
                k.ve(lambda h, mc=mc: h.tensor_tensor(
                    FM[:], PB[:, 0:512], RB[:, mc * 512:(mc + 1) * 512], ALU.mult),
                    ins=["PB", "RB"], outs=["FM"])
                k.se(lambda h, mc=mc: h.activation(SQB[:, mc * 512:(mc + 1) * 512],
                                                   FM[:], ACT.Silu), ins=["FM"], outs=["SQB"])
            k.dma("ge", ap["g_sp"].rearrange("p (a t) -> p a t", t=TC)[:, :, t0:t0 + 512],
                  SQB[:].rearrange("p (a t) -> p a t", t=512), ins=["SQB"], outs=["d:g_sp"])
        p.barrier()
        l1g.close()
        l1c = ExitStack()
        KPH = k.sbuf("KPH", [128, 4096], BF16, scope=l1c)
        VPH = k.sbuf("VPH", [128, 4096], BF16, scope=l1c)
        PTB = k.sbuf("PTB", [128, 1024], BF16, scope=l1c)
        PTB2 = k.sbuf("PTB2", [128, 1024], BF16, scope=l1c)

        if DBG == "g1":
            dump_bf16_to_xr("g_sp")
        for pair in range(8) if run_attn1 else []:
            for half in range(2):
                k.dma("sy", KPH[:, half * 2048:(half + 1) * 2048],
                      ap["exch1_out"][half][:, pair * 2048:(pair + 1) * 2048],
                      ins=["d:exch1_out"], outs=["KPH"])
                k.dma("sy", VPH[:, half * 2048:(half + 1) * 2048]
                      .rearrange("p (a b) -> p a b", b=128),
                      ap["exch1_out"][half][:, 16384:32768]
                      .rearrange("p (a b) -> p a b", b=1024)[:, :, pair * 128:(pair + 1) * 128],
                      ins=["d:exch1_out"], outs=["VPH"])
            for qt in range(4):
                q0 = pair * 2048 + qt * 512
                for sch in range(32):
                    bank, bn = (PA[:, 0:512], "PA") if sch % 2 == 0 else (PA[:, 512:1024], "PA")
                    pt, ptn = (PTB, "PTB") if sch % 2 == 0 else (PTB2, "PTB2")
                    k.mm(bank, KPH[0:64, sch * 128:(sch + 1) * 128],
                         QX[0:64, q0:q0 + 512], ins=["KPH", "QX"], outs=[bn])
                    k.p.emit("se", lambda h, bank=bank, pair=pair, pt=pt: h.activation(
                        pt[:, 0:512], bank, ACT.Exp, bias=MCOL[:, 2 * pair:2 * pair + 1],
                        scale=0.125), ins=[bn, "MCOL"], outs=[ptn])
                    k.mm(bank, KPH[64:128, sch * 128:(sch + 1) * 128],
                         QX[64:128, q0:q0 + 512], ins=["KPH", "QX"], outs=[bn])
                    k.p.emit("se", lambda h, bank=bank, pair=pair, pt=pt: h.activation(
                        pt[:, 512:1024], bank, ACT.Exp, bias=MCOL[:, 2 * pair + 1:2 * pair + 2],
                        scale=0.125), ins=[bn, "MCOL"], outs=[ptn])
                    k.mm(PB[0:64, 0:512], VPH[:, sch * 128: sch * 128 + 64], pt[:, 0:512],
                         ins=["VPH", ptn], outs=["PB"], start=(sch == 0), stop=(sch == 31))
                    k.mm(PB[64:128, 0:512], VPH[:, sch * 128 + 64: sch * 128 + 128],
                         pt[:, 512:1024],
                         ins=["VPH", ptn], outs=["PB"], start=(sch == 0), stop=(sch == 31))
                    k.mm(PC[0:1, 0:512], ONES[:], pt[:, 0:512],
                         ins=["ONESB", ptn], outs=["PC"], start=(sch == 0), stop=(sch == 31))
                    k.mm(PD[0:1, 0:512], ONES[:], pt[:, 512:1024],
                         ins=["ONESB", ptn], outs=["PD"], start=(sch == 0), stop=(sch == 31))
                k.ve(lambda h: h.tensor_copy(ROWS[0:1, 0:512], PC[0:1, 0:512]),
                     ins=["PC"], outs=["ROWS"])
                k.ve(lambda h: h.reciprocal(ROWS[0:1, 0:512], ROWS[0:1, 0:512]),
                     ins=["ROWS"], outs=["ROWS"])
                k.ve(lambda h: h.tensor_copy(ROWS[0:1, 512:1024], PD[0:1, 0:512]),
                     ins=["PD"], outs=["ROWS"])
                k.ve(lambda h: h.reciprocal(ROWS[0:1, 512:1024], ROWS[0:1, 512:1024]),
                     ins=["ROWS"], outs=["ROWS"])
                k.bcast_row(PA[:, 1024:1536], ROWS[0:1, 0:512], ins=["ROWS"], outs=["PA"])
                k.bcast_row(PA[:, 1536:2048], ROWS[0:1, 512:1024], ins=["ROWS"], outs=["PA"])
                k.ve(lambda h: h.tensor_copy(XST[:], PB[0:128, 0:512]),
                     ins=["PB"], outs=["XST"])
                k.ve(lambda h, pair=pair, q0=q0: h.tensor_tensor(
                    YF[0:64, q0:q0 + 512], XST[0:64, 0:512], PA[0:64, 1024:1536], ALU.mult),
                    ins=["XST", "PA"], outs=["YF"])
                k.ve(lambda h, pair=pair, q0=q0: h.tensor_tensor(
                    YF[64:128, q0:q0 + 512], XST[64:128, 0:512], PA[64:128, 1536:2048], ALU.mult),
                    ins=["XST", "PA"], outs=["YF"])
        for tc4 in range(4) if run_attn1 else []:
            t0 = tc4 * 512
            for mc in range(8):
                wb, wbn, nk = load_w_block("w_fc1", mc * 128, 128, idx=mc)
                for dc in range(8):
                    k.mm(PB[:, 0:512], wb[:, dc * 128:(dc + 1) * 128],
                         YF[:, dc * 2048 + t0: dc * 2048 + t0 + 512],
                         ins=[wbn, "YF"], outs=["PB"], start=(dc == 0), stop=(dc == 7))
                k.ve(lambda h, mc=mc: h.tensor_scalar(
                    SQB[:, mc * 512:(mc + 1) * 512], PB[:, 0:512],
                    FCB[:, mc:mc + 1], None, ALU.add), ins=["PB", "FCB"], outs=["SQB"])
            k.dma("ge", ap["s_sp"].rearrange("p (a t) -> p a t", t=TC)[:, :, t0:t0 + 512],
                  SQB[:].rearrange("p (a t) -> p a t", t=512), ins=["SQB"], outs=["d:s_sp"])
        if DBG == "s1":
            dump_bf16_to_xr("s_sp")
        if DBG is None:
            emit_combine()
        p.barrier()
        l1c.close()
        l1.close()


    if NLAYERS >= 3:
        emit_layernorm()
        run_g2 = DBG in (None, "g2", "s2")
        run_attn2 = DBG in (None, "s2")
        l2 = ExitStack()
        KV2 = k.sbuf("KV2", [128, 16], F32, scope=l2)
        SCG = k.sbuf("SCG", [128, 22], F32, scope=l2)
        SCU = k.sbuf("SCU", [128, 22], F32, scope=l2)
        SCD = k.sbuf("SCD", [128, 8], F32, scope=l2)
        CLP = k.sbuf("CLP", [1, 2], F32, scope=l2)
        COLX = k.sbuf("COLX", [128, 8], F32, scope=l2)
        QT2 = k.sbuf("QT2", [128, 512], F32, scope=l2)
        KT2 = k.sbuf("KT2", [128, 512], BF16, scope=l2)
        VT2 = k.sbuf("VT2", [128, 512], F32, scope=l2)
        FM2 = k.sbuf("FM2", [128, 512], F32, scope=l2)
        FM3 = k.sbuf("FM3", [128, 512], F32, scope=l2)
        BQ2 = k.sbuf("BQ2", [128, 8], F32, scope=l2)
        BK2 = k.sbuf("BK2", [128, 8], F32, scope=l2)
        BV2 = k.sbuf("BV2", [128, 8], F32, scope=l2)
        BO2 = k.sbuf("BO2", [128, 8], F32, scope=l2)
        HSG = k.sbuf("HSG", [128, 512], BF16, scope=l2)
        HIQ = k.sbuf("HIQ", [128, 512], I32, scope=l2)
        COLS2 = k.sbuf("COLS2", [128, 16], F32, scope=l2)
        l2q = ExitStack()
        RECB = k.sbuf("RECB", [128, 2816], F32, scope=l2q)
        WTQ = k.sbuf("WTQ", [128, 2816], F32, scope=l2q)
        WTQI = k.sbuf("WTQI", [128, 2816], I32, scope=l2q)
        WTQB = k.sbuf("WTQB", [128, 2816], BF16, scope=l2q)
        k.dram("wgq_sp", [D, 2816], BF16)
        k.dram("wuq_sp", [D, 2816], BF16)
        k.dram("wdq_sp", [2816, D], BF16)
        k.dram("h_sp", [128, 22 * TC], BF16)
        k.dram("rowb", [128, 32])
        k.dram("cc_gm_in", [1, 128])
        k.dram("cc_gm_out", [1, 128])
        k.dram("cc_kv_in", [128, 16])
        k.dram("cc_kv_out", [128, 16])
        k.dram("cc_hm_in", [1, 128])
        k.dram("cc_hm_out", [1, 128])
        for nm, t_ in (("bq2", BQ2), ("bk2", BK2), ("bv2", BV2), ("bo2", BO2)):
            k.dma("sy", t_[:], ap[nm], ins=["d:" + nm], outs=[nm.upper()])
        k.dma("sy", CLP[:], ap["clip2"], ins=["d:clip2"], outs=["CLP"])
        k.se(lambda h: h.activation(CLP[:], CLP[:], ACT.Sigmoid), ins=["CLP"], outs=["CLP"])

        def absmax_to_scalar(col_ap, col_name, dst_sl):
            k.ve(lambda h: h.memset(XST[:, 0:128], 0.0), outs=["XST"])
            k.ve(lambda h: h.tensor_copy(XST[:, 0:col_ap.shape[1]], col_ap),
                 ins=[col_name], outs=["XST"])
            k.tr(PE[:, 0:128], XST[:, 0:128], ins=["XST"], outs=["PE"])
            k.ve(lambda h: h.reduce_max(XST2[:, 0:1], PE[:, 0:128], AX.X),
                 ins=["PE"], outs=["XST2"])
            k.ve(lambda h: h.memset(XST[:, 0:128], 0.0), outs=["XST"])
            k.ve(lambda h: h.tensor_copy(XST[:, 0:1], XST2[:, 0:1]), ins=["XST2"], outs=["XST"])
            k.tr(PE[:, 0:128], XST[:, 0:128], ins=["XST"], outs=["PE"])
            k.ve(lambda h: h.memset(ROWS[0:1, 0:128], 0.0), outs=["ROWS"])
            k.ve(lambda h: h.reduce_max(ROWS[0:1, 0:1], PE[0:1, 0:128], AX.X),
                 ins=["PE"], outs=["ROWS"])

        def quant_weights(asis, trans, spill, nrow_tiles, ncols, scol, scoln):
            for t8 in range(nrow_tiles):
                k.dma("sy", WTQ[:, 0:ncols], ap[asis][t8 * 128:(t8 + 1) * 128, :],
                      ins=["d:" + asis], outs=["WTQ"])
                k.ve(lambda h, t8=t8, ncols=ncols: h.reduce_max(
                    scol[:, t8:t8 + 1], WTQ[:, 0:ncols], AX.X, apply_absolute_value=True),
                    ins=["WTQ"], outs=[scoln])
            k.ve(lambda h, n=nrow_tiles: h.tensor_scalar_mul(scol[:, 0:n], scol[:, 0:n],
                                                             1.0 / 127.0),
                 ins=[scoln], outs=[scoln])
            k.dma("ge", ap["rowb"][:, 0:nrow_tiles], scol[:, 0:nrow_tiles],
                  ins=[scoln], outs=["d:rowb"])
            nq = nrow_tiles * 128
            for cc in range((nq + 511) // 512):
                w = min(512, nq - cc * 512)
                for j in range((w + 127) // 128):
                    t8 = cc * 4 + j
                    k.p.emit("sy", lambda h, j=j, t8=t8: h.dma_start(
                        out=ROWS[0:1, j * 128:(j + 1) * 128],
                        in_=ap["rowb"].rearrange("p a -> a p")[t8:t8 + 1, :],
                        allow_slow_non_contiguous=True),
                        ins=["d:rowb"], outs=["ROWS"], dma=True)
                k.ve(lambda h, w=w: h.reciprocal(ROWS[0:1, 0:w], ROWS[0:1, 0:w]),
                     ins=["ROWS"], outs=["ROWS"])
                k.bcast_row(PB[:, 0:w], ROWS[0:1, 0:w], ins=["ROWS"], outs=["PB"])
                k.ve(lambda h, cc=cc, w=w: h.tensor_copy(RECB[:, cc * 512:cc * 512 + w],
                                                         PB[:, 0:w]), ins=["PB"], outs=["RECB"])
            ntile = (ap[trans].shape[0]) // 128
            for t8 in range(ntile):
                k.dma("sy", WTQ[:, 0:nq], ap[trans][t8 * 128:(t8 + 1) * 128, :],
                      ins=["d:" + trans], outs=["WTQ"])
                k.ve(lambda h, nq=nq: h.tensor_tensor(WTQ[:, 0:nq], WTQ[:, 0:nq],
                                                      RECB[:, 0:nq], ALU.mult),
                     ins=["WTQ", "RECB"], outs=["WTQ"])
                k.ve(lambda h, nq=nq: h.tensor_copy(WTQI[:, 0:nq], WTQ[:, 0:nq]),
                     ins=["WTQ"], outs=["WTQI"])
                k.ve(lambda h, nq=nq: h.tensor_scalar(WTQB[:, 0:nq], WTQI[:, 0:nq],
                                                      -128.0, 127.0, ALU.max, ALU.min),
                     ins=["WTQI"], outs=["WTQB"])
                k.dma("ge", ap[spill][t8 * 128:(t8 + 1) * 128, :], WTQB[:, 0:nq],
                      ins=["WTQB"], outs=["d:" + spill])

        if run_g2:
            quant_weights("wg2", "wg2T", "wgq_sp", 22, 1024, SCG, "SCG")
            quant_weights("wu2", "wu2T", "wuq_sp", 22, 1024, SCU, "SCU")
            quant_weights("wd2", "wd2T", "wdq_sp", 8, 2816, SCD, "SCD")

            for q in range(8):
                k.ve(lambda h, q=q: h.reduce_max(COLS2[:, 1:2], XNB[:, q * TC:(q + 1) * TC],
                                                 AX.X, apply_absolute_value=True),
                     ins=["XNB"], outs=["COLS2"])
                if q == 0:
                    k.ve(lambda h: h.tensor_copy(COLS2[:, 0:1], COLS2[:, 1:2]),
                         ins=["COLS2"], outs=["COLS2"])
                else:
                    k.ve(lambda h: h.tensor_tensor(COLS2[:, 0:1], COLS2[:, 0:1],
                                                   COLS2[:, 1:2], ALU.max),
                         ins=["COLS2"], outs=["COLS2"])
            absmax_to_scalar(COLS2[:, 0:1], "COLS2", np.s_[0:128])
            k.dma("ge", ap["cc_gm_in"], ROWS[0:1, 0:128], ins=["ROWS"], outs=["d:cc_gm_in"])
            k.cc("AllReduce", ALU.max, ALL8, ap["cc_gm_in"], ap["cc_gm_out"],
                 ins=["d:cc_gm_in"], outs=["d:cc_gm_out"])

        p.barrier()
        l2q.close()
        l2a = ExitStack()
        QF2 = k.sbuf("QF2", [128, 16384], BF16, scope=l2a)
        if run_attn2:
            k.ve(lambda h: h.memset(KV2[:], 0.0), outs=["KV2"])
            for tc4 in range(4):
                t0 = tc4 * 512
                for mc in range(8):
                    for wname, widx in (("wq2", 0), ("wk2", 1), ("wv2", 2)):
                        wb, wbn, nk = load_w_block(wname, mc * 128, 128, idx=mc * 3 + widx)
                        bank = PB if widx == 0 else (PC if widx == 1 else PD)
                        bn = "PB" if widx == 0 else ("PC" if widx == 1 else "PD")
                        for dc in range(8):
                            k.mm(bank[:, 0:512], wb[:, dc * 128:(dc + 1) * 128],
                                 XNB[:, dc * TC + t0: dc * TC + t0 + 512],
                                 ins=[wbn, "XNB"], outs=[bn],
                                 start=(dc == 0), stop=(dc == 7))
                    k.ve(lambda h, mc=mc: h.tensor_scalar(
                        QT2[:], PB[:, 0:512], BQ2[:, mc:mc + 1], None, ALU.add),
                        ins=["PB", "BQ2"], outs=["QT2"])
                    k.ve(lambda h: h.tensor_scalar_min(FM2[:], QT2[:], 0.0),
                         ins=["QT2"], outs=["FM2"])
                    k.se(lambda h: h.activation(FM3[:], FM2[:], ACT.Exp),
                         ins=["FM2"], outs=["FM3"])
                    k.p.emit("ve", lambda h, mc=mc, t0=t0: h.scalar_tensor_tensor(
                        QF2[:, mc * TC + t0: mc * TC + t0 + 512], QT2[:], 0.0, FM3[:],
                        ALU.max, ALU.add), ins=["QT2", "FM3"], outs=["QF2"])
                    k.ve(lambda h, mc=mc: h.tensor_scalar(
                        QT2[:], PC[:, 0:512], BK2[:, mc:mc + 1], None, ALU.add),
                        ins=["PC", "BK2"], outs=["QT2"])
                    k.ve(lambda h: h.tensor_scalar_min(FM2[:], QT2[:], 0.0),
                         ins=["QT2"], outs=["FM2"])
                    k.se(lambda h: h.activation(FM3[:], FM2[:], ACT.Exp),
                         ins=["FM2"], outs=["FM3"])
                    k.p.emit("ve", lambda h: h.scalar_tensor_tensor(
                        KT2[:], QT2[:], 0.0, FM3[:], ALU.max, ALU.add),
                        ins=["QT2", "FM3"], outs=["KT2"])
                    k.ve(lambda h, mc=mc: h.tensor_scalar(
                        VT2[:], PD[:, 0:512], BV2[:, mc:mc + 1], None, ALU.add),
                        ins=["PD", "BV2"], outs=["VT2"])
                    k.ve(lambda h: h.tensor_tensor(FM2[:], KT2[:], VT2[:], ALU.mult),
                         ins=["KT2", "VT2"], outs=["FM2"])
                    k.ve(lambda h: h.reduce_sum(COLS2[:, 2:3], FM2[:], AX.X),
                         ins=["FM2"], outs=["COLS2"])
                    k.ve(lambda h, mc=mc: h.tensor_tensor(KV2[:, mc:mc + 1], KV2[:, mc:mc + 1],
                                                          COLS2[:, 2:3], ALU.add),
                         ins=["COLS2", "KV2"], outs=["KV2"])
                    k.ve(lambda h: h.reduce_sum(COLS2[:, 2:3], KT2[:], AX.X),
                         ins=["KT2"], outs=["COLS2"])
                    k.ve(lambda h, mc=mc: h.tensor_tensor(KV2[:, 8 + mc:9 + mc],
                                                          KV2[:, 8 + mc:9 + mc],
                                                          COLS2[:, 2:3], ALU.add),
                         ins=["COLS2", "KV2"], outs=["KV2"])
            k.dma("ge", ap["cc_kv_in"], KV2[:], ins=["KV2"], outs=["d:cc_kv_in"])
            k.cc("AllReduce", ALU.add, PAIRS, ap["cc_kv_in"], ap["cc_kv_out"],
                 ins=["d:cc_kv_in"], outs=["d:cc_kv_out"])
            k.dma("sy", KV2[:], ap["cc_kv_out"], ins=["d:cc_kv_out"], outs=["KV2"])
            VF2 = k.sbuf("VF2", [128, 16384], BF16, scope=l2a)
            for tc4 in range(4):
                t0 = tc4 * 512
                for mc in range(8):
                    k.ve(lambda h, mc=mc, t0=t0: h.tensor_scalar(
                        FM2[:], QF2[:, mc * TC + t0: mc * TC + t0 + 512],
                        KV2[:, 8 + mc:9 + mc], 1e-6, ALU.mult, ALU.add),
                        ins=["QF2", "KV2"], outs=["FM2"])
                    k.ve(lambda h: h.reciprocal(FM2[:], FM2[:]), ins=["FM2"], outs=["FM2"])
                    k.ve(lambda h, mc=mc, t0=t0: h.tensor_scalar(
                        FM3[:], QF2[:, mc * TC + t0: mc * TC + t0 + 512],
                        KV2[:, mc:mc + 1], None, ALU.mult),
                        ins=["QF2", "KV2"], outs=["FM3"])
                    k.ve(lambda h, mc=mc, t0=t0: h.tensor_tensor(
                        VF2[:, mc * TC + t0: mc * TC + t0 + 512], FM3[:], FM2[:], ALU.mult),
                        ins=["FM3", "FM2"], outs=["VF2"])
            for tc4 in range(4):
                t0 = tc4 * 512
                for mc in range(8):
                    wb, wbn, nk = load_w_block("wo2", mc * 128, 128, idx=mc)
                    for dc in range(8):
                        k.mm(PB[:, 0:512], wb[:, dc * 128:(dc + 1) * 128],
                             VF2[:, dc * TC + t0: dc * TC + t0 + 512],
                             ins=[wbn, "VF2"], outs=["PB"], start=(dc == 0), stop=(dc == 7))
                    k.se(lambda h, mc=mc: h.activation(
                        SQB[:, mc * 512:(mc + 1) * 512], PB[:, 0:512], ACT.Gelu_apprx_tanh,
                        bias=BO2[:, mc:mc + 1]), ins=["PB", "BO2"], outs=["SQB"])
                k.dma("ge", ap["s_sp"].rearrange("p (a t) -> p a t", t=TC)[:, :, t0:t0 + 512],
                      SQB[:].rearrange("p (a t) -> p a t", t=512), ins=["SQB"], outs=["d:s_sp"])
        if DBG == "s2":
            dump_bf16_to_xr("s_sp")

        p.barrier()
        l2a.close()
        l2f = ExitStack()
        XQ2 = k.sbuf("XQ2", [128, 16384], BF16, scope=l2f)
        HQF = k.sbuf("HQF", [128, 22 * 512], BF16, scope=l2f)
        if run_g2:
            k.dma("sy", ROWS[0:1, 0:128], ap["cc_gm_out"], ins=["d:cc_gm_out"], outs=["ROWS"])
            k.ve(lambda h: h.tensor_tensor(ROWS[0:1, 1:2], ROWS[0:1, 0:1], CLP[0:1, 0:1],
                                           ALU.mult), ins=["ROWS", "CLP"], outs=["ROWS"])
            k.ve(lambda h: h.tensor_scalar_mul(ROWS[0:1, 1:2], ROWS[0:1, 1:2], 1.0 / 127.0),
                 ins=["ROWS"], outs=["ROWS"])
            k.ve(lambda h: h.reciprocal(ROWS[0:1, 2:3], ROWS[0:1, 1:2]),
                 ins=["ROWS"], outs=["ROWS"])
            k.bcast_row(PB[:, 0:1], ROWS[0:1, 1:2], ins=["ROWS"], outs=["PB"])
            k.ve(lambda h: h.tensor_copy(COLX[:, 0:1], PB[:, 0:1]), ins=["PB"], outs=["COLX"])
            k.bcast_row(PB[:, 0:1], ROWS[0:1, 2:3], ins=["ROWS"], outs=["PB"])
            k.ve(lambda h: h.tensor_copy(COLX[:, 1:2], PB[:, 0:1]), ins=["PB"], outs=["COLX"])
            for q in range(8):
                for tc4 in range(4):
                    sl = np.s_[:, q * TC + tc4 * 512: q * TC + (tc4 + 1) * 512]
                    k.ve(lambda h, sl=sl: h.tensor_scalar(
                        FM2[:], XNB[sl], COLX[:, 1:2], None, ALU.mult),
                        ins=["XNB", "COLX"], outs=["FM2"])
                    k.ve(lambda h: h.tensor_copy(HIQ[:], FM2[:]), ins=["FM2"], outs=["HIQ"])
                    k.ve(lambda h, sl=sl: h.tensor_scalar(
                        XQ2[sl], HIQ[:], -128.0, 127.0, ALU.max, ALU.min),
                        ins=["HIQ"], outs=["XQ2"])
            k.ve(lambda h: h.memset(COLS2[:, 4:5], 0.0), outs=["COLS2"])
            for tc4 in range(4):
                t0 = tc4 * 512
                for ic in range(22):
                    for wname, widx in (("wgq_sp", 0), ("wuq_sp", 1)):
                        wbb, wbn = (WB0, "WB0") if widx == 0 else (WB1, "WB1")
                        k.dma("sy", wbb[:, 0:1024].rearrange("p (a m) -> p a m", m=128),
                              ap[wname].rearrange("(a p) m -> p a m", p=128)
                              [:, :, ic * 128:(ic + 1) * 128],
                              ins=["d:" + wname], outs=[wbn])
                        bank, bn = (PB, "PB") if widx == 0 else (PC, "PC")
                        for dc in range(8):
                            k.mm(bank[:, 0:512], wbb[:, dc * 128:(dc + 1) * 128],
                                 XQ2[:, dc * TC + t0: dc * TC + t0 + 512],
                                 ins=[wbn, "XQ2"], outs=[bn],
                                 start=(dc == 0), stop=(dc == 7))
                    k.ve(lambda h, ic=ic: h.tensor_tensor(
                        COLS2[:, 2:3], SCG[:, ic:ic + 1], COLX[:, 0:1], ALU.mult),
                        ins=["SCG", "COLX"], outs=["COLS2"])
                    k.se(lambda h: h.activation(HSG[:], PB[:, 0:512], ACT.Silu,
                                                scale=COLS2[:, 2:3]),
                         ins=["PB", "COLS2"], outs=["HSG"])
                    k.ve(lambda h, ic=ic: h.tensor_tensor(
                        COLS2[:, 3:4], SCU[:, ic:ic + 1], COLX[:, 0:1], ALU.mult),
                        ins=["SCU", "COLX"], outs=["COLS2"])
                    k.ve(lambda h: h.tensor_tensor(FM2[:], HSG[:], PC[:, 0:512], ALU.mult),
                         ins=["HSG", "PC"], outs=["FM2"])
                    k.ve(lambda h: h.tensor_scalar(FM3[:], FM2[:], COLS2[:, 3:4], None,
                                                   ALU.mult), ins=["FM2", "COLS2"], outs=["FM3"])
                    k.ve(lambda h: h.tensor_copy(HSG[:], FM3[:]), ins=["FM3"], outs=["HSG"])
                    k.dma("ge", ap["h_sp"].rearrange("p (a t) -> p a t", t=TC)
                          [:, ic, t0:t0 + 512], HSG[:], ins=["HSG"], outs=["d:h_sp"])
                    k.ve(lambda h: h.reduce_max(COLS2[:, 5:6], FM3[:], AX.X,
                                                apply_absolute_value=True),
                         ins=["FM3"], outs=["COLS2"])
                    k.ve(lambda h: h.tensor_tensor(COLS2[:, 4:5], COLS2[:, 4:5],
                                                   COLS2[:, 5:6], ALU.max),
                         ins=["COLS2"], outs=["COLS2"])
            absmax_to_scalar(COLS2[:, 4:5], "COLS2", np.s_[0:128])
            k.dma("ge", ap["cc_hm_in"], ROWS[0:1, 0:128], ins=["ROWS"], outs=["d:cc_hm_in"])
            k.cc("AllReduce", ALU.max, ALL8, ap["cc_hm_in"], ap["cc_hm_out"],
                 ins=["d:cc_hm_in"], outs=["d:cc_hm_out"])
            k.dma("sy", ROWS[0:1, 0:128], ap["cc_hm_out"], ins=["d:cc_hm_out"], outs=["ROWS"])
            k.ve(lambda h: h.tensor_tensor(ROWS[0:1, 1:2], ROWS[0:1, 0:1], CLP[0:1, 1:2],
                                           ALU.mult), ins=["ROWS", "CLP"], outs=["ROWS"])
            k.ve(lambda h: h.tensor_scalar_mul(ROWS[0:1, 1:2], ROWS[0:1, 1:2], 1.0 / 127.0),
                 ins=["ROWS"], outs=["ROWS"])
            k.ve(lambda h: h.reciprocal(ROWS[0:1, 2:3], ROWS[0:1, 1:2]),
                 ins=["ROWS"], outs=["ROWS"])
            k.bcast_row(PB[:, 0:1], ROWS[0:1, 1:2], ins=["ROWS"], outs=["PB"])
            k.ve(lambda h: h.tensor_copy(COLX[:, 2:3], PB[:, 0:1]), ins=["PB"], outs=["COLX"])
            k.bcast_row(PB[:, 0:1], ROWS[0:1, 2:3], ins=["ROWS"], outs=["PB"])
            k.ve(lambda h: h.tensor_copy(COLX[:, 3:4], PB[:, 0:1]), ins=["PB"], outs=["COLX"])
            for tc4 in range(4):
                t0 = tc4 * 512
                for ic in range(22):
                    k.dma("sy", HSG[:], ap["h_sp"].rearrange("p (a t) -> p a t", t=TC)
                          [:, ic, t0:t0 + 512], ins=["d:h_sp"], outs=["HSG"])
                    k.ve(lambda h: h.tensor_scalar(FM2[:], HSG[:], COLX[:, 3:4], None,
                                                   ALU.mult), ins=["HSG", "COLX"], outs=["FM2"])
                    k.ve(lambda h: h.tensor_copy(HIQ[:], FM2[:]), ins=["FM2"], outs=["HIQ"])
                    k.ve(lambda h, ic=ic: h.tensor_scalar(
                        HQF[:, ic * 512:(ic + 1) * 512], HIQ[:], -128.0, 127.0,
                        ALU.max, ALU.min), ins=["HIQ"], outs=["HQF"])
                for mc in range(8):
                    for ic in range(22):
                        wbb, wbn = (WB0, "WB0") if ic % 2 == 0 else (WB1, "WB1")
                        k.dma("sy", wbb[:, 0:128],
                              ap["wdq_sp"][ic * 128:(ic + 1) * 128, mc * 128:(mc + 1) * 128],
                              ins=["d:wdq_sp"], outs=[wbn])
                        k.mm(PB[:, 0:512], wbb[:, 0:128],
                             HQF[:, ic * 512:(ic + 1) * 512],
                             ins=[wbn, "HQF"], outs=["PB"],
                             start=(ic == 0), stop=(ic == 21))
                    k.ve(lambda h, mc=mc: h.tensor_tensor(
                        COLS2[:, 2:3], SCD[:, mc:mc + 1], COLX[:, 2:3], ALU.mult),
                        ins=["SCD", "COLX"], outs=["COLS2"])
                    k.se(lambda h, mc=mc: h.activation(
                        SQB[:, mc * 512:(mc + 1) * 512], PB[:, 0:512], ACT.Silu,
                        scale=COLS2[:, 2:3]), ins=["PB", "COLS2"], outs=["SQB"])
                k.dma("ge", ap["g_sp"].rearrange("p (a t) -> p a t", t=TC)[:, :, t0:t0 + 512],
                      SQB[:].rearrange("p (a t) -> p a t", t=512), ins=["SQB"], outs=["d:g_sp"])
        if DBG == "g2":
            dump_bf16_to_xr("g_sp")
        if DBG is None:
            emit_combine()
        p.barrier()
        l2f.close()
        l2.close()


    if NLAYERS >= 4:
        emit_layernorm()
        if DBG == "xn3":
            dump_xnb_to_xr()
        run_g3 = DBG in (None, "g3", "s3")
        run_attn3 = DBG in (None, "s3", "m1", "m1xc", "m1y", "m1z", "m1xzb", "m1xcb", "m1yg", "m1xmr")
        l3 = ExitStack()
        NW3 = k.sbuf("NW3", [128, 16], F32, scope=l3)
        CW3 = k.sbuf("CW3", [128, 128], F32, scope=l3)
        CB3 = k.sbuf("CB3", [128, 32], F32, scope=l3)
        DTB3 = k.sbuf("DTB3", [128, 32], F32, scope=l3)
        AL3 = k.sbuf("AL3", [128, 512], F32, scope=l3)
        DP3 = k.sbuf("DP3", [128, 32], F32, scope=l3)
        B13 = k.sbuf("B13", [128, 32], F32, scope=l3)
        B23 = k.sbuf("B23", [128, 8], F32, scope=l3)
        XT3 = k.sbuf("XT3", [128, 24], F32, scope=l3)
        HINC = k.sbuf("HINC", [128, 256], F32, scope=l3)
        k.dram("xi_sp", [128, 8 * TC])
        k.dram("xz_sp", [128, 16 * TC], BF16)
        k.dram("xc_sp", [128, 16 * TC], BF16)
        k.dram("dc_sp", [128, 16 * TC], BF16)
        k.dram("y_sp", [128, 16 * TC], BF16)
        k.dram("cbc_sp", [128, 4 * 8192], BF16)
        k.dram("dbg_sp", [128, 16 * TC], BF16)
        k.dram("cc_t3_in", [128, 128])
        k.dram("cc_t3_out", [2, 128, 128])
        k.dram("cc_st_in", [128, 256])
        k.dram("cc_st_out", [2, 128, 256])
        for nm, t_ in (("nw3", NW3), ("cw3", CW3), ("cb3", CB3), ("dtb3", DTB3),
                       ("al3", AL3), ("dp3", DP3)):
            src = ap[nm]
            w = src.shape[2]
            k.dma("sy", t_[:, 0:w], src[0], ins=["d:" + nm], outs=[nm.upper()])
            k.dma("sy", t_[:, w:2 * w], src[1], ins=["d:" + nm], outs=[nm.upper()])
        k.se(lambda h: h.activation(AL3[:], AL3[:], ACT.Exp), ins=["AL3"], outs=["AL3"])
        k.ve(lambda h: h.tensor_scalar_mul(AL3[:], AL3[:], -1.0), ins=["AL3"], outs=["AL3"])
        k.dma("sy", B13[:], ap["b13"], ins=["d:b13"], outs=["B13"])
        k.dma("sy", B23[:], ap["b23"], ins=["d:b23"], outs=["B23"])

        if run_g3:
            l3g = ExitStack()
            H13 = k.sbuf("H13", [128, 32 * 512], BF16, scope=l3g)
            for tc4 in range(4):
                t0 = tc4 * 512
                for mf in range(32):
                    wb, wbn, nk = load_w_block("w13", mf * 128, 128, idx=mf)
                    for dc in range(8):
                        k.mm(PB[:, 0:512], wb[:, dc * 128:(dc + 1) * 128],
                             XNB[:, dc * TC + t0: dc * TC + t0 + 512],
                             ins=[wbn, "XNB"], outs=["PB"], start=(dc == 0), stop=(dc == 7))
                    k.se(lambda h, mf=mf: h.activation(
                        H13[:, mf * 512:(mf + 1) * 512], PB[:, 0:512], ACT.Gelu_apprx_tanh,
                        bias=B13[:, mf:mf + 1]), ins=["PB", "B13"], outs=["H13"])
                for mc in range(8):
                    for ff in range(32):
                        wbb, wbn = (WB0, "WB0") if ff % 2 == 0 else (WB1, "WB1")
                        k.dma("ge", wbb[:, 0:128],
                              ap["w23"][ff * 128:(ff + 1) * 128, mc * 128:(mc + 1) * 128],
                              ins=["d:w23"], outs=[wbn])
                        k.mm(PC[:, 0:512], wbb[:, 0:128],
                             H13[:, ff * 512:(ff + 1) * 512],
                             ins=[wbn, "H13"], outs=["PC"], start=(ff == 0), stop=(ff == 31))
                    k.se(lambda h, mc=mc: h.activation(
                        SQB[:, mc * 512:(mc + 1) * 512], PC[:, 0:512], ACT.Silu,
                        bias=B23[:, mc:mc + 1]), ins=["PC", "B23"], outs=["SQB"])
                k.dma("ge", ap["g_sp"].rearrange("p (a t) -> p a t", t=TC)[:, :, t0:t0 + 512],
                      SQB[:].rearrange("p (a t) -> p a t", t=512), ins=["SQB"], outs=["d:g_sp"])
            p.barrier()
            l3g.close()
        if DBG == "g3":
            dump_bf16_to_xr("g_sp")

        if run_attn3:
            for q in range(8):
                for tc4 in range(4):
                    st, stn = (XST, "XST") if tc4 % 2 == 0 else (XST2, "XST2")
                    k.ve(lambda h, st=st, q=q, tc4=tc4: h.tensor_copy(
                        st[:], XNB[:, q * TC + tc4 * 512: q * TC + (tc4 + 1) * 512]),
                        ins=["XNB"], outs=[stn])
                    k.dma("ge", ap["xi_sp"].rearrange("p (a t) -> p a t", t=TC)
                          [:, q, tc4 * 512:(tc4 + 1) * 512], st[:], ins=[stn], outs=["d:xi_sp"])

            TCK = 256
            NCH = TC // TCK
            l3m = ExitStack()
            XMR = k.sbuf("XMR", [128, 8 * TCK], BF16, scope=l3m)
            XSC = k.sbuf("XSC", [128, 8 * TCK], F32, scope=l3m)
            XIC = k.sbuf("XIC", [128, TCK], F32, scope=l3m)
            XIC2 = k.sbuf("XIC2", [128, TCK], F32, scope=l3m)
            XZB = k.sbuf("XZB", [128, 16 * TCK], BF16, scope=l3m)
            XCB = k.sbuf("XCB", [128, 16 * TCK], BF16, scope=l3m)
            UALL = k.sbuf("UALL", [128, 16 * TCK], BF16, scope=l3m)
            TAIL = k.sbuf("TAIL", [128, 48], BF16, scope=l3m)
            XZT = k.sbuf("XZT", [128, 48], F32, scope=l3m)
            DBC = k.sbuf("DBC", [128, TCK], F32, scope=l3m)
            DBCB = k.sbuf("DBCB", [128, TCK], BF16, scope=l3m)
            DLT = k.sbuf("DLT", [128, 16 * TCK], BF16, scope=l3m)
            DCC = k.sbuf("DCC", [128, 16 * TCK], BF16, scope=l3m)
            DCAR = k.sbuf("DCAR", [128, 17], F32, scope=l3m)
            BBALL = k.sbuf("BBALL", [128, 16 * TCK], BF16, scope=l3m)
            CCALL = k.sbuf("CCALL", [128, 16 * TCK], BF16, scope=l3m)
            DAT = k.sbuf("DAT", [128, TCK], F32, scope=l3m)
            DAT2 = k.sbuf("DAT2", [128, TCK], F32, scope=l3m)
            BXALL = k.sbuf("BXALL", [128, 16 * TCK], BF16, scope=l3m)
            HSALL = k.sbuf("HSALL", [128, 16 * TCK], F32, scope=l3m)
            DATB = k.sbuf("DATB", [128, TCK], BF16, scope=l3m)
            DATB2 = k.sbuf("DATB2", [128, TCK], BF16, scope=l3m)
            CARRY = k.sbuf("CARRY", [128, 256], F32, scope=l3m)
            YLBB = k.sbuf("YLBB", [128, TCK], BF16, scope=l3m)
            YLBB2 = k.sbuf("YLBB2", [128, TCK], BF16, scope=l3m)
            YTALL = k.sbuf("YTALL", [128, 16 * TCK], BF16, scope=l3m)
            YT = k.sbuf("YT", [128, TCK], F32, scope=l3m)
            ZSL = k.sbuf("ZSL", [128, TCK], F32, scope=l3m)
            YGALL = k.sbuf("YGALL", [128, 16 * TCK], BF16, scope=l3m)

            def mamba_layer(li):
                c16 = li * 16
                for dc in range(8):
                    k.dma("sy", XST2[:, dc * 3:dc * 3 + 3],
                          ap["xi_sp"].rearrange("p (a t) -> p a t", t=TC)[:, dc, 2045:2048],
                          ins=["d:xi_sp"], outs=["XST2"])
                k.ve(lambda h: h.memset(XST2[:, 24:128], 0.0), outs=["XST2"])
                k.dma("ge", ap["cc_t3_in"], XST2[:, 0:128], ins=["XST2"], outs=["d:cc_t3_in"])
                k.cc("AllGather", ALU.bypass, PAIRS, ap["cc_t3_in"], ap["cc_t3_out"],
                     ins=["d:cc_t3_in"], outs=["d:cc_t3_out"])
                k.dma("sy", XT3[:, 0:24], ap["cc_t3_out"][0][:, 0:24],
                      ins=["d:cc_t3_out"], outs=["XT3"])
                k.ve(lambda h: h.tensor_scalar(XT3[:], XT3[:], HSEL[:, 0:1], None, ALU.mult),
                     ins=["XT3", "HSEL"], outs=["XT3"])
                for tc4 in range(NCH):
                    t0 = tc4 * TCK
                    for dc in range(8):
                        k.dma("sy", XIC[:], ap["xi_sp"].rearrange("p (a t) -> p a t", t=TC)
                              [:, dc, t0:t0 + TCK], ins=["d:xi_sp"], outs=["XIC"])
                        k.ve(lambda h: h.tensor_tensor(XST[:, 0:TCK], XIC[:], XIC[:],
                                                       ALU.mult), ins=["XIC"], outs=["XST"])
                        k.ve(lambda h: h.tensor_copy(SQB[:, 0:TCK], XST[:, 0:TCK]),
                             ins=["XST"], outs=["SQB"])
                        k.mm(PB[0:1, 0:TCK], ONES[:], SQB[:, 0:TCK],
                             ins=["ONESB", "SQB"], outs=["PB"],
                             start=(dc == 0), stop=(dc == 7))
                        k.ve(lambda h, dc=dc: h.tensor_scalar(
                            XSC[:, dc * TCK:(dc + 1) * TCK], XIC[:],
                            NW3[:, li * 8 + dc:li * 8 + dc + 1],
                            None, ALU.mult), ins=["XIC", "NW3"], outs=["XSC"])
                    k.ve(lambda h: h.tensor_scalar(ROWS[0:1, 0:TCK], PB[0:1, 0:TCK],
                                                   1.0 / D, 1e-5, ALU.mult, ALU.add),
                         ins=["PB"], outs=["ROWS"])
                    k.ve(lambda h: h.reciprocal(ROWS[0:1, 0:TCK], ROWS[0:1, 0:TCK]),
                         ins=["ROWS"], outs=["ROWS"])
                    k.se(lambda h: h.activation(ROWS[0:1, 0:TCK], ROWS[0:1, 0:TCK], ACT.Sqrt),
                         ins=["ROWS"], outs=["ROWS"])
                    k.bcast_row(PC[:, 0:TCK], ROWS[0:1, 0:TCK], ins=["ROWS"], outs=["PC"])
                    for dc in range(8):
                        k.ve(lambda h, dc=dc: h.tensor_tensor(
                            XMR[:, dc * TCK:(dc + 1) * TCK], XSC[:, dc * TCK:(dc + 1) * TCK],
                            PC[:, 0:TCK], ALU.mult), ins=["XSC", "PC"], outs=["XMR"])
                    if DBG == "m1xmr":
                        k.dma("ge", ap["dbg_sp"].rearrange("p (a t) -> p a t", t=TC)
                              [:, 0:8, t0:t0 + TCK], XMR[:].rearrange("p (a t) -> p a t", t=TCK),
                              ins=["XMR"], outs=["d:dbg_sp"])
                    if tc4 == 0:
                        for dc in range(8):
                            k.ve(lambda h, dc=dc: h.tensor_tensor(
                                XST[:, dc * 3:dc * 3 + 3], XT3[:, dc * 3:dc * 3 + 3],
                                XT3[:, dc * 3:dc * 3 + 3], ALU.mult),
                                ins=["XT3"], outs=["XST"])
                        k.ve(lambda h: h.tensor_copy(SQB[:, 0:24], XST[:, 0:24]),
                             ins=["XST"], outs=["SQB"])
                        for dc in range(8):
                            k.mm(PD[0:1, dc * 3:dc * 3 + 3] if False else PD[0:1, 0:3],
                                 ONES[:], SQB[:, dc * 3:dc * 3 + 3],
                                 ins=["ONESB", "SQB"], outs=["PD"],
                                 start=(dc == 0), stop=(dc == 7))
                        k.ve(lambda h: h.tensor_scalar(ROWS[0:1, TCK:TCK + 3], PD[0:1, 0:3],
                                                       1.0 / D, 1e-5, ALU.mult, ALU.add),
                             ins=["PD"], outs=["ROWS"])
                        k.ve(lambda h: h.reciprocal(ROWS[0:1, TCK:TCK + 3], ROWS[0:1, TCK:TCK + 3]),
                             ins=["ROWS"], outs=["ROWS"])
                        k.se(lambda h: h.activation(ROWS[0:1, TCK:TCK + 3], ROWS[0:1, TCK:TCK + 3],
                                                    ACT.Sqrt), ins=["ROWS"], outs=["ROWS"])
                        k.bcast_row(PD[:, 4:7], ROWS[0:1, TCK:TCK + 3], ins=["ROWS"], outs=["PD"])
                        for dc in range(8):
                            k.p.emit("ve", lambda h, dc=dc: h.scalar_tensor_tensor(
                                XST2[:, dc * 3:dc * 3 + 3], XT3[:, dc * 3:dc * 3 + 3],
                                NW3[:, li * 8 + dc:li * 8 + dc + 1], PD[:, 4:7],
                                ALU.mult, ALU.mult), ins=["XT3", "NW3", "PD"], outs=["XST2"])
                        k.ve(lambda h: h.tensor_copy(GST[:, 0:24], XST2[:, 0:24]),
                             ins=["XST2"], outs=["GST"])
                    for mz in range(32):
                        wb, wbn, nk = load_w_block("inw3_l%d" % li, mz * 128, 128, idx=mz)
                        for dc in range(8):
                            k.mm(PB[:, 0:TCK], wb[:, dc * 128:(dc + 1) * 128],
                                 XMR[:, dc * TCK:(dc + 1) * TCK],
                                 ins=[wbn, "XMR"], outs=["PB"], start=(dc == 0), stop=(dc == 7))
                        if mz < 16:
                            k.ve(lambda h, mz=mz: h.tensor_copy(
                                XZB[:, mz * TCK:(mz + 1) * TCK], PB[:, 0:TCK]),
                                ins=["PB"], outs=["XZB"])
                        else:
                            k.ve(lambda h: h.tensor_copy(DBCB[:, 0:TCK], PB[:, 0:TCK]),
                                 ins=["PB"], outs=["DBCB"])
                            k.dma("sy", ap["xz_sp"].rearrange("p (a t) -> p a t", t=TC)
                                  [:, mz - 16, t0:t0 + TCK], DBCB[:, 0:TCK],
                                  ins=["DBCB"], outs=["d:xz_sp"])
                        if tc4 == 0 and mz < 16:
                            for dc in range(8):
                                k.mm(PC[:, 0:3], wb[:, dc * 128:(dc + 1) * 128],
                                     GST[:, dc * 3:dc * 3 + 3],
                                     ins=[wbn, "GST"], outs=["PC"],
                                     start=(dc == 0), stop=(dc == 7))
                            k.ve(lambda h, mz=mz: h.tensor_copy(
                                XZT[:, mz * 3:(mz + 1) * 3], PC[:, 0:3]),
                                ins=["PC"], outs=["XZT"])
                    if tc4 == 0:
                        k.ve(lambda h: h.tensor_scalar(TAIL[:], XZT[:], HSEL[:, 0:1],
                                                       None, ALU.mult),
                             ins=["XZT", "HSEL"], outs=["TAIL"])
                    for eh in range(16):
                        sl = np.s_[:, eh * TCK:(eh + 1) * TCK]
                        cwc = lambda kk, eh=eh: CW3[:, li * 64 + eh * 4 + kk: li * 64 + eh * 4 + kk + 1]
                        k.ve(lambda h, eh=eh, cwc=cwc: h.tensor_scalar(
                            XST[:, 3:TCK], XZB[:, eh * TCK: eh * TCK + TCK - 3], cwc(0),
                            None, ALU.mult), ins=["XZB", "CW3"], outs=["XST"])
                        k.ve(lambda h, eh=eh, cwc=cwc: h.tensor_scalar(
                            XST[:, 0:3], TAIL[:, eh * 3:(eh + 1) * 3], cwc(0),
                            None, ALU.mult), ins=["TAIL", "CW3"], outs=["XST"])
                        for kk in (1, 2):
                            k.p.emit("ve", lambda h, eh=eh, kk=kk, cwc=cwc: h.scalar_tensor_tensor(
                                XST[:, kk:TCK], XZB[:, eh * TCK: eh * TCK + TCK - kk],
                                cwc(3 - kk), XST[:, kk:TCK], ALU.mult, ALU.add),
                                ins=["XZB", "CW3", "XST"], outs=["XST"])
                            k.p.emit("ve", lambda h, eh=eh, kk=kk, cwc=cwc: h.scalar_tensor_tensor(
                                XST[:, 0:kk], TAIL[:, eh * 3 + 3 - kk: eh * 3 + 3], cwc(3 - kk),
                                XST[:, 0:kk], ALU.mult, ALU.add),
                                ins=["TAIL", "CW3", "XST"], outs=["XST"])
                        k.p.emit("ve", lambda h, sl=sl, cwc=cwc: h.scalar_tensor_tensor(
                            XST[:, 0:TCK], XZB[sl], cwc(3), XST[:, 0:TCK], ALU.mult, ALU.add),
                            ins=["XZB", "CW3", "XST"], outs=["XST"])
                        k.se(lambda h, sl=sl, eh=eh: h.activation(
                            XCB[sl], XST[:, 0:TCK], ACT.Silu,
                            bias=CB3[:, c16 + eh:c16 + eh + 1]), ins=["XST", "CB3"], outs=["XCB"])
                        k.ve(lambda h, eh=eh: h.tensor_copy(
                            TAIL[:, eh * 3:(eh + 1) * 3],
                            XZB[:, eh * TCK + TCK - 3: eh * TCK + TCK]), ins=["XZB"], outs=["TAIL"])
                    k.dma("ge", ap["xc_sp"].rearrange("p (a t) -> p a t", t=TC)
                          [:, :, t0:t0 + TCK], XCB[:].rearrange("p (a t) -> p a t", t=TCK),
                          ins=["XCB"], outs=["d:xc_sp"])
                    if DBG == "m1xzb":
                        k.dma("ge", ap["y_sp"].rearrange("p (a t) -> p a t", t=TC)
                              [:, :, t0:t0 + TCK], XZB[:].rearrange("p (a t) -> p a t", t=TCK),
                              ins=["XZB"], outs=["d:y_sp"])
                    if DBG == "m1xcb":
                        k.dma("ge", ap["y_sp"].rearrange("p (a t) -> p a t", t=TC)
                              [:, :, t0:t0 + TCK], XCB[:].rearrange("p (a t) -> p a t", t=TCK),
                              ins=["XCB"], outs=["d:y_sp"])
                    for eh in range(16):
                        wbb, wbn = (WB0, "WB0") if eh % 2 == 0 else (WB1, "WB1")
                        k.dma("ge", wbb[:, 0:96],
                              ap["xpw3_l%d" % li][eh * 128:(eh + 1) * 128, :],
                              ins=["d:xpw3_l%d" % li], outs=[wbn])
                        k.mm(PD[0:96, 0:TCK], wbb[:, 0:96], XCB[:, eh * TCK:(eh + 1) * TCK],
                             ins=[wbn, "XCB"], outs=["PD"], start=(eh == 0), stop=(eh == 15))
                    k.ve(lambda h: h.tensor_copy(DBC[0:96, 0:TCK], PD[0:96, 0:TCK]),
                         ins=["PD"], outs=["DBC"])
                    k.ve(lambda h: h.tensor_copy(DBCB[0:64, 0:TCK], DBC[0:64, 0:TCK]),
                         ins=["DBC"], outs=["DBCB"])
                    for eh in range(16):
                        wbb, wbn = (WB0, "WB0") if eh % 2 == 0 else (WB1, "WB1")
                        k.dma("ge", wbb[0:64, 0:128],
                              ap["dtw3_l%d" % li][:, eh * 128:(eh + 1) * 128],
                              ins=["d:dtw3_l%d" % li], outs=[wbn])
                        k.mm(PC[:, 0:TCK], wbb[0:64, 0:128], DBCB[0:64, 0:TCK],
                             ins=[wbn, "DBCB"], outs=["PC"])
                        k.ve(lambda h, eh=eh: h.tensor_scalar(
                            XST[:, 0:TCK], PC[:, 0:TCK], DTB3[:, c16 + eh:c16 + eh + 1],
                            None, ALU.add), ins=["PC", "DTB3"], outs=["XST"])
                        k.se(lambda h: h.activation(ZSL[:], XST[:, 0:TCK], ACT.Abs),
                             ins=["XST"], outs=["ZSL"])
                        k.se(lambda h: h.activation(ZSL[:], ZSL[:], ACT.Exp, scale=-1.0),
                             ins=["ZSL"], outs=["ZSL"])
                        k.ve(lambda h: h.tensor_scalar_add(ZSL[:], ZSL[:], 1.0),
                             ins=["ZSL"], outs=["ZSL"])
                        k.se(lambda h: h.activation(ZSL[:], ZSL[:], ACT.Ln),
                             ins=["ZSL"], outs=["ZSL"])
                        k.p.emit("ve", lambda h, eh=eh: h.scalar_tensor_tensor(
                            DLT[:, eh * TCK:(eh + 1) * TCK], XST[:, 0:TCK], 0.0, ZSL[:],
                            ALU.max, ALU.add), ins=["XST", "ZSL"], outs=["DLT"])
                        k.ve(lambda h, eh=eh: h.tensor_tensor(
                            UALL[:, eh * TCK:(eh + 1) * TCK], DLT[:, eh * TCK:(eh + 1) * TCK],
                            XCB[:, eh * TCK:(eh + 1) * TCK], ALU.mult),
                            ins=["DLT", "XCB"], outs=["UALL"])
                        if tc4 == 0:
                            k.ve(lambda h, eh=eh: h.memset(DCAR[:, eh:eh + 1], 0.0),
                                 outs=["DCAR"])
                        k.p.emit("ve", lambda h, eh=eh: h.tensor_tensor_scan(
                            DCC[:, eh * TCK:(eh + 1) * TCK], DLT[:, eh * TCK:(eh + 1) * TCK],
                            DLT[:, eh * TCK:(eh + 1) * TCK], DCAR[:, eh:eh + 1],
                            ALU.add, ALU.bypass), ins=["DLT", "DCAR"], outs=["DCC"])
                        k.ve(lambda h, eh=eh: h.reduce_sum(
                            DCAR[:, 16:17], DLT[:, eh * TCK:(eh + 1) * TCK], AX.X),
                            ins=["DLT"], outs=["DCAR"])
                        k.ve(lambda h, eh=eh: h.tensor_tensor(
                            DCAR[:, eh:eh + 1], DCAR[:, eh:eh + 1], DCAR[:, 16:17], ALU.add),
                            ins=["DCAR"], outs=["DCAR"])
                    k.dma("ge", ap["dc_sp"].rearrange("p (a t) -> p a t", t=TC)
                          [:, :, t0:t0 + TCK], DCC[:].rearrange("p (a t) -> p a t", t=TCK),
                          ins=["DCC"], outs=["d:dc_sp"])
                    for n in range(16):
                        for src_row, dst, dstn in ((64 + n, BBALL, "BBALL"),
                                                   (80 + n, CCALL, "CCALL")):
                            k.dma("ge", ROWS[0:1, 0:TCK], DBC[src_row:src_row + 1, 0:TCK],
                                  ins=["DBC"], outs=["ROWS"])
                            k.bcast_row(PB[:, 0:TCK], ROWS[0:1, 0:TCK],
                                        ins=["ROWS"], outs=["PB"])
                            k.ve(lambda h, n=n, dst=dst: h.tensor_copy(
                                dst[:, n * TCK:(n + 1) * TCK], PB[:, 0:TCK]),
                                ins=["PB"], outs=[dstn])
                    k.dma("ge", ap["cbc_sp"][:, tc4 * 16 * TCK:(tc4 + 1) * 16 * TCK], CCALL[:],
                          ins=["CCALL"], outs=["d:cbc_sp"])
                    if tc4 == 0:
                        k.ve(lambda h: h.memset(CARRY[:], 0.0), outs=["CARRY"])
                    for eh in range(16):
                        k.ve(lambda h, eh=eh: h.tensor_tensor(
                            BXALL[:].rearrange("p (n t) -> p n t", t=TCK),
                            UALL[:, eh * TCK:(eh + 1) * TCK]
                                .rearrange("p (o t) -> p o t", o=1)
                                .broadcast_to([128, 16, TCK]),
                            BBALL[:].rearrange("p (n t) -> p n t", t=TCK), ALU.mult),
                            ins=["UALL", "BBALL"], outs=["BXALL"])
                        for n in range(16):
                            ac = li * 256 + eh * 16 + n
                            cidx = eh * 16 + n
                            da, dan = (DAT, "DAT") if n % 2 == 0 else (DAT2, "DAT2")
                            k.se(lambda h, eh=eh, ac=ac, da=da: h.activation(
                                da[:], DLT[:, eh * TCK:(eh + 1) * TCK], ACT.Exp,
                                scale=AL3[:, ac:ac + 1]), ins=["DLT", "AL3"], outs=[dan])
                            k.p.emit("ve", lambda h, cidx=cidx, n=n, da=da: h.tensor_tensor_scan(
                                HSALL[:, n * TCK:(n + 1) * TCK], da[:],
                                BXALL[:, n * TCK:(n + 1) * TCK], CARRY[:, cidx:cidx + 1],
                                ALU.mult, ALU.add), ins=[dan, "BXALL", "CARRY"], outs=["HSALL"])
                        k.ve(lambda h, eh=eh: h.tensor_copy(
                            CARRY[:, eh * 16:(eh + 1) * 16],
                            HSALL[:].rearrange("p (n t) -> p n t", t=TCK)
                            [:, :, TCK - 1:TCK].rearrange("p n o -> p (n o)")),
                            ins=["HSALL"], outs=["CARRY"])
                        k.ve(lambda h: h.tensor_tensor(HSALL[:], HSALL[:], CCALL[:], ALU.mult),
                             ins=["HSALL", "CCALL"], outs=["HSALL"])
                        for lvl in (8, 4, 2, 1):
                            k.ve(lambda h, lvl=lvl: h.tensor_tensor(
                                HSALL[:, 0:lvl * TCK], HSALL[:, 0:lvl * TCK],
                                HSALL[:, lvl * TCK:2 * lvl * TCK], ALU.add),
                                ins=["HSALL"], outs=["HSALL"])
                        yb, ybn = (YLBB, "YLBB") if eh % 2 == 0 else (YLBB2, "YLBB2")
                        k.ve(lambda h, yb=yb: h.tensor_copy(yb[:], HSALL[:, 0:TCK]),
                             ins=["HSALL"], outs=[ybn])
                        if DBG not in ("m1xzb", "m1xcb"):
                            k.dma("ge", ap["y_sp"].rearrange("p (a t) -> p a t", t=TC)
                                  [:, eh, t0:t0 + TCK], yb[:], ins=[ybn], outs=["d:y_sp"])
                k.dma("ge", ap["cc_st_in"], CARRY[:], ins=["CARRY"], outs=["d:cc_st_in"])
                k.cc("AllGather", ALU.bypass, PAIRS, ap["cc_st_in"], ap["cc_st_out"],
                     ins=["d:cc_st_in"], outs=["d:cc_st_out"])
                k.dma("sy", HINC[:], ap["cc_st_out"][0], ins=["d:cc_st_out"], outs=["HINC"])
                k.ve(lambda h: h.tensor_scalar(HINC[:], HINC[:], HSEL[:, 0:1], None, ALU.mult),
                     ins=["HINC", "HSEL"], outs=["HINC"])
                for tc4 in range(NCH):
                    t0 = tc4 * TCK
                    k.dma("sy", CCALL[:], ap["cbc_sp"][:, tc4 * 16 * TCK:(tc4 + 1) * 16 * TCK],
                          ins=["d:cbc_sp"], outs=["CCALL"])
                    k.dma("sy", DCC[:].rearrange("p (a t) -> p a t", t=TCK),
                          ap["dc_sp"].rearrange("p (a t) -> p a t", t=TC)[:, :, t0:t0 + TCK],
                          ins=["d:dc_sp"], outs=["DCC"])
                    k.dma("sy", XCB[:].rearrange("p (a t) -> p a t", t=TCK),
                          ap["xc_sp"].rearrange("p (a t) -> p a t", t=TC)[:, :, t0:t0 + TCK],
                          ins=["d:xc_sp"], outs=["XCB"])
                    k.dma("sy", XZB[:].rearrange("p (a t) -> p a t", t=TCK),
                          ap["xz_sp"].rearrange("p (a t) -> p a t", t=TC)[:, :, t0:t0 + TCK],
                          ins=["d:xz_sp"], outs=["XZB"])
                    k.dma("sy", YTALL[:].rearrange("p (a t) -> p a t", t=TCK),
                          ap["y_sp"].rearrange("p (a t) -> p a t", t=TC)[:, :, t0:t0 + TCK],
                          ins=["d:y_sp"], outs=["YTALL"])
                    for eh in range(16):
                        for n in range(16):
                            ac = li * 256 + eh * 16 + n
                            cidx = eh * 16 + n
                            da, dan = (DATB, "DATB") if n % 2 == 0 else (DATB2, "DATB2")
                            k.se(lambda h, eh=eh, ac=ac, da=da: h.activation(
                                da[:], DCC[:, eh * TCK:(eh + 1) * TCK], ACT.Exp,
                                scale=AL3[:, ac:ac + 1]), ins=["DCC", "AL3"], outs=[dan])
                            k.p.emit("ve", lambda h, cidx=cidx, n=n, da=da: h.scalar_tensor_tensor(
                                BXALL[:, n * TCK:(n + 1) * TCK],
                                da[:], HINC[:, cidx:cidx + 1],
                                CCALL[:, n * TCK:(n + 1) * TCK], ALU.mult, ALU.mult),
                                ins=[dan, "HINC", "CCALL"], outs=["BXALL"])
                        k.ve(lambda h: h.tensor_tensor(
                            HSALL[:, 0:8 * TCK], BXALL[:, 0:8 * TCK],
                            BXALL[:, 8 * TCK:16 * TCK], ALU.add),
                            ins=["BXALL"], outs=["HSALL"])
                        for lvl in (4, 2, 1):
                            k.ve(lambda h, lvl=lvl: h.tensor_tensor(
                                HSALL[:, 0:lvl * TCK], HSALL[:, 0:lvl * TCK],
                                HSALL[:, lvl * TCK:2 * lvl * TCK], ALU.add),
                                ins=["HSALL"], outs=["HSALL"])
                        k.ve(lambda h, eh=eh: h.tensor_tensor(
                            YT[:], YTALL[:, eh * TCK:(eh + 1) * TCK], HSALL[:, 0:TCK],
                            ALU.add), ins=["YTALL", "HSALL"], outs=["YT"])
                        k.p.emit("ve", lambda h, eh=eh: h.scalar_tensor_tensor(
                            YT[:], XCB[:, eh * TCK:(eh + 1) * TCK],
                            DP3[:, c16 + eh:c16 + eh + 1], YT[:], ALU.mult, ALU.add),
                            ins=["XCB", "DP3", "YT"], outs=["YT"])
                        k.se(lambda h, eh=eh: h.activation(
                            ZSL[:], XZB[:, eh * TCK:(eh + 1) * TCK], ACT.Silu),
                            ins=["XZB"], outs=["ZSL"])
                        k.ve(lambda h, eh=eh: h.tensor_tensor(
                            YGALL[:, eh * TCK:(eh + 1) * TCK], YT[:], ZSL[:], ALU.mult),
                            ins=["YT", "ZSL"], outs=["YGALL"])
                    if DBG == "m1yg":
                        k.dma("ge", ap["dbg_sp"].rearrange("p (a t) -> p a t", t=TC)
                              [:, :, t0:t0 + TCK], YGALL[:].rearrange("p (a t) -> p a t", t=TCK),
                              ins=["YGALL"], outs=["d:dbg_sp"])
                    for mc in range(8):
                        for eh in range(16):
                            wbb, wbn = (WB0, "WB0") if eh % 2 == 0 else (WB1, "WB1")
                            k.dma("ge", wbb[:, 0:128],
                                  ap["opw3_l%d" % li][eh * 128:(eh + 1) * 128,
                                                      mc * 128:(mc + 1) * 128],
                                  ins=["d:opw3_l%d" % li], outs=[wbn])
                            k.mm(PB[:, 0:TCK], wbb[:, 0:128],
                                 YGALL[:, eh * TCK:(eh + 1) * TCK],
                                 ins=[wbn, "YGALL"], outs=["PB"],
                                 start=(eh == 0), stop=(eh == 15))
                        xb, xbn = (XIC, "XIC") if mc % 2 == 0 else (XIC2, "XIC2")
                        k.dma("sy", xb[:], ap["xi_sp"].rearrange("p (a t) -> p a t", t=TC)
                              [:, mc, t0:t0 + TCK], ins=["d:xi_sp"], outs=[xbn])
                        k.ve(lambda h, xb=xb: h.tensor_tensor(xb[:], xb[:], PB[:, 0:TCK],
                                                              ALU.add),
                             ins=["PB", xbn], outs=[xbn])
                        k.dma("ge", ap["xi_sp"].rearrange("p (a t) -> p a t", t=TC)
                              [:, mc, t0:t0 + TCK], xb[:], ins=[xbn], outs=["d:xi_sp"])

            for li in range(1 if DBG in ("m1", "m1xc", "m1y", "m1z", "m1xzb", "m1xcb", "m1yg", "m1xmr") else 2):
                mamba_layer(li)
            p.barrier()
            l3m.close()
        if DBG in ("m1xc", "m1y", "m1z", "m1xzb", "m1xcb", "m1yg", "m1xmr"):
            src = {"m1xc": "xc_sp", "m1y": "y_sp", "m1z": "xz_sp", "m1xzb": "y_sp", "m1xcb": "y_sp", "m1yg": "dbg_sp", "m1xmr": "dbg_sp"}[DBG]
            for dc in range(8):
                for q in range(4):
                    st, stn = (XST, "XST") if q % 2 == 0 else (XST2, "XST2")
                    k.dma("ge", st[:], ap[src].rearrange("p (a t) -> p a t", t=TC)
                          [:, dc, q * 512:(q + 1) * 512], ins=["d:" + src], outs=[stn])
                    k.dma("ge", ap["xr"].rearrange("p (a t) -> p a t", t=TC)
                          [:, dc, q * 512:(q + 1) * 512], st[:], ins=[stn], outs=["d:xr"])
        if DBG == "m1":
            for dc in range(8):
                for q in range(4):
                    st, stn = (XST, "XST") if q % 2 == 0 else (XST2, "XST2")
                    k.dma("sy", st[:], ap["xi_sp"].rearrange("p (a t) -> p a t", t=TC)
                          [:, dc, q * 512:(q + 1) * 512], ins=["d:xi_sp"], outs=[stn])
                    k.dma("ge", ap["xr"].rearrange("p (a t) -> p a t", t=TC)
                          [:, dc, q * 512:(q + 1) * 512], st[:], ins=[stn], outs=["d:xr"])
        if DBG == "s3":
            for dc in range(8):
                for q in range(4):
                    st, stn = (XST, "XST") if q % 2 == 0 else (XST2, "XST2")
                    k.dma("sy", st[:], ap["xi_sp"].rearrange("p (a t) -> p a t", t=TC)
                          [:, dc, q * 512:(q + 1) * 512], ins=["d:xi_sp"], outs=[stn])
                    k.dma("ge", ap["xr"].rearrange("p (a t) -> p a t", t=TC)
                          [:, dc, q * 512:(q + 1) * 512], st[:], ins=[stn], outs=["d:xr"])
        if DBG is None:
            for dc in range(8):
                k.dma("sy", GST[:], ap["g_sp"].rearrange("p (a t) -> p a t", t=TC)[:, dc],
                      ins=["d:g_sp"], outs=["GST"])
                for q in range(4):
                    st, stn = (XST, "XST") if q % 2 == 0 else (XST2, "XST2")
                    k.dma("sy", st[:], ap["xi_sp"].rearrange("p (a t) -> p a t", t=TC)
                          [:, dc, q * 512:(q + 1) * 512], ins=["d:xi_sp"], outs=[stn])
                    k.ve(lambda h, st=st, q=q: h.tensor_tensor(
                        PRODB[:, 0:512], st[:], GST[:, q * 512:(q + 1) * 512], ALU.mult),
                        ins=[stn, "GST"], outs=["PRODB"])
                    st2, stn2 = (XST2, "XST2") if q % 2 == 0 else (XST, "XST")
                    k.dma("sy", st2[:], ap["xr"].rearrange("p (a t) -> p a t", t=TC)
                          [:, dc, q * 512:(q + 1) * 512], ins=["d:xr"], outs=[stn2])
                    k.ve(lambda h, st2=st2: h.tensor_tensor(
                        st2[:], st2[:], PRODB[:, 0:512], ALU.add),
                        ins=[stn2, "PRODB"], outs=[stn2])
                    k.dma("ge", ap["xr"].rearrange("p (a t) -> p a t", t=TC)
                          [:, dc, q * 512:(q + 1) * 512], st2[:], ins=[stn2], outs=["d:xr"])
        p.barrier()
        l3.close()

    p.barrier()
    for tt in range(16):
        tok, tokn = (TOK, "TOK") if tt % 2 == 0 else (TOK2, "TOK2")
        st, stn = (XST, "XST") if tt % 2 == 0 else (XST2, "XST2")
        for dc in range(8):
            k.dma("sy", st[:, 0:128],
                  ap["xr"].rearrange("p (a t) -> p a t", t=TC)
                  [:, dc, tt * 128:(tt + 1) * 128], ins=["d:xr"], outs=[stn])
            bank, bn = (PB, "PB") if dc % 2 == 0 else (PC, "PC")
            k.tr(bank[:, 0:128], st[:, 0:128], ins=[stn], outs=[bn])
            k.ve(lambda h, tok=tok, bank=bank, dc=dc: h.tensor_copy(
                tok[:, dc * 128:(dc + 1) * 128], bank[:, 0:128]), ins=[bn], outs=[tokn])
        k.dma("ge", ap["y"][tt * 128:(tt + 1) * 128, :], tok[:], ins=[tokn], outs=["d:y"])
    p.finish()

    with nc.Block() as block:
        @block.tensor
        def _(h):
            for t in p.thunks["te"]:
                t(h)

        @block.vector
        def _(h):
            for t in p.thunks["ve"]:
                t(h)

        @block.scalar
        def _(h):
            for t in p.thunks["se"]:
                t(h)

        @block.gpsimd
        def _(h):
            for t in p.thunks["ge"]:
                t(h)

        @block.sync
        def _(h):
            for t in p.thunks["sy"]:
                t(h)
    return nc


_CACHE = {}


def _get_nc():
    if NLAYERS not in _CACHE:
        stk = ExitStack()
        nc = bass.Bass("TRN2", target_bir_lowering=False, debug=False)
        _build(nc, stk)
        _CACHE[NLAYERS] = nc
    return _CACHE[NLAYERS]


def kernel(**inputs):
    per_core = _host_prep(inputs)
    nc = _get_nc()
    res = run_bass_kernel_spmd(nc, per_core, core_ids=list(range(NCORES)))
    out = np.empty((4, 4096, 1024), np.float32)
    for c in range(NCORES):
        b, h = c // 2, c % 2
        out[b, h * TC:(h + 1) * TC, :] = res.results[c]["y"]
    return out
